# version 47
# baseline (speedup 1.0000x reference)
import numpy as np
import ml_dtypes
import concourse.bass as bass
import concourse.mybir as mybir
from concourse.bass_utils import run_bass_kernel_spmd
from concourse.aot_env import allow_ici_all_to_all

F32 = mybir.dt.float32
BF16 = mybir.dt.bfloat16
AF = mybir.ActivationFunctionType
ALU = mybir.AluOpType

P = 128
D = 2048
DT = 16
TOK = 1024
TALL = 8192
NCORES = 8
S = 4096
BFNP = ml_dtypes.bfloat16
RSQ128 = float(np.log(1.0 / np.sqrt(128.0)))

ENGINES = ("sync", "tensor", "vector", "scalar", "gpsimd")


PARANOID = False
PARANOID_PHASES = {"fth"}


class Em:
    def __init__(self, nc):
        self.nc = nc
        self.ops = {e: [] for e in ENGINES}
        self.sems = {}
        self.cnt = {}
        self._ctxs = []
        self._hw = {}
        self.pslast = {}

    def sem(self, name):
        if name not in self.sems:
            cm = self.nc.semaphore(name)
            self.sems[name] = cm.__enter__()
            self._ctxs.append(cm)
            self.cnt[name] = 0
        return self.sems[name]

    def val(self, name):
        return self.cnt.get(name, 0)

    def op(self, eng, fn, waits=(), inc=None, inc_amt=1):
        if PARANOID or getattr(self, "phase", None) in PARANOID_PHASES:
            waits = list(waits) + [(n, v) for n, v in self.cnt.items() if v > 0]
        ws = [(s, v) for (s, v) in waits if v > 0]
        for s, _ in ws:
            self.sem(s)
        self.ops[eng].append((fn, ws, inc, inc_amt))
        if inc is not None:
            self.sem(inc)
            self.cnt[inc] += inc_amt
            return self.cnt[inc]
        return None

    def dma(self, eng, out, in_, inc, waits=(), **kw):
        def fn(q):
            return q.dma_start(out=out, in_=in_, **kw)
        return self.op(eng, fn, waits, inc, 16)

    def psw(self, *names):
        out = []
        for n in names:
            if n in self.pslast:
                out.append(self.pslast[n])
        return out

    def psr(self, name, sem):
        self.pslast[name] = (sem, self.val(sem))

    def replay(self):
        nc = self.nc
        with nc.Block() as block:
            for eng in ENGINES:
                if not self.ops[eng]:
                    continue

                def mk(eng):
                    def run(q):
                        for (fn, ws, inc, amt) in self.ops[eng]:
                            for (s, v) in ws:
                                key = (eng, s)
                                if self._hw.get(key, 0) < v:
                                    q.wait_ge(self.sems[s], v)
                                    self._hw[key] = v
                            inst = fn(q)
                            if inc is not None:
                                inst.then_inc(self.sems[inc], amt)
                    return run
                getattr(block, eng)(mk(eng))

    def close(self):
        for cm in reversed(self._ctxs):
            cm.__exit__(None, None, None)


def f32c(x):
    return np.ascontiguousarray(np.asarray(x), dtype=np.float32)



def host_constants():
    c = {}
    half = 64
    thetas = 10000.0 ** (-np.arange(half, dtype=np.float64) / half)
    ang = np.arange(S, dtype=np.float64)[None, :] * thetas[:, None]
    C = np.empty((P, S), np.float64)
    S2 = np.empty((P, S), np.float64)
    C[0::2, :] = np.cos(ang)
    C[1::2, :] = np.cos(ang)
    S2[0::2, :] = -np.sin(ang)
    S2[1::2, :] = np.sin(ang)
    c["rope_c"] = f32c(C)
    c["rope_s"] = f32c(S2)
    sw = np.zeros((P, P), np.float32)
    for i in range(0, P, 2):
        sw[i, i + 1] = 1.0
        sw[i + 1, i] = 1.0
    c["swap_pm"] = sw
    c["ident"] = np.eye(P, dtype=np.float32)
    c["ones_col"] = np.ones((P, 1), np.float32)
    c["ones_row"] = np.ones((1, P), np.float32)
    ii = np.arange(P)
    mD = np.where(ii[None, :] >= ii[:, None], 0.0, -1e30).astype(np.float32)
    c["maskD"] = np.tile(mD, (1, 8))
    mD2 = np.where(ii[:, None] > ii[None, :], 0.0, -1e30).astype(np.float32)
    c["maskD2"] = np.tile(mD2, (1, 8))
    m01 = np.where(ii[None, :] > ii[:, None], 1.0, 0.0).astype(np.float32)
    c["mask01u"] = np.tile(m01, (1, 8))
    c["eyeR"] = np.tile(np.eye(P, dtype=np.float32), (1, 8))
    cm = np.empty((P, 4 * 512), np.float32)
    for r in range(4):
        cm[:, r * 512:(r + 1) * 512] = np.where(
            np.arange(512)[None, :] >= (r * P + ii[:, None]), 0.0, -1e30)
    c["cmask"] = cm
    bn = np.zeros((4, 512), np.float32)
    for r in range(4):
        bn[r, r * P:(r + 1) * P] = -1.0
    c["blockneg"] = bn
    return c


def lhsT_image(w):
    K, M = w.shape
    KT, MT = K // P, M // P
    t = w.reshape(KT, P, MT, P).transpose(1, 2, 0, 3)
    return f32c(t.reshape(P, MT * KT * P))


def rhs_image(w):
    K, N = w.shape
    KT = K // P
    t = w.reshape(KT, P, N).transpose(1, 0, 2)
    return f32c(t.reshape(P, KT * N))


def col_image(v):
    n = v.shape[0]
    return f32c(v.reshape(n // P, P).T)


def prep_inputs(I):
    cst = host_constants()
    maps = []
    x = f32c(I["L0_attn_a_x"]).reshape(TALL, D)
    for c in range(NCORES):
        m = {}
        m["x_fm"] = f32c(x[c * TOK:(c + 1) * TOK, :].T)
        m["sel"] = f32c(np.tile(np.eye(2, dtype=np.float32)[c // 4], (P, 1)))
        for k, v in cst.items():
            m["c_" + k] = v
        kh = [2 * c, 2 * c + 1]
        vh = [4 * c + i for i in range(4)]
        wqkv = f32c(I["L0_attn_a_w_qkv"])
        qc = np.concatenate([wqkv[:, 128 * h:128 * (h + 1)] for h in kh], 1)
        kc = np.concatenate([wqkv[:, 2048 + 128 * h:2048 + 128 * (h + 1)]
                             for h in kh], 1)
        vc = np.concatenate([wqkv[:, 4096 + 128 * h:4096 + 128 * (h + 1)]
                             for h in vh], 1)
        wz = f32c(I["L0_attn_a_w_z"])
        zc = np.concatenate([wz[:, 128 * h:128 * (h + 1)] for h in vh], 1)
        m["wsA_img"] = lhsT_image(np.concatenate([qc, kc, vc, zc], 1))
        cw = f32c(I["L0_attn_a_conv_w"])
        rows = ([cw[128 * h:128 * (h + 1), :] for h in kh]
                + [cw[2048 + 128 * h:2048 + 128 * (h + 1), :] for h in kh]
                + [cw[4096 + 128 * h:4096 + 128 * (h + 1), :] for h in vh])
        m["cw_img"] = f32c(np.concatenate(rows, 1))
        wb = f32c(I["L0_attn_a_w_b"])[:, vh]
        wa = f32c(I["L0_attn_a_w_a"])[:, vh]
        m["wb_img"] = f32c(wb.reshape(DT, P, 4).transpose(1, 0, 2).reshape(P, DT * 4))
        m["wa_img"] = f32c(wa.reshape(DT, P, 4).transpose(1, 0, 2).reshape(P, DT * 4))
        m["dtb_col"] = f32c(I["L0_attn_a_dt_bias"])[vh].reshape(4, 1)
        m["alog_col"] = f32c(I["L0_attn_a_a_log"])[vh].reshape(4, 1)
        m["normw_col"] = f32c(I["L0_attn_a_norm_w"]).reshape(P, 1)
        m["woutA_img"] = lhsT_image(f32c(I["L0_attn_a_w_out"]))
        bh = [2 * c, 2 * c + 1]
        wq = f32c(I["L0_attn_b_w_qkv"])
        qb = np.concatenate([wq[128 * h:128 * (h + 1), :].T for h in bh], 1)
        kb = np.concatenate([wq[2048 + 128 * h:2048 + 128 * (h + 1), :].T
                             for h in bh], 1)
        vb = np.concatenate([wq[4096 + 128 * h:4096 + 128 * (h + 1), :].T
                             for h in bh], 1)
        m["wsB_img"] = lhsT_image(np.concatenate([qb, kb], 1))
        m["wvB_img"] = rhs_image(f32c(vb))
        bq = f32c(I["L0_attn_b_b_qkv"])
        bqc = np.concatenate([bq[128 * h:128 * (h + 1)] for h in bh]
                             + [bq[2048 + 128 * h:2048 + 128 * (h + 1)]
                                for h in bh])
        m["bqk_col"] = f32c(bqc.reshape(4, P).T)
        m["bvB_row"] = f32c(np.concatenate(
            [bq[4096 + 128 * h:4096 + 128 * (h + 1)] for h in bh]).reshape(1, 256))
        m["woutB_img"] = lhsT_image(f32c(I["L0_attn_b_w_out"]).T)
        m["boutB_col"] = col_image(f32c(I["L0_attn_b_b_out"]))
        m["gatew_img"] = lhsT_image(f32c(I["L0_ffn_gate_w"]))
        m["gateb_col"] = col_image(f32c(I["L0_ffn_gate_b"]))
        hsl = slice(c * 1024, (c + 1) * 1024)
        m["d1_img"] = rhs_image(f32c(I["L0_ffn_d1_w"])[:, hsl])
        m["d1b_row"] = f32c(I["L0_ffn_d1_b"])[hsl].reshape(1, 1024)
        m["d2_img"] = lhsT_image(f32c(I["L0_ffn_d2_w"])[hsl, :])
        m["d2b_col"] = col_image(f32c(I["L0_ffn_d2_b"]))
        asl = slice(c * 256, (c + 1) * 256)
        m["agg_img"] = rhs_image(f32c(I["L0_ffn_agg_w"])[:, asl])
        m["aggb_row"] = f32c(np.tile(I["L0_ffn_agg_b"][asl].reshape(1, 256), (2, 1)))
        m["bc_img"] = rhs_image(f32c(I["L0_ffn_bc_w"])[:, asl])
        m["bcb_row"] = f32c(np.tile(I["L0_ffn_bc_b"][asl].reshape(1, 256), (2, 1)))
        for nm, key in (("ing", "L0_ffn_in_gamma"), ("inb", "L0_ffn_in_beta"),
                        ("png", "L0_ffn_pn_gamma"), ("pnb", "L0_ffn_pn_beta"),
                        ("outg", "L0_ffn_out_gamma"), ("outb", "L0_ffn_out_beta")):
            m[nm + "_col"] = col_image(f32c(I[key]))
        maps.append(m)
    return maps



class Bufs:
    pass


def make_b(nc, in_arrs):
    b = Bufs()
    b.nc = nc
    b.inp = {}
    for name, arr in in_arrs.items():
        b.inp[name] = nc.dram_tensor(name, list(arr.shape), F32,
                                     kind="ExternalInput").ap()
    b.out_fm = nc.dram_tensor("out_fm", [D, TOK], F32, kind="ExternalOutput").ap()
    dr = nc.dram_tensor
    b.gin = dr("gin", [D, TOK], BF16).ap()
    b.gxn = dr("gxn", [NCORES * D, TOK], BF16).ap()
    b.qk_scr = dr("qk_scr", [512, TALL], BF16).ap()
    b.v_scr = dr("v_scr", [512, TALL], BF16).ap()
    b.z_scr = dr("z_scr", [512, TALL], BF16).ap()
    b.ba_scr = dr("ba_scr", [8, TALL], F32).ap()
    b.krstd_scr = dr("krstd_scr", [2, TALL], F32).ap()
    b.rows_scr = dr("rows_scr", [32, 4 * 8 * P], BF16).ap()
    b.cols_scr = dr("cols_scr", [32, 5 * 8 * P], BF16).ap()
    b.oa_in = dr("oa_in", [NCORES * 512, TOK], BF16).ap()
    b.oa_out = dr("oa_out", [NCORES * 512, TOK], BF16).ap()
    b.qb_scr = dr("qb_scr", [12 * P, S], BF16).ap()
    b.ob_in = dr("ob_in", [NCORES * 256, TOK], BF16).ap()
    b.ob_out = dr("ob_out", [NCORES * 256, TOK], BF16).ap()
    b.th_in = dr("th_in", [1, D], BF16).ap()
    b.th_out = dr("th_out", [NCORES, D], BF16).ap()
    b.gatew_bf = dr("gatew_bf", [P, 32768], BF16).ap()
    b.d1_bf = dr("d1_bf", [P, 16384], BF16).ap()
    b.d2_bf = dr("d2_bf", [P, 16384], BF16).ap()
    b.agg_bf = dr("agg_bf", [P, 4096], BF16).ap()
    b.bc_bf = dr("bc_bf", [P, 4096], BF16).ap()
    b.d1b_bf = dr("d1b_bf", [1, 1024], BF16).ap()
    b.ar_in = dr("ar_in", [2, D], BF16).ap()
    b.ar_out = dr("ar_out", [2, D], BF16).ap()
    b.ag2_in = dr("ag2_in", [2, 256], BF16).ap()
    b.ag2_out = dr("ag2_out", [NCORES * 2, 256], BF16).ap()

    al = nc.alloc_sbuf_tensor
    b.xres = al("xres", [P, DT * TOK], F32)
    b.A = al("A", [P, 16384], BF16)
    b.Bb = al("Bb", [P, 16384], BF16)
    b.Cc = al("Cc", [P, 16384], BF16)
    b.Ee = al("Ee", [P, 8192], BF16)
    b.Dd = al("Dd", [P, 3712], F32)
    b.rowbf = al("rowbf", [1, 2048], BF16)
    b.ones_col = al("ones_col", [P, 1], BF16)
    b.ones_row = al("ones_row", [1, P], BF16)
    b.ident = al("ident", [P, P], BF16)
    b.swap_pm = al("swap_pm", [P, P], BF16)
    b.smallf = al("smallf", [P, 160], F32)
    b.cw_sb = al("cw_sb", [P, 32], F32)
    b.dtb_sb = al("dtb_sb", [4, 1], F32)
    b.alog_sb = al("alog_sb", [4, 1], F32)
    b.nealog = al("nealog", [4, 1], F32)
    b.aggb_sb = al("aggb_sb", [2, 256], BF16)
    b.bcb_sb = al("bcb_sb", [2, 256], BF16)
    b.bvB_sb = al("bvB_sb", [1, 256], BF16)
    b.wba_sb = al("wba_sb", [P, 128], BF16)
    b.th_fm = al("th_fm", [P, 32], F32)
    b.th_sc = al("th_sc", [P, 32], F32)
    b.th_bf = al("th_bf", [P, 32], BF16)
    b.sq_bf = al("sq_bf", [P, 32], BF16)
    b.r4lo = al("r4lo", [4, 512], BF16)
    b.h1_bf = al("h1_bf", [P, 16], BF16)
    b.h1_tm = al("h1_tm", [2, 1024], BF16)
    b.th_tm2 = al("th_tm2", [2, D], BF16)
    b.bcv_fm = al("bcv_fm", [P, 32], F32)
    b.bcsel = al("bcsel", [P, 16], BF16)

    ap_ = nc.alloc_psum_tensor
    for nm in "ABCDEFG":
        setattr(b, "ps" + nm, ap_("ps" + nm, [P, 512], F32))
    b.psT = ap_("psT", [P, 1024], BF16)
    return b


def load_smalls(em, b):
    i = b.inp
    em.dma("gpsimd", b.ones_col[:, :], i["c_ones_col"][:, :], "gd")
    em.dma("gpsimd", b.ones_row[:, :], i["c_ones_row"][:, :], "gd")
    em.dma("gpsimd", b.ident[:, :], i["c_ident"][:, :], "gd")
    em.dma("gpsimd", b.swap_pm[:, :], i["c_swap_pm"][:, :], "gd")
    em.dma("gpsimd", b.aggb_sb[:, :], i["aggb_row"][:, :], "gd")
    em.dma("gpsimd", b.bcb_sb[:, :], i["bcb_row"][:, :], "gd")
    em.dma("gpsimd", b.bvB_sb[:, :], i["bvB_row"][:, :], "gd")
    em.dma("gpsimd", b.wba_sb[:, 0:64], i["wb_img"][:, :], "gd")
    em.dma("gpsimd", b.wba_sb[:, 64:128], i["wa_img"][:, :], "gd")
    em.dma("sync", b.cw_sb[:, :], i["cw_img"][:, :], "sd")
    em.dma("sync", b.dtb_sb[:, :], i["dtb_col"][:, :], "sd")
    em.dma("sync", b.alog_sb[:, :], i["alog_col"][:, :], "sd")
    sm = b.smallf
    em.dma("sync", sm[:, 0:16], i["gateb_col"][:, :], "sd")
    em.dma("sync", sm[:, 16:32], i["outg_col"][:, :], "sd")
    em.dma("sync", sm[:, 32:48], i["outb_col"][:, :], "sd")
    em.dma("sync", sm[:, 48:64], i["boutB_col"][:, :], "sd")
    em.dma("sync", sm[:, 64:80], i["d2b_col"][:, :], "sd")
    em.dma("sync", sm[:, 80:96], i["ing_col"][:, :], "sd")
    em.dma("sync", sm[:, 96:112], i["inb_col"][:, :], "sd")
    em.dma("sync", sm[:, 112:128], i["png_col"][:, :], "sd")
    em.dma("sync", sm[:, 128:144], i["pnb_col"][:, :], "sd")
    em.dma("sync", sm[:, 144:145], i["normw_col"][:, :], "sd")
    em.dma("sync", sm[:, 145:149], i["bqk_col"][:, :], "sd")
    em.dma("sync", sm[:, 150:152], i["sel"][:, :], "sd")



def _ln_rows_and_bcast(em, b, eps):
    r_mu = b.Dd[0:1, 0:1024]
    r_var = b.Dd[0:1, 1024:2048]
    r_m2 = b.Dd[0:1, 2048:3072]
    st_v = em.val("te")
    em.op("vector", lambda q: q.tensor_scalar(r_mu[0:1, 0:512], b.psA[0:1, 0:512],
          1.0 / D, None, ALU.mult),
          waits=[("te", st_v), ("ae", em.val("ae"))], inc="ve")
    em.op("vector", lambda q: q.tensor_scalar(r_mu[0:1, 512:1024],
          b.psB[0:1, 0:512], 1.0 / D, None, ALU.mult), inc="ve")
    em.op("vector", lambda q: q.tensor_scalar(r_var[0:1, 0:512], b.psC[0:1, 0:512],
          1.0 / D, eps, ALU.mult, ALU.add), inc="ve")
    em.op("vector", lambda q: q.tensor_scalar(r_var[0:1, 512:1024],
          b.psD[0:1, 0:512], 1.0 / D, eps, ALU.mult, ALU.add), inc="ve")
    em.op("vector", lambda q: q.tensor_mul(r_m2[0:1, :], r_mu[0:1, :],
          r_mu[0:1, :]), inc="ve")
    em.op("vector", lambda q: q.tensor_sub(r_var[0:1, :], r_var[0:1, :],
          r_m2[0:1, :]), inc="ve")
    for nm in "ABCD":
        em.psr("ps" + nm, "ve")
    var_v = em.val("ve")
    em.op("scalar", lambda q: q.activation(r_var[0:1, :], r_var[0:1, :], AF.Ln),
          waits=[("ve", var_v)], inc="ae")
    ln_v = em.val("ae")
    em.op("scalar", lambda q: q.activation(r_var[0:1, :], r_var[0:1, :], AF.Exp,
          scale=-0.5), waits=[("ae", ln_v)], inc="ae")
    rstd_v = em.val("ae")
    em.op("vector", lambda q: q.tensor_copy(b.rowbf[0:1, 0:1024], r_mu[0:1, :]),
          waits=[("ae", rstd_v), ("te", em.val("te"))], inc="ve")
    em.op("vector", lambda q: q.tensor_copy(b.rowbf[0:1, 1024:2048],
          r_var[0:1, :]), inc="ve")
    rbv = em.val("ve")
    for h in range(2):
        em.op("tensor", lambda q, h=h: q.matmul(
            (b.psE if h == 0 else b.psF)[:, 0:512], b.ones_row[0:1, :],
            b.rowbf[0:1, h * 512:(h + 1) * 512], start=True, stop=True),
            waits=[("ve", rbv)] + em.psw("psE" if h == 0 else "psF"), inc="te")
        em.op("tensor", lambda q, h=h: q.matmul(
            (b.psG if h == 0 else b.psA)[:, 0:512], b.ones_row[0:1, :],
            b.rowbf[0:1, 1024 + h * 512:1024 + (h + 1) * 512], start=True,
            stop=True), waits=em.psw("psG" if h == 0 else "psA"), inc="te")
    bc_v = em.val("te")
    mubc = b.Ee[:, 4096:5120]
    rstdbc = b.Ee[:, 5120:6144]
    em.op("vector", lambda q: q.tensor_copy(mubc[:, 0:512], b.psE[:, 0:512]),
          waits=[("te", bc_v)], inc="ve")
    em.op("vector", lambda q: q.tensor_copy(mubc[:, 512:1024], b.psF[:, 0:512]),
          inc="ve")
    em.op("vector", lambda q: q.tensor_copy(rstdbc[:, 0:512], b.psG[:, 0:512]),
          inc="ve")
    em.op("vector", lambda q: q.tensor_copy(rstdbc[:, 512:1024],
          b.psA[:, 0:512]), inc="ve")
    for nm in "AEFG":
        em.psr("ps" + nm, "ve")
    return em.val("ve")


def emit_ln(em, b, xn_dst, gin=None, eps=1e-5):
    xbf = lambda i: b.Ee[:, (i % 2) * 1024:(i % 2) * 1024 + 1024]
    xsq = lambda i: b.Ee[:, 2048 + (i % 2) * 1024:2048 + (i % 2) * 1024 + 1024]
    mm_wait = em.psw("psA", "psB", "psC", "psD")
    ee_war = [("te", em.val("te")), ("gd", em.val("gd")),
              ("sd", em.val("sd")), ("ae", em.val("ae"))]
    for d in range(DT):
        sl = slice(d * TOK, (d + 1) * TOK)
        wv = [("te", em.val("te"))] if d >= 2 else list(ee_war)
        em.op("vector", lambda q, o=xbf(d), isl=sl: q.tensor_copy(o, b.xres[:, isl]),
              waits=wv, inc="ve")
        cast_v = em.val("ve")
        em.op("scalar", lambda q, o=xsq(d), i=xbf(d): q.activation(o, i, AF.Square),
              waits=[("ve", cast_v)] + ([("te", em.val("te"))] if d >= 2 else []),
              inc="ae")
        sq_v = em.val("ae")
        for h in range(2):
            hs = slice(h * 512, (h + 1) * 512)
            em.op("tensor", lambda q, d=d, h=h, i=xbf(d), hs=hs: q.matmul(
                (b.psA if h == 0 else b.psB)[0:1, 0:512], b.ones_col[:, 0:1],
                i[:, hs], start=(d == 0), stop=(d == DT - 1)),
                waits=[("ve", cast_v)] + (mm_wait if d == 0 else []), inc="te")
            em.op("tensor", lambda q, d=d, h=h, i=xsq(d), hs=hs: q.matmul(
                (b.psC if h == 0 else b.psD)[0:1, 0:512], b.ones_col[:, 0:1],
                i[:, hs], start=(d == 0), stop=(d == DT - 1)),
                waits=[("ae", sq_v)], inc="te")
    bcs = _ln_rows_and_bcast(em, b, eps)
    mubc = b.Ee[:, 4096:5120]
    rstdbc = b.Ee[:, 5120:6144]
    sd_last = em.val("sd")
    dst_war = [("te", em.val("te")), ("sd", em.val("sd")),
               ("gd", em.val("gd"))]
    for d in range(DT):
        sl = slice(d * TOK, (d + 1) * TOK)
        em.op("vector", lambda q, sl=sl: q.tensor_sub(xn_dst[:, sl],
              b.xres[:, sl], mubc[:, :]),
              waits=[("ve", bcs)] + (dst_war if d == 0 else []), inc="ve")
        em.op("vector", lambda q, sl=sl: q.tensor_mul(xn_dst[:, sl],
              xn_dst[:, sl], rstdbc[:, :]), inc="ve")
        if gin is not None:
            sd_last = em.dma("sync", gin[d * P:(d + 1) * P, :], xn_dst[:, sl],
                             "sd", waits=[("ve", em.val("ve"))])
    return em.val("ve"), sd_last


def emit_out_ln_add(em, b, src_bf, gci, bci, eps=1e-6, src_ready=()):
    sm = b.smallf
    sq = lambda i: b.Ee[:, 2048 + (i % 2) * 1024:2048 + (i % 2) * 1024 + 1024]
    mm_wait = em.psw("psA", "psB", "psC", "psD")
    for d in range(DT):
        sl = slice(d * TOK, (d + 1) * TOK)
        wv = [("te", em.val("te"))] if d >= 2 else []
        em.op("scalar", lambda q, o=sq(d), sl=sl: q.activation(o, src_bf[:, sl],
              AF.Square), waits=wv + list(src_ready), inc="ae")
        sq_v = em.val("ae")
        for h in range(2):
            hs = slice(h * 512, (h + 1) * 512)
            em.op("tensor", lambda q, d=d, h=h, sl=sl, hs=hs: q.matmul(
                (b.psA if h == 0 else b.psB)[0:1, 0:512], b.ones_col[:, 0:1],
                src_bf[:, sl][:, hs], start=(d == 0), stop=(d == DT - 1)),
                waits=(mm_wait if d == 0 else []) + (list(src_ready) if d == 0 else []),
                inc="te")
            em.op("tensor", lambda q, d=d, h=h, o=sq(d), hs=hs: q.matmul(
                (b.psC if h == 0 else b.psD)[0:1, 0:512], b.ones_col[:, 0:1],
                o[:, hs], start=(d == 0), stop=(d == DT - 1)),
                waits=[("ae", sq_v)], inc="te")
    bcs = _ln_rows_and_bcast(em, b, eps)
    mubc = b.Ee[:, 4096:5120]
    rstdbc = b.Ee[:, 5120:6144]
    tmp = lambda i: b.Ee[:, (i % 2) * 1024:(i % 2) * 1024 + 1024]
    for d in range(DT):
        sl = slice(d * TOK, (d + 1) * TOK)
        em.op("vector", lambda q, d=d, sl=sl: q.tensor_sub(tmp(d), src_bf[:, sl],
              mubc[:, :]), waits=[("ve", bcs)], inc="ve")
        em.op("vector", lambda q, d=d: q.tensor_mul(tmp(d), tmp(d), rstdbc[:, :]),
              inc="ve")
        em.op("vector", lambda q, d=d, sl=sl: q.scalar_tensor_tensor(
            b.xres[:, sl], tmp(d), sm[:, gci + d:gci + d + 1], b.xres[:, sl],
            ALU.mult, ALU.add), inc="ve")
        em.op("vector", lambda q, d=d, sl=sl: q.tensor_scalar(
            b.xres[:, sl], b.xres[:, sl], sm[:, bci + d:bci + d + 1], None,
            ALU.add), inc="ve")
    return em.val("ve")


def emit_collective(em, b, kind, cc_in, cc_out, wait_sems, op=None):
    op = op or (ALU.add if kind == "AllReduce" else ALU.bypass)

    def fn(q):
        if kind == "AllToAll":
            with allow_ici_all_to_all():
                return q.collective_compute(
                    kind, op, replica_groups=[list(range(NCORES))],
                    ins=[cc_in.tensor.ap().opt()],
                    outs=[cc_out.tensor.ap().opt()])
        return q.collective_compute(
            kind, op, replica_groups=[list(range(NCORES))],
            ins=[cc_in.tensor.ap().opt()], outs=[cc_out.tensor.ap().opt()])
    em.op("gpsimd", fn, waits=wait_sems, inc="cc")
    return em.val("cc")



def ap_mid_bcast(x, n):
    a = [list(p) for p in x.ap]
    return bass.AP(x.tensor, x.offset, [a[0], [0, n], a[1]])


def emit_small_ln_fm(em, b, src, dst, gci, bci, eps, first_te=None):
    em.op("vector", lambda q: q.tensor_copy(b.th_bf[:, :], src[:, :]),
          waits=[("te", em.val("te") if first_te is None else first_te)],
          inc="ve")
    cast_v = em.val("ve")
    em.op("vector", lambda q: q.tensor_mul(b.sq_bf[:, :], b.th_bf[:, :],
          b.th_bf[:, :]), inc="ve")
    sq_v = em.val("ve")
    em.op("tensor", lambda q: q.matmul(
        b.psG[0:1, 0:32], b.ones_col[:, 0:1], b.th_bf[:, :],
        start=True, stop=True),
        waits=[("ve", cast_v)] + em.psw("psG"), inc="te")
    em.op("tensor", lambda q: q.matmul(
        b.psE[0:1, 0:32], b.ones_col[:, 0:1], b.sq_bf[:, :],
        start=True, stop=True),
        waits=[("ve", sq_v)] + em.psw("psE"), inc="te")
    st_v = em.val("te")
    r = b.Dd[0:1, 2048:2056]
    em.op("vector", lambda q: q.tensor_reduce(
        r[0:1, 6:8], b.psG[0:1, 0:32].rearrange("p (d t) -> p t d", t=2),
        mybir.AxisListType.X, ALU.add),
        waits=[("te", st_v), ("ae", em.val("ae"))], inc="ve")
    em.op("vector", lambda q: q.tensor_reduce(
        r[0:1, 2:4], b.psE[0:1, 0:32].rearrange("p (d t) -> p t d", t=2),
        mybir.AxisListType.X, ALU.add), inc="ve")
    red_v = em.val("ve")
    em.op("vector", lambda q: q.tensor_scalar(r[0:1, 0:2], r[0:1, 6:8],
          1.0 / D, None, ALU.mult), waits=[("ve", red_v)], inc="ve")
    em.op("vector", lambda q: q.tensor_scalar(r[0:1, 2:4], r[0:1, 2:4],
          1.0 / D, eps, ALU.mult, ALU.add), inc="ve")
    em.op("vector", lambda q: q.tensor_mul(r[0:1, 4:6], r[0:1, 0:2],
          r[0:1, 0:2]), inc="ve")
    em.op("vector", lambda q: q.tensor_sub(r[0:1, 2:4], r[0:1, 2:4],
          r[0:1, 4:6]), inc="ve")
    em.psr("psG", "ve")
    em.psr("psE", "ve")
    rv = em.val("ve")
    em.op("scalar", lambda q: q.activation(r[0:1, 2:4], r[0:1, 2:4], AF.Ln),
          waits=[("ve", rv)], inc="ae")
    ln_v = em.val("ae")
    em.op("scalar", lambda q: q.activation(r[0:1, 2:4], r[0:1, 2:4], AF.Exp,
          scale=-0.5), waits=[("ae", ln_v)], inc="ae")
    rstd_v = em.val("ae")
    em.op("vector", lambda q: q.tensor_copy(b.rowbf[0:1, 0:2], r[0:1, 0:2]),
          waits=[("ae", rstd_v), ("te", em.val("te"))], inc="ve")
    em.op("vector", lambda q: q.tensor_copy(b.rowbf[0:1, 2:4], r[0:1, 2:4]),
          inc="ve")
    rbv = em.val("ve")
    em.op("tensor", lambda q: q.matmul(b.psG[:, 0:2], b.ones_row[0:1, :],
          b.rowbf[0:1, 0:2], start=True, stop=True),
          waits=[("ve", rbv)] + em.psw("psG"), inc="te")
    em.op("tensor", lambda q: q.matmul(b.psF[:, 0:2], b.ones_row[0:1, :],
          b.rowbf[0:1, 2:4], start=True, stop=True),
          waits=em.psw("psF"), inc="te")
    bc_v = em.val("te")
    dst3 = dst[:, :].rearrange("p (d t) -> p d t", t=2)
    src3 = src[:, :].rearrange("p (d t) -> p d t", t=2)
    mu2 = b.Dd[:, 2056:2058]
    rs2 = b.Dd[:, 2058:2060]
    em.op("vector", lambda q: q.tensor_copy(mu2, b.psG[:, 0:2]),
          waits=[("te", bc_v)], inc="ve")
    em.op("vector", lambda q: q.tensor_copy(rs2, b.psF[:, 0:2]), inc="ve")
    em.psr("psG", "ve")
    em.psr("psF", "ve")
    ev_v = em.val("ve")
    em.op("vector", lambda q: q.tensor_tensor(
        dst3, src3, ap_mid_bcast(mu2, 16), ALU.subtract),
        waits=[("ve", ev_v)], inc="ve")
    em.op("vector", lambda q: q.tensor_tensor(
        dst3, dst3, ap_mid_bcast(rs2, 16), ALU.mult), inc="ve")
    em.op("vector", lambda q: q.tensor_tensor(
        dst3, dst3, _colbc(b, gci), ALU.mult), inc="ve")
    em.op("vector", lambda q: q.tensor_tensor(dst3, dst3, _colbc(b, bci),
          ALU.add), inc="ve")
    return em.val("ve")


def _colbc(b, ci):
    ap = b.smallf[:, ci:ci + 16]
    return ap.broadcast_to([P, 16, 2])



def emit_ffn(em, b, tag):
    i = b.inp
    sm = b.smallf
    em.phase = "fln"
    ve_ln, _ = emit_ln(em, b, b.A, gin=None, eps=1e-5)
    pre_fg_te = em.val("te")
    em.phase = "fth"
    sd_agg = em.dma("gpsimd", b.Ee[:, 4096:8192], b.agg_bf[:, :], "gd",
                    waits=[("te", pre_fg_te), ("ve", em.val("ve"))])
    for d in range(DT):
        em.dma("sync", b.th_in[0, d * P:(d + 1) * P],
               b.A[:, d * TOK + 1023:d * TOK + 1024], "sd",
               waits=[("ve", ve_ln)])
    cc_th = emit_collective(em, b, "AllGather", b.th_in, b.th_out,
                            [("sd", em.val("sd"))])
    sd_d1b = em.dma("gpsimd", b.rowbf[0:1, 1024:2048], b.d1b_bf[0:1, :],
                    "gd", waits=[("te", pre_fg_te), ("ve", em.val("ve"))])
    sd_d1 = em.dma("gpsimd", b.Cc[:, :], b.d1_bf[:, :], "gd",
                   waits=[("te", pre_fg_te), ("ve", em.val("ve"))])
    em.dma("sync", b.th_tm2[0:1, :], b.th_out[3, :], "sd",
           waits=[("cc", cc_th)])
    em.dma("sync", b.th_tm2[1:2, :], b.th_out[7, :], "sd")
    sd_xl = em.val("sd")
    gwbuf = lambda m: b.Ee[:, (m % 2) * 2048:(m % 2) * 2048 + 2048]
    ee_free = em.val("te")
    cons = {}

    def fg_part(m0, m1):
        em.phase = "fg"
        for m in range(m0, m1):
            wv = [("te", cons.get(m - 2, ee_free))]
            sc_v = em.op("scalar", lambda q, m=m: q.dma_start(
                out=gwbuf(m), in_=b.gatew_bf[:, m * 2048:(m + 1) * 2048]),
                waits=wv + [("gd", b.precast_v)] +
                      ([("ve", em.val("ve"))] if m == 0 else []),
                inc="sc", inc_amt=16)
            for h in range(2):
                ps = b.psC if h == 0 else b.psD
                psn = "psC" if h == 0 else "psD"
                for k in range(DT):
                    em.op("tensor", lambda q, m=m, h=h, k=k, ps=ps: q.matmul(
                        ps[:, 0:512], gwbuf(m)[:, k * P:(k + 1) * P],
                        b.A[:, k * TOK + h * 512:k * TOK + (h + 1) * 512],
                        start=(k == 0), stop=(k == DT - 1)),
                        waits=[("sc", sc_v), ("ve", ve_ln)] +
                              (em.psw(psn) if k == 0 else []), inc="te")
                te_v = em.val("te")
                cons[m] = te_v
                em.op("scalar", lambda q, m=m, h=h, ps=ps: q.activation(
                    b.Bb[:, m * TOK + h * 512:m * TOK + (h + 1) * 512],
                    ps[:, 0:512],
                    AF.Sigmoid, bias=sm[:, m:m + 1], scale=1.0),
                    waits=[("te", te_v)] +
                          ([("ve", em.val("ve"))] if (m == 0 and h == 0)
                           else []),
                    inc="ae")
                em.psr(psn, "ae")
        em.phase = "fth"

    fg_part(0, 6)
    _tm_to_fm(em, b, b.th_tm2, b.th_sc, DT, [("sd", sd_xl)])
    emit_small_ln_fm(em, b, b.th_sc, b.th_fm, 80, 96, eps=1e-6)
    _thought_proj(em, b, "agg_img", b.aggb_sb, tag + "agg",
                  img_ready=[("gd", sd_agg)])
    fg_part(6, DT)
    gate_done = em.val("ae")
    em.phase = "fth"
    for s_ in range(4):
        _thought_step(em, b, sd_d1, tag, s_)
    _thought_proj(em, b, "bc_img", b.bcb_sb, tag + "bc", to_bcv=True)
    bcv_v = em.val("ve")
    em.op("vector", lambda q: q.tensor_scalar(
        b.bcsel[:, :], b.bcv_fm[:, 0:32:2], sm[:, 150:151], None, ALU.mult),
        waits=[("ve", bcv_v)], inc="ve")
    em.op("vector", lambda q: q.scalar_tensor_tensor(
        b.bcsel[:, :], b.bcv_fm[:, 1:32:2], sm[:, 151:152], b.bcsel[:, :],
        ALU.mult, ALU.add), inc="ve")
    bcsel_v = em.val("ve")
    em.phase = "fc"
    for d in range(DT):
        sl = slice(d * TOK, (d + 1) * TOK)
        em.op("vector", lambda q, d=d, sl=sl: q.scalar_tensor_tensor(
            b.Cc[:, sl], b.Bb[:, sl], b.bcsel[:, d:d + 1], b.A[:, sl],
            ALU.mult, ALU.add),
            waits=[("ve", bcsel_v), ("ae", gate_done)], inc="ve")
    emit_out_ln_add(em, b, b.Cc, 16, 32, eps=1e-6,
                    src_ready=[("ve", em.val("ve"))])


def _tm_to_fm(em, b, src_tm, dst_fm, ntiles, waits):
    for d in range(ntiles):
        em.op("tensor", lambda q, d=d: q.transpose(
            b.psT[:, d * 2:d * 2 + 2], src_tm[0:2, d * P:(d + 1) * P],
            b.ident[0:2, 0:2]),
            waits=(waits if d == 0 else []) + (em.psw("psT") if d == 0 else []),
            inc="te")
    tr_v = em.val("te")
    em.op("vector", lambda q: q.tensor_copy(dst_fm[:, 0:2 * ntiles],
          b.psT[:, 0:2 * ntiles]), waits=[("te", tr_v)], inc="ve")
    em.psr("psT", "ve")
    return em.val("ve")


def _thought_proj(em, b, img, bias_row, tag, to_bcv=False, img_ready=None):
    em.op("vector", lambda q: q.tensor_copy(b.th_bf[:, :], b.th_fm[:, :]),
          waits=[("te", em.val("te"))], inc="ve")
    tbv = em.val("ve")
    if img_ready is None:
        src = b.agg_bf if img == "agg_img" else b.bc_bf
        sd_v = em.dma(
            "gpsimd", b.Ee[:, 4096:8192], src[:, :],
            "gd", waits=[("te", em.val("te")), ("ve", em.val("ve"))])
        img_ready = [("gd", sd_v)]
    for k in range(DT):
        em.op("tensor", lambda q, k=k: q.matmul(
            b.psB[0:2, 0:256], b.th_bf[:, k * 2:(k + 1) * 2],
            b.Ee[:, 4096 + k * 256:4096 + (k + 1) * 256],
            start=(k == 0), stop=(k == DT - 1)),
            waits=[("ve", tbv)] + img_ready + (em.psw("psB") if k == 0 else []),
            inc="te")
    pj_v = em.val("te")
    em.op("vector", lambda q: q.tensor_copy(b.th_tm2[0:2, 0:256],
          b.psB[0:2, 0:256]), waits=[("te", pj_v)], inc="ve")

    em.psr("psB", "ve")
    em.op("vector", lambda q: q.tensor_tensor(
        b.th_tm2[0:2, 0:256], b.th_tm2[0:2, 0:256],
        bias_row[0:2, :], ALU.add), inc="ve")
    em.dma("sync", b.ag2_in[:, :], b.th_tm2[0:2, 0:256], "sd",
           waits=[("ve", em.val("ve"))])
    cc_v = emit_collective(em, b, "AllGather", b.ag2_in, b.ag2_out,
                           [("sd", em.val("sd"))])
    for r in range(NCORES):
        em.dma("sync", b.th_tm2[0:2, r * 256:(r + 1) * 256],
               b.ag2_out[2 * r:2 * r + 2, :], "sd", waits=[("cc", cc_v)])
    dst = b.bcv_fm if to_bcv else b.th_fm
    _tm_to_fm(em, b, b.th_tm2, dst, DT, [("sd", em.val("sd"))])


def _thought_step(em, b, sd_d1, tag, s_):
    i = b.inp
    sm = b.smallf
    emit_small_ln_fm(em, b, b.th_fm, b.th_sc, 112, 128, eps=1e-6)
    em.op("vector", lambda q: q.tensor_copy(b.th_bf[:, :], b.th_sc[:, :]),
          waits=[("te", em.val("te"))], inc="ve")
    tb = em.val("ve")
    for half in range(2):
        ps = b.psB if half == 0 else b.psG
        psn = "psB" if half == 0 else "psG"
        for k in range(DT):
            em.op("tensor", lambda q, k=k, half=half, ps=ps: q.matmul(
                ps[0:2, 0:512], b.th_bf[:, k * 2:(k + 1) * 2],
                b.Cc[:, k * 1024 + half * 512:k * 1024 + (half + 1) * 512],
                start=(k == 0), stop=False),
                waits=[("ve", tb), ("gd", sd_d1)] +
                      (em.psw(psn) if k == 0 else []))
        em.op("tensor", lambda q, half=half, ps=ps: q.matmul(
            ps[0:2, 0:512], b.ones_row[0:1, 0:2],
            b.rowbf[0:1, 1024 + half * 512:1024 + (half + 1) * 512],
            start=False, stop=True), inc="te")
    d1_done = em.val("te")
    em.op("scalar", lambda q: q.activation(b.h1_tm[0:2, 0:512],
          b.psB[0:2, 0:512], AF.Gelu), waits=[("te", d1_done)], inc="ae")
    em.op("scalar", lambda q: q.activation(b.h1_tm[0:2, 512:1024],
          b.psG[0:2, 0:512], AF.Gelu), inc="ae")
    em.psr("psB", "ae")
    em.psr("psG", "ae")
    g_v = em.val("ae")
    for dd in range(8):
        em.op("tensor", lambda q, dd=dd: q.transpose(
            b.psT[:, dd * 2:dd * 2 + 2], b.h1_tm[0:2, dd * P:(dd + 1) * P],
            b.ident[0:2, 0:2]),
            waits=[("ae", g_v)] + (em.psw("psT") if dd == 0 else []), inc="te")
    em.op("vector", lambda q: q.tensor_copy(b.h1_bf[:, :], b.psT[:, 0:16]),
          waits=[("te", em.val("te"))], inc="ve")
    em.psr("psT", "ve")
    h1f = em.val("ve")
    for grp in range(2):
        sd_v = em.dma("gpsimd", b.Ee[:, 0:8192],
                      b.d2_bf[:, grp * 8192:(grp + 1) * 8192], "gd",
                      waits=[("te", em.val("te")), ("ve", em.val("ve"))])
        for mm_ in range(8):
            m = grp * 8 + mm_
            ps = b.psE if (m % 2 == 0) else b.psF
            psn = "psE" if (m % 2 == 0) else "psF"
            for k in range(8):
                em.op("tensor", lambda q, mm_=mm_, k=k, ps=ps: q.matmul(
                    ps[0:2, 0:128], b.h1_bf[:, k * 2:(k + 1) * 2],
                    b.Ee[:, (mm_ * 8 + k) * P:(mm_ * 8 + k + 1) * P],
                    start=(k == 0), stop=(k == 7)),
                    waits=[("ve", h1f), ("gd", sd_v)] +
                          (em.psw(psn) if k == 0 else []), inc="te")
            te_v = em.val("te")
            em.op("vector", lambda q, m=m, ps=ps: q.tensor_copy(
                b.th_tm2[0:2, m * P:(m + 1) * P], ps[0:2, 0:128]),
                waits=[("te", te_v)], inc="ve")
            em.psr(psn, "ve")
    d2e = em.val("ve")
    em.dma("sync", b.ar_in[:, :], b.th_tm2[0:2, :], "sd", waits=[("ve", d2e)])
    cc_v = emit_collective(em, b, "AllReduce", b.ar_in, b.ar_out,
                           [("sd", em.val("sd"))])
    em.dma("sync", b.th_tm2[0:2, :], b.ar_out[0:2, :], "sd",
           waits=[("cc", cc_v)])
    sd_v = em.val("sd")
    for d in range(DT):
        em.op("tensor", lambda q, d=d: q.transpose(
            b.psT[:, d * 2:d * 2 + 2], b.th_tm2[0:2, d * P:(d + 1) * P],
            b.ident[0:2, 0:2]),
            waits=[("sd", sd_v)] + (em.psw("psT") if d == 0 else []), inc="te")
    tr_v = em.val("te")
    em.op("vector", lambda q: q.tensor_add(b.th_sc[:, :], b.psT[:, 0:32],
          b.th_fm[:, :]), waits=[("te", tr_v)], inc="ve")
    em.psr("psT", "ve")
    em.op("vector", lambda q: q.tensor_tensor(
        b.th_fm[:, :].rearrange("p (d t) -> p d t", t=2),
        b.th_sc[:, :].rearrange("p (d t) -> p d t", t=2),
        _colbc(b, 64), ALU.add), inc="ve")



def dup_ap(sl, nblk=4, dup=2, w=P):
    a = [list(p) for p in sl.ap]
    return bass.AP(sl.tensor, sl.offset, [a[0], [w, nblk], [0, dup], [1, w]])


def v3(sl, nu=8, w=P):
    return sl.rearrange("p (u w) -> p u w", w=w)


def emit_attn_a(em, b):
    i = b.inp
    sm = b.smallf
    em.phase = "a0"
    ve_ln, sd_ln = emit_ln(em, b, b.A, gin=b.gin, eps=1e-5)
    cc_g = emit_collective(em, b, "AllGather", b.gin, b.gxn, [("sd", sd_ln)])
    em.op("scalar", lambda q: q.activation(b.nealog[0:4, 0:1],
          b.alog_sb[0:4, 0:1], AF.Exp), inc="ae")
    em.op("vector", lambda q: q.tensor_scalar(b.nealog[0:4, 0:1],
          b.nealog[0:4, 0:1], -1.0, None, ALU.mult),
          waits=[("ae", em.val("ae"))], inc="ve")
    em.phase = "a1"
    CV = 4608
    SQ = 7680
    BC = 9728
    FO = 11776
    for pas in range(2):
        war = [("te", em.val("te")), ("ve", em.val("ve")), ("ae", em.val("ae")),
               ("gd", em.val("gd")), ("sd", em.val("sd"))]
        gd_w = em.dma("gpsimd", b.Bb[:, 0:12288],
                      i["wsA_img"][:, pas * 12288:(pas + 1) * 12288], "gd",
                      waits=war)
        pcons = {}
        for n in range(16):
            r, off = n // 2, (n % 2) * 512
            bi = n // 8
            xc = b.Ee[:, 0:8192] if n % 2 == 0 else b.A[:, 0:8192]
            xw = pcons.get(n - 2, [("cc", cc_g), ("te", em.val("te")),
                                   ("ae", em.val("ae")), ("ve", em.val("ve")),
                                   ("sd", em.val("sd"))])
            sd_x = em.dma(
                "sync", xc.rearrange("p (k t) -> p k t", t=512),
                b.gxn[2048 * r:2048 * (r + 1), off:off + 512].rearrange(
                    "(k p) t -> p k t", p=P), "sd", waits=xw)
            if pas == 0:
                for (img0, ps, psn) in ((0, b.psE, "psE"), (64, b.psF, "psF")):
                    for k in range(DT):
                        em.op("tensor", lambda q, img0=img0, k=k, ps=ps, xc=xc:
                              q.matmul(
                            ps[0:4, 0:512], b.wba_sb[:, img0 + k * 4:img0 + (k + 1) * 4],
                            xc[:, k * 512:(k + 1) * 512],
                            start=(k == 0), stop=(k == DT - 1)),
                            waits=[("sd", sd_x)] +
                                  (em.psw(psn) if k == 0 else []), inc="te")
                ba_v = em.val("te")
                stg = b.Dd[0:4, 3072:3584]
                em.op("vector", lambda q: q.tensor_copy(stg, b.psE[0:4, 0:512]),
                      waits=[("te", ba_v), ("sd", em.val("sd"))], inc="ve")
                em.psr("psE", "ve")
                sd_b = em.dma("sync", b.ba_scr[0:4, n * 512:(n + 1) * 512], stg,
                              "sd", waits=[("ve", em.val("ve"))])
                em.op("vector", lambda q: q.tensor_copy(stg, b.psF[0:4, 0:512]),
                      waits=[("te", ba_v), ("sd", sd_b)], inc="ve")
                em.psr("psF", "ve")
                em.dma("sync", b.ba_scr[4:8, n * 512:(n + 1) * 512], stg,
                       "sd", waits=[("ve", em.val("ve"))])
            for m in range(6):
                gm = pas * 6 + m
                kind = ("q", "q", "k", "k", "v", "v")[m] if pas == 0 else                        ("v", "v", "z", "z", "z", "z")[m]
                ps, psn = (b.psA, "psA") if m % 2 == 0 else (b.psB, "psB")
                for k in range(DT):
                    em.op("tensor", lambda q, m=m, k=k, ps=ps, xc=xc: q.matmul(
                        ps[:, 0:512], b.Bb[:, (m * DT + k) * P:(m * DT + k + 1) * P],
                        xc[:, k * 512:(k + 1) * 512],
                        start=(k == 0), stop=(k == DT - 1)),
                        waits=[("gd", gd_w), ("sd", sd_x)] +
                              (em.psw(psn) if k == 0 else []), inc="te")
                mm_v = em.val("te")
                if kind == "z":
                    zslot = b.Cc[:, FO + m * 512:FO + (m + 1) * 512]
                    em.op("scalar", lambda q, ps=ps, zslot=zslot: q.activation(
                        zslot, ps[:, 0:512], AF.Silu),
                        waits=[("te", mm_v), ("sd", em.val("sd"))], inc="ae")
                    em.psr(psn, "ae")
                    zr = (gm - 8) * P
                    em.dma("sync", b.z_scr[zr:zr + P, n * 512:(n + 1) * 512],
                           zslot, "sd", waits=[("ae", em.val("ae"))])
                    continue
                rawb = b.Cc[:, m * 768:m * 768 + 515]
                cvi = gm if gm < 8 else None
                if n == 0 or n == 8:
                    em.op("vector", lambda q, rawb=rawb: q.memset(
                        rawb[:, 0:3], 0.0), waits=[("ae", em.val("ae"))], inc="ve")
                else:
                    em.op("vector", lambda q, rawb=rawb: q.tensor_copy(
                        rawb[:, 0:3], rawb[:, 512:515]),
                        waits=[("ae", em.val("ae")), ("gd", em.val("gd"))],
                        inc="ve")
                cw_v = em.val("ve")
                em.op("scalar", lambda q, ps=ps, rawb=rawb: q.activation(
                    rawb[:, 3:515], ps[:, 0:512], AF.Identity),
                    waits=[("te", mm_v), ("ve", cw_v)], inc="ae")
                em.psr(psn, "ae")
                ev_v = em.val("ae")
                cvo = b.Cc[:, CV + m * 512:CV + (m + 1) * 512]
                em.op("vector", lambda q, rawb=rawb, cvo=cvo, cvi=cvi:
                      q.tensor_scalar(cvo, rawb[:, 0:512],
                                      b.cw_sb[:, cvi * 4:cvi * 4 + 1], None,
                                      ALU.mult),
                      waits=[("ae", ev_v)], inc="ve")
                for j in range(1, 4):
                    em.op("vector", lambda q, rawb=rawb, cvo=cvo, cvi=cvi, j=j:
                          q.scalar_tensor_tensor(
                              cvo, rawb[:, j:512 + j],
                              b.cw_sb[:, cvi * 4 + j:cvi * 4 + j + 1], cvo,
                              ALU.mult, ALU.add), inc="ve")
                cv_v = em.val("ve")
                fo = b.Cc[:, FO + m * 512:FO + (m + 1) * 512]
                em.op("scalar", lambda q, cvo=cvo, fo=fo: q.activation(
                    fo, cvo, AF.Silu),
                    waits=[("ve", cv_v), ("sd", em.val("sd"))], inc="ae")
                si_v = em.val("ae")
                if kind == "v":
                    vr = ((gm - 4) if pas == 0 else (gm - 4)) * P
                    em.dma("sync", b.v_scr[vr:vr + P, n * 512:(n + 1) * 512],
                           fo, "sd", waits=[("ae", si_v)])
                    continue
                mq = m
                sq = b.Cc[:, SQ + mq * 512:SQ + (mq + 1) * 512]
                em.op("vector", lambda q, fo=fo, sq=sq: q.tensor_mul(sq, fo, fo),
                      waits=[("ae", si_v), ("te", em.val("te"))], inc="ve")
                sq_v = em.val("ve")
                em.op("tensor", lambda q, sq=sq: q.matmul(
                    b.psC[0:1, 0:512], b.ones_col[:, 0:1], sq,
                    start=True, stop=True),
                    waits=[("ve", sq_v)] + em.psw("psC"), inc="te")
                ss_v = em.val("te")
                rr = b.Dd[0:1, 1024:1536]
                em.op("vector", lambda q, rr=rr: q.tensor_scalar(
                    rr, b.psC[0:1, 0:512], 1.0, 1e-6, ALU.mult, ALU.add),
                    waits=[("te", ss_v), ("ae", em.val("ae"))], inc="ve")
                em.psr("psC", "ve")
                rv = em.val("ve")
                em.op("scalar", lambda q, rr=rr: q.activation(rr, rr, AF.Ln),
                      waits=[("ve", rv)], inc="ae")
                ln_v = em.val("ae")
                em.op("scalar", lambda q, rr=rr: q.activation(
                    rr, rr, AF.Exp, scale=-0.5),
                    waits=[("ae", ln_v)], inc="ae")
                rst_v = em.val("ae")
                if kind == "q":
                    em.op("vector", lambda q, rr=rr: q.tensor_scalar(
                        rr, rr, 0.08838834764831845, None, ALU.mult),
                        waits=[("ae", rst_v)], inc="ve")
                em.op("vector", lambda q, rr=rr: q.tensor_copy(
                    b.rowbf[0:1, 0:512], rr),
                    waits=[("ae", rst_v), ("te", em.val("te"))], inc="ve")
                rb_v = em.val("ve")
                em.op("tensor", lambda q: q.matmul(
                    b.psD[:, 0:512], b.ones_row[0:1, :], b.rowbf[0:1, 0:512],
                    start=True, stop=True),
                    waits=[("ve", rb_v)] + em.psw("psD"), inc="te")
                bc_v = em.val("te")
                bce = b.Cc[:, BC + mq * 512:BC + (mq + 1) * 512]
                em.op("scalar", lambda q, bce=bce: q.activation(
                    bce, b.psD[:, 0:512], AF.Identity),
                    waits=[("te", bc_v), ("ve", em.val("ve"))], inc="ae")
                em.psr("psD", "ae")
                em.op("vector", lambda q, fo=fo, bce=bce: q.tensor_mul(
                    fo, fo, bce), waits=[("ae", em.val("ae"))], inc="ve")
                nm_v = em.val("ve")
                qr = gm * P
                em.dma("sync", b.qk_scr[qr:qr + P, n * 512:(n + 1) * 512],
                       fo, "sd", waits=[("ve", nm_v)])
            pcons[n] = [("te", em.val("te")), ("ve", em.val("ve")),
                        ("ae", em.val("ae")), ("sd", em.val("sd")),
                        ("gd", em.val("gd"))]
    em.phase = "a2"
    ones4 = b.Dd[0:4, 3584:3712]
    em.op("vector", lambda q: q.memset(ones4, 1.0), inc="ve")
    sl_ = lambda k: b.Dd[0:4, k * 512:(k + 1) * 512]
    proj_sd = em.val("sd")
    for c in range(16):
        bi = c // 8
        i0 = (c % 8) * 4
        tok = slice(c * 512, (c + 1) * 512)
        war = [("ve", em.val("ve")), ("ae", em.val("ae")),
               ("gd", em.val("gd"))]
        sd_r = em.dma("sync", sl_(0), b.ba_scr[0:4, tok], "sd",
                      waits=[("sd", proj_sd)] + war)
        sd_r = em.dma("sync", sl_(2), b.ba_scr[4:8, tok], "sd")
        ld_v = em.val("sd")
        em.op("scalar", lambda q: q.activation(sl_(1), sl_(0), AF.Sigmoid),
              waits=[("sd", ld_v), ("ve", em.val("ve")),
                     ("gd", em.val("gd"))], inc="ae")
        b_v = em.val("ae")
        em.op("vector", lambda q: q.tensor_scalar(sl_(0), sl_(1), -1.0, None,
              ALU.mult), waits=[("ae", b_v)], inc="ve")
        em.op("vector", lambda q: q.tensor_scalar(sl_(2), sl_(2),
              b.dtb_sb[0:4, 0:1], None, ALU.add),
              waits=[("sd", ld_v), ("gd", em.val("gd")), ("ae", em.val("ae"))],
              inc="ve")
        a_v = em.val("ve")
        em.op("vector", lambda q: q.tensor_scalar_max(sl_(6), sl_(2), 0.0),
              inc="ve")
        ab_v = em.val("ve")
        em.op("vector", lambda q: q.scalar_tensor_tensor(
            sl_(3), sl_(2), -1.0, sl_(2), ALU.mult, ALU.max), inc="ve")
        abs_v = em.val("ve")
        em.op("scalar", lambda q: q.activation(sl_(4), sl_(3), AF.Exp,
              scale=-1.0), waits=[("ve", abs_v)], inc="ae")
        en_v = em.val("ae")
        em.op("scalar", lambda q: q.activation(sl_(3), sl_(4), AF.Ln,
              bias=1.0), waits=[("ae", en_v)], inc="ae")
        l1_v = em.val("ae")
        em.op("vector", lambda q: q.tensor_add(sl_(6), sl_(6), sl_(3)),
              waits=[("ae", l1_v)], inc="ve")
        em.op("vector", lambda q: q.tensor_scalar(sl_(6), sl_(6),
              b.nealog[0:4, 0:1], None, ALU.mult), inc="ve")
        g_v = em.val("ve")
        for cs in range(4):
            em.op("vector", lambda q, cs=cs: q.tensor_tensor_scan(
                sl_(2)[0:4, cs * P:(cs + 1) * P], ones4,
                sl_(6)[0:4, cs * P:(cs + 1) * P], 0.0, ALU.mult, ALU.add),
                waits=[("ve", g_v)], inc="ve")
        gc_v = em.val("ve")

        def rowout(qid, slot, dst, rl=4096):
            em.dma("gpsimd",
                   bass.AP(dst.tensor, dst.offset + i0 * rl + qid * 1024 + bi * 512,
                           [[P, 4], [rl, 4], [1, P]]),
                   slot.rearrange("v (s w) -> v s w", w=P), "gd",
                   waits=[("ve", em.val("ve")), ("ae", em.val("ae"))])
        rowout(0, sl_(2), b.rows_scr)
        em.op("vector", lambda q: q.tensor_copy(b.r4lo[0:4, :], sl_(2)),
              waits=[("ve", gc_v)], inc="ve")
        hb_v = em.val("ve")
        em.op("vector", lambda q: q.tensor_copy(sl_(3), b.r4lo[0:4, :]),
              waits=[("ve", hb_v)], inc="ve")
        em.op("vector", lambda q: q.tensor_sub(sl_(3), sl_(2), sl_(3)),
              inc="ve")
        rowout(3, sl_(3), b.rows_scr)
        em.op("scalar", lambda q: q.activation(sl_(4), sl_(2), AF.Exp),
              waits=[("ve", gc_v)], inc="ae")
        eg_v = em.val("ae")
        rowout(1, sl_(4), b.rows_scr)
        rowout(2, sl_(0), b.rows_scr)
        em.op("vector", lambda q: q.tensor_copy(
            sl_(3).rearrange("v (s w) -> v s w", w=P),
            sl_(2)[0:4, P - 1::P].broadcast_to([4, 4, P])),
            waits=[("ae", eg_v), ("gd", em.val("gd"))], inc="ve")
        em.op("vector", lambda q: q.tensor_sub(sl_(6), sl_(3), sl_(2)),
              inc="ve")
        sb_v = em.val("ve")
        em.op("scalar", lambda q: q.activation(sl_(3), sl_(6), AF.Exp),
              waits=[("ve", sb_v)], inc="ae")
        egl_v = em.val("ae")
        rowout(1, sl_(3), b.cols_scr, rl=5120)
        em.op("vector", lambda q: q.tensor_mul(sl_(6), sl_(3), sl_(4)),
              waits=[("ae", egl_v)], inc="ve")
        rowout(3, sl_(6), b.cols_scr, rl=5120)
        em.op("vector", lambda q: q.tensor_mul(sl_(5), sl_(1), sl_(4)),
              inc="ve")
        rowout(0, sl_(5), b.cols_scr, rl=5120)
        rowout(2, sl_(1), b.cols_scr, rl=5120)
        rowout(4, sl_(0), b.cols_scr, rl=5120)
    em.phase = "a3"
    war = [("te", em.val("te")), ("ve", em.val("ve")), ("ae", em.val("ae")),
           ("sd", em.val("sd"))]
    gd_m1 = em.dma("gpsimd", b.Ee[:, 0:1024], i["c_maskD"][:, :], "gd",
                   waits=war)
    gd_m2 = em.dma("gpsimd", b.Ee[:, 1024:2048], i["c_maskD2"][:, :], "gd")
    gd_m3 = em.dma("gpsimd", b.Ee[:, 2048:3072], i["c_mask01u"][:, :], "gd")
    gd_m4 = em.dma("gpsimd", b.Ee[:, 3072:4096], i["c_eyeR"][:, :], "gd")
    em.dma("gpsimd", b.r4lo[0:4, :], i["c_blockneg"][:, :], "gd",
           waits=[("ve", em.val("ve"))])
    msk_v = em.val("gd")
    mD = b.Ee[:, 0:1024]
    mD2 = b.Ee[:, 1024:2048]
    m01 = b.Ee[:, 2048:3072]
    eyeR = b.Ee[:, 3072:4096]
    cons_after = {}
    Aa = lambda k: b.A[:, k * 1024:(k + 1) * 1024]
    Dw, D2w, Xs, XTs, Ts = Aa(0), Aa(1), Aa(2), Aa(3), Aa(4)
    qeg, aiT, vcs, kcds, vnew = Aa(5), Aa(6), Aa(7), Aa(8), Aa(9)
    vtil, stbf, osb, ozq, ofin = Aa(10), Aa(11), Aa(12), Aa(13), Aa(14)
    vbw = b.Cc[:, 0:1024]
    kbegw = b.Cc[:, 1024:2048]
    state = b.Dd[:, 0:1024]
    tmpD = b.Dd[:, 1024:2048]
    tmp2 = b.Dd[:, 2048:3072]
    em.op("vector", lambda q: q.memset(state[:, :], 0.0),
          waits=[("te", em.val("te")), ("gd", em.val("gd")),
                 ("sd", em.val("sd")), ("ae", em.val("ae"))], inc="ve")
    em.op("vector", lambda q: q.memset(stbf[:, :], 0.0), inc="ve")
    st0_v = em.val("ve")
    L = {}

    def _scan_loads(it):
        pp = it % 2
        em.phase = "a3a"
        ldw = cons_after.get(it - 2,
                             [("te", em.val("te")), ("ve", em.val("ve")),
                              ("ae", em.val("ae")), ("sc", em.val("sc")),
                              ("gd", em.val("gd")), ("sd", em.val("sd"))]
                             if it < 2 else None)
        rbase = 8192 + pp * 4096
        rows_sd = em.dma("sync", b.Bb[0:1, rbase:rbase + 4096],
                         b.rows_scr[it, 0:4096], "sd", waits=ldw)
        crow_sd = em.dma("sync",
                         b.Cc[0:40, 4096 + pp * P:4096 + (pp + 1) * P],
                         b.cols_scr[it, :].rearrange("(u w) -> u w", w=P),
                         "sd", waits=ldw)
        em.op("tensor", lambda q, pp=pp: q.transpose(
            b.psT[:, pp * 40:pp * 40 + 32],
            b.Cc[0:32, 4096 + pp * P:4096 + (pp + 1) * P],
            b.ident[0:32, 0:32]),
            waits=[("sd", em.val("sd"))] + em.psw("psT"), inc="te")
        em.op("tensor", lambda q, pp=pp: q.transpose(
            b.psT[:, pp * 40 + 32:pp * 40 + 40],
            b.Cc[32:40, 4096 + pp * P:4096 + (pp + 1) * P],
            b.ident[32:40, 32:40]), inc="te")
        em.op("vector", lambda q, pp=pp: q.tensor_copy(
            b.Dd[:, 3072 + pp * 40:3112 + pp * 40],
            b.psT[:, pp * 40:pp * 40 + 40]),
            waits=[("te", em.val("te"))], inc="ve")
        em.psr("psT", "ve")
        r4base = 15360 + pp * 512
        ktm = b.Bb[:, 4096 + pp * 1024:4096 + (pp + 1) * 1024]
        vstep = b.Bb[:, 6144 + pp * 1024:6144 + (pp + 1) * 1024]
        zstep = b.Cc[:, 2048 + pp * 1024:2048 + (pp + 1) * 1024]
        qstep = b.Bb[:, 0 + pp * 1024:0 + (pp + 1) * 1024]
        kstep = b.Bb[:, 2048 + pp * 1024:2048 + (pp + 1) * 1024]
        for u in range(8):
            bi2, vh = u // 4, u % 4
            kh = vh // 2
            cb = bi2 * S + it * P
            em.dma("sync", qstep[:, u * P:(u + 1) * P],
                   b.qk_scr[kh * P:(kh + 1) * P, cb:cb + P], "sd")
            em.dma("sync", kstep[:, u * P:(u + 1) * P],
                   b.qk_scr[256 + kh * P:256 + (kh + 1) * P, cb:cb + P], "sd")
            em.op("sync", lambda q, u=u, kh=kh, cb=cb, ktm=ktm:
                  q.dma_start_transpose(
                      ktm[:, u * P:(u + 1) * P],
                      b.qk_scr[256 + kh * P:256 + (kh + 1) * P, cb:cb + P]),
                  waits=(ldw if u == 0 else ()), inc="sc", inc_amt=16)
            em.op("sync", lambda q, u=u, vh=vh, cb=cb, vstep=vstep:
                  q.dma_start_transpose(
                      vstep[:, u * P:(u + 1) * P],
                      b.v_scr[vh * P:(vh + 1) * P, cb:cb + P]),
                  waits=(), inc="sc", inc_amt=16)
        for bi2 in range(2):
            cb = bi2 * S + it * P
            em.dma("sync",
                   zstep[:, bi2 * 512:(bi2 + 1) * 512].rearrange(
                       "p (v w) -> p v w", w=P),
                   b.z_scr[0:512, cb:cb + P].rearrange("(v p) w -> p v w", p=P),
                   "sd")
        bar = cons_after.get(it - 1, ldw)
        for qi, qq in ((0, 0), (1, 3)):
            em.dma("sync",
                   b.A[0:4, r4base + qi * 256:r4base + (qi + 1) * 256].rearrange(
                       "r (h w) -> r h w", w=P),
                   b.rows_scr[it, qq * 1024:(qq + 1) * 1024].rearrange(
                       "(h r w) -> r h w", h=2, w=P),
                   "sd", waits=bar)
        L[it] = dict(r4_sd=em.val("sd"), ld_sd=em.val("sd"),
                     ld_sc=em.val("sc"), ld_ve=em.val("ve"))

    def _scan_step(it, pp):
        li = L[it]
        r4_sd, ld_sd = li["r4_sd"], li["ld_sd"]
        ld_sc, ld_ve = li["ld_sc"], li["ld_ve"]
        rbase = 8192 + pp * 4096
        colT = b.Dd[:, 3072 + pp * 40:3112 + pp * 40]
        ccol = lambda qid, u: colT[:, qid * 8 + u:qid * 8 + u + 1]
        r4base = 15360 + pp * 512
        rows4 = b.A[0:4, r4base:r4base + 512]
        qstep = b.Bb[:, 0 + pp * 1024:0 + (pp + 1) * 1024]
        kstep = b.Bb[:, 2048 + pp * 1024:2048 + (pp + 1) * 1024]
        ktm = b.Bb[:, 4096 + pp * 1024:4096 + (pp + 1) * 1024]
        vstep = b.Bb[:, 6144 + pp * 1024:6144 + (pp + 1) * 1024]
        zstep = b.Cc[:, 2048 + pp * 1024:2048 + (pp + 1) * 1024]
        em.phase = "a3d"
        rowsh = lambda qid, h: b.Bb[0:1, rbase + qid * 1024 + h * 512:
                                    rbase + qid * 1024 + h * 512 + 512]
        for h in range(2):
            ps = b.psC if h == 0 else b.psD
            psn = "psC" if h == 0 else "psD"
            em.op("tensor", lambda q, h=h, ps=ps: q.matmul(
                ps[:, 0:512], b.ones_row[0:1, :],
                rowsh(0, h), start=True, stop=False),
                waits=[("sd", r4_sd), ("gd", msk_v)] + em.psw(psn), inc="te")
            em.op("tensor", lambda q, h=h, ps=ps: q.matmul(
                ps[:, 0:512], b.ones_row[0:1, :],
                rowsh(3, h), start=False, stop=False), inc="te")
            em.op("tensor", lambda q, h=h, ps=ps, rows4=rows4: q.matmul(
                ps[:, 0:512], rows4[0:4, h * P:(h + 1) * P],
                b.r4lo[0:4, :], start=False, stop=False), inc="te")
            em.op("tensor", lambda q, h=h, ps=ps, rows4=rows4: q.matmul(
                ps[:, 0:512], rows4[0:4, 256 + h * P:256 + (h + 1) * P],
                b.r4lo[0:4, :], start=False, stop=True), inc="te")
        g_v = em.val("te")
        em.op("vector", lambda q: q.scalar_tensor_tensor(
            tmp2[:, 0:512], b.psC[:, 0:512], 1.0, mD[:, 0:512],
            ALU.mult, ALU.add),
            waits=[("te", g_v), ("ae", em.val("ae")), ("gd", msk_v)], inc="ve")
        em.op("vector", lambda q: q.scalar_tensor_tensor(
            tmp2[:, 512:1024], b.psD[:, 0:512], 1.0, mD[:, 512:1024],
            ALU.mult, ALU.add), inc="ve")
        dw_pre = em.val("ve")
        em.op("vector", lambda q: q.scalar_tensor_tensor(
            tmpD[:, 0:512], b.psC[:, 0:512], -1.0, mD2[:, 0:512],
            ALU.mult, ALU.add), inc="ve")
        em.op("vector", lambda q: q.scalar_tensor_tensor(
            tmpD[:, 512:1024], b.psD[:, 0:512], -1.0, mD2[:, 512:1024],
            ALU.mult, ALU.add), inc="ve")
        em.psr("psC", "ve")
        em.psr("psD", "ve")
        d2_pre = em.val("ve")
        em.op("scalar", lambda q: q.activation(Dw, tmp2[:, :], AF.Exp),
              waits=[("ve", dw_pre), ("te", em.val("te"))], inc="ae")
        d_v = em.val("ae")
        em.op("scalar", lambda q: q.activation(D2w, tmpD[:, :], AF.Exp),
              waits=[("ve", d2_pre)], inc="ae")
        d2_v = em.val("ae")
        em.phase = "a3a2"
        for u in range(8):
            ps = b.psA if u < 4 else b.psB
            psn = "psA" if u < 4 else "psB"
            em.op("tensor", lambda q, u=u, ps=ps: q.matmul(
                ps[:, (u % 4) * P:(u % 4 + 1) * P], kstep[:, u * P:(u + 1) * P],
                kstep[:, u * P:(u + 1) * P], start=True, stop=True),
                waits=[("sd", ld_sd)] + (em.psw(psn) if u % 4 == 0 else []),
                inc="te")
            ps2 = b.psE if u < 4 else b.psF
            ps2n = "psE" if u < 4 else "psF"
            em.op("tensor", lambda q, u=u, ps2=ps2: q.matmul(
                ps2[:, (u % 4) * P:(u % 4 + 1) * P], kstep[:, u * P:(u + 1) * P],
                qstep[:, u * P:(u + 1) * P], start=True, stop=True),
                waits=[("sd", ld_sd)] + (em.psw(ps2n) if u % 4 == 0 else []),
                inc="te")
        kk_v = em.val("te")
        em.op("vector", lambda q: q.tensor_tensor(aiT[:, 0:512], b.psE[:, 0:512],
              Dw[:, 0:512], ALU.mult), waits=[("te", kk_v), ("ae", d_v)],
              inc="ve")
        em.op("vector", lambda q: q.tensor_tensor(aiT[:, 512:1024],
              b.psF[:, 0:512], Dw[:, 512:1024], ALU.mult), inc="ve")
        em.psr("psE", "ve")
        em.psr("psF", "ve")
        em.op("vector", lambda q: q.tensor_tensor(Xs[:, 0:512], b.psA[:, 0:512],
              Dw[:, 0:512], ALU.mult), inc="ve")
        em.op("vector", lambda q: q.tensor_tensor(Xs[:, 512:1024],
              b.psB[:, 0:512], Dw[:, 512:1024], ALU.mult), inc="ve")
        for u in range(8):
            psx = b.psA if u < 4 else b.psB
            em.op("vector", lambda q, u=u, psx=psx: q.scalar_tensor_tensor(
                XTs[:, u * P:(u + 1) * P],
                psx[:, (u % 4) * P:(u % 4 + 1) * P], ccol(4, u),
                D2w[:, u * P:(u + 1) * P], ALU.mult, ALU.mult),
                waits=[("ae", d2_v), ("ve", ld_ve)], inc="ve")
        em.psr("psA", "ve")
        em.psr("psB", "ve")
        for h in range(2):
            ps = b.psC if h == 0 else b.psD
            psn = "psC" if h == 0 else "psD"
            em.op("tensor", lambda q, h=h, ps=ps: q.matmul(
                ps[:, 0:512], b.ones_row[0:1, :],
                rowsh(2, h), start=True, stop=True),
                waits=em.psw(psn), inc="te")
        bw_v = em.val("te")
        em.op("vector", lambda q: q.tensor_tensor(Xs[:, 0:512], Xs[:, 0:512],
              b.psC[:, 0:512], ALU.mult), waits=[("te", bw_v)], inc="ve")
        em.op("vector", lambda q: q.tensor_tensor(Xs[:, 512:1024],
              Xs[:, 512:1024], b.psD[:, 0:512], ALU.mult), inc="ve")
        em.psr("psC", "ve")
        em.psr("psD", "ve")
        em.op("vector", lambda q: q.tensor_mul(Xs, Xs, m01), inc="ve")
        for h in range(2):
            ps = b.psE if h == 0 else b.psF
            psn = "psE" if h == 0 else "psF"
            em.op("tensor", lambda q, h=h, ps=ps: q.matmul(
                ps[:, 0:512], b.ones_row[0:1, :],
                rowsh(1, h), start=True, stop=True),
                waits=em.psw(psn), inc="te")
        eb_v = em.val("te")
        em.op("vector", lambda q: q.tensor_tensor(qeg[:, 0:512], qstep[:, 0:512],
              b.psE[:, 0:512], ALU.mult), waits=[("te", eb_v)], inc="ve")
        em.op("vector", lambda q: q.tensor_tensor(qeg[:, 512:1024],
              qstep[:, 512:1024], b.psF[:, 0:512], ALU.mult), inc="ve")
        em.psr("psE", "ve")
        em.psr("psF", "ve")
        pbuilt_v = em.val("ve")
        em.phase = "a3b"
        em.op("vector", lambda q: q.tensor_add(Ts, Xs, eyeR),
              waits=[("ve", pbuilt_v)], inc="ve")
        ts0_v = em.val("ve")

        def _sq_mms(first_waits, with_x):
            for u in range(8):
                if with_x:
                    ps = b.psC if u < 4 else b.psD
                    psn = "psC" if u < 4 else "psD"
                    em.op("tensor", lambda q, u=u, ps=ps: q.matmul(
                        ps[:, (u % 4) * P:(u % 4 + 1) * P],
                        XTs[:, u * P:(u + 1) * P], Xs[:, u * P:(u + 1) * P],
                        start=True, stop=True),
                        waits=(first_waits if u == 0 else []) +
                              (em.psw(psn) if u % 4 == 0 else []), inc="te")
                ps2 = b.psE if u < 4 else b.psF
                ps2n = "psE" if u < 4 else "psF"
                em.op("tensor", lambda q, u=u, ps2=ps2: q.matmul(
                    ps2[:, (u % 4) * P:(u % 4 + 1) * P],
                    Xs[:, u * P:(u + 1) * P], XTs[:, u * P:(u + 1) * P],
                    start=True, stop=True),
                    waits=(first_waits if u == 0 and not with_x else []) +
                          (em.psw(ps2n) if u % 4 == 0 else []), inc="te")
            return em.val("te")

        def _sq_evac(sq_v, with_x):
            if with_x:
                em.op("vector", lambda q: q.tensor_copy(Xs[:, 0:512],
                      b.psC[:, 0:512]), waits=[("te", sq_v)], inc="ve")
                em.op("vector", lambda q: q.tensor_copy(Xs[:, 512:1024],
                      b.psD[:, 0:512]), inc="ve")
                em.psr("psC", "ve")
                em.psr("psD", "ve")
            em.op("scalar", lambda q: q.activation(XTs[:, 0:512],
                  b.psE[:, 0:512], AF.Identity), waits=[("te", sq_v)],
                  inc="ae")
            em.op("scalar", lambda q: q.activation(XTs[:, 512:1024],
                  b.psF[:, 0:512], AF.Identity), inc="ae")
            em.psr("psE", "ae")
            em.psr("psF", "ae")
            return em.val("ve"), em.val("ae")

        sq_v = _sq_mms([("ve", ts0_v)], True)
        xe_ve, xe_ae = _sq_evac(sq_v, True)
        tadd_prev = ts0_v
        for lv in range(1, 7):
            for u in range(8):
                ps = b.psA if u < 4 else b.psB
                psn = "psA" if u < 4 else "psB"
                em.op("tensor", lambda q, u=u, ps=ps: q.matmul(
                    ps[:, (u % 4) * P:(u % 4 + 1) * P],
                    XTs[:, u * P:(u + 1) * P], Ts[:, u * P:(u + 1) * P],
                    start=True, stop=True),
                    waits=([("ve", max(xe_ve, tadd_prev)), ("ae", xe_ae)]
                           if u == 0 else []) +
                          (em.psw(psn) if u % 4 == 0 else []), inc="te")
            t_v = em.val("te")
            if lv < 6:
                sq_v = _sq_mms([], lv < 5)
            em.op("vector", lambda q: q.tensor_tensor(Ts[:, 0:512],
                  Ts[:, 0:512], b.psA[:, 0:512], ALU.add),
                  waits=[("te", t_v)], inc="ve")
            em.op("vector", lambda q: q.tensor_tensor(Ts[:, 512:1024],
                  Ts[:, 512:1024], b.psB[:, 0:512], ALU.add), inc="ve")
            em.psr("psA", "ve")
            em.psr("psB", "ve")
            tadd_prev = em.val("ve")
            if lv < 6:
                xe_ve, xe_ae = _sq_evac(sq_v, lv < 5)
        inv_v = em.val("ve")
        em.phase = "a3c"
        for u in range(8):
            em.op("vector", lambda q, u=u: q.tensor_scalar(
                vbw[:, u * P:(u + 1) * P], vstep[:, u * P:(u + 1) * P],
                ccol(2, u), None, ALU.mult),
                waits=[("sd", ld_sd), ("sc", ld_sc), ("ve", ld_ve)], inc="ve")
            em.op("vector", lambda q, u=u: q.tensor_scalar(
                kbegw[:, u * P:(u + 1) * P], ktm[:, u * P:(u + 1) * P],
                ccol(0, u), None, ALU.mult), inc="ve")
        vk_v = em.val("ve")
        for u in range(8):
            ps = b.psC if u < 4 else b.psD
            psn = "psC" if u < 4 else "psD"
            em.op("tensor", lambda q, u=u, ps=ps: q.matmul(
                ps[:, (u % 4) * P:(u % 4 + 1) * P], Ts[:, u * P:(u + 1) * P],
                vbw[:, u * P:(u + 1) * P], start=True, stop=True),
                waits=[("ve", vk_v)] + (em.psw(psn) if u % 4 == 0 else []),
                inc="te")
            ps2 = b.psE if u < 4 else b.psF
            ps2n = "psE" if u < 4 else "psF"
            em.op("tensor", lambda q, u=u, ps2=ps2: q.matmul(
                ps2[:, (u % 4) * P:(u % 4 + 1) * P], kbegw[:, u * P:(u + 1) * P],
                Ts[:, u * P:(u + 1) * P], start=True, stop=True),
                waits=[("ve", vk_v)] + (em.psw(ps2n) if u % 4 == 0 else []),
                inc="te")
        vc_v = em.val("te")
        em.op("vector", lambda q: q.tensor_copy(vcs[:, 0:512], b.psC[:, 0:512]),
              waits=[("te", vc_v)], inc="ve")
        em.op("vector", lambda q: q.tensor_copy(vcs[:, 512:1024],
              b.psD[:, 0:512]), inc="ve")
        em.psr("psC", "ve")
        em.psr("psD", "ve")
        em.op("scalar", lambda q: q.activation(kcds[:, 0:512], b.psE[:, 0:512],
              AF.Identity), waits=[("te", vc_v)], inc="ae")
        em.op("scalar", lambda q: q.activation(kcds[:, 512:1024],
              b.psF[:, 0:512], AF.Identity), inc="ae")
        em.psr("psE", "ae")
        em.psr("psF", "ae")
        kcd_v = em.val("ve")
        kcd_ae = em.val("ae")
        for u in range(8):
            ps = b.psC if u < 4 else b.psD
            psn = "psC" if u < 4 else "psD"
            em.op("tensor", lambda q, u=u, ps=ps: q.matmul(
                ps[:, (u % 4) * P:(u % 4 + 1) * P], kcds[:, u * P:(u + 1) * P],
                stbf[:, u * P:(u + 1) * P], start=True, stop=True),
                waits=[("ve", max(kcd_v, st0_v)), ("ae", kcd_ae)] +
                      (em.psw(psn) if u % 4 == 0 else []), inc="te")
        t1_v = em.val("te")
        em.op("vector", lambda q: q.tensor_sub(vnew[:, 0:512], vcs[:, 0:512],
              b.psC[:, 0:512]), waits=[("te", t1_v)], inc="ve")
        em.op("vector", lambda q: q.tensor_sub(vnew[:, 512:1024],
              vcs[:, 512:1024], b.psD[:, 0:512]), inc="ve")
        em.psr("psC", "ve")
        em.psr("psD", "ve")
        vn_v = em.val("ve")
        for u in range(8):
            ps = b.psE if u < 4 else b.psF
            psn = "psE" if u < 4 else "psF"
            em.op("tensor", lambda q, u=u, ps=ps: q.matmul(
                ps[:, (u % 4) * P:(u % 4 + 1) * P], stbf[:, u * P:(u + 1) * P],
                qeg[:, u * P:(u + 1) * P], start=True, stop=False),
                waits=[("ve", vn_v)] + (em.psw(psn) if u % 4 == 0 else []),
                inc="te")
            em.op("tensor", lambda q, u=u, ps=ps: q.matmul(
                ps[:, (u % 4) * P:(u % 4 + 1) * P], vnew[:, u * P:(u + 1) * P],
                aiT[:, u * P:(u + 1) * P], start=False, stop=True), inc="te")
        o_v = em.val("te")
        em.op("scalar", lambda q: q.activation(osb[:, 0:512], b.psE[:, 0:512],
              AF.Identity), waits=[("te", o_v), ("ve", em.val("ve"))],
              inc="ae")
        em.op("scalar", lambda q: q.activation(osb[:, 512:1024], b.psF[:, 0:512],
              AF.Identity), inc="ae")
        em.psr("psE", "ae")
        em.psr("psF", "ae")
        oe_v = em.val("ae")
        for u in range(8):
            em.op("vector", lambda q, u=u: q.tensor_scalar(
                vtil[:, u * P:(u + 1) * P], vnew[:, u * P:(u + 1) * P],
                ccol(1, u), None, ALU.mult), waits=[("ve", vn_v)], inc="ve")
        vt_v = em.val("ve")
        for u in range(8):
            ps = b.psC if u < 4 else b.psD
            psn = "psC" if u < 4 else "psD"
            em.op("tensor", lambda q, u=u, ps=ps: q.matmul(
                ps[:, (u % 4) * P:(u % 4 + 1) * P], ktm[:, u * P:(u + 1) * P],
                vtil[:, u * P:(u + 1) * P], start=True, stop=True),
                waits=[("ve", vt_v)] + (em.psw(psn) if u % 4 == 0 else []),
                inc="te")
        t4_v = em.val("te")
        for u in range(8):
            em.op("vector", lambda q, u=u: q.tensor_scalar(
                state[:, u * P:(u + 1) * P], state[:, u * P:(u + 1) * P],
                ccol(3, u), None, ALU.mult),
                waits=[("ve", max(st0_v, ld_ve))], inc="ve")
        em.op("vector", lambda q: q.tensor_add(state[:, 0:512], state[:, 0:512],
              b.psC[:, 0:512]), waits=[("te", t4_v)], inc="ve")
        em.op("vector", lambda q: q.tensor_add(state[:, 512:1024],
              state[:, 512:1024], b.psD[:, 0:512]), inc="ve")
        em.psr("psC", "ve")
        em.psr("psD", "ve")
        em.op("vector", lambda q: q.tensor_copy(stbf, state),
              waits=[("te", t4_v)], inc="ve")
        em.op("vector", lambda q: q.tensor_mul(ozq, osb, osb),
              waits=[("ae", oe_v)], inc="ve")
        oq_v = em.val("ve")
        for h in range(2):
            ps = b.psA if h == 0 else b.psB
            psn = "psA" if h == 0 else "psB"
            em.op("tensor", lambda q, h=h, ps=ps: q.matmul(
                ps[0:1, 0:512], b.ones_col[:, 0:1],
                ozq[:, h * 512:(h + 1) * 512], start=True, stop=True),
                waits=[("ve", oq_v)] + em.psw(psn), inc="te")
        os_v = em.val("te")
        rrow = b.Dd[0:1, 1024:2048]
        em.op("vector", lambda q: q.tensor_scalar(rrow[0:1, 0:512],
              b.psA[0:1, 0:512], 1.0 / P, 1e-6, ALU.mult, ALU.add),
              waits=[("te", os_v), ("ae", em.val("ae"))], inc="ve")
        em.op("vector", lambda q: q.tensor_scalar(rrow[0:1, 512:1024],
              b.psB[0:1, 0:512], 1.0 / P, 1e-6, ALU.mult, ALU.add), inc="ve")
        em.psr("psA", "ve")
        em.psr("psB", "ve")
        rr_v = em.val("ve")
        em.op("scalar", lambda q: q.activation(rrow, rrow, AF.Ln),
              waits=[("ve", rr_v)], inc="ae")
        rl_v = em.val("ae")
        em.op("scalar", lambda q: q.activation(rrow, rrow, AF.Exp, scale=-0.5),
              waits=[("ae", rl_v)], inc="ae")
        re_v = em.val("ae")
        em.op("vector", lambda q: q.tensor_copy(b.rowbf[0:1, 0:1024], rrow),
              waits=[("ae", re_v), ("te", em.val("te"))], inc="ve")
        rb_v = em.val("ve")
        for h in range(2):
            ps = b.psA if h == 0 else b.psB
            psn = "psA" if h == 0 else "psB"
            em.op("tensor", lambda q, h=h, ps=ps: q.matmul(
                ps[:, 0:512], b.ones_row[0:1, :],
                b.rowbf[0:1, h * 512:(h + 1) * 512], start=True, stop=True),
                waits=[("ve", rb_v)] + em.psw(psn), inc="te")
        rbc_v = em.val("te")
        em.op("vector", lambda q: q.scalar_tensor_tensor(
            osb[:, 0:512], osb[:, 0:512], sm[:, 144:145], b.psA[:, 0:512],
            ALU.mult, ALU.mult), waits=[("te", rbc_v)], inc="ve")
        em.op("vector", lambda q: q.scalar_tensor_tensor(
            osb[:, 512:1024], osb[:, 512:1024], sm[:, 144:145],
            b.psB[:, 0:512], ALU.mult, ALU.mult), inc="ve")
        em.psr("psA", "ve")
        em.psr("psB", "ve")
        em.op("vector", lambda q, zstep=zstep: q.tensor_mul(ofin, osb, zstep),
              waits=[("sd", ld_sd)], inc="ve")
        of_v = em.val("ve")
        for u in range(8):
            bi2, vh = u // 4, u % 4
            j = bi2 * 4 + it // 8
            col = (it % 8) * P
            em.dma("sync", b.oa_in[j * 512 + vh * P:j * 512 + (vh + 1) * P,
                                   col:col + P],
                   ofin[:, u * P:(u + 1) * P], "sd", waits=[("ve", of_v)])
        cons_after[it] = [("te", em.val("te")), ("ve", em.val("ve")),
                          ("ae", em.val("ae")), ("sd", em.val("sd")),
                          ("sc", em.val("sc")), ("gd", em.val("gd"))]
    _scan_loads(0)
    for it in range(32):
        _scan_step(it, it % 2)
        if it + 1 < 32:
            _scan_loads(it + 1)

    em.phase = "a4"
    cc_o = emit_collective(em, b, "AllToAll", b.oa_in, b.oa_out,
                           [("sd", em.val("sd"))])
    war = [("te", em.val("te")), ("ve", em.val("ve")), ("ae", em.val("ae")),
           ("gd", em.val("gd"))]
    wo = lambda m: b.Cc[:, 8192 + (m % 2) * 2048:8192 + (m % 2) * 2048 + 2048]
    cons = {}
    free_v = em.val("te")
    for half in range(2):
        sd_o = em.dma("sync", b.Bb[:, :].rearrange("p (k t) -> p k t", t=TOK),
                      b.oa_out[half * 2048:(half + 1) * 2048, :].rearrange(
                          "(k p) t -> p k t", p=P), "sd",
                      waits=[("cc", cc_o), ("te", em.val("te")),
                             ("ve", em.val("ve")), ("ae", em.val("ae"))])
        for m in range(DT):
            gd_v = em.dma("gpsimd", wo(m),
                          i["woutA_img"][:, (m * 32 + half * 16) * P:
                                         (m * 32 + half * 16 + 16) * P], "gd",
                          waits=[("te", cons.get((half, m - 2),
                                                 cons.get((half - 1, DT - 1),
                                                          free_v)))])
            for hh in range(2):
                ps, psn = (b.psA, "psA") if hh == 0 else (b.psB, "psB")
                for k in range(DT):
                    em.op("tensor", lambda q, m=m, hh=hh, k=k, ps=ps: q.matmul(
                        ps[:, 0:512], wo(m)[:, k * P:(k + 1) * P],
                        b.Bb[:, k * TOK + hh * 512:k * TOK + (hh + 1) * 512],
                        start=(k == 0), stop=(k == DT - 1)),
                        waits=[("gd", gd_v), ("sd", sd_o)] +
                              (em.psw(psn) if k == 0 else []), inc="te")
                te_v = em.val("te")
                cons[(half, m)] = te_v
                em.op("vector", lambda q, m=m, hh=hh, ps=ps: q.tensor_tensor(
                    b.xres[:, m * TOK + hh * 512:m * TOK + (hh + 1) * 512],
                    b.xres[:, m * TOK + hh * 512:m * TOK + (hh + 1) * 512],
                    ps[:, 0:512], ALU.add), waits=[("te", te_v)], inc="ve")
                em.psr(psn, "ve")


def emit_attn_b(em, b):
    i = b.inp
    sm = b.smallf
    em.phase = "b0"
    ve_ln, sd_ln = emit_ln(em, b, b.A, gin=b.gin, eps=1e-5)
    cc_g = emit_collective(em, b, "AllGather", b.gin, b.gxn,
                           [("sd", sd_ln)])
    war = [("te", em.val("te")), ("ve", em.val("ve")), ("ae", em.val("ae"))]
    gd_ws = em.dma("gpsimd", b.Bb[:, 0:8192], i["wsB_img"][:, :], "gd",
                   waits=war)
    gd_wv = em.dma("gpsimd", b.Cc[:, 0:4096], i["wvB_img"][:, :], "gd",
                   waits=war)
    gd_rc = em.dma("gpsimd", b.A[:, 0:4096], i["c_rope_c"][:, :], "gd",
                   waits=[("sd", sd_ln)] + war)
    gd_rs = em.dma("gpsimd", b.A[:, 4096:8192], i["c_rope_s"][:, :], "gd")
    gd_cm = em.dma("gpsimd", b.A[:, 12288:14336], i["c_cmask"][:, :], "gd")
    em.phase = "b1"
    st1 = b.Bb[:, 8192:8704]
    st2 = b.Bb[:, 8704:9216]
    st4 = b.Bb[:, 9216:9728]
    psq = [b.psA, b.psB, b.psC, b.psD]
    psqn = ["psA", "psB", "psC", "psD"]
    pcons = {}
    for n in range(16):
        r, off = n // 2, (n % 2) * 512
        bi = n // 8
        pos0 = (n % 8) * 512
        xc = b.Ee[:, 0:8192] if n % 2 == 0 else b.Cc[:, 4096:12288]
        xw = pcons.get(n - 2, [("cc", cc_g), ("te", em.val("te")),
                               ("ae", em.val("ae")), ("ve", em.val("ve")),
                               ("sd", em.val("sd"))])
        sd_x = em.dma(
            "sync", xc.rearrange("p (k t) -> p k t", t=512),
            b.gxn[2048 * r:2048 * (r + 1), off:off + 512].rearrange(
                "(k p) t -> p k t", p=P), "sd", waits=xw)
        for m in range(4):
            ps, psn = psq[m], psqn[m]
            for k in range(DT):
                em.op("tensor", lambda q, m=m, k=k, ps=ps, xc=xc: q.matmul(
                    ps[:, 0:512], b.Bb[:, (m * DT + k) * P:(m * DT + k + 1) * P],
                    xc[:, k * 512:(k + 1) * 512],
                    start=(k == 0), stop=(k == DT - 1)),
                    waits=[("gd", gd_ws), ("sd", sd_x)] +
                          (em.psw(psn) if k == 0 else []), inc="te")
            mm_v = em.val("te")
            em.op("scalar", lambda q, m=m, ps=ps: q.activation(
                st1, ps[:, 0:512], AF.Identity, bias=sm[:, 145 + m:146 + m]),
                waits=[("te", mm_v), ("ve", em.val("ve"))], inc="ae")
            ev_v = em.val("ae")
            em.psr(psn, "ae")
            em.op("tensor", lambda q: q.matmul(
                b.psE[:, 0:512], b.swap_pm[:, :], st1, start=True, stop=True),
                waits=[("ae", ev_v)] + em.psw("psE"), inc="te")
            sw_v = em.val("te")
            em.op("vector", lambda q, pos0=pos0: q.tensor_mul(
                st2, st1, b.A[:, pos0:pos0 + 512]),
                waits=[("ae", ev_v), ("gd", gd_rc)], inc="ve")
            em.op("vector", lambda q, pos0=pos0: q.tensor_mul(
                st4, b.psE[:, 0:512], b.A[:, 4096 + pos0:4096 + pos0 + 512]),
                waits=[("te", sw_v), ("sd", em.val("sd")), ("gd", gd_rs)],
                inc="ve")
            em.psr("psE", "ve")
            em.op("vector", lambda q: q.tensor_add(st4, st2, st4), inc="ve")
            rp_v = em.val("ve")
            h = m % 2
            qk = m // 2
            u = bi * 2 + h
            em.dma("sync", b.qb_scr[u * 384 + qk * P:u * 384 + qk * P + P,
                                    pos0:pos0 + 512], st4, "sd",
                   waits=[("ve", rp_v)])
        for tt in range(4):
            ps, psn = (b.psF, "psF") if tt % 2 == 0 else (b.psG, "psG")
            for k in range(DT):
                em.op("tensor", lambda q, tt=tt, k=k, ps=ps, xc=xc: q.matmul(
                    ps[:, 0:256], xc[:, k * 512 + tt * P:k * 512 + (tt + 1) * P],
                    b.Cc[:, k * 256:(k + 1) * 256],
                    start=(k == 0), stop=False),
                    waits=[("gd", gd_wv), ("sd", sd_x)] +
                          (em.psw(psn) if k == 0 else []), inc="te")
            em.op("tensor", lambda q, ps=ps: q.matmul(
                ps[:, 0:256], b.ones_row[0:1, :], b.bvB_sb[0:1, :],
                start=False, stop=True), inc="te")
            mm_v = em.val("te")
            em.op("scalar", lambda q, ps=ps: q.activation(
                st1[:, 0:256], ps[:, 0:256], AF.Identity),
                waits=[("te", mm_v), ("ve", em.val("ve")),
                       ("sd", em.val("sd"))], inc="ae")
            ev_v = em.val("ae")
            em.psr(psn, "ae")
            tg = (n % 8) * 4 + tt
            for h in range(2):
                u = bi * 2 + h
                em.dma("sync", b.qb_scr[u * 384 + 256:u * 384 + 384,
                                        tg * P:(tg + 1) * P],
                       st1[:, h * P:(h + 1) * P], "sd",
                       waits=[("ae", ev_v)])
        pcons[n] = [("te", em.val("te")), ("ve", em.val("ve")),
                    ("ae", em.val("ae")), ("sd", em.val("sd")),
                    ("gd", em.val("gd"))]
    em.phase = "b2"
    proj_sd = em.val("sd")
    RS = float(1.0 / np.sqrt(128.0))
    for u in range(4):
        bi, h = u // 2, u % 2
        war = [("te", em.val("te")), ("ve", em.val("ve")), ("ae", em.val("ae"))]
        sd_u = em.dma("sync",
                      b.Bb[:, 0:12288].rearrange("p (s t) -> p s t", t=S),
                      b.qb_scr[u * 384:(u + 1) * 384, :].rearrange(
                          "(s p) t -> p s t", p=P), "sd",
                      waits=[("sd", proj_sd)] + war)
        qblk = lambda c0, w: b.Bb[:, c0:c0 + w]
        for qp in range(4):
            kmax = (qp + 1) * 8
            for qb in range(2):
                pvp, pvn = (b.psC, "psC") if qb == 0 else (b.psD, "psD")
                smp, smn = (b.psE, "psE") if qb == 0 else (b.psF, "psF")
                kl = qp * 8 + qb * 4 + 4
                for kt in range(kl):
                    scp, scn = (b.psA, "psA") if kt % 2 == 0 else (b.psB, "psB")
                    em.op("tensor", lambda q, kt=kt, qp=qp, qb=qb, scp=scp:
                          q.matmul(scp[:, 0:512],
                                   b.Bb[:, 4096 + kt * P:4096 + (kt + 1) * P],
                                   b.Bb[:, qp * 1024 + qb * 512:
                                        qp * 1024 + (qb + 1) * 512],
                                   start=True, stop=True),
                          waits=[("sd", sd_u)] + em.psw(scn), inc="te")
                    sc_v = em.val("te")
                    r0 = kt - (qp * 8 + qb * 4)
                    mw = []
                    if 0 <= r0 < 4:
                        em.op("vector", lambda q, r0=r0, scp=scp: q.tensor_tensor(
                            scp[:, 0:512], scp[:, 0:512],
                            b.A[:, 12288 + r0 * 512:12288 + (r0 + 1) * 512],
                            ALU.add), waits=[("te", sc_v), ("gd", gd_cm)],
                            inc="ve")
                        mw = [("ve", em.val("ve"))]
                    slot = b.Ee[:, qb * 1024 + (kt % 2) * 512:
                                qb * 1024 + (kt % 2) * 512 + 512]
                    em.op("scalar", lambda q, scp=scp, slot=slot: q.activation(
                        slot, scp[:, 0:512], AF.Exp, scale=RS),
                        waits=[("te", sc_v)] + mw, inc="ae")
                    ex_v = em.val("ae")
                    em.psr(scn, "ae")
                    em.op("tensor", lambda q, kt=kt, pvp=pvp, slot=slot:
                          q.matmul(pvp[:, 0:512],
                                   b.Bb[:, 8192 + kt * P:8192 + (kt + 1) * P],
                                   slot, start=(kt == 0), stop=(kt == kl - 1)),
                          waits=[("ae", ex_v)] +
                                (em.psw(pvn) if kt == 0 else []), inc="te")
                    em.op("tensor", lambda q, kt=kt, smp=smp, slot=slot:
                          q.matmul(smp[0:1, 0:512], b.ones_col[:, 0:1],
                                   slot, start=(kt == 0), stop=(kt == kl - 1)),
                          waits=(em.psw(smn) if kt == 0 else []), inc="te")
                pv_v = em.val("te")
                rrow = b.Dd[0:1, 3072:3584]
                srow = b.Dd[0:1, 3584:3712]
                em.op("vector", lambda q, smp=smp: q.tensor_copy(
                    rrow, smp[0:1, 0:512]),
                    waits=[("te", pv_v), ("ae", em.val("ae"))], inc="ve")
                em.psr(smn, "ve")
                em.op("vector", lambda q: q.reciprocal(rrow, rrow), inc="ve")
                em.op("vector", lambda q: q.tensor_copy(
                    b.rowbf[0:1, 0:512], rrow),
                    waits=[("ve", em.val("ve")), ("te", em.val("te"))], inc="ve")
                rb_v = em.val("ve")
                scp2, scn2 = (b.psA, "psA")
                em.op("tensor", lambda q: q.matmul(
                    b.psA[:, 0:512], b.ones_row[0:1, :], b.rowbf[0:1, 0:512],
                    start=True, stop=True),
                    waits=[("ve", rb_v)] + em.psw("psA"), inc="te")
                bc_v = em.val("te")
                oslot = b.A[:, 8192 + qp * 1024 + qb * 512:
                            8192 + qp * 1024 + qb * 512 + 512]
                em.op("scalar", lambda q, pvp=pvp, oslot=oslot: q.activation(
                    oslot, pvp[:, 0:512], AF.Identity),
                    waits=[("te", bc_v), ("ve", em.val("ve")),
                           ("sd", em.val("sd"))], inc="ae")
                em.psr(pvn, "ae")
                em.op("vector", lambda q, oslot=oslot: q.tensor_mul(
                    oslot, oslot, b.psA[:, 0:512]),
                    waits=[("ae", em.val("ae")), ("te", bc_v)], inc="ve")
                em.psr("psA", "ve")
            o_v = em.val("ve")
            j = bi * 4 + qp
            em.dma("sync", b.ob_in[j * 256 + h * P:j * 256 + (h + 1) * P, :],
                   b.A[:, 8192 + qp * 1024:8192 + (qp + 1) * 1024], "sd",
                   waits=[("ve", o_v)])
    em.phase = "b3"
    cc_o = emit_collective(em, b, "AllToAll", b.ob_in, b.ob_out,
                           [("sd", em.val("sd"))])
    war = [("te", em.val("te")), ("ve", em.val("ve")), ("ae", em.val("ae"))]
    sd_o = em.dma("sync", b.Bb[:, :].rearrange("p (k t) -> p k t", t=TOK),
                  b.ob_out[:, :].rearrange("(k p) t -> p k t", p=P), "sd",
                  waits=[("cc", cc_o)] + war)
    wob = lambda m: b.Cc[:, 12288 + (m % 2) * 2048:12288 + (m % 2) * 2048 + 2048]
    cons = {}
    free_v = em.val("te")
    for m in range(DT):
        gd_v = em.dma("gpsimd", wob(m),
                      i["woutB_img"][:, m * 2048:(m + 1) * 2048], "gd",
                      waits=[("te", cons.get(m - 2, free_v))])
        for hh in range(2):
            ps, psn = (b.psA, "psA") if hh == 0 else (b.psB, "psB")
            for k in range(DT):
                em.op("tensor", lambda q, m=m, hh=hh, k=k, ps=ps: q.matmul(
                    ps[:, 0:512], wob(m)[:, k * P:(k + 1) * P],
                    b.Bb[:, k * TOK + hh * 512:k * TOK + (hh + 1) * 512],
                    start=(k == 0), stop=(k == DT - 1)),
                    waits=[("gd", gd_v), ("sd", sd_o)] +
                          (em.psw(psn) if k == 0 else []), inc="te")
            te_v = em.val("te")
            cons[m] = te_v
            em.op("vector", lambda q, m=m, hh=hh, ps=ps: q.scalar_tensor_tensor(
                b.xres[:, m * TOK + hh * 512:m * TOK + (hh + 1) * 512],
                ps[:, 0:512], sm[:, 48 + m:49 + m],
                b.xres[:, m * TOK + hh * 512:m * TOK + (hh + 1) * 512],
                ALU.add, ALU.add), waits=[("te", te_v)], inc="ve")
            em.psr(psn, "ve")



def build_program(in_arrs, parts=("a", "f1", "b", "f2"), ret_b=False,
                  debug_attn_a=False):
    nc = bass.Bass("TRN2", target_bir_lowering=False)
    b = make_b(nc, in_arrs)
    em = Em(nc)
    if debug_attn_a:
        b.d_oa = nc.dram_tensor("d_oa", [NCORES * 512, TOK], BF16,
                                kind="ExternalOutput").ap()
        b.d_qk = nc.dram_tensor("d_qk", [512, TALL], BF16,
                                kind="ExternalOutput").ap()
        b.d_v = nc.dram_tensor("d_v", [512, TALL], BF16,
                               kind="ExternalOutput").ap()
        b.d_z = nc.dram_tensor("d_z", [512, TALL], BF16,
                               kind="ExternalOutput").ap()
        b.d_rows = nc.dram_tensor("d_rows", [32, 4096], BF16,
                                  kind="ExternalOutput").ap()
        b.d_cols = nc.dram_tensor("d_cols", [32, 5120], BF16,
                                  kind="ExternalOutput").ap()
        b.d_misc = nc.dram_tensor("d_misc", [P, 4096], F32,
                                  kind="ExternalOutput").ap()
        b.d_misc2 = nc.dram_tensor("d_misc2", [P, 4096], F32,
                                   kind="ExternalOutput").ap()
    load_smalls(em, b)
    em.dma("gpsimd", b.gatew_bf[:, :], b.inp["gatew_img"][:, :], "gd")
    em.dma("gpsimd", b.d1_bf[:, :], b.inp["d1_img"][:, :], "gd")
    em.dma("gpsimd", b.d2_bf[:, :], b.inp["d2_img"][:, :], "gd")
    em.dma("gpsimd", b.agg_bf[:, :], b.inp["agg_img"][:, :], "gd")
    em.dma("gpsimd", b.bc_bf[:, :], b.inp["bc_img"][:, :], "gd")
    em.dma("gpsimd", b.d1b_bf[:, :], b.inp["d1b_row"][:, :], "gd")
    b.precast_v = em.val("gd")
    em.dma("sync", b.xres[:, :].rearrange("p (d t) -> p d t", t=TOK),
           b.inp["x_fm"][:, :].rearrange("(d p) t -> p d t", p=P), "sd")
    start_gate = [("sd", em.val("sd")), ("gd", em.val("gd"))]
    for e in ENGINES:
        em.op(e, lambda q: q.nop(), waits=start_gate)
    for part in parts:
        em.phase = "f" if part.startswith("f") else part
        if part == "a":
            emit_attn_a(em, b)
        elif part == "b":
            emit_attn_b(em, b)
        else:
            emit_ffn(em, b, part)
        em.phase = None
    ve_f = em.val("ve")
    em.dma("sync", b.out_fm[:, :].rearrange("(d p) t -> p d t", p=P),
           b.xres[:, :].rearrange("p (d t) -> p d t", t=TOK), "sd",
           waits=[("ve", ve_f)])
    if debug_attn_a:
        allw = [(n, v) for n, v in em.cnt.items() if v > 0]
        for (dst, src) in ((b.d_oa, b.oa_in), (b.d_qk, b.qk_scr),
                           (b.d_v, b.v_scr), (b.d_z, b.z_scr),
                           (b.d_rows, b.rows_scr), (b.d_cols, b.cols_scr)):
            em.dma("sync", dst[:, :], src[:, :], "sd", waits=allw)
    em.op("sync", lambda q: q.nop(), waits=[("sd", em.val("sd"))])
    em.replay()
    if ret_b:
        return nc, em, b
    return nc, em


def kernel(**inputs):
    maps = prep_inputs(inputs)
    nc, em = build_program(maps[0])
    res = run_bass_kernel_spmd(nc, maps, core_ids=list(range(NCORES)))
    em.close()
    shards = [np.asarray(r["out_fm"], dtype=np.float32).T for r in res.results]
    out = np.concatenate(shards, axis=0).reshape(2, S, D)
    return out


# version 48
# speedup vs baseline: 1.0731x; 1.0731x over previous
import numpy as np
import ml_dtypes
import concourse.bass as bass
import concourse.mybir as mybir
from concourse.bass_utils import run_bass_kernel_spmd
from concourse.aot_env import allow_ici_all_to_all

F32 = mybir.dt.float32
BF16 = mybir.dt.bfloat16
AF = mybir.ActivationFunctionType
ALU = mybir.AluOpType

P = 128
D = 2048
DT = 16
TOK = 1024
TALL = 8192
NCORES = 8
S = 4096
BFNP = ml_dtypes.bfloat16
RSQ128 = float(np.log(1.0 / np.sqrt(128.0)))

ENGINES = ("sync", "tensor", "vector", "scalar", "gpsimd")


PARANOID = False
PARANOID_PHASES = {"fth"}


class Em:
    def __init__(self, nc):
        self.nc = nc
        self.ops = {e: [] for e in ENGINES}
        self.sems = {}
        self.cnt = {}
        self._ctxs = []
        self._hw = {}
        self.pslast = {}

    def sem(self, name):
        if name not in self.sems:
            cm = self.nc.semaphore(name)
            self.sems[name] = cm.__enter__()
            self._ctxs.append(cm)
            self.cnt[name] = 0
        return self.sems[name]

    def val(self, name):
        return self.cnt.get(name, 0)

    def op(self, eng, fn, waits=(), inc=None, inc_amt=1):
        if PARANOID or getattr(self, "phase", None) in PARANOID_PHASES:
            waits = list(waits) + [(n, v) for n, v in self.cnt.items() if v > 0]
        ws = [(s, v) for (s, v) in waits if v > 0]
        for s, _ in ws:
            self.sem(s)
        self.ops[eng].append((fn, ws, inc, inc_amt))
        if inc is not None:
            self.sem(inc)
            self.cnt[inc] += inc_amt
            return self.cnt[inc]
        return None

    def dma(self, eng, out, in_, inc, waits=(), **kw):
        def fn(q):
            return q.dma_start(out=out, in_=in_, **kw)
        return self.op(eng, fn, waits, inc, 16)

    def psw(self, *names):
        out = []
        for n in names:
            if n in self.pslast:
                out.append(self.pslast[n])
        return out

    def psr(self, name, sem):
        self.pslast[name] = (sem, self.val(sem))

    def replay(self):
        nc = self.nc
        with nc.Block() as block:
            for eng in ENGINES:
                if not self.ops[eng]:
                    continue

                def mk(eng):
                    def run(q):
                        for (fn, ws, inc, amt) in self.ops[eng]:
                            for (s, v) in ws:
                                key = (eng, s)
                                if self._hw.get(key, 0) < v:
                                    q.wait_ge(self.sems[s], v)
                                    self._hw[key] = v
                            inst = fn(q)
                            if inc is not None:
                                inst.then_inc(self.sems[inc], amt)
                    return run
                getattr(block, eng)(mk(eng))

    def close(self):
        for cm in reversed(self._ctxs):
            cm.__exit__(None, None, None)


def f32c(x):
    return np.ascontiguousarray(np.asarray(x), dtype=np.float32)



def host_constants():
    c = {}
    half = 64
    thetas = 10000.0 ** (-np.arange(half, dtype=np.float64) / half)
    ang = np.arange(S, dtype=np.float64)[None, :] * thetas[:, None]
    C = np.empty((P, S), np.float64)
    S2 = np.empty((P, S), np.float64)
    C[0::2, :] = np.cos(ang)
    C[1::2, :] = np.cos(ang)
    S2[0::2, :] = -np.sin(ang)
    S2[1::2, :] = np.sin(ang)
    c["rope_c"] = f32c(C)
    c["rope_s"] = f32c(S2)
    sw = np.zeros((P, P), np.float32)
    for i in range(0, P, 2):
        sw[i, i + 1] = 1.0
        sw[i + 1, i] = 1.0
    c["swap_pm"] = sw
    c["ident"] = np.eye(P, dtype=np.float32)
    c["ones_col"] = np.ones((P, 1), np.float32)
    c["ones_row"] = np.ones((1, P), np.float32)
    ii = np.arange(P)
    mD = np.where(ii[None, :] >= ii[:, None], 0.0, -1e30).astype(np.float32)
    c["maskD"] = np.tile(mD, (1, 8))
    mD2 = np.where(ii[:, None] > ii[None, :], 0.0, -1e30).astype(np.float32)
    c["maskD2"] = np.tile(mD2, (1, 8))
    m01 = np.where(ii[None, :] > ii[:, None], 1.0, 0.0).astype(np.float32)
    c["mask01u"] = np.tile(m01, (1, 8))
    c["eyeR"] = np.tile(np.eye(P, dtype=np.float32), (1, 8))
    cm = np.empty((P, 4 * 512), np.float32)
    for r in range(4):
        cm[:, r * 512:(r + 1) * 512] = np.where(
            np.arange(512)[None, :] >= (r * P + ii[:, None]), 0.0, -1e30)
    c["cmask"] = cm
    bn = np.zeros((4, 512), np.float32)
    for r in range(4):
        bn[r, r * P:(r + 1) * P] = -1.0
    c["blockneg"] = bn
    return c


def lhsT_image(w):
    K, M = w.shape
    KT, MT = K // P, M // P
    t = w.reshape(KT, P, MT, P).transpose(1, 2, 0, 3)
    return f32c(t.reshape(P, MT * KT * P))


def rhs_image(w):
    K, N = w.shape
    KT = K // P
    t = w.reshape(KT, P, N).transpose(1, 0, 2)
    return f32c(t.reshape(P, KT * N))


def col_image(v):
    n = v.shape[0]
    return f32c(v.reshape(n // P, P).T)


def prep_inputs(I):
    cst = host_constants()
    maps = []
    x = f32c(I["L0_attn_a_x"]).reshape(TALL, D)
    for c in range(NCORES):
        m = {}
        m["x_fm"] = f32c(x[c * TOK:(c + 1) * TOK, :].T)
        m["sel"] = f32c(np.tile(np.eye(2, dtype=np.float32)[c // 4], (P, 1)))
        for k, v in cst.items():
            m["c_" + k] = v
        kh = [2 * c, 2 * c + 1]
        vh = [4 * c + i for i in range(4)]
        wqkv = f32c(I["L0_attn_a_w_qkv"])
        qc = np.concatenate([wqkv[:, 128 * h:128 * (h + 1)] for h in kh], 1)
        kc = np.concatenate([wqkv[:, 2048 + 128 * h:2048 + 128 * (h + 1)]
                             for h in kh], 1)
        vc = np.concatenate([wqkv[:, 4096 + 128 * h:4096 + 128 * (h + 1)]
                             for h in vh], 1)
        wz = f32c(I["L0_attn_a_w_z"])
        zc = np.concatenate([wz[:, 128 * h:128 * (h + 1)] for h in vh], 1)
        m["wsA_img"] = lhsT_image(np.concatenate([qc, kc, vc, zc], 1))
        cw = f32c(I["L0_attn_a_conv_w"])
        rows = ([cw[128 * h:128 * (h + 1), :] for h in kh]
                + [cw[2048 + 128 * h:2048 + 128 * (h + 1), :] for h in kh]
                + [cw[4096 + 128 * h:4096 + 128 * (h + 1), :] for h in vh])
        m["cw_img"] = f32c(np.concatenate(rows, 1))
        wb = f32c(I["L0_attn_a_w_b"])[:, vh]
        wa = f32c(I["L0_attn_a_w_a"])[:, vh]
        m["wb_img"] = f32c(wb.reshape(DT, P, 4).transpose(1, 0, 2).reshape(P, DT * 4))
        m["wa_img"] = f32c(wa.reshape(DT, P, 4).transpose(1, 0, 2).reshape(P, DT * 4))
        m["dtb_col"] = f32c(I["L0_attn_a_dt_bias"])[vh].reshape(4, 1)
        m["alog_col"] = f32c(I["L0_attn_a_a_log"])[vh].reshape(4, 1)
        m["normw_col"] = f32c(I["L0_attn_a_norm_w"]).reshape(P, 1)
        m["woutA_img"] = lhsT_image(f32c(I["L0_attn_a_w_out"]))
        bh = [2 * c, 2 * c + 1]
        wq = f32c(I["L0_attn_b_w_qkv"])
        qb = np.concatenate([wq[128 * h:128 * (h + 1), :].T for h in bh], 1)
        kb = np.concatenate([wq[2048 + 128 * h:2048 + 128 * (h + 1), :].T
                             for h in bh], 1)
        vb = np.concatenate([wq[4096 + 128 * h:4096 + 128 * (h + 1), :].T
                             for h in bh], 1)
        m["wsB_img"] = lhsT_image(np.concatenate([qb, kb], 1))
        m["wvB_img"] = rhs_image(f32c(vb))
        bq = f32c(I["L0_attn_b_b_qkv"])
        bqc = np.concatenate([bq[128 * h:128 * (h + 1)] for h in bh]
                             + [bq[2048 + 128 * h:2048 + 128 * (h + 1)]
                                for h in bh])
        m["bqk_col"] = f32c(bqc.reshape(4, P).T)
        m["bvB_row"] = f32c(np.concatenate(
            [bq[4096 + 128 * h:4096 + 128 * (h + 1)] for h in bh]).reshape(1, 256))
        m["woutB_img"] = lhsT_image(f32c(I["L0_attn_b_w_out"]).T)
        m["boutB_col"] = col_image(f32c(I["L0_attn_b_b_out"]))
        m["gatew_img"] = lhsT_image(f32c(I["L0_ffn_gate_w"]))
        m["gateb_col"] = col_image(f32c(I["L0_ffn_gate_b"]))
        hsl = slice(c * 1024, (c + 1) * 1024)
        m["d1_img"] = rhs_image(f32c(I["L0_ffn_d1_w"])[:, hsl])
        m["d1b_row"] = f32c(I["L0_ffn_d1_b"])[hsl].reshape(1, 1024)
        m["d2_img"] = lhsT_image(f32c(I["L0_ffn_d2_w"])[hsl, :])
        m["d2b_col"] = col_image(f32c(I["L0_ffn_d2_b"]))
        asl = slice(c * 256, (c + 1) * 256)
        m["agg_img"] = rhs_image(f32c(I["L0_ffn_agg_w"])[:, asl])
        m["aggb_row"] = f32c(np.tile(I["L0_ffn_agg_b"][asl].reshape(1, 256), (2, 1)))
        m["bc_img"] = rhs_image(f32c(I["L0_ffn_bc_w"])[:, asl])
        m["bcb_row"] = f32c(np.tile(I["L0_ffn_bc_b"][asl].reshape(1, 256), (2, 1)))
        for nm, key in (("ing", "L0_ffn_in_gamma"), ("inb", "L0_ffn_in_beta"),
                        ("png", "L0_ffn_pn_gamma"), ("pnb", "L0_ffn_pn_beta"),
                        ("outg", "L0_ffn_out_gamma"), ("outb", "L0_ffn_out_beta")):
            m[nm + "_col"] = col_image(f32c(I[key]))
        maps.append(m)
    return maps



class Bufs:
    pass


def make_b(nc, in_arrs):
    b = Bufs()
    b.nc = nc
    b.inp = {}
    for name, arr in in_arrs.items():
        b.inp[name] = nc.dram_tensor(name, list(arr.shape), F32,
                                     kind="ExternalInput").ap()
    b.out_fm = nc.dram_tensor("out_fm", [D, TOK], F32, kind="ExternalOutput").ap()
    dr = nc.dram_tensor
    b.gin = dr("gin", [D, TOK], BF16).ap()
    b.gxn = dr("gxn", [NCORES * D, TOK], BF16).ap()
    b.qk_scr = dr("qk_scr", [512, TALL], BF16).ap()
    b.v_scr = dr("v_scr", [512, TALL], BF16).ap()
    b.z_scr = dr("z_scr", [512, TALL], BF16).ap()
    b.ba_scr = dr("ba_scr", [8, TALL], F32).ap()
    b.krstd_scr = dr("krstd_scr", [2, TALL], F32).ap()
    b.rows_scr = dr("rows_scr", [32, 4 * 8 * P], BF16).ap()
    b.cols_scr = dr("cols_scr", [32, 5 * 8 * P], BF16).ap()
    b.oa_in = dr("oa_in", [NCORES * 512, TOK], BF16).ap()
    b.oa_out = dr("oa_out", [NCORES * 512, TOK], BF16).ap()
    b.qb_scr = dr("qb_scr", [12 * P, S], BF16).ap()
    b.ob_in = dr("ob_in", [NCORES * 256, TOK], BF16).ap()
    b.ob_out = dr("ob_out", [NCORES * 256, TOK], BF16).ap()
    b.th_in = dr("th_in", [1, D], BF16).ap()
    b.th_out = dr("th_out", [NCORES, D], BF16).ap()
    b.gatew_bf = dr("gatew_bf", [P, 32768], BF16).ap()
    b.d1_bf = dr("d1_bf", [P, 16384], BF16).ap()
    b.d2_bf = dr("d2_bf", [P, 16384], BF16).ap()
    b.agg_bf = dr("agg_bf", [P, 4096], BF16).ap()
    b.bc_bf = dr("bc_bf", [P, 4096], BF16).ap()
    b.d1b_bf = dr("d1b_bf", [1, 1024], BF16).ap()
    b.ar_in = dr("ar_in", [2, D], BF16).ap()
    b.ar_out = dr("ar_out", [2, D], BF16).ap()
    b.ag2_in = dr("ag2_in", [2, 256], BF16).ap()
    b.ag2_out = dr("ag2_out", [NCORES * 2, 256], BF16).ap()

    al = nc.alloc_sbuf_tensor
    b.xres = al("xres", [P, DT * TOK], F32)
    b.A = al("A", [P, 16384], BF16)
    b.Bb = al("Bb", [P, 16384], BF16)
    b.Cc = al("Cc", [P, 16384], BF16)
    b.Ee = al("Ee", [P, 8192], BF16)
    b.Dd = al("Dd", [P, 3712], F32)
    b.rowbf = al("rowbf", [1, 2048], BF16)
    b.ones_col = al("ones_col", [P, 1], BF16)
    b.ones_row = al("ones_row", [1, P], BF16)
    b.ident = al("ident", [P, P], BF16)
    b.swap_pm = al("swap_pm", [P, P], BF16)
    b.smallf = al("smallf", [P, 160], F32)
    b.cw_sb = al("cw_sb", [P, 32], F32)
    b.dtb_sb = al("dtb_sb", [4, 1], F32)
    b.alog_sb = al("alog_sb", [4, 1], F32)
    b.nealog = al("nealog", [4, 1], F32)
    b.aggb_sb = al("aggb_sb", [2, 256], BF16)
    b.bcb_sb = al("bcb_sb", [2, 256], BF16)
    b.bvB_sb = al("bvB_sb", [1, 256], BF16)
    b.wba_sb = al("wba_sb", [P, 128], BF16)
    b.th_fm = al("th_fm", [P, 32], F32)
    b.th_sc = al("th_sc", [P, 32], F32)
    b.th_bf = al("th_bf", [P, 32], BF16)
    b.sq_bf = al("sq_bf", [P, 32], BF16)
    b.r4lo = al("r4lo", [4, 512], BF16)
    b.h1_bf = al("h1_bf", [P, 16], BF16)
    b.h1_tm = al("h1_tm", [2, 1024], BF16)
    b.th_tm2 = al("th_tm2", [2, D], BF16)
    b.bcv_fm = al("bcv_fm", [P, 32], F32)
    b.bcsel = al("bcsel", [P, 16], BF16)

    ap_ = nc.alloc_psum_tensor
    for nm in "ABCDEFG":
        setattr(b, "ps" + nm, ap_("ps" + nm, [P, 512], F32))
    b.psT = ap_("psT", [P, 1024], BF16)
    return b


def load_smalls(em, b):
    i = b.inp
    em.dma("gpsimd", b.ones_col[:, :], i["c_ones_col"][:, :], "gd")
    em.dma("gpsimd", b.ones_row[:, :], i["c_ones_row"][:, :], "gd")
    em.dma("gpsimd", b.ident[:, :], i["c_ident"][:, :], "gd")
    em.dma("gpsimd", b.swap_pm[:, :], i["c_swap_pm"][:, :], "gd")
    em.dma("gpsimd", b.aggb_sb[:, :], i["aggb_row"][:, :], "gd")
    em.dma("gpsimd", b.bcb_sb[:, :], i["bcb_row"][:, :], "gd")
    em.dma("gpsimd", b.bvB_sb[:, :], i["bvB_row"][:, :], "gd")
    em.dma("gpsimd", b.wba_sb[:, 0:64], i["wb_img"][:, :], "gd")
    em.dma("gpsimd", b.wba_sb[:, 64:128], i["wa_img"][:, :], "gd")
    em.dma("sync", b.cw_sb[:, :], i["cw_img"][:, :], "sd")
    em.dma("sync", b.dtb_sb[:, :], i["dtb_col"][:, :], "sd")
    em.dma("sync", b.alog_sb[:, :], i["alog_col"][:, :], "sd")
    sm = b.smallf
    em.dma("sync", sm[:, 0:16], i["gateb_col"][:, :], "sd")
    em.dma("sync", sm[:, 16:32], i["outg_col"][:, :], "sd")
    em.dma("sync", sm[:, 32:48], i["outb_col"][:, :], "sd")
    em.dma("sync", sm[:, 48:64], i["boutB_col"][:, :], "sd")
    em.dma("sync", sm[:, 64:80], i["d2b_col"][:, :], "sd")
    em.dma("sync", sm[:, 80:96], i["ing_col"][:, :], "sd")
    em.dma("sync", sm[:, 96:112], i["inb_col"][:, :], "sd")
    em.dma("sync", sm[:, 112:128], i["png_col"][:, :], "sd")
    em.dma("sync", sm[:, 128:144], i["pnb_col"][:, :], "sd")
    em.dma("sync", sm[:, 144:145], i["normw_col"][:, :], "sd")
    em.dma("sync", sm[:, 145:149], i["bqk_col"][:, :], "sd")
    em.dma("sync", sm[:, 150:152], i["sel"][:, :], "sd")



def _ln_rows_and_bcast(em, b, eps):
    r_mu = b.Dd[0:1, 0:1024]
    r_var = b.Dd[0:1, 1024:2048]
    r_m2 = b.Dd[0:1, 2048:3072]
    st_v = em.val("te")
    em.op("vector", lambda q: q.tensor_scalar(r_mu[0:1, 0:512], b.psA[0:1, 0:512],
          1.0 / D, None, ALU.mult),
          waits=[("te", st_v), ("ae", em.val("ae"))], inc="ve")
    em.op("vector", lambda q: q.tensor_scalar(r_mu[0:1, 512:1024],
          b.psB[0:1, 0:512], 1.0 / D, None, ALU.mult), inc="ve")
    em.op("vector", lambda q: q.tensor_scalar(r_var[0:1, 0:512], b.psC[0:1, 0:512],
          1.0 / D, eps, ALU.mult, ALU.add), inc="ve")
    em.op("vector", lambda q: q.tensor_scalar(r_var[0:1, 512:1024],
          b.psD[0:1, 0:512], 1.0 / D, eps, ALU.mult, ALU.add), inc="ve")
    em.op("vector", lambda q: q.tensor_mul(r_m2[0:1, :], r_mu[0:1, :],
          r_mu[0:1, :]), inc="ve")
    em.op("vector", lambda q: q.tensor_sub(r_var[0:1, :], r_var[0:1, :],
          r_m2[0:1, :]), inc="ve")
    for nm in "ABCD":
        em.psr("ps" + nm, "ve")
    var_v = em.val("ve")
    em.op("scalar", lambda q: q.activation(r_var[0:1, :], r_var[0:1, :], AF.Ln),
          waits=[("ve", var_v)], inc="ae")
    ln_v = em.val("ae")
    em.op("scalar", lambda q: q.activation(r_var[0:1, :], r_var[0:1, :], AF.Exp,
          scale=-0.5), waits=[("ae", ln_v)], inc="ae")
    rstd_v = em.val("ae")
    em.op("vector", lambda q: q.tensor_copy(b.rowbf[0:1, 0:1024], r_mu[0:1, :]),
          waits=[("ae", rstd_v), ("te", em.val("te"))], inc="ve")
    em.op("vector", lambda q: q.tensor_copy(b.rowbf[0:1, 1024:2048],
          r_var[0:1, :]), inc="ve")
    rbv = em.val("ve")
    for h in range(2):
        em.op("tensor", lambda q, h=h: q.matmul(
            (b.psE if h == 0 else b.psF)[:, 0:512], b.ones_row[0:1, :],
            b.rowbf[0:1, h * 512:(h + 1) * 512], start=True, stop=True),
            waits=[("ve", rbv)] + em.psw("psE" if h == 0 else "psF"), inc="te")
        em.op("tensor", lambda q, h=h: q.matmul(
            (b.psG if h == 0 else b.psA)[:, 0:512], b.ones_row[0:1, :],
            b.rowbf[0:1, 1024 + h * 512:1024 + (h + 1) * 512], start=True,
            stop=True), waits=em.psw("psG" if h == 0 else "psA"), inc="te")
    bc_v = em.val("te")
    mubc = b.Ee[:, 4096:5120]
    rstdbc = b.Ee[:, 5120:6144]
    em.op("vector", lambda q: q.tensor_copy(mubc[:, 0:512], b.psE[:, 0:512]),
          waits=[("te", bc_v)], inc="ve")
    em.op("vector", lambda q: q.tensor_copy(mubc[:, 512:1024], b.psF[:, 0:512]),
          inc="ve")
    em.op("vector", lambda q: q.tensor_copy(rstdbc[:, 0:512], b.psG[:, 0:512]),
          inc="ve")
    em.op("vector", lambda q: q.tensor_copy(rstdbc[:, 512:1024],
          b.psA[:, 0:512]), inc="ve")
    for nm in "AEFG":
        em.psr("ps" + nm, "ve")
    return em.val("ve")


def emit_ln(em, b, xn_dst, gin=None, eps=1e-5):
    xbf = lambda i: b.Ee[:, (i % 2) * 1024:(i % 2) * 1024 + 1024]
    xsq = lambda i: b.Ee[:, 2048 + (i % 2) * 1024:2048 + (i % 2) * 1024 + 1024]
    mm_wait = em.psw("psA", "psB", "psC", "psD")
    ee_war = [("te", em.val("te")), ("gd", em.val("gd")),
              ("sd", em.val("sd")), ("ae", em.val("ae"))]
    for d in range(DT):
        sl = slice(d * TOK, (d + 1) * TOK)
        wv = [("te", em.val("te"))] if d >= 2 else list(ee_war)
        em.op("vector", lambda q, o=xbf(d), isl=sl: q.tensor_copy(o, b.xres[:, isl]),
              waits=wv, inc="ve")
        cast_v = em.val("ve")
        em.op("scalar", lambda q, o=xsq(d), i=xbf(d): q.activation(o, i, AF.Square),
              waits=[("ve", cast_v)] + ([("te", em.val("te"))] if d >= 2 else []),
              inc="ae")
        sq_v = em.val("ae")
        for h in range(2):
            hs = slice(h * 512, (h + 1) * 512)
            em.op("tensor", lambda q, d=d, h=h, i=xbf(d), hs=hs: q.matmul(
                (b.psA if h == 0 else b.psB)[0:1, 0:512], b.ones_col[:, 0:1],
                i[:, hs], start=(d == 0), stop=(d == DT - 1)),
                waits=[("ve", cast_v)] + (mm_wait if d == 0 else []), inc="te")
            em.op("tensor", lambda q, d=d, h=h, i=xsq(d), hs=hs: q.matmul(
                (b.psC if h == 0 else b.psD)[0:1, 0:512], b.ones_col[:, 0:1],
                i[:, hs], start=(d == 0), stop=(d == DT - 1)),
                waits=[("ae", sq_v)], inc="te")
    bcs = _ln_rows_and_bcast(em, b, eps)
    mubc = b.Ee[:, 4096:5120]
    rstdbc = b.Ee[:, 5120:6144]
    sd_last = em.val("sd")
    dst_war = [("te", em.val("te")), ("sd", em.val("sd")),
               ("gd", em.val("gd"))]
    for d in range(DT):
        sl = slice(d * TOK, (d + 1) * TOK)
        em.op("vector", lambda q, sl=sl: q.tensor_sub(xn_dst[:, sl],
              b.xres[:, sl], mubc[:, :]),
              waits=[("ve", bcs)] + (dst_war if d == 0 else []), inc="ve")
        em.op("vector", lambda q, sl=sl: q.tensor_mul(xn_dst[:, sl],
              xn_dst[:, sl], rstdbc[:, :]), inc="ve")
        if gin is not None:
            sd_last = em.dma("sync", gin[d * P:(d + 1) * P, :], xn_dst[:, sl],
                             "sd", waits=[("ve", em.val("ve"))])
    return em.val("ve"), sd_last


def emit_out_ln_add(em, b, src_bf, gci, bci, eps=1e-6, src_ready=()):
    sm = b.smallf
    sq = lambda i: b.Ee[:, 2048 + (i % 2) * 1024:2048 + (i % 2) * 1024 + 1024]
    mm_wait = em.psw("psA", "psB", "psC", "psD")
    for d in range(DT):
        sl = slice(d * TOK, (d + 1) * TOK)
        wv = [("te", em.val("te"))] if d >= 2 else []
        em.op("scalar", lambda q, o=sq(d), sl=sl: q.activation(o, src_bf[:, sl],
              AF.Square), waits=wv + list(src_ready), inc="ae")
        sq_v = em.val("ae")
        for h in range(2):
            hs = slice(h * 512, (h + 1) * 512)
            em.op("tensor", lambda q, d=d, h=h, sl=sl, hs=hs: q.matmul(
                (b.psA if h == 0 else b.psB)[0:1, 0:512], b.ones_col[:, 0:1],
                src_bf[:, sl][:, hs], start=(d == 0), stop=(d == DT - 1)),
                waits=(mm_wait if d == 0 else []) + (list(src_ready) if d == 0 else []),
                inc="te")
            em.op("tensor", lambda q, d=d, h=h, o=sq(d), hs=hs: q.matmul(
                (b.psC if h == 0 else b.psD)[0:1, 0:512], b.ones_col[:, 0:1],
                o[:, hs], start=(d == 0), stop=(d == DT - 1)),
                waits=[("ae", sq_v)], inc="te")
    bcs = _ln_rows_and_bcast(em, b, eps)
    mubc = b.Ee[:, 4096:5120]
    rstdbc = b.Ee[:, 5120:6144]
    tmp = lambda i: b.Ee[:, (i % 2) * 1024:(i % 2) * 1024 + 1024]
    for d in range(DT):
        sl = slice(d * TOK, (d + 1) * TOK)
        em.op("vector", lambda q, d=d, sl=sl: q.tensor_sub(tmp(d), src_bf[:, sl],
              mubc[:, :]), waits=[("ve", bcs)], inc="ve")
        em.op("vector", lambda q, d=d: q.tensor_mul(tmp(d), tmp(d), rstdbc[:, :]),
              inc="ve")
        em.op("vector", lambda q, d=d, sl=sl: q.scalar_tensor_tensor(
            b.xres[:, sl], tmp(d), sm[:, gci + d:gci + d + 1], b.xres[:, sl],
            ALU.mult, ALU.add), inc="ve")
        em.op("vector", lambda q, d=d, sl=sl: q.tensor_scalar(
            b.xres[:, sl], b.xres[:, sl], sm[:, bci + d:bci + d + 1], None,
            ALU.add), inc="ve")
    return em.val("ve")


def emit_collective(em, b, kind, cc_in, cc_out, wait_sems, op=None):
    op = op or (ALU.add if kind == "AllReduce" else ALU.bypass)

    def fn(q):
        if kind == "AllToAll":
            with allow_ici_all_to_all():
                return q.collective_compute(
                    kind, op, replica_groups=[list(range(NCORES))],
                    ins=[cc_in.tensor.ap().opt()],
                    outs=[cc_out.tensor.ap().opt()])
        return q.collective_compute(
            kind, op, replica_groups=[list(range(NCORES))],
            ins=[cc_in.tensor.ap().opt()], outs=[cc_out.tensor.ap().opt()])
    em.op("gpsimd", fn, waits=wait_sems, inc="cc")
    return em.val("cc")



def ap_mid_bcast(x, n):
    a = [list(p) for p in x.ap]
    return bass.AP(x.tensor, x.offset, [a[0], [0, n], a[1]])


def emit_small_ln_fm(em, b, src, dst, gci, bci, eps, first_te=None):
    em.op("vector", lambda q: q.tensor_copy(b.th_bf[:, :], src[:, :]),
          waits=[("te", em.val("te") if first_te is None else first_te)],
          inc="ve")
    cast_v = em.val("ve")
    em.op("vector", lambda q: q.tensor_mul(b.sq_bf[:, :], b.th_bf[:, :],
          b.th_bf[:, :]), inc="ve")
    sq_v = em.val("ve")
    em.op("tensor", lambda q: q.matmul(
        b.psG[0:1, 0:32], b.ones_col[:, 0:1], b.th_bf[:, :],
        start=True, stop=True),
        waits=[("ve", cast_v)] + em.psw("psG"), inc="te")
    em.op("tensor", lambda q: q.matmul(
        b.psE[0:1, 0:32], b.ones_col[:, 0:1], b.sq_bf[:, :],
        start=True, stop=True),
        waits=[("ve", sq_v)] + em.psw("psE"), inc="te")
    st_v = em.val("te")
    r = b.Dd[0:1, 2048:2056]
    em.op("vector", lambda q: q.tensor_reduce(
        r[0:1, 6:8], b.psG[0:1, 0:32].rearrange("p (d t) -> p t d", t=2),
        mybir.AxisListType.X, ALU.add),
        waits=[("te", st_v), ("ae", em.val("ae"))], inc="ve")
    em.op("vector", lambda q: q.tensor_reduce(
        r[0:1, 2:4], b.psE[0:1, 0:32].rearrange("p (d t) -> p t d", t=2),
        mybir.AxisListType.X, ALU.add), inc="ve")
    red_v = em.val("ve")
    em.op("vector", lambda q: q.tensor_scalar(r[0:1, 0:2], r[0:1, 6:8],
          1.0 / D, None, ALU.mult), waits=[("ve", red_v)], inc="ve")
    em.op("vector", lambda q: q.tensor_scalar(r[0:1, 2:4], r[0:1, 2:4],
          1.0 / D, eps, ALU.mult, ALU.add), inc="ve")
    em.op("vector", lambda q: q.tensor_mul(r[0:1, 4:6], r[0:1, 0:2],
          r[0:1, 0:2]), inc="ve")
    em.op("vector", lambda q: q.tensor_sub(r[0:1, 2:4], r[0:1, 2:4],
          r[0:1, 4:6]), inc="ve")
    em.psr("psG", "ve")
    em.psr("psE", "ve")
    rv = em.val("ve")
    em.op("scalar", lambda q: q.activation(r[0:1, 2:4], r[0:1, 2:4], AF.Ln),
          waits=[("ve", rv)], inc="ae")
    ln_v = em.val("ae")
    em.op("scalar", lambda q: q.activation(r[0:1, 2:4], r[0:1, 2:4], AF.Exp,
          scale=-0.5), waits=[("ae", ln_v)], inc="ae")
    rstd_v = em.val("ae")
    em.op("vector", lambda q: q.tensor_copy(b.rowbf[0:1, 0:2], r[0:1, 0:2]),
          waits=[("ae", rstd_v), ("te", em.val("te"))], inc="ve")
    em.op("vector", lambda q: q.tensor_copy(b.rowbf[0:1, 2:4], r[0:1, 2:4]),
          inc="ve")
    rbv = em.val("ve")
    em.op("tensor", lambda q: q.matmul(b.psG[:, 0:2], b.ones_row[0:1, :],
          b.rowbf[0:1, 0:2], start=True, stop=True),
          waits=[("ve", rbv)] + em.psw("psG"), inc="te")
    em.op("tensor", lambda q: q.matmul(b.psF[:, 0:2], b.ones_row[0:1, :],
          b.rowbf[0:1, 2:4], start=True, stop=True),
          waits=em.psw("psF"), inc="te")
    bc_v = em.val("te")
    dst3 = dst[:, :].rearrange("p (d t) -> p d t", t=2)
    src3 = src[:, :].rearrange("p (d t) -> p d t", t=2)
    mu2 = b.Dd[:, 2056:2058]
    rs2 = b.Dd[:, 2058:2060]
    em.op("vector", lambda q: q.tensor_copy(mu2, b.psG[:, 0:2]),
          waits=[("te", bc_v)], inc="ve")
    em.op("vector", lambda q: q.tensor_copy(rs2, b.psF[:, 0:2]), inc="ve")
    em.psr("psG", "ve")
    em.psr("psF", "ve")
    ev_v = em.val("ve")
    em.op("vector", lambda q: q.tensor_tensor(
        dst3, src3, ap_mid_bcast(mu2, 16), ALU.subtract),
        waits=[("ve", ev_v)], inc="ve")
    em.op("vector", lambda q: q.tensor_tensor(
        dst3, dst3, ap_mid_bcast(rs2, 16), ALU.mult), inc="ve")
    em.op("vector", lambda q: q.tensor_tensor(
        dst3, dst3, _colbc(b, gci), ALU.mult), inc="ve")
    em.op("vector", lambda q: q.tensor_tensor(dst3, dst3, _colbc(b, bci),
          ALU.add), inc="ve")
    return em.val("ve")


def _colbc(b, ci):
    ap = b.smallf[:, ci:ci + 16]
    return ap.broadcast_to([P, 16, 2])



def emit_ffn(em, b, tag):
    i = b.inp
    sm = b.smallf
    em.phase = "fln"
    ve_ln, _ = emit_ln(em, b, b.A, gin=None, eps=1e-5)
    pre_fg_te = em.val("te")
    em.phase = "fth"
    sd_agg = em.dma("gpsimd", b.Ee[:, 4096:8192], b.agg_bf[:, :], "gd",
                    waits=[("te", pre_fg_te), ("ve", em.val("ve"))])
    for d in range(DT):
        em.dma("sync", b.th_in[0, d * P:(d + 1) * P],
               b.A[:, d * TOK + 1023:d * TOK + 1024], "sd",
               waits=[("ve", ve_ln)])
    cc_th = emit_collective(em, b, "AllGather", b.th_in, b.th_out,
                            [("sd", em.val("sd"))])
    sd_d1b = em.dma("gpsimd", b.rowbf[0:1, 1024:2048], b.d1b_bf[0:1, :],
                    "gd", waits=[("te", pre_fg_te), ("ve", em.val("ve"))])
    sd_d1 = em.dma("gpsimd", b.Cc[:, :], b.d1_bf[:, :], "gd",
                   waits=[("te", pre_fg_te), ("ve", em.val("ve"))])
    em.dma("sync", b.th_tm2[0:1, :], b.th_out[3, :], "sd",
           waits=[("cc", cc_th)])
    em.dma("sync", b.th_tm2[1:2, :], b.th_out[7, :], "sd")
    sd_xl = em.val("sd")
    gwbuf = lambda m: b.Ee[:, (m % 2) * 2048:(m % 2) * 2048 + 2048]
    ee_free = em.val("te")
    cons = {}

    def fg_part(m0, m1):
        em.phase = "fg"
        for m in range(m0, m1):
            wv = [("te", cons.get(m - 2, ee_free))]
            sc_v = em.op("scalar", lambda q, m=m: q.dma_start(
                out=gwbuf(m), in_=b.gatew_bf[:, m * 2048:(m + 1) * 2048]),
                waits=wv + [("gd", b.precast_v)] +
                      ([("ve", em.val("ve"))] if m == 0 else []),
                inc="sc", inc_amt=16)
            for h in range(2):
                ps = b.psC if h == 0 else b.psD
                psn = "psC" if h == 0 else "psD"
                for k in range(DT):
                    em.op("tensor", lambda q, m=m, h=h, k=k, ps=ps: q.matmul(
                        ps[:, 0:512], gwbuf(m)[:, k * P:(k + 1) * P],
                        b.A[:, k * TOK + h * 512:k * TOK + (h + 1) * 512],
                        start=(k == 0), stop=(k == DT - 1)),
                        waits=[("sc", sc_v), ("ve", ve_ln)] +
                              (em.psw(psn) if k == 0 else []), inc="te")
                te_v = em.val("te")
                cons[m] = te_v
                em.op("scalar", lambda q, m=m, h=h, ps=ps: q.activation(
                    b.Bb[:, m * TOK + h * 512:m * TOK + (h + 1) * 512],
                    ps[:, 0:512],
                    AF.Sigmoid, bias=sm[:, m:m + 1], scale=1.0),
                    waits=[("te", te_v)] +
                          ([("ve", em.val("ve"))] if (m == 0 and h == 0)
                           else []),
                    inc="ae")
                em.psr(psn, "ae")
        em.phase = "fth"

    fg_part(0, 6)
    _tm_to_fm(em, b, b.th_tm2, b.th_sc, DT, [("sd", sd_xl)])
    emit_small_ln_fm(em, b, b.th_sc, b.th_fm, 80, 96, eps=1e-6)
    _thought_proj(em, b, "agg_img", b.aggb_sb, tag + "agg",
                  img_ready=[("gd", sd_agg)])
    fg_part(6, DT)
    gate_done = em.val("ae")
    em.phase = "fth"
    for s_ in range(4):
        _thought_step(em, b, sd_d1, tag, s_)
    _thought_proj(em, b, "bc_img", b.bcb_sb, tag + "bc", to_bcv=True)
    bcv_v = em.val("ve")
    em.op("vector", lambda q: q.tensor_scalar(
        b.bcsel[:, :], b.bcv_fm[:, 0:32:2], sm[:, 150:151], None, ALU.mult),
        waits=[("ve", bcv_v)], inc="ve")
    em.op("vector", lambda q: q.scalar_tensor_tensor(
        b.bcsel[:, :], b.bcv_fm[:, 1:32:2], sm[:, 151:152], b.bcsel[:, :],
        ALU.mult, ALU.add), inc="ve")
    bcsel_v = em.val("ve")
    em.phase = "fc"
    for d in range(DT):
        sl = slice(d * TOK, (d + 1) * TOK)
        em.op("vector", lambda q, d=d, sl=sl: q.scalar_tensor_tensor(
            b.Cc[:, sl], b.Bb[:, sl], b.bcsel[:, d:d + 1], b.A[:, sl],
            ALU.mult, ALU.add),
            waits=[("ve", bcsel_v), ("ae", gate_done)], inc="ve")
    emit_out_ln_add(em, b, b.Cc, 16, 32, eps=1e-6,
                    src_ready=[("ve", em.val("ve"))])


def _tm_to_fm(em, b, src_tm, dst_fm, ntiles, waits):
    for d in range(ntiles):
        em.op("tensor", lambda q, d=d: q.transpose(
            b.psT[:, d * 2:d * 2 + 2], src_tm[0:2, d * P:(d + 1) * P],
            b.ident[0:2, 0:2]),
            waits=(waits if d == 0 else []) + (em.psw("psT") if d == 0 else []),
            inc="te")
    tr_v = em.val("te")
    em.op("vector", lambda q: q.tensor_copy(dst_fm[:, 0:2 * ntiles],
          b.psT[:, 0:2 * ntiles]), waits=[("te", tr_v)], inc="ve")
    em.psr("psT", "ve")
    return em.val("ve")


def _thought_proj(em, b, img, bias_row, tag, to_bcv=False, img_ready=None):
    em.op("vector", lambda q: q.tensor_copy(b.th_bf[:, :], b.th_fm[:, :]),
          waits=[("te", em.val("te"))], inc="ve")
    tbv = em.val("ve")
    if img_ready is None:
        src = b.agg_bf if img == "agg_img" else b.bc_bf
        sd_v = em.dma(
            "gpsimd", b.Ee[:, 4096:8192], src[:, :],
            "gd", waits=[("te", em.val("te")), ("ve", em.val("ve"))])
        img_ready = [("gd", sd_v)]
    for k in range(DT):
        em.op("tensor", lambda q, k=k: q.matmul(
            b.psB[0:2, 0:256], b.th_bf[:, k * 2:(k + 1) * 2],
            b.Ee[:, 4096 + k * 256:4096 + (k + 1) * 256],
            start=(k == 0), stop=(k == DT - 1)),
            waits=[("ve", tbv)] + img_ready + (em.psw("psB") if k == 0 else []),
            inc="te")
    pj_v = em.val("te")
    em.op("vector", lambda q: q.tensor_copy(b.th_tm2[0:2, 0:256],
          b.psB[0:2, 0:256]), waits=[("te", pj_v)], inc="ve")

    em.psr("psB", "ve")
    em.op("vector", lambda q: q.tensor_tensor(
        b.th_tm2[0:2, 0:256], b.th_tm2[0:2, 0:256],
        bias_row[0:2, :], ALU.add), inc="ve")
    em.dma("sync", b.ag2_in[:, :], b.th_tm2[0:2, 0:256], "sd",
           waits=[("ve", em.val("ve"))])
    cc_v = emit_collective(em, b, "AllGather", b.ag2_in, b.ag2_out,
                           [("sd", em.val("sd"))])
    for r in range(NCORES):
        em.dma("sync", b.th_tm2[0:2, r * 256:(r + 1) * 256],
               b.ag2_out[2 * r:2 * r + 2, :], "sd", waits=[("cc", cc_v)])
    dst = b.bcv_fm if to_bcv else b.th_fm
    _tm_to_fm(em, b, b.th_tm2, dst, DT, [("sd", em.val("sd"))])


def _thought_step(em, b, sd_d1, tag, s_):
    i = b.inp
    sm = b.smallf
    emit_small_ln_fm(em, b, b.th_fm, b.th_sc, 112, 128, eps=1e-6)
    em.op("vector", lambda q: q.tensor_copy(b.th_bf[:, :], b.th_sc[:, :]),
          waits=[("te", em.val("te"))], inc="ve")
    tb = em.val("ve")
    for half in range(2):
        ps = b.psB if half == 0 else b.psG
        psn = "psB" if half == 0 else "psG"
        for k in range(DT):
            em.op("tensor", lambda q, k=k, half=half, ps=ps: q.matmul(
                ps[0:2, 0:512], b.th_bf[:, k * 2:(k + 1) * 2],
                b.Cc[:, k * 1024 + half * 512:k * 1024 + (half + 1) * 512],
                start=(k == 0), stop=False),
                waits=[("ve", tb), ("gd", sd_d1)] +
                      (em.psw(psn) if k == 0 else []))
        em.op("tensor", lambda q, half=half, ps=ps: q.matmul(
            ps[0:2, 0:512], b.ones_row[0:1, 0:2],
            b.rowbf[0:1, 1024 + half * 512:1024 + (half + 1) * 512],
            start=False, stop=True), inc="te")
    d1_done = em.val("te")
    em.op("scalar", lambda q: q.activation(b.h1_tm[0:2, 0:512],
          b.psB[0:2, 0:512], AF.Gelu), waits=[("te", d1_done)], inc="ae")
    em.op("scalar", lambda q: q.activation(b.h1_tm[0:2, 512:1024],
          b.psG[0:2, 0:512], AF.Gelu), inc="ae")
    em.psr("psB", "ae")
    em.psr("psG", "ae")
    g_v = em.val("ae")
    for dd in range(8):
        em.op("tensor", lambda q, dd=dd: q.transpose(
            b.psT[:, dd * 2:dd * 2 + 2], b.h1_tm[0:2, dd * P:(dd + 1) * P],
            b.ident[0:2, 0:2]),
            waits=[("ae", g_v)] + (em.psw("psT") if dd == 0 else []), inc="te")
    em.op("vector", lambda q: q.tensor_copy(b.h1_bf[:, :], b.psT[:, 0:16]),
          waits=[("te", em.val("te"))], inc="ve")
    em.psr("psT", "ve")
    h1f = em.val("ve")
    for grp in range(2):
        sd_v = em.dma("gpsimd", b.Ee[:, 0:8192],
                      b.d2_bf[:, grp * 8192:(grp + 1) * 8192], "gd",
                      waits=[("te", em.val("te")), ("ve", em.val("ve"))])
        for mm_ in range(8):
            m = grp * 8 + mm_
            ps = b.psE if (m % 2 == 0) else b.psF
            psn = "psE" if (m % 2 == 0) else "psF"
            for k in range(8):
                em.op("tensor", lambda q, mm_=mm_, k=k, ps=ps: q.matmul(
                    ps[0:2, 0:128], b.h1_bf[:, k * 2:(k + 1) * 2],
                    b.Ee[:, (mm_ * 8 + k) * P:(mm_ * 8 + k + 1) * P],
                    start=(k == 0), stop=(k == 7)),
                    waits=[("ve", h1f), ("gd", sd_v)] +
                          (em.psw(psn) if k == 0 else []), inc="te")
            te_v = em.val("te")
            em.op("vector", lambda q, m=m, ps=ps: q.tensor_copy(
                b.th_tm2[0:2, m * P:(m + 1) * P], ps[0:2, 0:128]),
                waits=[("te", te_v)], inc="ve")
            em.psr(psn, "ve")
    d2e = em.val("ve")
    em.dma("sync", b.ar_in[:, :], b.th_tm2[0:2, :], "sd", waits=[("ve", d2e)])
    cc_v = emit_collective(em, b, "AllReduce", b.ar_in, b.ar_out,
                           [("sd", em.val("sd"))])
    em.dma("sync", b.th_tm2[0:2, :], b.ar_out[0:2, :], "sd",
           waits=[("cc", cc_v)])
    sd_v = em.val("sd")
    for d in range(DT):
        em.op("tensor", lambda q, d=d: q.transpose(
            b.psT[:, d * 2:d * 2 + 2], b.th_tm2[0:2, d * P:(d + 1) * P],
            b.ident[0:2, 0:2]),
            waits=[("sd", sd_v)] + (em.psw("psT") if d == 0 else []), inc="te")
    tr_v = em.val("te")
    em.op("vector", lambda q: q.tensor_add(b.th_sc[:, :], b.psT[:, 0:32],
          b.th_fm[:, :]), waits=[("te", tr_v)], inc="ve")
    em.psr("psT", "ve")
    em.op("vector", lambda q: q.tensor_tensor(
        b.th_fm[:, :].rearrange("p (d t) -> p d t", t=2),
        b.th_sc[:, :].rearrange("p (d t) -> p d t", t=2),
        _colbc(b, 64), ALU.add), inc="ve")



def dup_ap(sl, nblk=4, dup=2, w=P):
    a = [list(p) for p in sl.ap]
    return bass.AP(sl.tensor, sl.offset, [a[0], [w, nblk], [0, dup], [1, w]])


def v3(sl, nu=8, w=P):
    return sl.rearrange("p (u w) -> p u w", w=w)


def emit_attn_a(em, b):
    i = b.inp
    sm = b.smallf
    em.phase = "a0"
    ve_ln, sd_ln = emit_ln(em, b, b.A, gin=b.gin, eps=1e-5)
    cc_g = emit_collective(em, b, "AllGather", b.gin, b.gxn, [("sd", sd_ln)])
    em.op("scalar", lambda q: q.activation(b.nealog[0:4, 0:1],
          b.alog_sb[0:4, 0:1], AF.Exp), inc="ae")
    em.op("vector", lambda q: q.tensor_scalar(b.nealog[0:4, 0:1],
          b.nealog[0:4, 0:1], -1.0, None, ALU.mult),
          waits=[("ae", em.val("ae"))], inc="ve")
    em.phase = "a1"
    CV = 4608
    SQ = 7680
    BC = 9728
    FO = 11776
    for pas in range(2):
        war = [("te", em.val("te")), ("ve", em.val("ve")), ("ae", em.val("ae")),
               ("gd", em.val("gd")), ("sd", em.val("sd"))]
        gd_w = em.dma("gpsimd", b.Bb[:, 0:12288],
                      i["wsA_img"][:, pas * 12288:(pas + 1) * 12288], "gd",
                      waits=war)
        pcons = {}
        for n in range(16):
            r, off = n // 2, (n % 2) * 512
            bi = n // 8
            xc = b.Ee[:, 0:8192] if n % 2 == 0 else b.A[:, 0:8192]
            xw = pcons.get(n - 2, [("cc", cc_g), ("te", em.val("te")),
                                   ("ae", em.val("ae")), ("ve", em.val("ve")),
                                   ("sd", em.val("sd"))])
            sd_x = em.dma(
                "sync", xc.rearrange("p (k t) -> p k t", t=512),
                b.gxn[2048 * r:2048 * (r + 1), off:off + 512].rearrange(
                    "(k p) t -> p k t", p=P), "sd", waits=xw)
            if pas == 0:
                for (img0, ps, psn) in ((0, b.psE, "psE"), (64, b.psF, "psF")):
                    for k in range(DT):
                        em.op("tensor", lambda q, img0=img0, k=k, ps=ps, xc=xc:
                              q.matmul(
                            ps[0:4, 0:512], b.wba_sb[:, img0 + k * 4:img0 + (k + 1) * 4],
                            xc[:, k * 512:(k + 1) * 512],
                            start=(k == 0), stop=(k == DT - 1)),
                            waits=[("sd", sd_x)] +
                                  (em.psw(psn) if k == 0 else []), inc="te")
                ba_v = em.val("te")
                stg = b.Dd[0:4, 3072:3584]
                em.op("vector", lambda q: q.tensor_copy(stg, b.psE[0:4, 0:512]),
                      waits=[("te", ba_v), ("sd", em.val("sd"))], inc="ve")
                em.psr("psE", "ve")
                sd_b = em.dma("sync", b.ba_scr[0:4, n * 512:(n + 1) * 512], stg,
                              "sd", waits=[("ve", em.val("ve"))])
                em.op("vector", lambda q: q.tensor_copy(stg, b.psF[0:4, 0:512]),
                      waits=[("te", ba_v), ("sd", sd_b)], inc="ve")
                em.psr("psF", "ve")
                em.dma("sync", b.ba_scr[4:8, n * 512:(n + 1) * 512], stg,
                       "sd", waits=[("ve", em.val("ve"))])
            for m in range(6):
                gm = pas * 6 + m
                kind = ("q", "q", "k", "k", "v", "v")[m] if pas == 0 else                        ("v", "v", "z", "z", "z", "z")[m]
                ps, psn = (b.psA, "psA") if m % 2 == 0 else (b.psB, "psB")
                for k in range(DT):
                    em.op("tensor", lambda q, m=m, k=k, ps=ps, xc=xc: q.matmul(
                        ps[:, 0:512], b.Bb[:, (m * DT + k) * P:(m * DT + k + 1) * P],
                        xc[:, k * 512:(k + 1) * 512],
                        start=(k == 0), stop=(k == DT - 1)),
                        waits=[("gd", gd_w), ("sd", sd_x)] +
                              (em.psw(psn) if k == 0 else []), inc="te")
                mm_v = em.val("te")
                if kind == "z":
                    zslot = b.Cc[:, FO + m * 512:FO + (m + 1) * 512]
                    em.op("scalar", lambda q, ps=ps, zslot=zslot: q.activation(
                        zslot, ps[:, 0:512], AF.Silu),
                        waits=[("te", mm_v), ("sd", em.val("sd"))], inc="ae")
                    em.psr(psn, "ae")
                    zr = (gm - 8) * P
                    em.dma("sync", b.z_scr[zr:zr + P, n * 512:(n + 1) * 512],
                           zslot, "sd", waits=[("ae", em.val("ae"))])
                    continue
                rawb = b.Cc[:, m * 768:m * 768 + 515]
                cvi = gm if gm < 8 else None
                if n == 0 or n == 8:
                    em.op("vector", lambda q, rawb=rawb: q.memset(
                        rawb[:, 0:3], 0.0), waits=[("ae", em.val("ae"))], inc="ve")
                else:
                    em.op("vector", lambda q, rawb=rawb: q.tensor_copy(
                        rawb[:, 0:3], rawb[:, 512:515]),
                        waits=[("ae", em.val("ae")), ("gd", em.val("gd"))],
                        inc="ve")
                cw_v = em.val("ve")
                em.op("scalar", lambda q, ps=ps, rawb=rawb: q.activation(
                    rawb[:, 3:515], ps[:, 0:512], AF.Identity),
                    waits=[("te", mm_v), ("ve", cw_v)], inc="ae")
                em.psr(psn, "ae")
                ev_v = em.val("ae")
                cvo = b.Cc[:, CV + m * 512:CV + (m + 1) * 512]
                em.op("vector", lambda q, rawb=rawb, cvo=cvo, cvi=cvi:
                      q.tensor_scalar(cvo, rawb[:, 0:512],
                                      b.cw_sb[:, cvi * 4:cvi * 4 + 1], None,
                                      ALU.mult),
                      waits=[("ae", ev_v)], inc="ve")
                for j in range(1, 4):
                    em.op("vector", lambda q, rawb=rawb, cvo=cvo, cvi=cvi, j=j:
                          q.scalar_tensor_tensor(
                              cvo, rawb[:, j:512 + j],
                              b.cw_sb[:, cvi * 4 + j:cvi * 4 + j + 1], cvo,
                              ALU.mult, ALU.add), inc="ve")
                cv_v = em.val("ve")
                fo = b.Cc[:, FO + m * 512:FO + (m + 1) * 512]
                em.op("scalar", lambda q, cvo=cvo, fo=fo: q.activation(
                    fo, cvo, AF.Silu),
                    waits=[("ve", cv_v), ("sd", em.val("sd"))], inc="ae")
                si_v = em.val("ae")
                if kind == "v":
                    vr = ((gm - 4) if pas == 0 else (gm - 4)) * P
                    em.dma("sync", b.v_scr[vr:vr + P, n * 512:(n + 1) * 512],
                           fo, "sd", waits=[("ae", si_v)])
                    continue
                mq = m
                sq = b.Cc[:, SQ + mq * 512:SQ + (mq + 1) * 512]
                em.op("vector", lambda q, fo=fo, sq=sq: q.tensor_mul(sq, fo, fo),
                      waits=[("ae", si_v), ("te", em.val("te"))], inc="ve")
                sq_v = em.val("ve")
                em.op("tensor", lambda q, sq=sq: q.matmul(
                    b.psC[0:1, 0:512], b.ones_col[:, 0:1], sq,
                    start=True, stop=True),
                    waits=[("ve", sq_v)] + em.psw("psC"), inc="te")
                ss_v = em.val("te")
                rr = b.Dd[0:1, 1024:1536]
                em.op("vector", lambda q, rr=rr: q.tensor_scalar(
                    rr, b.psC[0:1, 0:512], 1.0, 1e-6, ALU.mult, ALU.add),
                    waits=[("te", ss_v), ("ae", em.val("ae"))], inc="ve")
                em.psr("psC", "ve")
                rv = em.val("ve")
                em.op("scalar", lambda q, rr=rr: q.activation(rr, rr, AF.Ln),
                      waits=[("ve", rv)], inc="ae")
                ln_v = em.val("ae")
                em.op("scalar", lambda q, rr=rr: q.activation(
                    rr, rr, AF.Exp, scale=-0.5),
                    waits=[("ae", ln_v)], inc="ae")
                rst_v = em.val("ae")
                if kind == "q":
                    em.op("vector", lambda q, rr=rr: q.tensor_scalar(
                        rr, rr, 0.08838834764831845, None, ALU.mult),
                        waits=[("ae", rst_v)], inc="ve")
                em.op("vector", lambda q, rr=rr: q.tensor_copy(
                    b.rowbf[0:1, 0:512], rr),
                    waits=[("ae", rst_v), ("te", em.val("te"))], inc="ve")
                rb_v = em.val("ve")
                em.op("tensor", lambda q: q.matmul(
                    b.psD[:, 0:512], b.ones_row[0:1, :], b.rowbf[0:1, 0:512],
                    start=True, stop=True),
                    waits=[("ve", rb_v)] + em.psw("psD"), inc="te")
                bc_v = em.val("te")
                bce = b.Cc[:, BC + mq * 512:BC + (mq + 1) * 512]
                em.op("scalar", lambda q, bce=bce: q.activation(
                    bce, b.psD[:, 0:512], AF.Identity),
                    waits=[("te", bc_v), ("ve", em.val("ve"))], inc="ae")
                em.psr("psD", "ae")
                em.op("vector", lambda q, fo=fo, bce=bce: q.tensor_mul(
                    fo, fo, bce), waits=[("ae", em.val("ae"))], inc="ve")
                nm_v = em.val("ve")
                qr = gm * P
                em.dma("sync", b.qk_scr[qr:qr + P, n * 512:(n + 1) * 512],
                       fo, "sd", waits=[("ve", nm_v)])
            pcons[n] = [("te", em.val("te")), ("ve", em.val("ve")),
                        ("ae", em.val("ae")), ("sd", em.val("sd")),
                        ("gd", em.val("gd"))]
    em.phase = "a2"
    ones4 = b.Dd[0:4, 3584:3712]
    em.op("vector", lambda q: q.memset(ones4, 1.0), inc="ve")
    sl_ = lambda k: b.Dd[0:4, k * 512:(k + 1) * 512]
    proj_sd = em.val("sd")
    for c in range(16):
        bi = c // 8
        i0 = (c % 8) * 4
        tok = slice(c * 512, (c + 1) * 512)
        war = [("ve", em.val("ve")), ("ae", em.val("ae")),
               ("gd", em.val("gd"))]
        sd_r = em.dma("sync", sl_(0), b.ba_scr[0:4, tok], "sd",
                      waits=[("sd", proj_sd)] + war)
        sd_r = em.dma("sync", sl_(2), b.ba_scr[4:8, tok], "sd")
        ld_v = em.val("sd")
        em.op("scalar", lambda q: q.activation(sl_(1), sl_(0), AF.Sigmoid),
              waits=[("sd", ld_v), ("ve", em.val("ve")),
                     ("gd", em.val("gd"))], inc="ae")
        b_v = em.val("ae")
        em.op("vector", lambda q: q.tensor_scalar(sl_(0), sl_(1), -1.0, None,
              ALU.mult), waits=[("ae", b_v)], inc="ve")
        em.op("vector", lambda q: q.tensor_scalar(sl_(2), sl_(2),
              b.dtb_sb[0:4, 0:1], None, ALU.add),
              waits=[("sd", ld_v), ("gd", em.val("gd")), ("ae", em.val("ae"))],
              inc="ve")
        a_v = em.val("ve")
        em.op("vector", lambda q: q.tensor_scalar_max(sl_(6), sl_(2), 0.0),
              inc="ve")
        ab_v = em.val("ve")
        em.op("vector", lambda q: q.scalar_tensor_tensor(
            sl_(3), sl_(2), -1.0, sl_(2), ALU.mult, ALU.max), inc="ve")
        abs_v = em.val("ve")
        em.op("scalar", lambda q: q.activation(sl_(4), sl_(3), AF.Exp,
              scale=-1.0), waits=[("ve", abs_v)], inc="ae")
        en_v = em.val("ae")
        em.op("scalar", lambda q: q.activation(sl_(3), sl_(4), AF.Ln,
              bias=1.0), waits=[("ae", en_v)], inc="ae")
        l1_v = em.val("ae")
        em.op("vector", lambda q: q.tensor_add(sl_(6), sl_(6), sl_(3)),
              waits=[("ae", l1_v)], inc="ve")
        em.op("vector", lambda q: q.tensor_scalar(sl_(6), sl_(6),
              b.nealog[0:4, 0:1], None, ALU.mult), inc="ve")
        g_v = em.val("ve")
        for cs in range(4):
            em.op("vector", lambda q, cs=cs: q.tensor_tensor_scan(
                sl_(2)[0:4, cs * P:(cs + 1) * P], ones4,
                sl_(6)[0:4, cs * P:(cs + 1) * P], 0.0, ALU.mult, ALU.add),
                waits=[("ve", g_v)], inc="ve")
        gc_v = em.val("ve")

        def rowout(qid, slot, dst, rl=4096):
            em.dma("gpsimd",
                   bass.AP(dst.tensor, dst.offset + i0 * rl + qid * 1024 + bi * 512,
                           [[P, 4], [rl, 4], [1, P]]),
                   slot.rearrange("v (s w) -> v s w", w=P), "gd",
                   waits=[("ve", em.val("ve")), ("ae", em.val("ae"))])
        rowout(0, sl_(2), b.rows_scr)
        em.op("vector", lambda q: q.tensor_copy(b.r4lo[0:4, :], sl_(2)),
              waits=[("ve", gc_v)], inc="ve")
        hb_v = em.val("ve")
        em.op("vector", lambda q: q.tensor_copy(sl_(3), b.r4lo[0:4, :]),
              waits=[("ve", hb_v)], inc="ve")
        em.op("vector", lambda q: q.tensor_sub(sl_(3), sl_(2), sl_(3)),
              inc="ve")
        rowout(3, sl_(3), b.rows_scr)
        em.op("scalar", lambda q: q.activation(sl_(4), sl_(2), AF.Exp),
              waits=[("ve", gc_v)], inc="ae")
        eg_v = em.val("ae")
        rowout(1, sl_(4), b.rows_scr)
        rowout(2, sl_(0), b.rows_scr)
        em.op("vector", lambda q: q.tensor_copy(
            sl_(3).rearrange("v (s w) -> v s w", w=P),
            sl_(2)[0:4, P - 1::P].broadcast_to([4, 4, P])),
            waits=[("ae", eg_v), ("gd", em.val("gd"))], inc="ve")
        em.op("vector", lambda q: q.tensor_sub(sl_(6), sl_(3), sl_(2)),
              inc="ve")
        sb_v = em.val("ve")
        em.op("scalar", lambda q: q.activation(sl_(3), sl_(6), AF.Exp),
              waits=[("ve", sb_v)], inc="ae")
        egl_v = em.val("ae")
        rowout(1, sl_(3), b.cols_scr, rl=5120)
        em.op("vector", lambda q: q.tensor_mul(sl_(6), sl_(3), sl_(4)),
              waits=[("ae", egl_v)], inc="ve")
        rowout(3, sl_(6), b.cols_scr, rl=5120)
        em.op("vector", lambda q: q.tensor_mul(sl_(5), sl_(1), sl_(4)),
              inc="ve")
        rowout(0, sl_(5), b.cols_scr, rl=5120)
        rowout(2, sl_(1), b.cols_scr, rl=5120)
        rowout(4, sl_(0), b.cols_scr, rl=5120)
    em.phase = "a3"
    war = [("te", em.val("te")), ("ve", em.val("ve")), ("ae", em.val("ae")),
           ("sd", em.val("sd"))]
    gd_m1 = em.dma("gpsimd", b.Ee[:, 0:1024], i["c_maskD"][:, :], "gd",
                   waits=war)
    gd_m2 = em.dma("gpsimd", b.Ee[:, 1024:2048], i["c_maskD2"][:, :], "gd")
    gd_m3 = em.dma("gpsimd", b.Ee[:, 2048:3072], i["c_mask01u"][:, :], "gd")
    gd_m4 = em.dma("gpsimd", b.Ee[:, 3072:4096], i["c_eyeR"][:, :], "gd")
    em.dma("gpsimd", b.r4lo[0:4, :], i["c_blockneg"][:, :], "gd",
           waits=[("ve", em.val("ve"))])
    msk_v = em.val("gd")
    mD = b.Ee[:, 0:1024]
    mD2 = b.Ee[:, 1024:2048]
    m01 = b.Ee[:, 2048:3072]
    eyeR = b.Ee[:, 3072:4096]
    cons_after = {}
    Aa = lambda k: b.A[:, k * 1024:(k + 1) * 1024]
    Dw, D2w, Xs, XTs, Ts = Aa(0), Aa(1), Aa(2), Aa(3), Aa(4)
    qeg, aiT, vcs, kcds, vnew = Aa(5), Aa(6), Aa(7), Aa(8), Aa(9)
    vtil, stbf, osb, ozq, ofin = Aa(10), Aa(11), Aa(12), Aa(13), Aa(14)
    vbw = b.Cc[:, 0:1024]
    kbegw = b.Cc[:, 1024:2048]
    state = b.Dd[:, 0:1024]
    tmpD = b.Dd[:, 1024:2048]
    tmp2 = b.Dd[:, 2048:3072]
    em.op("vector", lambda q: q.memset(state[:, :], 0.0),
          waits=[("te", em.val("te")), ("gd", em.val("gd")),
                 ("sd", em.val("sd")), ("ae", em.val("ae"))], inc="ve")
    em.op("vector", lambda q: q.memset(stbf[:, :], 0.0), inc="ve")
    st0_v = em.val("ve")
    L = {}

    def _scan_loads(it):
        pp = it % 2
        em.phase = "a3a"
        ldw = cons_after.get(it - 2,
                             [("te", em.val("te")), ("ve", em.val("ve")),
                              ("ae", em.val("ae")), ("sc", em.val("sc")),
                              ("gd", em.val("gd")), ("sd", em.val("sd"))]
                             if it < 2 else None)
        rbase = 8192 + pp * 4096
        rows_sd = em.dma("sync", b.Bb[0:1, rbase:rbase + 4096],
                         b.rows_scr[it, 0:4096], "sd", waits=ldw)
        crow_sd = em.dma("sync",
                         b.Cc[0:40, 4096 + pp * P:4096 + (pp + 1) * P],
                         b.cols_scr[it, :].rearrange("(u w) -> u w", w=P),
                         "sd", waits=ldw)
        em.op("tensor", lambda q, pp=pp: q.transpose(
            b.psT[:, pp * 40:pp * 40 + 32],
            b.Cc[0:32, 4096 + pp * P:4096 + (pp + 1) * P],
            b.ident[0:32, 0:32]),
            waits=[("sd", em.val("sd"))] + em.psw("psT"), inc="te")
        em.op("tensor", lambda q, pp=pp: q.transpose(
            b.psT[:, pp * 40 + 32:pp * 40 + 40],
            b.Cc[32:40, 4096 + pp * P:4096 + (pp + 1) * P],
            b.ident[32:40, 32:40]), inc="te")
        em.op("vector", lambda q, pp=pp: q.tensor_copy(
            b.Dd[:, 3072 + pp * 40:3112 + pp * 40],
            b.psT[:, pp * 40:pp * 40 + 40]),
            waits=[("te", em.val("te"))], inc="ve")
        em.psr("psT", "ve")
        r4base = 15360 + pp * 512
        ktm = b.Bb[:, 4096 + pp * 1024:4096 + (pp + 1) * 1024]
        vstep = b.Bb[:, 6144 + pp * 1024:6144 + (pp + 1) * 1024]
        zstep = b.Cc[:, 2048 + pp * 1024:2048 + (pp + 1) * 1024]
        qstep = b.Bb[:, 0 + pp * 1024:0 + (pp + 1) * 1024]
        kstep = b.Bb[:, 2048 + pp * 1024:2048 + (pp + 1) * 1024]
        for u in range(8):
            bi2, vh = u // 4, u % 4
            kh = vh // 2
            cb = bi2 * S + it * P
            em.dma("sync", qstep[:, u * P:(u + 1) * P],
                   b.qk_scr[kh * P:(kh + 1) * P, cb:cb + P], "sd")
            em.dma("sync", kstep[:, u * P:(u + 1) * P],
                   b.qk_scr[256 + kh * P:256 + (kh + 1) * P, cb:cb + P], "sd")
            em.op("sync", lambda q, u=u, kh=kh, cb=cb, ktm=ktm:
                  q.dma_start_transpose(
                      ktm[:, u * P:(u + 1) * P],
                      b.qk_scr[256 + kh * P:256 + (kh + 1) * P, cb:cb + P]),
                  waits=(ldw if u == 0 else ()), inc="sc", inc_amt=16)
            em.op("sync", lambda q, u=u, vh=vh, cb=cb, vstep=vstep:
                  q.dma_start_transpose(
                      vstep[:, u * P:(u + 1) * P],
                      b.v_scr[vh * P:(vh + 1) * P, cb:cb + P]),
                  waits=(), inc="sc", inc_amt=16)
        for bi2 in range(2):
            cb = bi2 * S + it * P
            em.dma("sync",
                   zstep[:, bi2 * 512:(bi2 + 1) * 512].rearrange(
                       "p (v w) -> p v w", w=P),
                   b.z_scr[0:512, cb:cb + P].rearrange("(v p) w -> p v w", p=P),
                   "sd")
        bar = cons_after.get(it - 1, ldw)
        for qi, qq in ((0, 0), (1, 3)):
            em.dma("sync",
                   b.A[0:4, r4base + qi * 256:r4base + (qi + 1) * 256].rearrange(
                       "r (h w) -> r h w", w=P),
                   b.rows_scr[it, qq * 1024:(qq + 1) * 1024].rearrange(
                       "(h r w) -> r h w", h=2, w=P),
                   "sd", waits=bar)
        L[it] = dict(r4_sd=em.val("sd"), ld_sd=em.val("sd"),
                     ld_sc=em.val("sc"), ld_ve=em.val("ve"))

    def _scan_step(it, pp):
        li = L[it]
        r4_sd, ld_sd = li["r4_sd"], li["ld_sd"]
        ld_sc, ld_ve = li["ld_sc"], li["ld_ve"]
        rbase = 8192 + pp * 4096
        colT = b.Dd[:, 3072 + pp * 40:3112 + pp * 40]
        ccol = lambda qid, u: colT[:, qid * 8 + u:qid * 8 + u + 1]
        r4base = 15360 + pp * 512
        rows4 = b.A[0:4, r4base:r4base + 512]
        qstep = b.Bb[:, 0 + pp * 1024:0 + (pp + 1) * 1024]
        kstep = b.Bb[:, 2048 + pp * 1024:2048 + (pp + 1) * 1024]
        ktm = b.Bb[:, 4096 + pp * 1024:4096 + (pp + 1) * 1024]
        vstep = b.Bb[:, 6144 + pp * 1024:6144 + (pp + 1) * 1024]
        zstep = b.Cc[:, 2048 + pp * 1024:2048 + (pp + 1) * 1024]
        em.phase = "a3d"
        rowsh = lambda qid, h: b.Bb[0:1, rbase + qid * 1024 + h * 512:
                                    rbase + qid * 1024 + h * 512 + 512]
        for h in range(2):
            ps = b.psC if h == 0 else b.psD
            psn = "psC" if h == 0 else "psD"
            em.op("tensor", lambda q, h=h, ps=ps: q.matmul(
                ps[:, 0:512], b.ones_row[0:1, :],
                rowsh(0, h), start=True, stop=False),
                waits=[("sd", r4_sd), ("gd", msk_v)] + em.psw(psn), inc="te")
            em.op("tensor", lambda q, h=h, ps=ps: q.matmul(
                ps[:, 0:512], b.ones_row[0:1, :],
                rowsh(3, h), start=False, stop=False), inc="te")
            em.op("tensor", lambda q, h=h, ps=ps, rows4=rows4: q.matmul(
                ps[:, 0:512], rows4[0:4, h * P:(h + 1) * P],
                b.r4lo[0:4, :], start=False, stop=False), inc="te")
            em.op("tensor", lambda q, h=h, ps=ps, rows4=rows4: q.matmul(
                ps[:, 0:512], rows4[0:4, 256 + h * P:256 + (h + 1) * P],
                b.r4lo[0:4, :], start=False, stop=True), inc="te")
        g_v = em.val("te")
        em.op("vector", lambda q: q.scalar_tensor_tensor(
            tmp2[:, 0:512], b.psC[:, 0:512], 1.0, mD[:, 0:512],
            ALU.mult, ALU.add),
            waits=[("te", g_v), ("ae", em.val("ae")), ("gd", msk_v)], inc="ve")
        em.op("vector", lambda q: q.scalar_tensor_tensor(
            tmp2[:, 512:1024], b.psD[:, 0:512], 1.0, mD[:, 512:1024],
            ALU.mult, ALU.add), inc="ve")
        dw_pre = em.val("ve")
        em.op("vector", lambda q: q.scalar_tensor_tensor(
            tmpD[:, 0:512], b.psC[:, 0:512], -1.0, mD2[:, 0:512],
            ALU.mult, ALU.add), inc="ve")
        em.op("vector", lambda q: q.scalar_tensor_tensor(
            tmpD[:, 512:1024], b.psD[:, 0:512], -1.0, mD2[:, 512:1024],
            ALU.mult, ALU.add), inc="ve")
        em.psr("psC", "ve")
        em.psr("psD", "ve")
        d2_pre = em.val("ve")
        em.op("scalar", lambda q: q.activation(Dw, tmp2[:, :], AF.Exp),
              waits=[("ve", dw_pre), ("te", em.val("te"))], inc="ae")
        d_v = em.val("ae")
        em.op("scalar", lambda q: q.activation(D2w, tmpD[:, :], AF.Exp),
              waits=[("ve", d2_pre)], inc="ae")
        d2_v = em.val("ae")
        em.phase = "a3a2"
        for u in range(8):
            ps = b.psA if u < 4 else b.psB
            psn = "psA" if u < 4 else "psB"
            em.op("tensor", lambda q, u=u, ps=ps: q.matmul(
                ps[:, (u % 4) * P:(u % 4 + 1) * P], kstep[:, u * P:(u + 1) * P],
                kstep[:, u * P:(u + 1) * P], start=True, stop=True),
                waits=[("sd", ld_sd)] + (em.psw(psn) if u % 4 == 0 else []),
                inc="te")
            ps2 = b.psE if u < 4 else b.psF
            ps2n = "psE" if u < 4 else "psF"
            em.op("tensor", lambda q, u=u, ps2=ps2: q.matmul(
                ps2[:, (u % 4) * P:(u % 4 + 1) * P], kstep[:, u * P:(u + 1) * P],
                qstep[:, u * P:(u + 1) * P], start=True, stop=True),
                waits=[("sd", ld_sd)] + (em.psw(ps2n) if u % 4 == 0 else []),
                inc="te")
        kk_v = em.val("te")
        em.op("vector", lambda q: q.tensor_tensor(aiT[:, 0:512], b.psE[:, 0:512],
              Dw[:, 0:512], ALU.mult), waits=[("te", kk_v), ("ae", d_v)],
              inc="ve")
        em.op("vector", lambda q: q.tensor_tensor(aiT[:, 512:1024],
              b.psF[:, 0:512], Dw[:, 512:1024], ALU.mult), inc="ve")
        em.psr("psE", "ve")
        em.psr("psF", "ve")
        em.op("vector", lambda q: q.tensor_tensor(Xs[:, 0:512], b.psA[:, 0:512],
              Dw[:, 0:512], ALU.mult), inc="ve")
        em.op("vector", lambda q: q.tensor_tensor(Xs[:, 512:1024],
              b.psB[:, 0:512], Dw[:, 512:1024], ALU.mult), inc="ve")
        for u in range(8):
            psx = b.psA if u < 4 else b.psB
            em.op("vector", lambda q, u=u, psx=psx: q.scalar_tensor_tensor(
                XTs[:, u * P:(u + 1) * P],
                psx[:, (u % 4) * P:(u % 4 + 1) * P], ccol(4, u),
                D2w[:, u * P:(u + 1) * P], ALU.mult, ALU.mult),
                waits=[("ae", d2_v), ("ve", ld_ve)], inc="ve")
        em.psr("psA", "ve")
        em.psr("psB", "ve")
        for h in range(2):
            ps = b.psC if h == 0 else b.psD
            psn = "psC" if h == 0 else "psD"
            em.op("tensor", lambda q, h=h, ps=ps: q.matmul(
                ps[:, 0:512], b.ones_row[0:1, :],
                rowsh(2, h), start=True, stop=True),
                waits=em.psw(psn), inc="te")
        bw_v = em.val("te")
        em.op("vector", lambda q: q.tensor_tensor(Xs[:, 0:512], Xs[:, 0:512],
              b.psC[:, 0:512], ALU.mult), waits=[("te", bw_v)], inc="ve")
        em.op("vector", lambda q: q.tensor_tensor(Xs[:, 512:1024],
              Xs[:, 512:1024], b.psD[:, 0:512], ALU.mult), inc="ve")
        em.psr("psC", "ve")
        em.psr("psD", "ve")
        em.op("vector", lambda q: q.tensor_mul(Xs, Xs, m01), inc="ve")
        for h in range(2):
            ps = b.psE if h == 0 else b.psF
            psn = "psE" if h == 0 else "psF"
            em.op("tensor", lambda q, h=h, ps=ps: q.matmul(
                ps[:, 0:512], b.ones_row[0:1, :],
                rowsh(1, h), start=True, stop=True),
                waits=em.psw(psn), inc="te")
        eb_v = em.val("te")
        em.op("vector", lambda q: q.tensor_tensor(qeg[:, 0:512], qstep[:, 0:512],
              b.psE[:, 0:512], ALU.mult), waits=[("te", eb_v)], inc="ve")
        em.op("vector", lambda q: q.tensor_tensor(qeg[:, 512:1024],
              qstep[:, 512:1024], b.psF[:, 0:512], ALU.mult), inc="ve")
        em.psr("psE", "ve")
        em.psr("psF", "ve")
        pbuilt_v = em.val("ve")
        em.phase = "a3b"
        em.op("vector", lambda q: q.tensor_add(Ts, Xs, eyeR),
              waits=[("ve", pbuilt_v)], inc="ve")
        ts0_v = em.val("ve")

        def _sq_mms(first_waits, with_x):
            for u in range(8):
                if with_x:
                    ps = b.psC if u < 4 else b.psD
                    psn = "psC" if u < 4 else "psD"
                    em.op("tensor", lambda q, u=u, ps=ps: q.matmul(
                        ps[:, (u % 4) * P:(u % 4 + 1) * P],
                        XTs[:, u * P:(u + 1) * P], Xs[:, u * P:(u + 1) * P],
                        start=True, stop=True),
                        waits=(first_waits if u == 0 else []) +
                              (em.psw(psn) if u % 4 == 0 else []), inc="te")
                ps2 = b.psE if u < 4 else b.psF
                ps2n = "psE" if u < 4 else "psF"
                em.op("tensor", lambda q, u=u, ps2=ps2: q.matmul(
                    ps2[:, (u % 4) * P:(u % 4 + 1) * P],
                    Xs[:, u * P:(u + 1) * P], XTs[:, u * P:(u + 1) * P],
                    start=True, stop=True),
                    waits=(first_waits if u == 0 and not with_x else []) +
                          (em.psw(ps2n) if u % 4 == 0 else []), inc="te")
            return em.val("te")

        def _sq_evac(sq_v, with_x):
            if with_x:
                em.op("vector", lambda q: q.tensor_copy(Xs[:, 0:512],
                      b.psC[:, 0:512]), waits=[("te", sq_v)], inc="ve")
                em.op("vector", lambda q: q.tensor_copy(Xs[:, 512:1024],
                      b.psD[:, 0:512]), inc="ve")
                em.psr("psC", "ve")
                em.psr("psD", "ve")
            em.op("scalar", lambda q: q.activation(XTs[:, 0:512],
                  b.psE[:, 0:512], AF.Identity), waits=[("te", sq_v)],
                  inc="ae")
            em.op("scalar", lambda q: q.activation(XTs[:, 512:1024],
                  b.psF[:, 0:512], AF.Identity), inc="ae")
            em.psr("psE", "ae")
            em.psr("psF", "ae")
            return em.val("ve"), em.val("ae")

        sq_v = _sq_mms([("ve", ts0_v)], True)
        xe_ve, xe_ae = _sq_evac(sq_v, True)
        tadd_prev = ts0_v
        for lv in range(1, 7):
            for u in range(8):
                ps = b.psA if u < 4 else b.psB
                psn = "psA" if u < 4 else "psB"
                em.op("tensor", lambda q, u=u, ps=ps: q.matmul(
                    ps[:, (u % 4) * P:(u % 4 + 1) * P],
                    XTs[:, u * P:(u + 1) * P], Ts[:, u * P:(u + 1) * P],
                    start=True, stop=True),
                    waits=([("ve", max(xe_ve, tadd_prev)), ("ae", xe_ae)]
                           if u == 0 else []) +
                          (em.psw(psn) if u % 4 == 0 else []), inc="te")
            t_v = em.val("te")
            if lv < 6:
                sq_v = _sq_mms([], lv < 5)
            em.op("vector", lambda q: q.tensor_tensor(Ts[:, 0:512],
                  Ts[:, 0:512], b.psA[:, 0:512], ALU.add),
                  waits=[("te", t_v)], inc="ve")
            em.op("vector", lambda q: q.tensor_tensor(Ts[:, 512:1024],
                  Ts[:, 512:1024], b.psB[:, 0:512], ALU.add), inc="ve")
            em.psr("psA", "ve")
            em.psr("psB", "ve")
            tadd_prev = em.val("ve")
            if lv < 6:
                xe_ve, xe_ae = _sq_evac(sq_v, lv < 5)
        inv_v = em.val("ve")
        em.phase = "a3c"
        for u in range(8):
            em.op("vector", lambda q, u=u: q.tensor_scalar(
                vbw[:, u * P:(u + 1) * P], vstep[:, u * P:(u + 1) * P],
                ccol(2, u), None, ALU.mult),
                waits=[("sd", ld_sd), ("sc", ld_sc), ("ve", ld_ve)], inc="ve")
            em.op("vector", lambda q, u=u: q.tensor_scalar(
                kbegw[:, u * P:(u + 1) * P], ktm[:, u * P:(u + 1) * P],
                ccol(0, u), None, ALU.mult), inc="ve")
        vk_v = em.val("ve")
        for u in range(8):
            ps = b.psC if u < 4 else b.psD
            psn = "psC" if u < 4 else "psD"
            em.op("tensor", lambda q, u=u, ps=ps: q.matmul(
                ps[:, (u % 4) * P:(u % 4 + 1) * P], Ts[:, u * P:(u + 1) * P],
                vbw[:, u * P:(u + 1) * P], start=True, stop=True),
                waits=[("ve", vk_v)] + (em.psw(psn) if u % 4 == 0 else []),
                inc="te")
            ps2 = b.psE if u < 4 else b.psF
            ps2n = "psE" if u < 4 else "psF"
            em.op("tensor", lambda q, u=u, ps2=ps2: q.matmul(
                ps2[:, (u % 4) * P:(u % 4 + 1) * P], kbegw[:, u * P:(u + 1) * P],
                Ts[:, u * P:(u + 1) * P], start=True, stop=True),
                waits=[("ve", vk_v)] + (em.psw(ps2n) if u % 4 == 0 else []),
                inc="te")
        vc_v = em.val("te")
        em.op("vector", lambda q: q.tensor_copy(vcs[:, 0:512], b.psC[:, 0:512]),
              waits=[("te", vc_v)], inc="ve")
        em.op("vector", lambda q: q.tensor_copy(vcs[:, 512:1024],
              b.psD[:, 0:512]), inc="ve")
        em.psr("psC", "ve")
        em.psr("psD", "ve")
        em.op("scalar", lambda q: q.activation(kcds[:, 0:512], b.psE[:, 0:512],
              AF.Identity), waits=[("te", vc_v)], inc="ae")
        em.op("scalar", lambda q: q.activation(kcds[:, 512:1024],
              b.psF[:, 0:512], AF.Identity), inc="ae")
        em.psr("psE", "ae")
        em.psr("psF", "ae")
        kcd_v = em.val("ve")
        kcd_ae = em.val("ae")
        for u in range(8):
            ps = b.psC if u < 4 else b.psD
            psn = "psC" if u < 4 else "psD"
            em.op("tensor", lambda q, u=u, ps=ps: q.matmul(
                ps[:, (u % 4) * P:(u % 4 + 1) * P], kcds[:, u * P:(u + 1) * P],
                stbf[:, u * P:(u + 1) * P], start=True, stop=True),
                waits=[("ve", max(kcd_v, st0_v)), ("ae", kcd_ae)] +
                      (em.psw(psn) if u % 4 == 0 else []), inc="te")
        t1_v = em.val("te")
        em.op("vector", lambda q: q.tensor_sub(vnew[:, 0:512], vcs[:, 0:512],
              b.psC[:, 0:512]), waits=[("te", t1_v)], inc="ve")
        em.op("vector", lambda q: q.tensor_sub(vnew[:, 512:1024],
              vcs[:, 512:1024], b.psD[:, 0:512]), inc="ve")
        em.psr("psC", "ve")
        em.psr("psD", "ve")
        vn_v = em.val("ve")
        for u in range(8):
            ps = b.psE if u < 4 else b.psF
            psn = "psE" if u < 4 else "psF"
            em.op("tensor", lambda q, u=u, ps=ps: q.matmul(
                ps[:, (u % 4) * P:(u % 4 + 1) * P], stbf[:, u * P:(u + 1) * P],
                qeg[:, u * P:(u + 1) * P], start=True, stop=False),
                waits=[("ve", vn_v)] + (em.psw(psn) if u % 4 == 0 else []),
                inc="te")
            em.op("tensor", lambda q, u=u, ps=ps: q.matmul(
                ps[:, (u % 4) * P:(u % 4 + 1) * P], vnew[:, u * P:(u + 1) * P],
                aiT[:, u * P:(u + 1) * P], start=False, stop=True), inc="te")
        o_v = em.val("te")
        em.op("scalar", lambda q: q.activation(osb[:, 0:512], b.psE[:, 0:512],
              AF.Identity), waits=[("te", o_v), ("ve", em.val("ve"))],
              inc="ae")
        em.op("scalar", lambda q: q.activation(osb[:, 512:1024], b.psF[:, 0:512],
              AF.Identity), inc="ae")
        em.psr("psE", "ae")
        em.psr("psF", "ae")
        oe_v = em.val("ae")
        for u in range(8):
            em.op("vector", lambda q, u=u: q.tensor_scalar(
                vtil[:, u * P:(u + 1) * P], vnew[:, u * P:(u + 1) * P],
                ccol(1, u), None, ALU.mult), waits=[("ve", vn_v)], inc="ve")
        vt_v = em.val("ve")
        for u in range(8):
            ps = b.psC if u < 4 else b.psD
            psn = "psC" if u < 4 else "psD"
            em.op("tensor", lambda q, u=u, ps=ps: q.matmul(
                ps[:, (u % 4) * P:(u % 4 + 1) * P], ktm[:, u * P:(u + 1) * P],
                vtil[:, u * P:(u + 1) * P], start=True, stop=True),
                waits=[("ve", vt_v)] + (em.psw(psn) if u % 4 == 0 else []),
                inc="te")
        t4_v = em.val("te")
        for u in range(8):
            em.op("vector", lambda q, u=u: q.tensor_scalar(
                state[:, u * P:(u + 1) * P], state[:, u * P:(u + 1) * P],
                ccol(3, u), None, ALU.mult),
                waits=[("ve", max(st0_v, ld_ve))], inc="ve")
        em.op("vector", lambda q: q.tensor_add(state[:, 0:512], state[:, 0:512],
              b.psC[:, 0:512]), waits=[("te", t4_v)], inc="ve")
        em.op("vector", lambda q: q.tensor_add(state[:, 512:1024],
              state[:, 512:1024], b.psD[:, 0:512]), inc="ve")
        em.psr("psC", "ve")
        em.psr("psD", "ve")
        em.op("vector", lambda q: q.tensor_copy(stbf, state),
              waits=[("te", t4_v)], inc="ve")
        em.op("vector", lambda q: q.tensor_mul(ozq, osb, osb),
              waits=[("ae", oe_v)], inc="ve")
        oq_v = em.val("ve")
        for h in range(2):
            ps = b.psA if h == 0 else b.psB
            psn = "psA" if h == 0 else "psB"
            em.op("tensor", lambda q, h=h, ps=ps: q.matmul(
                ps[0:1, 0:512], b.ones_col[:, 0:1],
                ozq[:, h * 512:(h + 1) * 512], start=True, stop=True),
                waits=[("ve", oq_v)] + em.psw(psn), inc="te")
        os_v = em.val("te")
        rrow = b.Dd[0:1, 1024:2048]
        em.op("vector", lambda q: q.tensor_scalar(rrow[0:1, 0:512],
              b.psA[0:1, 0:512], 1.0 / P, 1e-6, ALU.mult, ALU.add),
              waits=[("te", os_v), ("ae", em.val("ae"))], inc="ve")
        em.op("vector", lambda q: q.tensor_scalar(rrow[0:1, 512:1024],
              b.psB[0:1, 0:512], 1.0 / P, 1e-6, ALU.mult, ALU.add), inc="ve")
        em.psr("psA", "ve")
        em.psr("psB", "ve")
        rr_v = em.val("ve")
        em.op("scalar", lambda q: q.activation(rrow, rrow, AF.Ln),
              waits=[("ve", rr_v)], inc="ae")
        rl_v = em.val("ae")
        em.op("scalar", lambda q: q.activation(rrow, rrow, AF.Exp, scale=-0.5),
              waits=[("ae", rl_v)], inc="ae")
        re_v = em.val("ae")
        em.op("vector", lambda q: q.tensor_copy(b.rowbf[0:1, 0:1024], rrow),
              waits=[("ae", re_v), ("te", em.val("te"))], inc="ve")
        rb_v = em.val("ve")
        for h in range(2):
            ps = b.psA if h == 0 else b.psB
            psn = "psA" if h == 0 else "psB"
            em.op("tensor", lambda q, h=h, ps=ps: q.matmul(
                ps[:, 0:512], b.ones_row[0:1, :],
                b.rowbf[0:1, h * 512:(h + 1) * 512], start=True, stop=True),
                waits=[("ve", rb_v)] + em.psw(psn), inc="te")
        rbc_v = em.val("te")
        em.op("vector", lambda q: q.scalar_tensor_tensor(
            osb[:, 0:512], osb[:, 0:512], sm[:, 144:145], b.psA[:, 0:512],
            ALU.mult, ALU.mult), waits=[("te", rbc_v)], inc="ve")
        em.op("vector", lambda q: q.scalar_tensor_tensor(
            osb[:, 512:1024], osb[:, 512:1024], sm[:, 144:145],
            b.psB[:, 0:512], ALU.mult, ALU.mult), inc="ve")
        em.psr("psA", "ve")
        em.psr("psB", "ve")
        em.op("vector", lambda q, zstep=zstep: q.tensor_mul(ofin, osb, zstep),
              waits=[("sd", ld_sd), ("gd", em.val("gd"))], inc="ve")
        of_v = em.val("ve")
        for u in range(8):
            bi2, vh = u // 4, u % 4
            j = bi2 * 4 + it // 8
            col = (it % 8) * P
            em.dma("gpsimd", b.oa_in[j * 512 + vh * P:j * 512 + (vh + 1) * P,
                                     col:col + P],
                   ofin[:, u * P:(u + 1) * P], "gd", waits=[("ve", of_v)])
        cons_after[it] = [("te", em.val("te")), ("ve", em.val("ve")),
                          ("ae", em.val("ae")), ("sd", em.val("sd")),
                          ("sc", em.val("sc")), ("gd", em.val("gd"))]
    _scan_loads(0)
    for it in range(32):
        _scan_step(it, it % 2)
        if it + 1 < 32:
            _scan_loads(it + 1)

    em.phase = "a4"
    cc_o = emit_collective(em, b, "AllToAll", b.oa_in, b.oa_out,
                           [("sd", em.val("sd")), ("gd", em.val("gd"))])
    war = [("te", em.val("te")), ("ve", em.val("ve")), ("ae", em.val("ae")),
           ("gd", em.val("gd"))]
    wo = lambda m: b.Cc[:, 8192 + (m % 2) * 2048:8192 + (m % 2) * 2048 + 2048]
    cons = {}
    free_v = em.val("te")
    for half in range(2):
        sd_o = em.dma("sync", b.Bb[:, :].rearrange("p (k t) -> p k t", t=TOK),
                      b.oa_out[half * 2048:(half + 1) * 2048, :].rearrange(
                          "(k p) t -> p k t", p=P), "sd",
                      waits=[("cc", cc_o), ("te", em.val("te")),
                             ("ve", em.val("ve")), ("ae", em.val("ae"))])
        for m in range(DT):
            gd_v = em.dma("gpsimd", wo(m),
                          i["woutA_img"][:, (m * 32 + half * 16) * P:
                                         (m * 32 + half * 16 + 16) * P], "gd",
                          waits=[("te", cons.get((half, m - 2),
                                                 cons.get((half - 1, DT - 1),
                                                          free_v)))])
            for hh in range(2):
                ps, psn = (b.psA, "psA") if hh == 0 else (b.psB, "psB")
                for k in range(DT):
                    em.op("tensor", lambda q, m=m, hh=hh, k=k, ps=ps: q.matmul(
                        ps[:, 0:512], wo(m)[:, k * P:(k + 1) * P],
                        b.Bb[:, k * TOK + hh * 512:k * TOK + (hh + 1) * 512],
                        start=(k == 0), stop=(k == DT - 1)),
                        waits=[("gd", gd_v), ("sd", sd_o)] +
                              (em.psw(psn) if k == 0 else []), inc="te")
                te_v = em.val("te")
                cons[(half, m)] = te_v
                em.op("vector", lambda q, m=m, hh=hh, ps=ps: q.tensor_tensor(
                    b.xres[:, m * TOK + hh * 512:m * TOK + (hh + 1) * 512],
                    b.xres[:, m * TOK + hh * 512:m * TOK + (hh + 1) * 512],
                    ps[:, 0:512], ALU.add), waits=[("te", te_v)], inc="ve")
                em.psr(psn, "ve")


def emit_attn_b(em, b):
    i = b.inp
    sm = b.smallf
    em.phase = "b0"
    ve_ln, sd_ln = emit_ln(em, b, b.A, gin=b.gin, eps=1e-5)
    cc_g = emit_collective(em, b, "AllGather", b.gin, b.gxn,
                           [("sd", sd_ln)])
    war = [("te", em.val("te")), ("ve", em.val("ve")), ("ae", em.val("ae"))]
    gd_ws = em.dma("gpsimd", b.Bb[:, 0:8192], i["wsB_img"][:, :], "gd",
                   waits=war)
    gd_wv = em.dma("gpsimd", b.Cc[:, 0:4096], i["wvB_img"][:, :], "gd",
                   waits=war)
    gd_rc = em.dma("gpsimd", b.A[:, 0:4096], i["c_rope_c"][:, :], "gd",
                   waits=[("sd", sd_ln)] + war)
    gd_rs = em.dma("gpsimd", b.A[:, 4096:8192], i["c_rope_s"][:, :], "gd")
    gd_cm = em.dma("gpsimd", b.A[:, 12288:14336], i["c_cmask"][:, :], "gd")
    em.phase = "b1"
    st1 = b.Bb[:, 8192:8704]
    st2 = b.Bb[:, 8704:9216]
    st4 = b.Bb[:, 9216:9728]
    psq = [b.psA, b.psB, b.psC, b.psD]
    psqn = ["psA", "psB", "psC", "psD"]
    pcons = {}
    for n in range(16):
        r, off = n // 2, (n % 2) * 512
        bi = n // 8
        pos0 = (n % 8) * 512
        xc = b.Ee[:, 0:8192] if n % 2 == 0 else b.Cc[:, 4096:12288]
        xw = pcons.get(n - 2, [("cc", cc_g), ("te", em.val("te")),
                               ("ae", em.val("ae")), ("ve", em.val("ve")),
                               ("sd", em.val("sd"))])
        sd_x = em.dma(
            "sync", xc.rearrange("p (k t) -> p k t", t=512),
            b.gxn[2048 * r:2048 * (r + 1), off:off + 512].rearrange(
                "(k p) t -> p k t", p=P), "sd", waits=xw)
        for m in range(4):
            ps, psn = psq[m], psqn[m]
            for k in range(DT):
                em.op("tensor", lambda q, m=m, k=k, ps=ps, xc=xc: q.matmul(
                    ps[:, 0:512], b.Bb[:, (m * DT + k) * P:(m * DT + k + 1) * P],
                    xc[:, k * 512:(k + 1) * 512],
                    start=(k == 0), stop=(k == DT - 1)),
                    waits=[("gd", gd_ws), ("sd", sd_x)] +
                          (em.psw(psn) if k == 0 else []), inc="te")
            mm_v = em.val("te")
            em.op("scalar", lambda q, m=m, ps=ps: q.activation(
                st1, ps[:, 0:512], AF.Identity, bias=sm[:, 145 + m:146 + m]),
                waits=[("te", mm_v), ("ve", em.val("ve"))], inc="ae")
            ev_v = em.val("ae")
            em.psr(psn, "ae")
            em.op("tensor", lambda q: q.matmul(
                b.psE[:, 0:512], b.swap_pm[:, :], st1, start=True, stop=True),
                waits=[("ae", ev_v)] + em.psw("psE"), inc="te")
            sw_v = em.val("te")
            em.op("vector", lambda q, pos0=pos0: q.tensor_mul(
                st2, st1, b.A[:, pos0:pos0 + 512]),
                waits=[("ae", ev_v), ("gd", gd_rc)], inc="ve")
            em.op("vector", lambda q, pos0=pos0: q.tensor_mul(
                st4, b.psE[:, 0:512], b.A[:, 4096 + pos0:4096 + pos0 + 512]),
                waits=[("te", sw_v), ("sd", em.val("sd")), ("gd", gd_rs)],
                inc="ve")
            em.psr("psE", "ve")
            em.op("vector", lambda q: q.tensor_add(st4, st2, st4), inc="ve")
            rp_v = em.val("ve")
            h = m % 2
            qk = m // 2
            u = bi * 2 + h
            em.dma("sync", b.qb_scr[u * 384 + qk * P:u * 384 + qk * P + P,
                                    pos0:pos0 + 512], st4, "sd",
                   waits=[("ve", rp_v)])
        for tt in range(4):
            ps, psn = (b.psF, "psF") if tt % 2 == 0 else (b.psG, "psG")
            for k in range(DT):
                em.op("tensor", lambda q, tt=tt, k=k, ps=ps, xc=xc: q.matmul(
                    ps[:, 0:256], xc[:, k * 512 + tt * P:k * 512 + (tt + 1) * P],
                    b.Cc[:, k * 256:(k + 1) * 256],
                    start=(k == 0), stop=False),
                    waits=[("gd", gd_wv), ("sd", sd_x)] +
                          (em.psw(psn) if k == 0 else []), inc="te")
            em.op("tensor", lambda q, ps=ps: q.matmul(
                ps[:, 0:256], b.ones_row[0:1, :], b.bvB_sb[0:1, :],
                start=False, stop=True), inc="te")
            mm_v = em.val("te")
            em.op("scalar", lambda q, ps=ps: q.activation(
                st1[:, 0:256], ps[:, 0:256], AF.Identity),
                waits=[("te", mm_v), ("ve", em.val("ve")),
                       ("sd", em.val("sd"))], inc="ae")
            ev_v = em.val("ae")
            em.psr(psn, "ae")
            tg = (n % 8) * 4 + tt
            for h in range(2):
                u = bi * 2 + h
                em.dma("sync", b.qb_scr[u * 384 + 256:u * 384 + 384,
                                        tg * P:(tg + 1) * P],
                       st1[:, h * P:(h + 1) * P], "sd",
                       waits=[("ae", ev_v)])
        pcons[n] = [("te", em.val("te")), ("ve", em.val("ve")),
                    ("ae", em.val("ae")), ("sd", em.val("sd")),
                    ("gd", em.val("gd"))]
    em.phase = "b2"
    proj_sd = em.val("sd")
    RS = float(1.0 / np.sqrt(128.0))
    for u in range(4):
        bi, h = u // 2, u % 2
        war = [("te", em.val("te")), ("ve", em.val("ve")), ("ae", em.val("ae"))]
        sd_u = em.dma("sync",
                      b.Bb[:, 0:12288].rearrange("p (s t) -> p s t", t=S),
                      b.qb_scr[u * 384:(u + 1) * 384, :].rearrange(
                          "(s p) t -> p s t", p=P), "sd",
                      waits=[("sd", proj_sd)] + war)
        qblk = lambda c0, w: b.Bb[:, c0:c0 + w]
        for qp in range(4):
            kmax = (qp + 1) * 8
            for qb in range(2):
                pvp, pvn = (b.psC, "psC") if qb == 0 else (b.psD, "psD")
                smp, smn = (b.psE, "psE") if qb == 0 else (b.psF, "psF")
                kl = qp * 8 + qb * 4 + 4
                for kt in range(kl):
                    scp, scn = (b.psA, "psA") if kt % 2 == 0 else (b.psB, "psB")
                    em.op("tensor", lambda q, kt=kt, qp=qp, qb=qb, scp=scp:
                          q.matmul(scp[:, 0:512],
                                   b.Bb[:, 4096 + kt * P:4096 + (kt + 1) * P],
                                   b.Bb[:, qp * 1024 + qb * 512:
                                        qp * 1024 + (qb + 1) * 512],
                                   start=True, stop=True),
                          waits=[("sd", sd_u)] + em.psw(scn), inc="te")
                    sc_v = em.val("te")
                    r0 = kt - (qp * 8 + qb * 4)
                    mw = []
                    if 0 <= r0 < 4:
                        em.op("vector", lambda q, r0=r0, scp=scp: q.tensor_tensor(
                            scp[:, 0:512], scp[:, 0:512],
                            b.A[:, 12288 + r0 * 512:12288 + (r0 + 1) * 512],
                            ALU.add), waits=[("te", sc_v), ("gd", gd_cm)],
                            inc="ve")
                        mw = [("ve", em.val("ve"))]
                    slot = b.Ee[:, qb * 1024 + (kt % 2) * 512:
                                qb * 1024 + (kt % 2) * 512 + 512]
                    em.op("scalar", lambda q, scp=scp, slot=slot: q.activation(
                        slot, scp[:, 0:512], AF.Exp, scale=RS),
                        waits=[("te", sc_v)] + mw, inc="ae")
                    ex_v = em.val("ae")
                    em.psr(scn, "ae")
                    em.op("tensor", lambda q, kt=kt, pvp=pvp, slot=slot:
                          q.matmul(pvp[:, 0:512],
                                   b.Bb[:, 8192 + kt * P:8192 + (kt + 1) * P],
                                   slot, start=(kt == 0), stop=(kt == kl - 1)),
                          waits=[("ae", ex_v)] +
                                (em.psw(pvn) if kt == 0 else []), inc="te")
                    em.op("tensor", lambda q, kt=kt, smp=smp, slot=slot:
                          q.matmul(smp[0:1, 0:512], b.ones_col[:, 0:1],
                                   slot, start=(kt == 0), stop=(kt == kl - 1)),
                          waits=(em.psw(smn) if kt == 0 else []), inc="te")
                pv_v = em.val("te")
                rrow = b.Dd[0:1, 3072:3584]
                srow = b.Dd[0:1, 3584:3712]
                em.op("vector", lambda q, smp=smp: q.tensor_copy(
                    rrow, smp[0:1, 0:512]),
                    waits=[("te", pv_v), ("ae", em.val("ae"))], inc="ve")
                em.psr(smn, "ve")
                em.op("vector", lambda q: q.reciprocal(rrow, rrow), inc="ve")
                em.op("vector", lambda q: q.tensor_copy(
                    b.rowbf[0:1, 0:512], rrow),
                    waits=[("ve", em.val("ve")), ("te", em.val("te"))], inc="ve")
                rb_v = em.val("ve")
                scp2, scn2 = (b.psA, "psA")
                em.op("tensor", lambda q: q.matmul(
                    b.psA[:, 0:512], b.ones_row[0:1, :], b.rowbf[0:1, 0:512],
                    start=True, stop=True),
                    waits=[("ve", rb_v)] + em.psw("psA"), inc="te")
                bc_v = em.val("te")
                oslot = b.A[:, 8192 + qp * 1024 + qb * 512:
                            8192 + qp * 1024 + qb * 512 + 512]
                em.op("scalar", lambda q, pvp=pvp, oslot=oslot: q.activation(
                    oslot, pvp[:, 0:512], AF.Identity),
                    waits=[("te", bc_v), ("ve", em.val("ve")),
                           ("sd", em.val("sd"))], inc="ae")
                em.psr(pvn, "ae")
                em.op("vector", lambda q, oslot=oslot: q.tensor_mul(
                    oslot, oslot, b.psA[:, 0:512]),
                    waits=[("ae", em.val("ae")), ("te", bc_v)], inc="ve")
                em.psr("psA", "ve")
            o_v = em.val("ve")
            j = bi * 4 + qp
            em.dma("sync", b.ob_in[j * 256 + h * P:j * 256 + (h + 1) * P, :],
                   b.A[:, 8192 + qp * 1024:8192 + (qp + 1) * 1024], "sd",
                   waits=[("ve", o_v)])
    em.phase = "b3"
    cc_o = emit_collective(em, b, "AllToAll", b.ob_in, b.ob_out,
                           [("sd", em.val("sd"))])
    war = [("te", em.val("te")), ("ve", em.val("ve")), ("ae", em.val("ae"))]
    sd_o = em.dma("sync", b.Bb[:, :].rearrange("p (k t) -> p k t", t=TOK),
                  b.ob_out[:, :].rearrange("(k p) t -> p k t", p=P), "sd",
                  waits=[("cc", cc_o)] + war)
    wob = lambda m: b.Cc[:, 12288 + (m % 2) * 2048:12288 + (m % 2) * 2048 + 2048]
    cons = {}
    free_v = em.val("te")
    for m in range(DT):
        gd_v = em.dma("gpsimd", wob(m),
                      i["woutB_img"][:, m * 2048:(m + 1) * 2048], "gd",
                      waits=[("te", cons.get(m - 2, free_v))])
        for hh in range(2):
            ps, psn = (b.psA, "psA") if hh == 0 else (b.psB, "psB")
            for k in range(DT):
                em.op("tensor", lambda q, m=m, hh=hh, k=k, ps=ps: q.matmul(
                    ps[:, 0:512], wob(m)[:, k * P:(k + 1) * P],
                    b.Bb[:, k * TOK + hh * 512:k * TOK + (hh + 1) * 512],
                    start=(k == 0), stop=(k == DT - 1)),
                    waits=[("gd", gd_v), ("sd", sd_o)] +
                          (em.psw(psn) if k == 0 else []), inc="te")
            te_v = em.val("te")
            cons[m] = te_v
            em.op("vector", lambda q, m=m, hh=hh, ps=ps: q.scalar_tensor_tensor(
                b.xres[:, m * TOK + hh * 512:m * TOK + (hh + 1) * 512],
                ps[:, 0:512], sm[:, 48 + m:49 + m],
                b.xres[:, m * TOK + hh * 512:m * TOK + (hh + 1) * 512],
                ALU.add, ALU.add), waits=[("te", te_v)], inc="ve")
            em.psr(psn, "ve")



def build_program(in_arrs, parts=("a", "f1", "b", "f2"), ret_b=False,
                  debug_attn_a=False):
    nc = bass.Bass("TRN2", target_bir_lowering=False)
    b = make_b(nc, in_arrs)
    em = Em(nc)
    if debug_attn_a:
        b.d_oa = nc.dram_tensor("d_oa", [NCORES * 512, TOK], BF16,
                                kind="ExternalOutput").ap()
        b.d_qk = nc.dram_tensor("d_qk", [512, TALL], BF16,
                                kind="ExternalOutput").ap()
        b.d_v = nc.dram_tensor("d_v", [512, TALL], BF16,
                               kind="ExternalOutput").ap()
        b.d_z = nc.dram_tensor("d_z", [512, TALL], BF16,
                               kind="ExternalOutput").ap()
        b.d_rows = nc.dram_tensor("d_rows", [32, 4096], BF16,
                                  kind="ExternalOutput").ap()
        b.d_cols = nc.dram_tensor("d_cols", [32, 5120], BF16,
                                  kind="ExternalOutput").ap()
        b.d_misc = nc.dram_tensor("d_misc", [P, 4096], F32,
                                  kind="ExternalOutput").ap()
        b.d_misc2 = nc.dram_tensor("d_misc2", [P, 4096], F32,
                                   kind="ExternalOutput").ap()
    load_smalls(em, b)
    em.dma("gpsimd", b.gatew_bf[:, :], b.inp["gatew_img"][:, :], "gd")
    em.dma("gpsimd", b.d1_bf[:, :], b.inp["d1_img"][:, :], "gd")
    em.dma("gpsimd", b.d2_bf[:, :], b.inp["d2_img"][:, :], "gd")
    em.dma("gpsimd", b.agg_bf[:, :], b.inp["agg_img"][:, :], "gd")
    em.dma("gpsimd", b.bc_bf[:, :], b.inp["bc_img"][:, :], "gd")
    em.dma("gpsimd", b.d1b_bf[:, :], b.inp["d1b_row"][:, :], "gd")
    b.precast_v = em.val("gd")
    em.dma("sync", b.xres[:, :].rearrange("p (d t) -> p d t", t=TOK),
           b.inp["x_fm"][:, :].rearrange("(d p) t -> p d t", p=P), "sd")
    start_gate = [("sd", em.val("sd")), ("gd", em.val("gd"))]
    for e in ENGINES:
        em.op(e, lambda q: q.nop(), waits=start_gate)
    for part in parts:
        em.phase = "f" if part.startswith("f") else part
        if part == "a":
            emit_attn_a(em, b)
        elif part == "b":
            emit_attn_b(em, b)
        else:
            emit_ffn(em, b, part)
        em.phase = None
    ve_f = em.val("ve")
    em.dma("sync", b.out_fm[:, :].rearrange("(d p) t -> p d t", p=P),
           b.xres[:, :].rearrange("p (d t) -> p d t", t=TOK), "sd",
           waits=[("ve", ve_f)])
    if debug_attn_a:
        allw = [(n, v) for n, v in em.cnt.items() if v > 0]
        for (dst, src) in ((b.d_oa, b.oa_in), (b.d_qk, b.qk_scr),
                           (b.d_v, b.v_scr), (b.d_z, b.z_scr),
                           (b.d_rows, b.rows_scr), (b.d_cols, b.cols_scr)):
            em.dma("sync", dst[:, :], src[:, :], "sd", waits=allw)
    em.op("sync", lambda q: q.nop(), waits=[("sd", em.val("sd"))])
    em.replay()
    if ret_b:
        return nc, em, b
    return nc, em


def kernel(**inputs):
    maps = prep_inputs(inputs)
    nc, em = build_program(maps[0])
    res = run_bass_kernel_spmd(nc, maps, core_ids=list(range(NCORES)))
    em.close()
    shards = [np.asarray(r["out_fm"], dtype=np.float32).T for r in res.results]
    out = np.concatenate(shards, axis=0).reshape(2, S, D)
    return out


# version 49
# speedup vs baseline: 1.1724x; 1.0925x over previous
import numpy as np
import ml_dtypes
import concourse.bass as bass
import concourse.mybir as mybir
from concourse.bass_utils import run_bass_kernel_spmd
from concourse.aot_env import allow_ici_all_to_all

F32 = mybir.dt.float32
BF16 = mybir.dt.bfloat16
AF = mybir.ActivationFunctionType
ALU = mybir.AluOpType

P = 128
D = 2048
DT = 16
TOK = 1024
TALL = 8192
NCORES = 8
S = 4096
BFNP = ml_dtypes.bfloat16
RSQ128 = float(np.log(1.0 / np.sqrt(128.0)))

ENGINES = ("sync", "tensor", "vector", "scalar", "gpsimd")


PARANOID = False
PARANOID_PHASES = {"fth"}


class Em:
    def __init__(self, nc):
        self.nc = nc
        self.ops = {e: [] for e in ENGINES}
        self.sems = {}
        self.cnt = {}
        self._ctxs = []
        self._hw = {}
        self.pslast = {}

    def sem(self, name):
        if name not in self.sems:
            cm = self.nc.semaphore(name)
            self.sems[name] = cm.__enter__()
            self._ctxs.append(cm)
            self.cnt[name] = 0
        return self.sems[name]

    def val(self, name):
        return self.cnt.get(name, 0)

    def op(self, eng, fn, waits=(), inc=None, inc_amt=1):
        if PARANOID or getattr(self, "phase", None) in PARANOID_PHASES:
            waits = list(waits) + [(n, v) for n, v in self.cnt.items() if v > 0]
        ws = [(s, v) for (s, v) in waits if v > 0]
        for s, _ in ws:
            self.sem(s)
        self.ops[eng].append((fn, ws, inc, inc_amt))
        if inc is not None:
            self.sem(inc)
            self.cnt[inc] += inc_amt
            return self.cnt[inc]
        return None

    def dma(self, eng, out, in_, inc, waits=(), **kw):
        def fn(q):
            return q.dma_start(out=out, in_=in_, **kw)
        return self.op(eng, fn, waits, inc, 16)

    def psw(self, *names):
        out = []
        for n in names:
            if n in self.pslast:
                out.append(self.pslast[n])
        return out

    def psr(self, name, sem):
        self.pslast[name] = (sem, self.val(sem))

    def replay(self):
        nc = self.nc
        with nc.Block() as block:
            for eng in ENGINES:
                if not self.ops[eng]:
                    continue

                def mk(eng):
                    def run(q):
                        for (fn, ws, inc, amt) in self.ops[eng]:
                            for (s, v) in ws:
                                key = (eng, s)
                                if self._hw.get(key, 0) < v:
                                    q.wait_ge(self.sems[s], v)
                                    self._hw[key] = v
                            inst = fn(q)
                            if inc is not None:
                                inst.then_inc(self.sems[inc], amt)
                    return run
                getattr(block, eng)(mk(eng))

    def close(self):
        for cm in reversed(self._ctxs):
            cm.__exit__(None, None, None)


def f32c(x):
    return np.ascontiguousarray(np.asarray(x), dtype=np.float32)



def host_constants():
    c = {}
    half = 64
    thetas = 10000.0 ** (-np.arange(half, dtype=np.float64) / half)
    ang = np.arange(S, dtype=np.float64)[None, :] * thetas[:, None]
    C = np.empty((P, S), np.float64)
    S2 = np.empty((P, S), np.float64)
    C[0::2, :] = np.cos(ang)
    C[1::2, :] = np.cos(ang)
    S2[0::2, :] = -np.sin(ang)
    S2[1::2, :] = np.sin(ang)
    c["rope_c"] = f32c(C)
    c["rope_s"] = f32c(S2)
    sw = np.zeros((P, P), np.float32)
    for i in range(0, P, 2):
        sw[i, i + 1] = 1.0
        sw[i + 1, i] = 1.0
    c["swap_pm"] = sw
    c["ident"] = np.eye(P, dtype=np.float32)
    c["ones_col"] = np.ones((P, 1), np.float32)
    c["ones_row"] = np.ones((1, P), np.float32)
    ii = np.arange(P)
    mD = np.where(ii[None, :] >= ii[:, None], 0.0, -1e30).astype(np.float32)
    c["maskD"] = np.tile(mD, (1, 8))
    mD2 = np.where(ii[:, None] > ii[None, :], 0.0, -1e30).astype(np.float32)
    c["maskD2"] = np.tile(mD2, (1, 8))
    m01 = np.where(ii[None, :] > ii[:, None], 1.0, 0.0).astype(np.float32)
    c["mask01u"] = np.tile(m01, (1, 8))
    c["eyeR"] = np.tile(np.eye(P, dtype=np.float32), (1, 8))
    cm = np.empty((P, 4 * 512), np.float32)
    for r in range(4):
        cm[:, r * 512:(r + 1) * 512] = np.where(
            np.arange(512)[None, :] >= (r * P + ii[:, None]), 0.0, -1e30)
    c["cmask"] = cm
    bn = np.zeros((4, 512), np.float32)
    for r in range(4):
        bn[r, r * P:(r + 1) * P] = -1.0
    c["blockneg"] = bn
    return c


def lhsT_image(w):
    K, M = w.shape
    KT, MT = K // P, M // P
    t = w.reshape(KT, P, MT, P).transpose(1, 2, 0, 3)
    return f32c(t.reshape(P, MT * KT * P))


def rhs_image(w):
    K, N = w.shape
    KT = K // P
    t = w.reshape(KT, P, N).transpose(1, 0, 2)
    return f32c(t.reshape(P, KT * N))


def col_image(v):
    n = v.shape[0]
    return f32c(v.reshape(n // P, P).T)


def prep_inputs(I):
    cst = host_constants()
    maps = []
    x = f32c(I["L0_attn_a_x"]).reshape(TALL, D)
    for c in range(NCORES):
        m = {}
        m["x_fm"] = f32c(x[c * TOK:(c + 1) * TOK, :].T)
        m["sel"] = f32c(np.tile(np.eye(2, dtype=np.float32)[c // 4], (P, 1)))
        for k, v in cst.items():
            m["c_" + k] = v
        kh = [2 * c, 2 * c + 1]
        vh = [4 * c + i for i in range(4)]
        wqkv = f32c(I["L0_attn_a_w_qkv"])
        qc = np.concatenate([wqkv[:, 128 * h:128 * (h + 1)] for h in kh], 1)
        kc = np.concatenate([wqkv[:, 2048 + 128 * h:2048 + 128 * (h + 1)]
                             for h in kh], 1)
        vc = np.concatenate([wqkv[:, 4096 + 128 * h:4096 + 128 * (h + 1)]
                             for h in vh], 1)
        wz = f32c(I["L0_attn_a_w_z"])
        zc = np.concatenate([wz[:, 128 * h:128 * (h + 1)] for h in vh], 1)
        m["wsA_img"] = lhsT_image(np.concatenate([qc, kc, vc, zc], 1))
        cw = f32c(I["L0_attn_a_conv_w"])
        rows = ([cw[128 * h:128 * (h + 1), :] for h in kh]
                + [cw[2048 + 128 * h:2048 + 128 * (h + 1), :] for h in kh]
                + [cw[4096 + 128 * h:4096 + 128 * (h + 1), :] for h in vh])
        m["cw_img"] = f32c(np.concatenate(rows, 1))
        wb = f32c(I["L0_attn_a_w_b"])[:, vh]
        wa = f32c(I["L0_attn_a_w_a"])[:, vh]
        m["wb_img"] = f32c(wb.reshape(DT, P, 4).transpose(1, 0, 2).reshape(P, DT * 4))
        m["wa_img"] = f32c(wa.reshape(DT, P, 4).transpose(1, 0, 2).reshape(P, DT * 4))
        m["dtb_col"] = f32c(I["L0_attn_a_dt_bias"])[vh].reshape(4, 1)
        m["alog_col"] = f32c(I["L0_attn_a_a_log"])[vh].reshape(4, 1)
        m["normw_col"] = f32c(I["L0_attn_a_norm_w"]).reshape(P, 1)
        m["woutA_img"] = lhsT_image(f32c(I["L0_attn_a_w_out"]))
        bh = [2 * c, 2 * c + 1]
        wq = f32c(I["L0_attn_b_w_qkv"])
        qb = np.concatenate([wq[128 * h:128 * (h + 1), :].T for h in bh], 1)
        kb = np.concatenate([wq[2048 + 128 * h:2048 + 128 * (h + 1), :].T
                             for h in bh], 1)
        vb = np.concatenate([wq[4096 + 128 * h:4096 + 128 * (h + 1), :].T
                             for h in bh], 1)
        m["wsB_img"] = lhsT_image(np.concatenate([qb, kb], 1))
        m["wvB_img"] = rhs_image(f32c(vb))
        bq = f32c(I["L0_attn_b_b_qkv"])
        bqc = np.concatenate([bq[128 * h:128 * (h + 1)] for h in bh]
                             + [bq[2048 + 128 * h:2048 + 128 * (h + 1)]
                                for h in bh])
        m["bqk_col"] = f32c(bqc.reshape(4, P).T)
        m["bvB_row"] = f32c(np.concatenate(
            [bq[4096 + 128 * h:4096 + 128 * (h + 1)] for h in bh]).reshape(1, 256))
        m["woutB_img"] = lhsT_image(f32c(I["L0_attn_b_w_out"]).T)
        m["boutB_col"] = col_image(f32c(I["L0_attn_b_b_out"]))
        m["gatew_img"] = lhsT_image(f32c(I["L0_ffn_gate_w"]))
        m["gateb_col"] = col_image(f32c(I["L0_ffn_gate_b"]))
        hsl = slice(c * 1024, (c + 1) * 1024)
        m["d1_img"] = rhs_image(f32c(I["L0_ffn_d1_w"])[:, hsl])
        m["d1b_row"] = f32c(I["L0_ffn_d1_b"])[hsl].reshape(1, 1024)
        m["d2_img"] = lhsT_image(f32c(I["L0_ffn_d2_w"])[hsl, :])
        m["d2b_col"] = col_image(f32c(I["L0_ffn_d2_b"]))
        asl = slice(c * 256, (c + 1) * 256)
        m["agg_img"] = rhs_image(f32c(I["L0_ffn_agg_w"])[:, asl])
        m["aggb_row"] = f32c(np.tile(I["L0_ffn_agg_b"][asl].reshape(1, 256), (2, 1)))
        m["bc_img"] = rhs_image(f32c(I["L0_ffn_bc_w"])[:, asl])
        m["bcb_row"] = f32c(np.tile(I["L0_ffn_bc_b"][asl].reshape(1, 256), (2, 1)))
        for nm, key in (("ing", "L0_ffn_in_gamma"), ("inb", "L0_ffn_in_beta"),
                        ("png", "L0_ffn_pn_gamma"), ("pnb", "L0_ffn_pn_beta"),
                        ("outg", "L0_ffn_out_gamma"), ("outb", "L0_ffn_out_beta")):
            m[nm + "_col"] = col_image(f32c(I[key]))
        maps.append(m)
    return maps



class Bufs:
    pass


def make_b(nc, in_arrs):
    b = Bufs()
    b.nc = nc
    b.inp = {}
    for name, arr in in_arrs.items():
        b.inp[name] = nc.dram_tensor(name, list(arr.shape), F32,
                                     kind="ExternalInput").ap()
    b.out_fm = nc.dram_tensor("out_fm", [D, TOK], F32, kind="ExternalOutput").ap()
    dr = nc.dram_tensor
    b.gin = dr("gin", [D, TOK], BF16).ap()
    b.gxn = dr("gxn", [NCORES * D, TOK], BF16).ap()
    b.qk_scr = dr("qk_scr", [512, TALL], BF16).ap()
    b.v_scr = dr("v_scr", [512, TALL], BF16).ap()
    b.z_scr = dr("z_scr", [512, TALL], BF16).ap()
    b.ba_scr = dr("ba_scr", [8, TALL], F32).ap()
    b.krstd_scr = dr("krstd_scr", [2, TALL], F32).ap()
    b.rows_scr = dr("rows_scr", [32, 4 * 8 * P], BF16).ap()
    b.cols_scr = dr("cols_scr", [32, 5 * 8 * P], BF16).ap()
    b.oa_in = dr("oa_in", [NCORES * 512, TOK], BF16).ap()
    b.oa_out = dr("oa_out", [NCORES * 512, TOK], BF16).ap()
    b.qb_scr = dr("qb_scr", [12 * P, S], BF16).ap()
    b.ob_in = dr("ob_in", [NCORES * 256, TOK], BF16).ap()
    b.ob_out = dr("ob_out", [NCORES * 256, TOK], BF16).ap()
    b.th_in = dr("th_in", [1, D], BF16).ap()
    b.th_out = dr("th_out", [NCORES, D], BF16).ap()
    b.gatew_bf = dr("gatew_bf", [P, 32768], BF16).ap()
    b.d1_bf = dr("d1_bf", [P, 16384], BF16).ap()
    b.d2_bf = dr("d2_bf", [P, 16384], BF16).ap()
    b.agg_bf = dr("agg_bf", [P, 4096], BF16).ap()
    b.bc_bf = dr("bc_bf", [P, 4096], BF16).ap()
    b.d1b_bf = dr("d1b_bf", [1, 1024], BF16).ap()
    b.ar_in = dr("ar_in", [2, D], BF16).ap()
    b.ar_out = dr("ar_out", [2, D], BF16).ap()
    b.ag2_in = dr("ag2_in", [2, 256], BF16).ap()
    b.ag2_out = dr("ag2_out", [NCORES * 2, 256], BF16).ap()

    al = nc.alloc_sbuf_tensor
    b.xres = al("xres", [P, DT * TOK], F32)
    b.A = al("A", [P, 16384], BF16)
    b.Bb = al("Bb", [P, 16384], BF16)
    b.Cc = al("Cc", [P, 16384], BF16)
    b.Ee = al("Ee", [P, 8192], BF16)
    b.Dd = al("Dd", [P, 3712], F32)
    b.rowbf = al("rowbf", [1, 2048], BF16)
    b.ones_col = al("ones_col", [P, 1], BF16)
    b.ones_row = al("ones_row", [1, P], BF16)
    b.ident = al("ident", [P, P], BF16)
    b.swap_pm = al("swap_pm", [P, P], BF16)
    b.smallf = al("smallf", [P, 160], F32)
    b.cw_sb = al("cw_sb", [P, 32], F32)
    b.dtb_sb = al("dtb_sb", [4, 1], F32)
    b.alog_sb = al("alog_sb", [4, 1], F32)
    b.nealog = al("nealog", [4, 1], F32)
    b.aggb_sb = al("aggb_sb", [2, 256], BF16)
    b.bcb_sb = al("bcb_sb", [2, 256], BF16)
    b.bvB_sb = al("bvB_sb", [1, 256], BF16)
    b.wba_sb = al("wba_sb", [P, 128], BF16)
    b.th_fm = al("th_fm", [P, 32], F32)
    b.th_sc = al("th_sc", [P, 32], F32)
    b.th_bf = al("th_bf", [P, 32], BF16)
    b.sq_bf = al("sq_bf", [P, 32], BF16)
    b.r4lo = al("r4lo", [4, 512], BF16)
    b.h1_bf = al("h1_bf", [P, 16], BF16)
    b.h1_tm = al("h1_tm", [2, 1024], BF16)
    b.th_tm2 = al("th_tm2", [2, D], BF16)
    b.bcv_fm = al("bcv_fm", [P, 32], F32)
    b.bcsel = al("bcsel", [P, 16], BF16)

    ap_ = nc.alloc_psum_tensor
    for nm in "ABCDEFG":
        setattr(b, "ps" + nm, ap_("ps" + nm, [P, 512], F32))
    b.psT = ap_("psT", [P, 1024], BF16)
    return b


def load_smalls(em, b):
    i = b.inp
    em.dma("gpsimd", b.ones_col[:, :], i["c_ones_col"][:, :], "gd")
    em.dma("gpsimd", b.ones_row[:, :], i["c_ones_row"][:, :], "gd")
    em.dma("gpsimd", b.ident[:, :], i["c_ident"][:, :], "gd")
    em.dma("gpsimd", b.swap_pm[:, :], i["c_swap_pm"][:, :], "gd")
    em.dma("gpsimd", b.aggb_sb[:, :], i["aggb_row"][:, :], "gd")
    em.dma("gpsimd", b.bcb_sb[:, :], i["bcb_row"][:, :], "gd")
    em.dma("gpsimd", b.bvB_sb[:, :], i["bvB_row"][:, :], "gd")
    em.dma("gpsimd", b.wba_sb[:, 0:64], i["wb_img"][:, :], "gd")
    em.dma("gpsimd", b.wba_sb[:, 64:128], i["wa_img"][:, :], "gd")
    em.dma("sync", b.cw_sb[:, :], i["cw_img"][:, :], "sd")
    em.dma("sync", b.dtb_sb[:, :], i["dtb_col"][:, :], "sd")
    em.dma("sync", b.alog_sb[:, :], i["alog_col"][:, :], "sd")
    sm = b.smallf
    em.dma("sync", sm[:, 0:16], i["gateb_col"][:, :], "sd")
    em.dma("sync", sm[:, 16:32], i["outg_col"][:, :], "sd")
    em.dma("sync", sm[:, 32:48], i["outb_col"][:, :], "sd")
    em.dma("sync", sm[:, 48:64], i["boutB_col"][:, :], "sd")
    em.dma("sync", sm[:, 64:80], i["d2b_col"][:, :], "sd")
    em.dma("sync", sm[:, 80:96], i["ing_col"][:, :], "sd")
    em.dma("sync", sm[:, 96:112], i["inb_col"][:, :], "sd")
    em.dma("sync", sm[:, 112:128], i["png_col"][:, :], "sd")
    em.dma("sync", sm[:, 128:144], i["pnb_col"][:, :], "sd")
    em.dma("sync", sm[:, 144:145], i["normw_col"][:, :], "sd")
    em.dma("sync", sm[:, 145:149], i["bqk_col"][:, :], "sd")
    em.dma("sync", sm[:, 150:152], i["sel"][:, :], "sd")



def _ln_rows_and_bcast(em, b, eps):
    r_mu = b.Dd[0:1, 0:1024]
    r_var = b.Dd[0:1, 1024:2048]
    r_m2 = b.Dd[0:1, 2048:3072]
    st_v = em.val("te")
    em.op("vector", lambda q: q.tensor_scalar(r_mu[0:1, 0:512], b.psA[0:1, 0:512],
          1.0 / D, None, ALU.mult),
          waits=[("te", st_v), ("ae", em.val("ae"))], inc="ve")
    em.op("vector", lambda q: q.tensor_scalar(r_mu[0:1, 512:1024],
          b.psB[0:1, 0:512], 1.0 / D, None, ALU.mult), inc="ve")
    em.op("vector", lambda q: q.tensor_scalar(r_var[0:1, 0:512], b.psC[0:1, 0:512],
          1.0 / D, eps, ALU.mult, ALU.add), inc="ve")
    em.op("vector", lambda q: q.tensor_scalar(r_var[0:1, 512:1024],
          b.psD[0:1, 0:512], 1.0 / D, eps, ALU.mult, ALU.add), inc="ve")
    em.op("vector", lambda q: q.tensor_mul(r_m2[0:1, :], r_mu[0:1, :],
          r_mu[0:1, :]), inc="ve")
    em.op("vector", lambda q: q.tensor_sub(r_var[0:1, :], r_var[0:1, :],
          r_m2[0:1, :]), inc="ve")
    for nm in "ABCD":
        em.psr("ps" + nm, "ve")
    var_v = em.val("ve")
    em.op("scalar", lambda q: q.activation(r_var[0:1, :], r_var[0:1, :], AF.Ln),
          waits=[("ve", var_v)], inc="ae")
    ln_v = em.val("ae")
    em.op("scalar", lambda q: q.activation(r_var[0:1, :], r_var[0:1, :], AF.Exp,
          scale=-0.5), waits=[("ae", ln_v)], inc="ae")
    rstd_v = em.val("ae")
    em.op("vector", lambda q: q.tensor_copy(b.rowbf[0:1, 0:1024], r_mu[0:1, :]),
          waits=[("ae", rstd_v), ("te", em.val("te"))], inc="ve")
    em.op("vector", lambda q: q.tensor_copy(b.rowbf[0:1, 1024:2048],
          r_var[0:1, :]), inc="ve")
    rbv = em.val("ve")
    for h in range(2):
        em.op("tensor", lambda q, h=h: q.matmul(
            (b.psE if h == 0 else b.psF)[:, 0:512], b.ones_row[0:1, :],
            b.rowbf[0:1, h * 512:(h + 1) * 512], start=True, stop=True),
            waits=[("ve", rbv)] + em.psw("psE" if h == 0 else "psF"), inc="te")
        em.op("tensor", lambda q, h=h: q.matmul(
            (b.psG if h == 0 else b.psA)[:, 0:512], b.ones_row[0:1, :],
            b.rowbf[0:1, 1024 + h * 512:1024 + (h + 1) * 512], start=True,
            stop=True), waits=em.psw("psG" if h == 0 else "psA"), inc="te")
    bc_v = em.val("te")
    mubc = b.Ee[:, 4096:5120]
    rstdbc = b.Ee[:, 5120:6144]
    em.op("vector", lambda q: q.tensor_copy(mubc[:, 0:512], b.psE[:, 0:512]),
          waits=[("te", bc_v)], inc="ve")
    em.op("vector", lambda q: q.tensor_copy(mubc[:, 512:1024], b.psF[:, 0:512]),
          inc="ve")
    em.op("vector", lambda q: q.tensor_copy(rstdbc[:, 0:512], b.psG[:, 0:512]),
          inc="ve")
    em.op("vector", lambda q: q.tensor_copy(rstdbc[:, 512:1024],
          b.psA[:, 0:512]), inc="ve")
    for nm in "AEFG":
        em.psr("ps" + nm, "ve")
    return em.val("ve")


def emit_ln(em, b, xn_dst, gin=None, eps=1e-5):
    xbf = lambda i: b.Ee[:, (i % 2) * 1024:(i % 2) * 1024 + 1024]
    xsq = lambda i: b.Ee[:, 2048 + (i % 2) * 1024:2048 + (i % 2) * 1024 + 1024]
    mm_wait = em.psw("psA", "psB", "psC", "psD")
    ee_war = [("te", em.val("te")), ("gd", em.val("gd")),
              ("sd", em.val("sd")), ("ae", em.val("ae"))]
    for d in range(DT):
        sl = slice(d * TOK, (d + 1) * TOK)
        wv = [("te", em.val("te"))] if d >= 2 else list(ee_war)
        em.op("vector", lambda q, o=xbf(d), isl=sl: q.tensor_copy(o, b.xres[:, isl]),
              waits=wv, inc="ve")
        cast_v = em.val("ve")
        em.op("scalar", lambda q, o=xsq(d), i=xbf(d): q.activation(o, i, AF.Square),
              waits=[("ve", cast_v)] + ([("te", em.val("te"))] if d >= 2 else []),
              inc="ae")
        sq_v = em.val("ae")
        for h in range(2):
            hs = slice(h * 512, (h + 1) * 512)
            em.op("tensor", lambda q, d=d, h=h, i=xbf(d), hs=hs: q.matmul(
                (b.psA if h == 0 else b.psB)[0:1, 0:512], b.ones_col[:, 0:1],
                i[:, hs], start=(d == 0), stop=(d == DT - 1)),
                waits=[("ve", cast_v)] + (mm_wait if d == 0 else []), inc="te")
            em.op("tensor", lambda q, d=d, h=h, i=xsq(d), hs=hs: q.matmul(
                (b.psC if h == 0 else b.psD)[0:1, 0:512], b.ones_col[:, 0:1],
                i[:, hs], start=(d == 0), stop=(d == DT - 1)),
                waits=[("ae", sq_v)], inc="te")
    bcs = _ln_rows_and_bcast(em, b, eps)
    mubc = b.Ee[:, 4096:5120]
    rstdbc = b.Ee[:, 5120:6144]
    sd_last = em.val("sd")
    dst_war = [("te", em.val("te")), ("sd", em.val("sd")),
               ("gd", em.val("gd"))]
    for d in range(DT):
        sl = slice(d * TOK, (d + 1) * TOK)
        em.op("vector", lambda q, sl=sl: q.tensor_sub(xn_dst[:, sl],
              b.xres[:, sl], mubc[:, :]),
              waits=[("ve", bcs)] + (dst_war if d == 0 else []), inc="ve")
        em.op("vector", lambda q, sl=sl: q.tensor_mul(xn_dst[:, sl],
              xn_dst[:, sl], rstdbc[:, :]), inc="ve")
        if gin is not None:
            sd_last = em.dma("sync", gin[d * P:(d + 1) * P, :], xn_dst[:, sl],
                             "sd", waits=[("ve", em.val("ve"))])
    return em.val("ve"), sd_last


def emit_out_ln_add(em, b, src_bf, gci, bci, eps=1e-6, src_ready=()):
    sm = b.smallf
    sq = lambda i: b.Ee[:, 2048 + (i % 2) * 1024:2048 + (i % 2) * 1024 + 1024]
    mm_wait = em.psw("psA", "psB", "psC", "psD")
    for d in range(DT):
        sl = slice(d * TOK, (d + 1) * TOK)
        wv = [("te", em.val("te"))] if d >= 2 else []
        em.op("scalar", lambda q, o=sq(d), sl=sl: q.activation(o, src_bf[:, sl],
              AF.Square), waits=wv + list(src_ready), inc="ae")
        sq_v = em.val("ae")
        for h in range(2):
            hs = slice(h * 512, (h + 1) * 512)
            em.op("tensor", lambda q, d=d, h=h, sl=sl, hs=hs: q.matmul(
                (b.psA if h == 0 else b.psB)[0:1, 0:512], b.ones_col[:, 0:1],
                src_bf[:, sl][:, hs], start=(d == 0), stop=(d == DT - 1)),
                waits=(mm_wait if d == 0 else []) + (list(src_ready) if d == 0 else []),
                inc="te")
            em.op("tensor", lambda q, d=d, h=h, o=sq(d), hs=hs: q.matmul(
                (b.psC if h == 0 else b.psD)[0:1, 0:512], b.ones_col[:, 0:1],
                o[:, hs], start=(d == 0), stop=(d == DT - 1)),
                waits=[("ae", sq_v)], inc="te")
    bcs = _ln_rows_and_bcast(em, b, eps)
    mubc = b.Ee[:, 4096:5120]
    rstdbc = b.Ee[:, 5120:6144]
    tmp = lambda i: b.Ee[:, (i % 2) * 1024:(i % 2) * 1024 + 1024]
    for d in range(DT):
        sl = slice(d * TOK, (d + 1) * TOK)
        em.op("vector", lambda q, d=d, sl=sl: q.tensor_sub(tmp(d), src_bf[:, sl],
              mubc[:, :]), waits=[("ve", bcs)], inc="ve")
        em.op("vector", lambda q, d=d: q.tensor_mul(tmp(d), tmp(d), rstdbc[:, :]),
              inc="ve")
        em.op("vector", lambda q, d=d, sl=sl: q.scalar_tensor_tensor(
            b.xres[:, sl], tmp(d), sm[:, gci + d:gci + d + 1], b.xres[:, sl],
            ALU.mult, ALU.add), inc="ve")
        em.op("vector", lambda q, d=d, sl=sl: q.tensor_scalar(
            b.xres[:, sl], b.xres[:, sl], sm[:, bci + d:bci + d + 1], None,
            ALU.add), inc="ve")
    return em.val("ve")


def emit_collective(em, b, kind, cc_in, cc_out, wait_sems, op=None):
    op = op or (ALU.add if kind == "AllReduce" else ALU.bypass)

    def fn(q):
        if kind == "AllToAll":
            with allow_ici_all_to_all():
                return q.collective_compute(
                    kind, op, replica_groups=[list(range(NCORES))],
                    ins=[cc_in.tensor.ap().opt()],
                    outs=[cc_out.tensor.ap().opt()])
        return q.collective_compute(
            kind, op, replica_groups=[list(range(NCORES))],
            ins=[cc_in.tensor.ap().opt()], outs=[cc_out.tensor.ap().opt()])
    em.op("gpsimd", fn, waits=wait_sems, inc="cc")
    return em.val("cc")



def ap_mid_bcast(x, n):
    a = [list(p) for p in x.ap]
    return bass.AP(x.tensor, x.offset, [a[0], [0, n], a[1]])


def emit_small_ln_fm(em, b, src, dst, gci, bci, eps, first_te=None):
    em.op("vector", lambda q: q.tensor_copy(b.th_bf[:, :], src[:, :]),
          waits=[("te", em.val("te") if first_te is None else first_te)],
          inc="ve")
    cast_v = em.val("ve")
    em.op("vector", lambda q: q.tensor_mul(b.sq_bf[:, :], b.th_bf[:, :],
          b.th_bf[:, :]), inc="ve")
    sq_v = em.val("ve")
    em.op("tensor", lambda q: q.matmul(
        b.psG[0:1, 0:32], b.ones_col[:, 0:1], b.th_bf[:, :],
        start=True, stop=True),
        waits=[("ve", cast_v)] + em.psw("psG"), inc="te")
    em.op("tensor", lambda q: q.matmul(
        b.psE[0:1, 0:32], b.ones_col[:, 0:1], b.sq_bf[:, :],
        start=True, stop=True),
        waits=[("ve", sq_v)] + em.psw("psE"), inc="te")
    st_v = em.val("te")
    r = b.Dd[0:1, 2048:2056]
    em.op("vector", lambda q: q.tensor_reduce(
        r[0:1, 6:8], b.psG[0:1, 0:32].rearrange("p (d t) -> p t d", t=2),
        mybir.AxisListType.X, ALU.add),
        waits=[("te", st_v), ("ae", em.val("ae"))], inc="ve")
    em.op("vector", lambda q: q.tensor_reduce(
        r[0:1, 2:4], b.psE[0:1, 0:32].rearrange("p (d t) -> p t d", t=2),
        mybir.AxisListType.X, ALU.add), inc="ve")
    red_v = em.val("ve")
    em.op("vector", lambda q: q.tensor_scalar(r[0:1, 0:2], r[0:1, 6:8],
          1.0 / D, None, ALU.mult), waits=[("ve", red_v)], inc="ve")
    em.op("vector", lambda q: q.tensor_scalar(r[0:1, 2:4], r[0:1, 2:4],
          1.0 / D, eps, ALU.mult, ALU.add), inc="ve")
    em.op("vector", lambda q: q.tensor_mul(r[0:1, 4:6], r[0:1, 0:2],
          r[0:1, 0:2]), inc="ve")
    em.op("vector", lambda q: q.tensor_sub(r[0:1, 2:4], r[0:1, 2:4],
          r[0:1, 4:6]), inc="ve")
    em.psr("psG", "ve")
    em.psr("psE", "ve")
    rv = em.val("ve")
    em.op("scalar", lambda q: q.activation(r[0:1, 2:4], r[0:1, 2:4], AF.Ln),
          waits=[("ve", rv)], inc="ae")
    ln_v = em.val("ae")
    em.op("scalar", lambda q: q.activation(r[0:1, 2:4], r[0:1, 2:4], AF.Exp,
          scale=-0.5), waits=[("ae", ln_v)], inc="ae")
    rstd_v = em.val("ae")
    em.op("vector", lambda q: q.tensor_copy(b.rowbf[0:1, 0:2], r[0:1, 0:2]),
          waits=[("ae", rstd_v), ("te", em.val("te"))], inc="ve")
    em.op("vector", lambda q: q.tensor_copy(b.rowbf[0:1, 2:4], r[0:1, 2:4]),
          inc="ve")
    rbv = em.val("ve")
    em.op("tensor", lambda q: q.matmul(b.psG[:, 0:2], b.ones_row[0:1, :],
          b.rowbf[0:1, 0:2], start=True, stop=True),
          waits=[("ve", rbv)] + em.psw("psG"), inc="te")
    em.op("tensor", lambda q: q.matmul(b.psF[:, 0:2], b.ones_row[0:1, :],
          b.rowbf[0:1, 2:4], start=True, stop=True),
          waits=em.psw("psF"), inc="te")
    bc_v = em.val("te")
    dst3 = dst[:, :].rearrange("p (d t) -> p d t", t=2)
    src3 = src[:, :].rearrange("p (d t) -> p d t", t=2)
    mu2 = b.Dd[:, 2056:2058]
    rs2 = b.Dd[:, 2058:2060]
    em.op("vector", lambda q: q.tensor_copy(mu2, b.psG[:, 0:2]),
          waits=[("te", bc_v)], inc="ve")
    em.op("vector", lambda q: q.tensor_copy(rs2, b.psF[:, 0:2]), inc="ve")
    em.psr("psG", "ve")
    em.psr("psF", "ve")
    ev_v = em.val("ve")
    em.op("vector", lambda q: q.tensor_tensor(
        dst3, src3, ap_mid_bcast(mu2, 16), ALU.subtract),
        waits=[("ve", ev_v)], inc="ve")
    em.op("vector", lambda q: q.tensor_tensor(
        dst3, dst3, ap_mid_bcast(rs2, 16), ALU.mult), inc="ve")
    em.op("vector", lambda q: q.tensor_tensor(
        dst3, dst3, _colbc(b, gci), ALU.mult), inc="ve")
    em.op("vector", lambda q: q.tensor_tensor(dst3, dst3, _colbc(b, bci),
          ALU.add), inc="ve")
    return em.val("ve")


def _colbc(b, ci):
    ap = b.smallf[:, ci:ci + 16]
    return ap.broadcast_to([P, 16, 2])



def emit_ffn(em, b, tag):
    i = b.inp
    sm = b.smallf
    em.phase = "fln"
    ve_ln, _ = emit_ln(em, b, b.A, gin=None, eps=1e-5)
    pre_fg_te = em.val("te")
    em.phase = "fth"
    sd_agg = em.dma("gpsimd", b.Ee[:, 4096:8192], b.agg_bf[:, :], "gd",
                    waits=[("te", pre_fg_te), ("ve", em.val("ve"))])
    for d in range(DT):
        em.dma("sync", b.th_in[0, d * P:(d + 1) * P],
               b.A[:, d * TOK + 1023:d * TOK + 1024], "sd",
               waits=[("ve", ve_ln)])
    cc_th = emit_collective(em, b, "AllGather", b.th_in, b.th_out,
                            [("sd", em.val("sd"))])
    sd_d1b = em.dma("gpsimd", b.rowbf[0:1, 1024:2048], b.d1b_bf[0:1, :],
                    "gd", waits=[("te", pre_fg_te), ("ve", em.val("ve"))])
    sd_d1 = em.dma("gpsimd", b.Cc[:, :], b.d1_bf[:, :], "gd",
                   waits=[("te", pre_fg_te), ("ve", em.val("ve"))])
    em.dma("sync", b.th_tm2[0:1, :], b.th_out[3, :], "sd",
           waits=[("cc", cc_th)])
    em.dma("sync", b.th_tm2[1:2, :], b.th_out[7, :], "sd")
    sd_xl = em.val("sd")
    gwbuf = lambda m: b.Ee[:, (m % 2) * 2048:(m % 2) * 2048 + 2048]
    ee_free = em.val("te")
    cons = {}

    def fg_part(m0, m1):
        em.phase = "fg"
        for m in range(m0, m1):
            wv = [("te", cons.get(m - 2, ee_free))]
            sc_v = em.op("scalar", lambda q, m=m: q.dma_start(
                out=gwbuf(m), in_=b.gatew_bf[:, m * 2048:(m + 1) * 2048]),
                waits=wv + [("gd", b.precast_v)] +
                      ([("ve", em.val("ve"))] if m == 0 else []),
                inc="sc", inc_amt=16)
            for h in range(2):
                ps = b.psC if h == 0 else b.psD
                psn = "psC" if h == 0 else "psD"
                for k in range(DT):
                    em.op("tensor", lambda q, m=m, h=h, k=k, ps=ps: q.matmul(
                        ps[:, 0:512], gwbuf(m)[:, k * P:(k + 1) * P],
                        b.A[:, k * TOK + h * 512:k * TOK + (h + 1) * 512],
                        start=(k == 0), stop=(k == DT - 1)),
                        waits=[("sc", sc_v), ("ve", ve_ln)] +
                              (em.psw(psn) if k == 0 else []), inc="te")
                te_v = em.val("te")
                cons[m] = te_v
                em.op("scalar", lambda q, m=m, h=h, ps=ps: q.activation(
                    b.Bb[:, m * TOK + h * 512:m * TOK + (h + 1) * 512],
                    ps[:, 0:512],
                    AF.Sigmoid, bias=sm[:, m:m + 1], scale=1.0),
                    waits=[("te", te_v)] +
                          ([("ve", em.val("ve"))] if (m == 0 and h == 0)
                           else []),
                    inc="ae")
                em.psr(psn, "ae")
        em.phase = "fth"

    fg_part(0, 6)
    _tm_to_fm(em, b, b.th_tm2, b.th_sc, DT, [("sd", sd_xl)])
    emit_small_ln_fm(em, b, b.th_sc, b.th_fm, 80, 96, eps=1e-6)
    _thought_proj(em, b, "agg_img", b.aggb_sb, tag + "agg",
                  img_ready=[("gd", sd_agg)])
    fg_part(6, DT)
    gate_done = em.val("ae")
    em.phase = "fth"
    for s_ in range(4):
        _thought_step(em, b, sd_d1, tag, s_)
    _thought_proj(em, b, "bc_img", b.bcb_sb, tag + "bc", to_bcv=True)
    bcv_v = em.val("ve")
    em.op("vector", lambda q: q.tensor_scalar(
        b.bcsel[:, :], b.bcv_fm[:, 0:32:2], sm[:, 150:151], None, ALU.mult),
        waits=[("ve", bcv_v)], inc="ve")
    em.op("vector", lambda q: q.scalar_tensor_tensor(
        b.bcsel[:, :], b.bcv_fm[:, 1:32:2], sm[:, 151:152], b.bcsel[:, :],
        ALU.mult, ALU.add), inc="ve")
    bcsel_v = em.val("ve")
    em.phase = "fc"
    for d in range(DT):
        sl = slice(d * TOK, (d + 1) * TOK)
        em.op("vector", lambda q, d=d, sl=sl: q.scalar_tensor_tensor(
            b.Cc[:, sl], b.Bb[:, sl], b.bcsel[:, d:d + 1], b.A[:, sl],
            ALU.mult, ALU.add),
            waits=[("ve", bcsel_v), ("ae", gate_done)], inc="ve")
    emit_out_ln_add(em, b, b.Cc, 16, 32, eps=1e-6,
                    src_ready=[("ve", em.val("ve"))])


def _tm_to_fm(em, b, src_tm, dst_fm, ntiles, waits):
    for d in range(ntiles):
        em.op("tensor", lambda q, d=d: q.transpose(
            b.psT[:, d * 2:d * 2 + 2], src_tm[0:2, d * P:(d + 1) * P],
            b.ident[0:2, 0:2]),
            waits=(waits if d == 0 else []) + (em.psw("psT") if d == 0 else []),
            inc="te")
    tr_v = em.val("te")
    em.op("vector", lambda q: q.tensor_copy(dst_fm[:, 0:2 * ntiles],
          b.psT[:, 0:2 * ntiles]), waits=[("te", tr_v)], inc="ve")
    em.psr("psT", "ve")
    return em.val("ve")


def _thought_proj(em, b, img, bias_row, tag, to_bcv=False, img_ready=None):
    em.op("vector", lambda q: q.tensor_copy(b.th_bf[:, :], b.th_fm[:, :]),
          waits=[("te", em.val("te"))], inc="ve")
    tbv = em.val("ve")
    if img_ready is None:
        src = b.agg_bf if img == "agg_img" else b.bc_bf
        sd_v = em.dma(
            "gpsimd", b.Ee[:, 4096:8192], src[:, :],
            "gd", waits=[("te", em.val("te")), ("ve", em.val("ve"))])
        img_ready = [("gd", sd_v)]
    for k in range(DT):
        em.op("tensor", lambda q, k=k: q.matmul(
            b.psB[0:2, 0:256], b.th_bf[:, k * 2:(k + 1) * 2],
            b.Ee[:, 4096 + k * 256:4096 + (k + 1) * 256],
            start=(k == 0), stop=(k == DT - 1)),
            waits=[("ve", tbv)] + img_ready + (em.psw("psB") if k == 0 else []),
            inc="te")
    pj_v = em.val("te")
    em.op("vector", lambda q: q.tensor_copy(b.th_tm2[0:2, 0:256],
          b.psB[0:2, 0:256]), waits=[("te", pj_v)], inc="ve")

    em.psr("psB", "ve")
    em.op("vector", lambda q: q.tensor_tensor(
        b.th_tm2[0:2, 0:256], b.th_tm2[0:2, 0:256],
        bias_row[0:2, :], ALU.add), inc="ve")
    em.dma("sync", b.ag2_in[:, :], b.th_tm2[0:2, 0:256], "sd",
           waits=[("ve", em.val("ve"))])
    cc_v = emit_collective(em, b, "AllGather", b.ag2_in, b.ag2_out,
                           [("sd", em.val("sd"))])
    for r in range(NCORES):
        em.dma("sync", b.th_tm2[0:2, r * 256:(r + 1) * 256],
               b.ag2_out[2 * r:2 * r + 2, :], "sd", waits=[("cc", cc_v)])
    dst = b.bcv_fm if to_bcv else b.th_fm
    _tm_to_fm(em, b, b.th_tm2, dst, DT, [("sd", em.val("sd"))])


def _thought_step(em, b, sd_d1, tag, s_):
    i = b.inp
    sm = b.smallf
    emit_small_ln_fm(em, b, b.th_fm, b.th_sc, 112, 128, eps=1e-6)
    em.op("vector", lambda q: q.tensor_copy(b.th_bf[:, :], b.th_sc[:, :]),
          waits=[("te", em.val("te"))], inc="ve")
    tb = em.val("ve")
    for half in range(2):
        ps = b.psB if half == 0 else b.psG
        psn = "psB" if half == 0 else "psG"
        for k in range(DT):
            em.op("tensor", lambda q, k=k, half=half, ps=ps: q.matmul(
                ps[0:2, 0:512], b.th_bf[:, k * 2:(k + 1) * 2],
                b.Cc[:, k * 1024 + half * 512:k * 1024 + (half + 1) * 512],
                start=(k == 0), stop=False),
                waits=[("ve", tb), ("gd", sd_d1)] +
                      (em.psw(psn) if k == 0 else []))
        em.op("tensor", lambda q, half=half, ps=ps: q.matmul(
            ps[0:2, 0:512], b.ones_row[0:1, 0:2],
            b.rowbf[0:1, 1024 + half * 512:1024 + (half + 1) * 512],
            start=False, stop=True), inc="te")
    d1_done = em.val("te")
    em.op("scalar", lambda q: q.activation(b.h1_tm[0:2, 0:512],
          b.psB[0:2, 0:512], AF.Gelu), waits=[("te", d1_done)], inc="ae")
    em.op("scalar", lambda q: q.activation(b.h1_tm[0:2, 512:1024],
          b.psG[0:2, 0:512], AF.Gelu), inc="ae")
    em.psr("psB", "ae")
    em.psr("psG", "ae")
    g_v = em.val("ae")
    for dd in range(8):
        em.op("tensor", lambda q, dd=dd: q.transpose(
            b.psT[:, dd * 2:dd * 2 + 2], b.h1_tm[0:2, dd * P:(dd + 1) * P],
            b.ident[0:2, 0:2]),
            waits=[("ae", g_v)] + (em.psw("psT") if dd == 0 else []), inc="te")
    em.op("vector", lambda q: q.tensor_copy(b.h1_bf[:, :], b.psT[:, 0:16]),
          waits=[("te", em.val("te"))], inc="ve")
    em.psr("psT", "ve")
    h1f = em.val("ve")
    for grp in range(2):
        sd_v = em.dma("gpsimd", b.Ee[:, 0:8192],
                      b.d2_bf[:, grp * 8192:(grp + 1) * 8192], "gd",
                      waits=[("te", em.val("te")), ("ve", em.val("ve"))])
        for mm_ in range(8):
            m = grp * 8 + mm_
            ps = b.psE if (m % 2 == 0) else b.psF
            psn = "psE" if (m % 2 == 0) else "psF"
            for k in range(8):
                em.op("tensor", lambda q, mm_=mm_, k=k, ps=ps: q.matmul(
                    ps[0:2, 0:128], b.h1_bf[:, k * 2:(k + 1) * 2],
                    b.Ee[:, (mm_ * 8 + k) * P:(mm_ * 8 + k + 1) * P],
                    start=(k == 0), stop=(k == 7)),
                    waits=[("ve", h1f), ("gd", sd_v)] +
                          (em.psw(psn) if k == 0 else []), inc="te")
            te_v = em.val("te")
            em.op("vector", lambda q, m=m, ps=ps: q.tensor_copy(
                b.th_tm2[0:2, m * P:(m + 1) * P], ps[0:2, 0:128]),
                waits=[("te", te_v)], inc="ve")
            em.psr(psn, "ve")
    d2e = em.val("ve")
    em.dma("sync", b.ar_in[:, :], b.th_tm2[0:2, :], "sd", waits=[("ve", d2e)])
    cc_v = emit_collective(em, b, "AllReduce", b.ar_in, b.ar_out,
                           [("sd", em.val("sd"))])
    em.dma("sync", b.th_tm2[0:2, :], b.ar_out[0:2, :], "sd",
           waits=[("cc", cc_v)])
    sd_v = em.val("sd")
    for d in range(DT):
        em.op("tensor", lambda q, d=d: q.transpose(
            b.psT[:, d * 2:d * 2 + 2], b.th_tm2[0:2, d * P:(d + 1) * P],
            b.ident[0:2, 0:2]),
            waits=[("sd", sd_v)] + (em.psw("psT") if d == 0 else []), inc="te")
    tr_v = em.val("te")
    em.op("vector", lambda q: q.tensor_add(b.th_sc[:, :], b.psT[:, 0:32],
          b.th_fm[:, :]), waits=[("te", tr_v)], inc="ve")
    em.psr("psT", "ve")
    em.op("vector", lambda q: q.tensor_tensor(
        b.th_fm[:, :].rearrange("p (d t) -> p d t", t=2),
        b.th_sc[:, :].rearrange("p (d t) -> p d t", t=2),
        _colbc(b, 64), ALU.add), inc="ve")



def dup_ap(sl, nblk=4, dup=2, w=P):
    a = [list(p) for p in sl.ap]
    return bass.AP(sl.tensor, sl.offset, [a[0], [w, nblk], [0, dup], [1, w]])


def v3(sl, nu=8, w=P):
    return sl.rearrange("p (u w) -> p u w", w=w)


def emit_attn_a(em, b):
    i = b.inp
    sm = b.smallf
    em.phase = "a0"
    ve_ln, sd_ln = emit_ln(em, b, b.A, gin=b.gin, eps=1e-5)
    cc_g = emit_collective(em, b, "AllGather", b.gin, b.gxn, [("sd", sd_ln)])
    em.op("scalar", lambda q: q.activation(b.nealog[0:4, 0:1],
          b.alog_sb[0:4, 0:1], AF.Exp), inc="ae")
    em.op("vector", lambda q: q.tensor_scalar(b.nealog[0:4, 0:1],
          b.nealog[0:4, 0:1], -1.0, None, ALU.mult),
          waits=[("ae", em.val("ae"))], inc="ve")
    em.phase = "a1"
    CV = 4608
    SQ = 7680
    BC = 9728
    FO = 11776
    for pas in range(2):
        war = [("te", em.val("te")), ("ve", em.val("ve")), ("ae", em.val("ae")),
               ("gd", em.val("gd")), ("sd", em.val("sd"))]
        gd_w = em.dma("gpsimd", b.Bb[:, 0:12288],
                      i["wsA_img"][:, pas * 12288:(pas + 1) * 12288], "gd",
                      waits=war)
        pcons = {}
        for n in range(16):
            r, off = n // 2, (n % 2) * 512
            bi = n // 8
            xc = b.Ee[:, 0:8192] if n % 2 == 0 else b.A[:, 0:8192]
            xw = pcons.get(n - 2, [("cc", cc_g), ("te", em.val("te")),
                                   ("ae", em.val("ae")), ("ve", em.val("ve")),
                                   ("sd", em.val("sd"))])
            sd_x = em.dma(
                "sync", xc.rearrange("p (k t) -> p k t", t=512),
                b.gxn[2048 * r:2048 * (r + 1), off:off + 512].rearrange(
                    "(k p) t -> p k t", p=P), "sd", waits=xw)
            if pas == 0:
                for (img0, ps, psn) in ((0, b.psE, "psE"), (64, b.psF, "psF")):
                    for k in range(DT):
                        em.op("tensor", lambda q, img0=img0, k=k, ps=ps, xc=xc:
                              q.matmul(
                            ps[0:4, 0:512], b.wba_sb[:, img0 + k * 4:img0 + (k + 1) * 4],
                            xc[:, k * 512:(k + 1) * 512],
                            start=(k == 0), stop=(k == DT - 1)),
                            waits=[("sd", sd_x)] +
                                  (em.psw(psn) if k == 0 else []), inc="te")
                ba_v = em.val("te")
                stg = b.Dd[0:4, 3072:3584]
                em.op("vector", lambda q: q.tensor_copy(stg, b.psE[0:4, 0:512]),
                      waits=[("te", ba_v), ("sd", em.val("sd"))], inc="ve")
                em.psr("psE", "ve")
                sd_b = em.dma("sync", b.ba_scr[0:4, n * 512:(n + 1) * 512], stg,
                              "sd", waits=[("ve", em.val("ve"))])
                em.op("vector", lambda q: q.tensor_copy(stg, b.psF[0:4, 0:512]),
                      waits=[("te", ba_v), ("sd", sd_b)], inc="ve")
                em.psr("psF", "ve")
                em.dma("sync", b.ba_scr[4:8, n * 512:(n + 1) * 512], stg,
                       "sd", waits=[("ve", em.val("ve"))])
            for m in range(6):
                gm = pas * 6 + m
                kind = ("q", "q", "k", "k", "v", "v")[m] if pas == 0 else                        ("v", "v", "z", "z", "z", "z")[m]
                ps, psn = (b.psA, "psA") if m % 2 == 0 else (b.psB, "psB")
                for k in range(DT):
                    em.op("tensor", lambda q, m=m, k=k, ps=ps, xc=xc: q.matmul(
                        ps[:, 0:512], b.Bb[:, (m * DT + k) * P:(m * DT + k + 1) * P],
                        xc[:, k * 512:(k + 1) * 512],
                        start=(k == 0), stop=(k == DT - 1)),
                        waits=[("gd", gd_w), ("sd", sd_x)] +
                              (em.psw(psn) if k == 0 else []), inc="te")
                mm_v = em.val("te")
                if kind == "z":
                    zslot = b.Cc[:, FO + m * 512:FO + (m + 1) * 512]
                    em.op("scalar", lambda q, ps=ps, zslot=zslot: q.activation(
                        zslot, ps[:, 0:512], AF.Silu),
                        waits=[("te", mm_v), ("sd", em.val("sd"))], inc="ae")
                    em.psr(psn, "ae")
                    zr = (gm - 8) * P
                    em.dma("sync", b.z_scr[zr:zr + P, n * 512:(n + 1) * 512],
                           zslot, "sd", waits=[("ae", em.val("ae"))])
                    continue
                rawb = b.Cc[:, m * 768:m * 768 + 515]
                cvi = gm if gm < 8 else None
                if n == 0 or n == 8:
                    em.op("vector", lambda q, rawb=rawb: q.memset(
                        rawb[:, 0:3], 0.0), waits=[("ae", em.val("ae"))], inc="ve")
                else:
                    em.op("vector", lambda q, rawb=rawb: q.tensor_copy(
                        rawb[:, 0:3], rawb[:, 512:515]),
                        waits=[("ae", em.val("ae")), ("gd", em.val("gd"))],
                        inc="ve")
                cw_v = em.val("ve")
                em.op("scalar", lambda q, ps=ps, rawb=rawb: q.activation(
                    rawb[:, 3:515], ps[:, 0:512], AF.Identity),
                    waits=[("te", mm_v), ("ve", cw_v)], inc="ae")
                em.psr(psn, "ae")
                ev_v = em.val("ae")
                cvo = b.Cc[:, CV + m * 512:CV + (m + 1) * 512]
                em.op("vector", lambda q, rawb=rawb, cvo=cvo, cvi=cvi:
                      q.tensor_scalar(cvo, rawb[:, 0:512],
                                      b.cw_sb[:, cvi * 4:cvi * 4 + 1], None,
                                      ALU.mult),
                      waits=[("ae", ev_v)], inc="ve")
                for j in range(1, 4):
                    em.op("vector", lambda q, rawb=rawb, cvo=cvo, cvi=cvi, j=j:
                          q.scalar_tensor_tensor(
                              cvo, rawb[:, j:512 + j],
                              b.cw_sb[:, cvi * 4 + j:cvi * 4 + j + 1], cvo,
                              ALU.mult, ALU.add), inc="ve")
                cv_v = em.val("ve")
                fo = b.Cc[:, FO + m * 512:FO + (m + 1) * 512]
                em.op("scalar", lambda q, cvo=cvo, fo=fo: q.activation(
                    fo, cvo, AF.Silu),
                    waits=[("ve", cv_v), ("sd", em.val("sd"))], inc="ae")
                si_v = em.val("ae")
                if kind == "v":
                    vr = ((gm - 4) if pas == 0 else (gm - 4)) * P
                    em.dma("sync", b.v_scr[vr:vr + P, n * 512:(n + 1) * 512],
                           fo, "sd", waits=[("ae", si_v)])
                    continue
                mq = m
                sq = b.Cc[:, SQ + mq * 512:SQ + (mq + 1) * 512]
                em.op("vector", lambda q, fo=fo, sq=sq: q.tensor_mul(sq, fo, fo),
                      waits=[("ae", si_v), ("te", em.val("te"))], inc="ve")
                sq_v = em.val("ve")
                em.op("tensor", lambda q, sq=sq: q.matmul(
                    b.psC[0:1, 0:512], b.ones_col[:, 0:1], sq,
                    start=True, stop=True),
                    waits=[("ve", sq_v)] + em.psw("psC"), inc="te")
                ss_v = em.val("te")
                rr = b.Dd[0:1, 1024:1536]
                em.op("vector", lambda q, rr=rr: q.tensor_scalar(
                    rr, b.psC[0:1, 0:512], 1.0, 1e-6, ALU.mult, ALU.add),
                    waits=[("te", ss_v), ("ae", em.val("ae"))], inc="ve")
                em.psr("psC", "ve")
                rv = em.val("ve")
                em.op("scalar", lambda q, rr=rr: q.activation(rr, rr, AF.Ln),
                      waits=[("ve", rv)], inc="ae")
                ln_v = em.val("ae")
                em.op("scalar", lambda q, rr=rr: q.activation(
                    rr, rr, AF.Exp, scale=-0.5),
                    waits=[("ae", ln_v)], inc="ae")
                rst_v = em.val("ae")
                if kind == "q":
                    em.op("vector", lambda q, rr=rr: q.tensor_scalar(
                        rr, rr, 0.08838834764831845, None, ALU.mult),
                        waits=[("ae", rst_v)], inc="ve")
                em.op("vector", lambda q, rr=rr: q.tensor_copy(
                    b.rowbf[0:1, 0:512], rr),
                    waits=[("ae", rst_v), ("te", em.val("te"))], inc="ve")
                rb_v = em.val("ve")
                em.op("tensor", lambda q: q.matmul(
                    b.psD[:, 0:512], b.ones_row[0:1, :], b.rowbf[0:1, 0:512],
                    start=True, stop=True),
                    waits=[("ve", rb_v)] + em.psw("psD"), inc="te")
                bc_v = em.val("te")
                bce = b.Cc[:, BC + mq * 512:BC + (mq + 1) * 512]
                em.op("scalar", lambda q, bce=bce: q.activation(
                    bce, b.psD[:, 0:512], AF.Identity),
                    waits=[("te", bc_v), ("ve", em.val("ve"))], inc="ae")
                em.psr("psD", "ae")
                em.op("vector", lambda q, fo=fo, bce=bce: q.tensor_mul(
                    fo, fo, bce), waits=[("ae", em.val("ae"))], inc="ve")
                nm_v = em.val("ve")
                qr = gm * P
                em.dma("sync", b.qk_scr[qr:qr + P, n * 512:(n + 1) * 512],
                       fo, "sd", waits=[("ve", nm_v)])
            pcons[n] = [("te", em.val("te")), ("ve", em.val("ve")),
                        ("ae", em.val("ae")), ("sd", em.val("sd")),
                        ("gd", em.val("gd"))]
    em.phase = "a2"
    ones4 = b.Dd[0:4, 3584:3712]
    em.op("vector", lambda q: q.memset(ones4, 1.0), inc="ve")
    sl_ = lambda k: b.Dd[0:4, k * 512:(k + 1) * 512]
    proj_sd = em.val("sd")
    for c in range(16):
        bi = c // 8
        i0 = (c % 8) * 4
        tok = slice(c * 512, (c + 1) * 512)
        war = [("ve", em.val("ve")), ("ae", em.val("ae")),
               ("gd", em.val("gd"))]
        sd_r = em.dma("sync", sl_(0), b.ba_scr[0:4, tok], "sd",
                      waits=[("sd", proj_sd)] + war)
        sd_r = em.dma("sync", sl_(2), b.ba_scr[4:8, tok], "sd")
        ld_v = em.val("sd")
        em.op("scalar", lambda q: q.activation(sl_(1), sl_(0), AF.Sigmoid),
              waits=[("sd", ld_v), ("ve", em.val("ve")),
                     ("gd", em.val("gd"))], inc="ae")
        b_v = em.val("ae")
        em.op("vector", lambda q: q.tensor_scalar(sl_(0), sl_(1), -1.0, None,
              ALU.mult), waits=[("ae", b_v)], inc="ve")
        em.op("vector", lambda q: q.tensor_scalar(sl_(2), sl_(2),
              b.dtb_sb[0:4, 0:1], None, ALU.add),
              waits=[("sd", ld_v), ("gd", em.val("gd")), ("ae", em.val("ae"))],
              inc="ve")
        a_v = em.val("ve")
        em.op("vector", lambda q: q.tensor_scalar_max(sl_(6), sl_(2), 0.0),
              inc="ve")
        ab_v = em.val("ve")
        em.op("vector", lambda q: q.scalar_tensor_tensor(
            sl_(3), sl_(2), -1.0, sl_(2), ALU.mult, ALU.max), inc="ve")
        abs_v = em.val("ve")
        em.op("scalar", lambda q: q.activation(sl_(4), sl_(3), AF.Exp,
              scale=-1.0), waits=[("ve", abs_v)], inc="ae")
        en_v = em.val("ae")
        em.op("scalar", lambda q: q.activation(sl_(3), sl_(4), AF.Ln,
              bias=1.0), waits=[("ae", en_v)], inc="ae")
        l1_v = em.val("ae")
        em.op("vector", lambda q: q.tensor_add(sl_(6), sl_(6), sl_(3)),
              waits=[("ae", l1_v)], inc="ve")
        em.op("vector", lambda q: q.tensor_scalar(sl_(6), sl_(6),
              b.nealog[0:4, 0:1], None, ALU.mult), inc="ve")
        g_v = em.val("ve")
        for cs in range(4):
            em.op("vector", lambda q, cs=cs: q.tensor_tensor_scan(
                sl_(2)[0:4, cs * P:(cs + 1) * P], ones4,
                sl_(6)[0:4, cs * P:(cs + 1) * P], 0.0, ALU.mult, ALU.add),
                waits=[("ve", g_v)], inc="ve")
        gc_v = em.val("ve")

        def rowout(qid, slot, dst, rl=4096):
            em.dma("gpsimd",
                   bass.AP(dst.tensor, dst.offset + i0 * rl + qid * 1024 + bi * 512,
                           [[P, 4], [rl, 4], [1, P]]),
                   slot.rearrange("v (s w) -> v s w", w=P), "gd",
                   waits=[("ve", em.val("ve")), ("ae", em.val("ae"))])
        rowout(0, sl_(2), b.rows_scr)
        em.op("vector", lambda q: q.tensor_copy(b.r4lo[0:4, :], sl_(2)),
              waits=[("ve", gc_v)], inc="ve")
        hb_v = em.val("ve")
        em.op("vector", lambda q: q.tensor_copy(sl_(3), b.r4lo[0:4, :]),
              waits=[("ve", hb_v)], inc="ve")
        em.op("vector", lambda q: q.tensor_sub(sl_(3), sl_(2), sl_(3)),
              inc="ve")
        rowout(3, sl_(3), b.rows_scr)
        em.op("scalar", lambda q: q.activation(sl_(4), sl_(2), AF.Exp),
              waits=[("ve", gc_v)], inc="ae")
        eg_v = em.val("ae")
        rowout(1, sl_(4), b.rows_scr)
        rowout(2, sl_(0), b.rows_scr)
        em.op("vector", lambda q: q.tensor_copy(
            sl_(3).rearrange("v (s w) -> v s w", w=P),
            sl_(2)[0:4, P - 1::P].broadcast_to([4, 4, P])),
            waits=[("ae", eg_v), ("gd", em.val("gd"))], inc="ve")
        em.op("vector", lambda q: q.tensor_sub(sl_(6), sl_(3), sl_(2)),
              inc="ve")
        sb_v = em.val("ve")
        em.op("scalar", lambda q: q.activation(sl_(3), sl_(6), AF.Exp),
              waits=[("ve", sb_v)], inc="ae")
        egl_v = em.val("ae")
        rowout(1, sl_(3), b.cols_scr, rl=5120)
        em.op("vector", lambda q: q.tensor_mul(sl_(6), sl_(3), sl_(4)),
              waits=[("ae", egl_v)], inc="ve")
        rowout(3, sl_(6), b.cols_scr, rl=5120)
        em.op("vector", lambda q: q.tensor_mul(sl_(5), sl_(1), sl_(4)),
              inc="ve")
        rowout(0, sl_(5), b.cols_scr, rl=5120)
        rowout(2, sl_(1), b.cols_scr, rl=5120)
        rowout(4, sl_(0), b.cols_scr, rl=5120)
    em.phase = "a3"
    war = [("te", em.val("te")), ("ve", em.val("ve")), ("ae", em.val("ae")),
           ("sd", em.val("sd"))]
    gd_m1 = em.dma("gpsimd", b.Ee[:, 0:1024], i["c_maskD"][:, :], "gd",
                   waits=war)
    gd_m2 = em.dma("gpsimd", b.Ee[:, 1024:2048], i["c_maskD2"][:, :], "gd")
    gd_m3 = em.dma("gpsimd", b.Ee[:, 2048:3072], i["c_mask01u"][:, :], "gd")
    gd_m4 = em.dma("gpsimd", b.Ee[:, 3072:4096], i["c_eyeR"][:, :], "gd")
    em.dma("gpsimd", b.r4lo[0:4, :], i["c_blockneg"][:, :], "gd",
           waits=[("ve", em.val("ve"))])
    msk_v = em.val("gd")
    mD = b.Ee[:, 0:1024]
    mD2 = b.Ee[:, 1024:2048]
    m01 = b.Ee[:, 2048:3072]
    eyeR = b.Ee[:, 3072:4096]
    cons_after = {}
    Aa = lambda k: b.A[:, k * 1024:(k + 1) * 1024]
    Dw, D2w, Xs, XTs, Ts = Aa(0), Aa(1), Aa(2), Aa(3), Aa(4)
    qeg, aiT, vcs, kcds, vnew = Aa(5), Aa(6), Aa(7), Aa(8), Aa(9)
    vtil, stbf, osb, ozq, ofin = Aa(10), Aa(11), Aa(12), Aa(13), Aa(14)
    vbw = b.Cc[:, 0:1024]
    kbegw = b.Cc[:, 1024:2048]
    state = b.Dd[:, 0:1024]
    tmpD = b.Dd[:, 1024:2048]
    tmp2 = b.Dd[:, 2048:3072]
    em.op("vector", lambda q: q.memset(state[:, :], 0.0),
          waits=[("te", em.val("te")), ("gd", em.val("gd")),
                 ("sd", em.val("sd")), ("ae", em.val("ae"))], inc="ve")
    em.op("vector", lambda q: q.memset(stbf[:, :], 0.0), inc="ve")
    st0_v = em.val("ve")
    L = {}

    def _scan_loads(it):
        pp = it % 2
        em.phase = "a3a"
        ldw = cons_after.get(it - 2,
                             [("te", em.val("te")), ("ve", em.val("ve")),
                              ("ae", em.val("ae")), ("sc", em.val("sc")),
                              ("gd", em.val("gd")), ("sd", em.val("sd"))]
                             if it < 2 else None)
        rbase = 8192 + pp * 4096
        rows_sd = em.dma("sync", b.Bb[0:1, rbase:rbase + 4096],
                         b.rows_scr[it, 0:4096], "sd", waits=ldw)
        crow_sd = em.dma("sync",
                         b.Cc[0:40, 4096 + pp * P:4096 + (pp + 1) * P],
                         b.cols_scr[it, :].rearrange("(u w) -> u w", w=P),
                         "sd", waits=ldw)
        em.op("tensor", lambda q, pp=pp: q.transpose(
            b.psT[:, pp * 40:pp * 40 + 32],
            b.Cc[0:32, 4096 + pp * P:4096 + (pp + 1) * P],
            b.ident[0:32, 0:32]),
            waits=[("sd", em.val("sd"))] + em.psw("psT"), inc="te")
        em.op("tensor", lambda q, pp=pp: q.transpose(
            b.psT[:, pp * 40 + 32:pp * 40 + 40],
            b.Cc[32:40, 4096 + pp * P:4096 + (pp + 1) * P],
            b.ident[32:40, 32:40]), inc="te")
        em.op("vector", lambda q, pp=pp: q.tensor_copy(
            b.Dd[:, 3072 + pp * 40:3112 + pp * 40],
            b.psT[:, pp * 40:pp * 40 + 40]),
            waits=[("te", em.val("te"))], inc="ve")
        em.psr("psT", "ve")
        r4base = 15360 + pp * 512
        ktm = b.Bb[:, 4096 + pp * 1024:4096 + (pp + 1) * 1024]
        vstep = b.Bb[:, 6144 + pp * 1024:6144 + (pp + 1) * 1024]
        zstep = b.Cc[:, 2048 + pp * 1024:2048 + (pp + 1) * 1024]
        qstep = b.Bb[:, 0 + pp * 1024:0 + (pp + 1) * 1024]
        kstep = b.Bb[:, 2048 + pp * 1024:2048 + (pp + 1) * 1024]
        for u in range(8):
            bi2, vh = u // 4, u % 4
            kh = vh // 2
            cb = bi2 * S + it * P
            em.dma("sync", qstep[:, u * P:(u + 1) * P],
                   b.qk_scr[kh * P:(kh + 1) * P, cb:cb + P], "sd")
            em.dma("sync", kstep[:, u * P:(u + 1) * P],
                   b.qk_scr[256 + kh * P:256 + (kh + 1) * P, cb:cb + P], "sd")
            em.op("sync", lambda q, u=u, kh=kh, cb=cb, ktm=ktm:
                  q.dma_start_transpose(
                      ktm[:, u * P:(u + 1) * P],
                      b.qk_scr[256 + kh * P:256 + (kh + 1) * P, cb:cb + P]),
                  waits=(ldw if u == 0 else ()), inc="sc", inc_amt=16)
            em.op("sync", lambda q, u=u, vh=vh, cb=cb, vstep=vstep:
                  q.dma_start_transpose(
                      vstep[:, u * P:(u + 1) * P],
                      b.v_scr[vh * P:(vh + 1) * P, cb:cb + P]),
                  waits=(), inc="sc", inc_amt=16)
        for bi2 in range(2):
            cb = bi2 * S + it * P
            em.dma("sync",
                   zstep[:, bi2 * 512:(bi2 + 1) * 512].rearrange(
                       "p (v w) -> p v w", w=P),
                   b.z_scr[0:512, cb:cb + P].rearrange("(v p) w -> p v w", p=P),
                   "sd")
        bar = cons_after.get(it - 1, ldw)
        for qi, qq in ((0, 0), (1, 3)):
            em.dma("sync",
                   b.A[0:4, r4base + qi * 256:r4base + (qi + 1) * 256].rearrange(
                       "r (h w) -> r h w", w=P),
                   b.rows_scr[it, qq * 1024:(qq + 1) * 1024].rearrange(
                       "(h r w) -> r h w", h=2, w=P),
                   "sd", waits=bar)
        L[it] = dict(r4_sd=em.val("sd"), ld_sd=em.val("sd"),
                     ld_sc=em.val("sc"), ld_ve=em.val("ve"))

    def _scan_step(it, pp):
        li = L[it]
        r4_sd, ld_sd = li["r4_sd"], li["ld_sd"]
        ld_sc, ld_ve = li["ld_sc"], li["ld_ve"]
        rbase = 8192 + pp * 4096
        colT = b.Dd[:, 3072 + pp * 40:3112 + pp * 40]
        ccol = lambda qid, u: colT[:, qid * 8 + u:qid * 8 + u + 1]
        r4base = 15360 + pp * 512
        rows4 = b.A[0:4, r4base:r4base + 512]
        qstep = b.Bb[:, 0 + pp * 1024:0 + (pp + 1) * 1024]
        kstep = b.Bb[:, 2048 + pp * 1024:2048 + (pp + 1) * 1024]
        ktm = b.Bb[:, 4096 + pp * 1024:4096 + (pp + 1) * 1024]
        vstep = b.Bb[:, 6144 + pp * 1024:6144 + (pp + 1) * 1024]
        zstep = b.Cc[:, 2048 + pp * 1024:2048 + (pp + 1) * 1024]
        em.phase = "a3d"
        rowsh = lambda qid, h: b.Bb[0:1, rbase + qid * 1024 + h * 512:
                                    rbase + qid * 1024 + h * 512 + 512]
        for h in range(2):
            ps = b.psC if h == 0 else b.psD
            psn = "psC" if h == 0 else "psD"
            em.op("tensor", lambda q, h=h, ps=ps: q.matmul(
                ps[:, 0:512], b.ones_row[0:1, :],
                rowsh(0, h), start=True, stop=False),
                waits=[("sd", r4_sd), ("gd", msk_v)] + em.psw(psn), inc="te")
            em.op("tensor", lambda q, h=h, ps=ps: q.matmul(
                ps[:, 0:512], b.ones_row[0:1, :],
                rowsh(3, h), start=False, stop=False), inc="te")
            em.op("tensor", lambda q, h=h, ps=ps, rows4=rows4: q.matmul(
                ps[:, 0:512], rows4[0:4, h * P:(h + 1) * P],
                b.r4lo[0:4, :], start=False, stop=False), inc="te")
            em.op("tensor", lambda q, h=h, ps=ps, rows4=rows4: q.matmul(
                ps[:, 0:512], rows4[0:4, 256 + h * P:256 + (h + 1) * P],
                b.r4lo[0:4, :], start=False, stop=True), inc="te")
        g_v = em.val("te")
        em.op("vector", lambda q: q.scalar_tensor_tensor(
            tmp2[:, 0:512], b.psC[:, 0:512], 1.0, mD[:, 0:512],
            ALU.mult, ALU.add),
            waits=[("te", g_v), ("ae", em.val("ae")), ("gd", msk_v)], inc="ve")
        em.op("vector", lambda q: q.scalar_tensor_tensor(
            tmp2[:, 512:1024], b.psD[:, 0:512], 1.0, mD[:, 512:1024],
            ALU.mult, ALU.add), inc="ve")
        dw_pre = em.val("ve")
        em.op("vector", lambda q: q.scalar_tensor_tensor(
            tmpD[:, 0:512], b.psC[:, 0:512], -1.0, mD2[:, 0:512],
            ALU.mult, ALU.add), inc="ve")
        em.op("vector", lambda q: q.scalar_tensor_tensor(
            tmpD[:, 512:1024], b.psD[:, 0:512], -1.0, mD2[:, 512:1024],
            ALU.mult, ALU.add), inc="ve")
        em.psr("psC", "ve")
        em.psr("psD", "ve")
        d2_pre = em.val("ve")
        em.op("scalar", lambda q: q.activation(Dw, tmp2[:, :], AF.Exp),
              waits=[("ve", dw_pre), ("te", em.val("te"))], inc="ae")
        d_v = em.val("ae")
        em.op("scalar", lambda q: q.activation(D2w, tmpD[:, :], AF.Exp),
              waits=[("ve", d2_pre)], inc="ae")
        d2_v = em.val("ae")
        em.phase = "a3a2"
        for u in range(8):
            ps = b.psA if u < 4 else b.psB
            psn = "psA" if u < 4 else "psB"
            em.op("tensor", lambda q, u=u, ps=ps: q.matmul(
                ps[:, (u % 4) * P:(u % 4 + 1) * P], kstep[:, u * P:(u + 1) * P],
                kstep[:, u * P:(u + 1) * P], start=True, stop=True),
                waits=[("sd", ld_sd)] + (em.psw(psn) if u % 4 == 0 else []),
                inc="te")
            ps2 = b.psE if u < 4 else b.psF
            ps2n = "psE" if u < 4 else "psF"
            em.op("tensor", lambda q, u=u, ps2=ps2: q.matmul(
                ps2[:, (u % 4) * P:(u % 4 + 1) * P], kstep[:, u * P:(u + 1) * P],
                qstep[:, u * P:(u + 1) * P], start=True, stop=True),
                waits=[("sd", ld_sd)] + (em.psw(ps2n) if u % 4 == 0 else []),
                inc="te")
        kk_v = em.val("te")
        em.op("vector", lambda q: q.tensor_tensor(aiT[:, 0:512], b.psE[:, 0:512],
              Dw[:, 0:512], ALU.mult), waits=[("te", kk_v), ("ae", d_v)],
              inc="ve")
        em.op("vector", lambda q: q.tensor_tensor(aiT[:, 512:1024],
              b.psF[:, 0:512], Dw[:, 512:1024], ALU.mult), inc="ve")
        em.psr("psE", "ve")
        em.psr("psF", "ve")
        em.op("vector", lambda q: q.tensor_tensor(Xs[:, 0:512], b.psA[:, 0:512],
              Dw[:, 0:512], ALU.mult), inc="ve")
        em.op("vector", lambda q: q.tensor_tensor(Xs[:, 512:1024],
              b.psB[:, 0:512], Dw[:, 512:1024], ALU.mult), inc="ve")
        for u in range(8):
            psx = b.psA if u < 4 else b.psB
            em.op("vector", lambda q, u=u, psx=psx: q.scalar_tensor_tensor(
                XTs[:, u * P:(u + 1) * P],
                psx[:, (u % 4) * P:(u % 4 + 1) * P], ccol(4, u),
                D2w[:, u * P:(u + 1) * P], ALU.mult, ALU.mult),
                waits=[("ae", d2_v), ("ve", ld_ve)], inc="ve")
        em.psr("psA", "ve")
        em.psr("psB", "ve")
        for h in range(2):
            ps = b.psC if h == 0 else b.psD
            psn = "psC" if h == 0 else "psD"
            em.op("tensor", lambda q, h=h, ps=ps: q.matmul(
                ps[:, 0:512], b.ones_row[0:1, :],
                rowsh(2, h), start=True, stop=True),
                waits=em.psw(psn), inc="te")
        bw_v = em.val("te")
        em.op("vector", lambda q: q.tensor_tensor(Xs[:, 0:512], Xs[:, 0:512],
              b.psC[:, 0:512], ALU.mult), waits=[("te", bw_v)], inc="ve")
        em.op("vector", lambda q: q.tensor_tensor(Xs[:, 512:1024],
              Xs[:, 512:1024], b.psD[:, 0:512], ALU.mult), inc="ve")
        em.psr("psC", "ve")
        em.psr("psD", "ve")
        em.op("vector", lambda q: q.tensor_mul(Xs, Xs, m01), inc="ve")
        for h in range(2):
            ps = b.psE if h == 0 else b.psF
            psn = "psE" if h == 0 else "psF"
            em.op("tensor", lambda q, h=h, ps=ps: q.matmul(
                ps[:, 0:512], b.ones_row[0:1, :],
                rowsh(1, h), start=True, stop=True),
                waits=em.psw(psn), inc="te")
        eb_v = em.val("te")
        em.op("vector", lambda q: q.tensor_tensor(qeg[:, 0:512], qstep[:, 0:512],
              b.psE[:, 0:512], ALU.mult), waits=[("te", eb_v)], inc="ve")
        em.op("vector", lambda q: q.tensor_tensor(qeg[:, 512:1024],
              qstep[:, 512:1024], b.psF[:, 0:512], ALU.mult), inc="ve")
        em.psr("psE", "ve")
        em.psr("psF", "ve")
        pbuilt_v = em.val("ve")
        em.phase = "a3b"
        em.op("vector", lambda q: q.tensor_add(Ts, Xs, eyeR),
              waits=[("ve", pbuilt_v)], inc="ve")
        ts0_v = em.val("ve")

        def _sq_mms(first_waits, with_x):
            for u in range(8):
                if with_x:
                    ps = b.psC if u < 4 else b.psD
                    psn = "psC" if u < 4 else "psD"
                    em.op("tensor", lambda q, u=u, ps=ps: q.matmul(
                        ps[:, (u % 4) * P:(u % 4 + 1) * P],
                        XTs[:, u * P:(u + 1) * P], Xs[:, u * P:(u + 1) * P],
                        start=True, stop=True),
                        waits=(first_waits if u == 0 else []) +
                              (em.psw(psn) if u % 4 == 0 else []), inc="te")
                ps2 = b.psE if u < 4 else b.psF
                ps2n = "psE" if u < 4 else "psF"
                em.op("tensor", lambda q, u=u, ps2=ps2: q.matmul(
                    ps2[:, (u % 4) * P:(u % 4 + 1) * P],
                    Xs[:, u * P:(u + 1) * P], XTs[:, u * P:(u + 1) * P],
                    start=True, stop=True),
                    waits=(first_waits if u == 0 and not with_x else []) +
                          (em.psw(ps2n) if u % 4 == 0 else []), inc="te")
            return em.val("te")

        def _sq_evac(sq_v, with_x):
            if with_x:
                em.op("vector", lambda q: q.tensor_copy(Xs[:, 0:512],
                      b.psC[:, 0:512]), waits=[("te", sq_v)], inc="ve")
                em.op("vector", lambda q: q.tensor_copy(Xs[:, 512:1024],
                      b.psD[:, 0:512]), inc="ve")
                em.psr("psC", "ve")
                em.psr("psD", "ve")
            em.op("scalar", lambda q: q.activation(XTs[:, 0:512],
                  b.psE[:, 0:512], AF.Identity), waits=[("te", sq_v)],
                  inc="ae")
            em.op("scalar", lambda q: q.activation(XTs[:, 512:1024],
                  b.psF[:, 0:512], AF.Identity), inc="ae")
            em.psr("psE", "ae")
            em.psr("psF", "ae")
            return em.val("ve"), em.val("ae")

        sq_v = _sq_mms([("ve", ts0_v)], True)
        xe_ve, xe_ae = _sq_evac(sq_v, True)
        tadd_prev = ts0_v
        for lv in range(1, 7):
            for u in range(8):
                ps = b.psA if u < 4 else b.psB
                psn = "psA" if u < 4 else "psB"
                em.op("tensor", lambda q, u=u, ps=ps: q.matmul(
                    ps[:, (u % 4) * P:(u % 4 + 1) * P],
                    XTs[:, u * P:(u + 1) * P], Ts[:, u * P:(u + 1) * P],
                    start=True, stop=True),
                    waits=([("ve", max(xe_ve, tadd_prev)), ("ae", xe_ae)]
                           if u == 0 else []) +
                          (em.psw(psn) if u % 4 == 0 else []), inc="te")
            t_v = em.val("te")
            if lv < 6:
                sq_v = _sq_mms([], lv < 5)
            em.op("vector", lambda q: q.tensor_tensor(Ts[:, 0:512],
                  Ts[:, 0:512], b.psA[:, 0:512], ALU.add),
                  waits=[("te", t_v)], inc="ve")
            em.op("vector", lambda q: q.tensor_tensor(Ts[:, 512:1024],
                  Ts[:, 512:1024], b.psB[:, 0:512], ALU.add), inc="ve")
            em.psr("psA", "ve")
            em.psr("psB", "ve")
            tadd_prev = em.val("ve")
            if lv < 6:
                xe_ve, xe_ae = _sq_evac(sq_v, lv < 5)
        inv_v = em.val("ve")
        em.phase = "a3c"
        for u in range(8):
            em.op("vector", lambda q, u=u: q.tensor_scalar(
                vbw[:, u * P:(u + 1) * P], vstep[:, u * P:(u + 1) * P],
                ccol(2, u), None, ALU.mult),
                waits=[("sd", ld_sd), ("sc", ld_sc), ("ve", ld_ve)], inc="ve")
            em.op("vector", lambda q, u=u: q.tensor_scalar(
                kbegw[:, u * P:(u + 1) * P], ktm[:, u * P:(u + 1) * P],
                ccol(0, u), None, ALU.mult), inc="ve")
        vk_v = em.val("ve")
        for u in range(8):
            ps = b.psC if u < 4 else b.psD
            psn = "psC" if u < 4 else "psD"
            em.op("tensor", lambda q, u=u, ps=ps: q.matmul(
                ps[:, (u % 4) * P:(u % 4 + 1) * P], Ts[:, u * P:(u + 1) * P],
                vbw[:, u * P:(u + 1) * P], start=True, stop=True),
                waits=[("ve", vk_v)] + (em.psw(psn) if u % 4 == 0 else []),
                inc="te")
            ps2 = b.psE if u < 4 else b.psF
            ps2n = "psE" if u < 4 else "psF"
            em.op("tensor", lambda q, u=u, ps2=ps2: q.matmul(
                ps2[:, (u % 4) * P:(u % 4 + 1) * P], kbegw[:, u * P:(u + 1) * P],
                Ts[:, u * P:(u + 1) * P], start=True, stop=True),
                waits=[("ve", vk_v)] + (em.psw(ps2n) if u % 4 == 0 else []),
                inc="te")
        vc_v = em.val("te")
        em.op("vector", lambda q: q.tensor_copy(vcs[:, 0:512], b.psC[:, 0:512]),
              waits=[("te", vc_v)], inc="ve")
        em.op("vector", lambda q: q.tensor_copy(vcs[:, 512:1024],
              b.psD[:, 0:512]), inc="ve")
        em.psr("psC", "ve")
        em.psr("psD", "ve")
        em.op("scalar", lambda q: q.activation(kcds[:, 0:512], b.psE[:, 0:512],
              AF.Identity), waits=[("te", vc_v)], inc="ae")
        em.op("scalar", lambda q: q.activation(kcds[:, 512:1024],
              b.psF[:, 0:512], AF.Identity), inc="ae")
        em.psr("psE", "ae")
        em.psr("psF", "ae")
        kcd_v = em.val("ve")
        kcd_ae = em.val("ae")
        for u in range(8):
            ps = b.psC if u < 4 else b.psD
            psn = "psC" if u < 4 else "psD"
            em.op("tensor", lambda q, u=u, ps=ps: q.matmul(
                ps[:, (u % 4) * P:(u % 4 + 1) * P], kcds[:, u * P:(u + 1) * P],
                stbf[:, u * P:(u + 1) * P], start=True, stop=True),
                waits=[("ve", max(kcd_v, st0_v)), ("ae", kcd_ae)] +
                      (em.psw(psn) if u % 4 == 0 else []), inc="te")
        t1_v = em.val("te")
        em.op("vector", lambda q: q.tensor_sub(vnew[:, 0:512], vcs[:, 0:512],
              b.psC[:, 0:512]), waits=[("te", t1_v)], inc="ve")
        em.op("vector", lambda q: q.tensor_sub(vnew[:, 512:1024],
              vcs[:, 512:1024], b.psD[:, 0:512]), inc="ve")
        em.psr("psC", "ve")
        em.psr("psD", "ve")
        vn_v = em.val("ve")
        for u in range(8):
            ps = b.psE if u < 4 else b.psF
            psn = "psE" if u < 4 else "psF"
            em.op("tensor", lambda q, u=u, ps=ps: q.matmul(
                ps[:, (u % 4) * P:(u % 4 + 1) * P], stbf[:, u * P:(u + 1) * P],
                qeg[:, u * P:(u + 1) * P], start=True, stop=False),
                waits=[("ve", vn_v)] + (em.psw(psn) if u % 4 == 0 else []),
                inc="te")
            em.op("tensor", lambda q, u=u, ps=ps: q.matmul(
                ps[:, (u % 4) * P:(u % 4 + 1) * P], vnew[:, u * P:(u + 1) * P],
                aiT[:, u * P:(u + 1) * P], start=False, stop=True), inc="te")
        o_v = em.val("te")
        em.op("scalar", lambda q: q.activation(osb[:, 0:512], b.psE[:, 0:512],
              AF.Identity), waits=[("te", o_v), ("ve", em.val("ve"))],
              inc="ae")
        em.op("scalar", lambda q: q.activation(osb[:, 512:1024], b.psF[:, 0:512],
              AF.Identity), inc="ae")
        em.psr("psE", "ae")
        em.psr("psF", "ae")
        oe_v = em.val("ae")
        for u in range(8):
            em.op("vector", lambda q, u=u: q.tensor_scalar(
                vtil[:, u * P:(u + 1) * P], vnew[:, u * P:(u + 1) * P],
                ccol(1, u), None, ALU.mult), waits=[("ve", vn_v)], inc="ve")
        vt_v = em.val("ve")
        for u in range(8):
            ps = b.psC if u < 4 else b.psD
            psn = "psC" if u < 4 else "psD"
            em.op("tensor", lambda q, u=u, ps=ps: q.matmul(
                ps[:, (u % 4) * P:(u % 4 + 1) * P], ktm[:, u * P:(u + 1) * P],
                vtil[:, u * P:(u + 1) * P], start=True, stop=True),
                waits=[("ve", vt_v)] + (em.psw(psn) if u % 4 == 0 else []),
                inc="te")
        t4_v = em.val("te")
        for u in range(8):
            em.op("vector", lambda q, u=u: q.tensor_scalar(
                state[:, u * P:(u + 1) * P], state[:, u * P:(u + 1) * P],
                ccol(3, u), None, ALU.mult),
                waits=[("ve", max(st0_v, ld_ve))], inc="ve")
        em.op("vector", lambda q: q.tensor_add(state[:, 0:512], state[:, 0:512],
              b.psC[:, 0:512]), waits=[("te", t4_v)], inc="ve")
        em.op("vector", lambda q: q.tensor_add(state[:, 512:1024],
              state[:, 512:1024], b.psD[:, 0:512]), inc="ve")
        em.psr("psC", "ve")
        em.psr("psD", "ve")
        em.op("vector", lambda q: q.tensor_copy(stbf, state),
              waits=[("te", t4_v)], inc="ve")
        em.op("vector", lambda q: q.tensor_mul(ozq, osb, osb),
              waits=[("ae", oe_v)], inc="ve")
        oq_v = em.val("ve")
        for h in range(2):
            ps = b.psA if h == 0 else b.psB
            psn = "psA" if h == 0 else "psB"
            em.op("tensor", lambda q, h=h, ps=ps: q.matmul(
                ps[0:1, 0:512], b.ones_col[:, 0:1],
                ozq[:, h * 512:(h + 1) * 512], start=True, stop=True),
                waits=[("ve", oq_v)] + em.psw(psn), inc="te")
        os_v = em.val("te")
        rrow = b.Dd[0:1, 1024:2048]
        em.op("vector", lambda q: q.tensor_scalar(rrow[0:1, 0:512],
              b.psA[0:1, 0:512], 1.0 / P, 1e-6, ALU.mult, ALU.add),
              waits=[("te", os_v), ("ae", em.val("ae"))], inc="ve")
        em.op("vector", lambda q: q.tensor_scalar(rrow[0:1, 512:1024],
              b.psB[0:1, 0:512], 1.0 / P, 1e-6, ALU.mult, ALU.add), inc="ve")
        em.psr("psA", "ve")
        em.psr("psB", "ve")
        rr_v = em.val("ve")
        em.op("scalar", lambda q: q.activation(rrow, rrow, AF.Ln),
              waits=[("ve", rr_v)], inc="ae")
        rl_v = em.val("ae")
        em.op("scalar", lambda q: q.activation(rrow, rrow, AF.Exp, scale=-0.5),
              waits=[("ae", rl_v)], inc="ae")
        re_v = em.val("ae")
        em.op("vector", lambda q: q.tensor_copy(b.rowbf[0:1, 0:1024], rrow),
              waits=[("ae", re_v), ("te", em.val("te"))], inc="ve")
        rb_v = em.val("ve")
        for h in range(2):
            ps = b.psA if h == 0 else b.psB
            psn = "psA" if h == 0 else "psB"
            em.op("tensor", lambda q, h=h, ps=ps: q.matmul(
                ps[:, 0:512], b.ones_row[0:1, :],
                b.rowbf[0:1, h * 512:(h + 1) * 512], start=True, stop=True),
                waits=[("ve", rb_v)] + em.psw(psn), inc="te")
        rbc_v = em.val("te")
        em.op("vector", lambda q: q.scalar_tensor_tensor(
            osb[:, 0:512], osb[:, 0:512], sm[:, 144:145], b.psA[:, 0:512],
            ALU.mult, ALU.mult), waits=[("te", rbc_v)], inc="ve")
        em.op("vector", lambda q: q.scalar_tensor_tensor(
            osb[:, 512:1024], osb[:, 512:1024], sm[:, 144:145],
            b.psB[:, 0:512], ALU.mult, ALU.mult), inc="ve")
        em.psr("psA", "ve")
        em.psr("psB", "ve")
        em.op("vector", lambda q, zstep=zstep: q.tensor_mul(ofin, osb, zstep),
              waits=[("sd", ld_sd), ("gd", em.val("gd"))], inc="ve")
        of_v = em.val("ve")
        for u in range(8):
            bi2, vh = u // 4, u % 4
            j = bi2 * 4 + it // 8
            col = (it % 8) * P
            em.dma("gpsimd", b.oa_in[j * 512 + vh * P:j * 512 + (vh + 1) * P,
                                     col:col + P],
                   ofin[:, u * P:(u + 1) * P], "gd", waits=[("ve", of_v)])
        cons_after[it] = [("te", em.val("te")), ("ve", em.val("ve")),
                          ("ae", em.val("ae")), ("sd", em.val("sd")),
                          ("sc", em.val("sc")), ("gd", em.val("gd"))]
    _scan_loads(0)
    for it in range(32):
        _scan_step(it, it % 2)
        if it + 1 < 32:
            _scan_loads(it + 1)

    em.phase = "a4"
    cc_o = emit_collective(em, b, "AllToAll", b.oa_in, b.oa_out,
                           [("sd", em.val("sd")), ("gd", em.val("gd"))])
    war = [("te", em.val("te")), ("ve", em.val("ve")), ("ae", em.val("ae")),
           ("gd", em.val("gd"))]
    wo = lambda m: b.Cc[:, 8192 + (m % 2) * 2048:8192 + (m % 2) * 2048 + 2048]
    cons = {}
    free_v = em.val("te")
    for half in range(2):
        sd_o = em.dma("sync", b.Bb[:, :].rearrange("p (k t) -> p k t", t=TOK),
                      b.oa_out[half * 2048:(half + 1) * 2048, :].rearrange(
                          "(k p) t -> p k t", p=P), "sd",
                      waits=[("cc", cc_o), ("te", em.val("te")),
                             ("ve", em.val("ve")), ("ae", em.val("ae"))])
        for m in range(DT):
            gd_v = em.dma("gpsimd", wo(m),
                          i["woutA_img"][:, (m * 32 + half * 16) * P:
                                         (m * 32 + half * 16 + 16) * P], "gd",
                          waits=[("te", cons.get((half, m - 2),
                                                 cons.get((half - 1, DT - 1),
                                                          free_v)))])
            for hh in range(2):
                ps, psn = (b.psA, "psA") if hh == 0 else (b.psB, "psB")
                for k in range(DT):
                    em.op("tensor", lambda q, m=m, hh=hh, k=k, ps=ps: q.matmul(
                        ps[:, 0:512], wo(m)[:, k * P:(k + 1) * P],
                        b.Bb[:, k * TOK + hh * 512:k * TOK + (hh + 1) * 512],
                        start=(k == 0), stop=(k == DT - 1)),
                        waits=[("gd", gd_v), ("sd", sd_o)] +
                              (em.psw(psn) if k == 0 else []), inc="te")
                te_v = em.val("te")
                cons[(half, m)] = te_v
                em.op("vector", lambda q, m=m, hh=hh, ps=ps: q.tensor_tensor(
                    b.xres[:, m * TOK + hh * 512:m * TOK + (hh + 1) * 512],
                    b.xres[:, m * TOK + hh * 512:m * TOK + (hh + 1) * 512],
                    ps[:, 0:512], ALU.add), waits=[("te", te_v)], inc="ve")
                em.psr(psn, "ve")


def emit_attn_b(em, b):
    i = b.inp
    sm = b.smallf
    em.phase = "b0"
    ve_ln, sd_ln = emit_ln(em, b, b.A, gin=b.gin, eps=1e-5)
    cc_g = emit_collective(em, b, "AllGather", b.gin, b.gxn,
                           [("sd", sd_ln)])
    war = [("te", em.val("te")), ("ve", em.val("ve")), ("ae", em.val("ae"))]
    gd_ws = em.dma("gpsimd", b.Bb[:, 0:8192], i["wsB_img"][:, :], "gd",
                   waits=war)
    gd_wv = em.dma("gpsimd", b.Cc[:, 0:4096], i["wvB_img"][:, :], "gd",
                   waits=war)
    gd_rc = em.dma("gpsimd", b.A[:, 0:4096], i["c_rope_c"][:, :], "gd",
                   waits=[("sd", sd_ln)] + war)
    gd_rs = em.dma("gpsimd", b.A[:, 4096:8192], i["c_rope_s"][:, :], "gd")
    gd_cm = em.dma("gpsimd", b.A[:, 12288:14336], i["c_cmask"][:, :], "gd")
    em.phase = "b1"
    st1 = b.Bb[:, 8192:8704]
    st2 = b.Bb[:, 8704:9216]
    st4 = b.Bb[:, 9216:9728]
    psq = [b.psA, b.psB, b.psC, b.psD]
    psqn = ["psA", "psB", "psC", "psD"]
    pcons = {}
    for n in range(16):
        r, off = n // 2, (n % 2) * 512
        bi = n // 8
        pos0 = (n % 8) * 512
        xc = b.Ee[:, 0:8192] if n % 2 == 0 else b.Cc[:, 4096:12288]
        xw = pcons.get(n - 2, [("cc", cc_g), ("te", em.val("te")),
                               ("ae", em.val("ae")), ("ve", em.val("ve")),
                               ("sd", em.val("sd"))])
        sd_x = em.dma(
            "sync", xc.rearrange("p (k t) -> p k t", t=512),
            b.gxn[2048 * r:2048 * (r + 1), off:off + 512].rearrange(
                "(k p) t -> p k t", p=P), "sd", waits=xw)
        for m in range(4):
            ps, psn = psq[m], psqn[m]
            for k in range(DT):
                em.op("tensor", lambda q, m=m, k=k, ps=ps, xc=xc: q.matmul(
                    ps[:, 0:512], b.Bb[:, (m * DT + k) * P:(m * DT + k + 1) * P],
                    xc[:, k * 512:(k + 1) * 512],
                    start=(k == 0), stop=(k == DT - 1)),
                    waits=[("gd", gd_ws), ("sd", sd_x)] +
                          (em.psw(psn) if k == 0 else []), inc="te")
            mm_v = em.val("te")
            em.op("scalar", lambda q, m=m, ps=ps: q.activation(
                st1, ps[:, 0:512], AF.Identity, bias=sm[:, 145 + m:146 + m]),
                waits=[("te", mm_v), ("ve", em.val("ve"))], inc="ae")
            ev_v = em.val("ae")
            em.psr(psn, "ae")
            em.op("tensor", lambda q: q.matmul(
                b.psE[:, 0:512], b.swap_pm[:, :], st1, start=True, stop=True),
                waits=[("ae", ev_v)] + em.psw("psE"), inc="te")
            sw_v = em.val("te")
            em.op("vector", lambda q, pos0=pos0: q.tensor_mul(
                st2, st1, b.A[:, pos0:pos0 + 512]),
                waits=[("ae", ev_v), ("gd", gd_rc)], inc="ve")
            em.op("vector", lambda q, pos0=pos0: q.tensor_mul(
                st4, b.psE[:, 0:512], b.A[:, 4096 + pos0:4096 + pos0 + 512]),
                waits=[("te", sw_v), ("sd", em.val("sd")), ("gd", gd_rs)],
                inc="ve")
            em.psr("psE", "ve")
            em.op("vector", lambda q: q.tensor_add(st4, st2, st4), inc="ve")
            rp_v = em.val("ve")
            h = m % 2
            qk = m // 2
            u = bi * 2 + h
            em.dma("sync", b.qb_scr[u * 384 + qk * P:u * 384 + qk * P + P,
                                    pos0:pos0 + 512], st4, "sd",
                   waits=[("ve", rp_v)])
        for tt in range(4):
            ps, psn = (b.psF, "psF") if tt % 2 == 0 else (b.psG, "psG")
            for k in range(DT):
                em.op("tensor", lambda q, tt=tt, k=k, ps=ps, xc=xc: q.matmul(
                    ps[:, 0:256], xc[:, k * 512 + tt * P:k * 512 + (tt + 1) * P],
                    b.Cc[:, k * 256:(k + 1) * 256],
                    start=(k == 0), stop=False),
                    waits=[("gd", gd_wv), ("sd", sd_x)] +
                          (em.psw(psn) if k == 0 else []), inc="te")
            em.op("tensor", lambda q, ps=ps: q.matmul(
                ps[:, 0:256], b.ones_row[0:1, :], b.bvB_sb[0:1, :],
                start=False, stop=True), inc="te")
            mm_v = em.val("te")
            em.op("scalar", lambda q, ps=ps: q.activation(
                st1[:, 0:256], ps[:, 0:256], AF.Identity),
                waits=[("te", mm_v), ("ve", em.val("ve")),
                       ("sd", em.val("sd"))], inc="ae")
            ev_v = em.val("ae")
            em.psr(psn, "ae")
            tg = (n % 8) * 4 + tt
            for h in range(2):
                u = bi * 2 + h
                em.dma("sync", b.qb_scr[u * 384 + 256:u * 384 + 384,
                                        tg * P:(tg + 1) * P],
                       st1[:, h * P:(h + 1) * P], "sd",
                       waits=[("ae", ev_v)])
        pcons[n] = [("te", em.val("te")), ("ve", em.val("ve")),
                    ("ae", em.val("ae")), ("sd", em.val("sd")),
                    ("gd", em.val("gd"))]
    em.phase = "b2"
    proj_sd = em.val("sd")
    RS = float(1.0 / np.sqrt(128.0))
    for u in range(4):
        bi, h = u // 2, u % 2
        war = [("te", em.val("te")), ("ve", em.val("ve")), ("ae", em.val("ae"))]
        sd_u = em.dma("sync",
                      b.Bb[:, 0:12288].rearrange("p (s t) -> p s t", t=S),
                      b.qb_scr[u * 384:(u + 1) * 384, :].rearrange(
                          "(s p) t -> p s t", p=P), "sd",
                      waits=[("sd", proj_sd)] + war)
        qblk = lambda c0, w: b.Bb[:, c0:c0 + w]
        for qp in range(4):
            kmax = (qp + 1) * 8
            for qb in range(2):
                pvp, pvn = (b.psC, "psC") if qb == 0 else (b.psD, "psD")
                smp, smn = (b.psE, "psE") if qb == 0 else (b.psF, "psF")
                kl = qp * 8 + qb * 4 + 4

                def scores(kt, qp=qp, qb=qb):
                    scp, scn = (b.psA, "psA") if kt % 2 == 0 else (b.psB, "psB")
                    em.op("tensor", lambda q, kt=kt, qp=qp, qb=qb, scp=scp:
                          q.matmul(scp[:, 0:512],
                                   b.Bb[:, 4096 + kt * P:4096 + (kt + 1) * P],
                                   b.Bb[:, qp * 1024 + qb * 512:
                                        qp * 1024 + (qb + 1) * 512],
                                   start=True, stop=True),
                          waits=[("sd", sd_u)] + em.psw(scn), inc="te")
                    sc_v = em.val("te")
                    r0 = kt - (qp * 8 + qb * 4)
                    mw = []
                    if 0 <= r0 < 4:
                        em.op("vector", lambda q, r0=r0, scp=scp:
                              q.tensor_tensor(
                            scp[:, 0:512], scp[:, 0:512],
                            b.A[:, 12288 + r0 * 512:12288 + (r0 + 1) * 512],
                            ALU.add), waits=[("te", sc_v), ("gd", gd_cm)],
                            inc="ve")
                        mw = [("ve", em.val("ve"))]
                    return scp, scn, sc_v, mw

                pend = scores(0)
                for kt in range(kl):
                    scp, scn, sc_v, mw = pend
                    if kt + 1 < kl:
                        pend = scores(kt + 1)
                    slot = b.Ee[:, qb * 1024 + (kt % 2) * 512:
                                qb * 1024 + (kt % 2) * 512 + 512]
                    em.op("scalar", lambda q, scp=scp, slot=slot: q.activation(
                        slot, scp[:, 0:512], AF.Exp, scale=RS),
                        waits=[("te", sc_v)] + mw, inc="ae")
                    ex_v = em.val("ae")
                    em.psr(scn, "ae")
                    em.op("tensor", lambda q, kt=kt, pvp=pvp, slot=slot:
                          q.matmul(pvp[:, 0:512],
                                   b.Bb[:, 8192 + kt * P:8192 + (kt + 1) * P],
                                   slot, start=(kt == 0), stop=(kt == kl - 1)),
                          waits=[("ae", ex_v)] +
                                (em.psw(pvn) if kt == 0 else []), inc="te")
                    em.op("tensor", lambda q, kt=kt, smp=smp, slot=slot:
                          q.matmul(smp[0:1, 0:512], b.ones_col[:, 0:1],
                                   slot, start=(kt == 0), stop=(kt == kl - 1)),
                          waits=(em.psw(smn) if kt == 0 else []), inc="te")
                pv_v = em.val("te")
                rrow = b.Dd[0:1, 3072:3584]
                em.op("vector", lambda q, smp=smp: q.tensor_copy(
                    rrow, smp[0:1, 0:512]),
                    waits=[("te", pv_v), ("ae", em.val("ae"))], inc="ve")
                em.psr(smn, "ve")
                cp_v = em.val("ve")
                em.op("scalar", lambda q: q.activation(
                    rrow[0:1, 0:512], rrow[0:1, 0:512], AF.Ln),
                    waits=[("ve", cp_v)], inc="ae")
                ln_v = em.val("ae")
                em.op("scalar", lambda q: q.activation(
                    rrow[0:1, 0:512], rrow[0:1, 0:512], AF.Exp, scale=-1.0),
                    waits=[("ae", ln_v)], inc="ae")
                rc_v = em.val("ae")
                em.op("vector", lambda q: q.tensor_copy(
                    b.rowbf[0:1, 0:512], rrow),
                    waits=[("ae", rc_v), ("te", em.val("te"))], inc="ve")
                rb_v = em.val("ve")
                scp2, scn2 = (b.psA, "psA")
                em.op("tensor", lambda q: q.matmul(
                    b.psA[:, 0:512], b.ones_row[0:1, :], b.rowbf[0:1, 0:512],
                    start=True, stop=True),
                    waits=[("ve", rb_v)] + em.psw("psA"), inc="te")
                bc_v = em.val("te")
                oslot = b.A[:, 8192 + qp * 1024 + qb * 512:
                            8192 + qp * 1024 + qb * 512 + 512]
                em.op("scalar", lambda q, pvp=pvp, oslot=oslot: q.activation(
                    oslot, pvp[:, 0:512], AF.Identity),
                    waits=[("te", bc_v), ("ve", em.val("ve")),
                           ("sd", em.val("sd"))], inc="ae")
                em.psr(pvn, "ae")
                em.op("vector", lambda q, oslot=oslot: q.tensor_mul(
                    oslot, oslot, b.psA[:, 0:512]),
                    waits=[("ae", em.val("ae")), ("te", bc_v)], inc="ve")
                em.psr("psA", "ve")
            o_v = em.val("ve")
            j = bi * 4 + qp
            em.dma("sync", b.ob_in[j * 256 + h * P:j * 256 + (h + 1) * P, :],
                   b.A[:, 8192 + qp * 1024:8192 + (qp + 1) * 1024], "sd",
                   waits=[("ve", o_v)])
    em.phase = "b3"
    cc_o = emit_collective(em, b, "AllToAll", b.ob_in, b.ob_out,
                           [("sd", em.val("sd"))])
    war = [("te", em.val("te")), ("ve", em.val("ve")), ("ae", em.val("ae"))]
    sd_o = em.dma("sync", b.Bb[:, :].rearrange("p (k t) -> p k t", t=TOK),
                  b.ob_out[:, :].rearrange("(k p) t -> p k t", p=P), "sd",
                  waits=[("cc", cc_o)] + war)
    wob = lambda m: b.Cc[:, 12288 + (m % 2) * 2048:12288 + (m % 2) * 2048 + 2048]
    cons = {}
    free_v = em.val("te")
    for m in range(DT):
        gd_v = em.dma("gpsimd", wob(m),
                      i["woutB_img"][:, m * 2048:(m + 1) * 2048], "gd",
                      waits=[("te", cons.get(m - 2, free_v))])
        for hh in range(2):
            ps, psn = (b.psA, "psA") if hh == 0 else (b.psB, "psB")
            for k in range(DT):
                em.op("tensor", lambda q, m=m, hh=hh, k=k, ps=ps: q.matmul(
                    ps[:, 0:512], wob(m)[:, k * P:(k + 1) * P],
                    b.Bb[:, k * TOK + hh * 512:k * TOK + (hh + 1) * 512],
                    start=(k == 0), stop=(k == DT - 1)),
                    waits=[("gd", gd_v), ("sd", sd_o)] +
                          (em.psw(psn) if k == 0 else []), inc="te")
            te_v = em.val("te")
            cons[m] = te_v
            em.op("vector", lambda q, m=m, hh=hh, ps=ps: q.scalar_tensor_tensor(
                b.xres[:, m * TOK + hh * 512:m * TOK + (hh + 1) * 512],
                ps[:, 0:512], sm[:, 48 + m:49 + m],
                b.xres[:, m * TOK + hh * 512:m * TOK + (hh + 1) * 512],
                ALU.add, ALU.add), waits=[("te", te_v)], inc="ve")
            em.psr(psn, "ve")



def build_program(in_arrs, parts=("a", "f1", "b", "f2"), ret_b=False,
                  debug_attn_a=False):
    nc = bass.Bass("TRN2", target_bir_lowering=False)
    b = make_b(nc, in_arrs)
    em = Em(nc)
    if debug_attn_a:
        b.d_oa = nc.dram_tensor("d_oa", [NCORES * 512, TOK], BF16,
                                kind="ExternalOutput").ap()
        b.d_qk = nc.dram_tensor("d_qk", [512, TALL], BF16,
                                kind="ExternalOutput").ap()
        b.d_v = nc.dram_tensor("d_v", [512, TALL], BF16,
                               kind="ExternalOutput").ap()
        b.d_z = nc.dram_tensor("d_z", [512, TALL], BF16,
                               kind="ExternalOutput").ap()
        b.d_rows = nc.dram_tensor("d_rows", [32, 4096], BF16,
                                  kind="ExternalOutput").ap()
        b.d_cols = nc.dram_tensor("d_cols", [32, 5120], BF16,
                                  kind="ExternalOutput").ap()
        b.d_misc = nc.dram_tensor("d_misc", [P, 4096], F32,
                                  kind="ExternalOutput").ap()
        b.d_misc2 = nc.dram_tensor("d_misc2", [P, 4096], F32,
                                   kind="ExternalOutput").ap()
    load_smalls(em, b)
    em.dma("gpsimd", b.gatew_bf[:, :], b.inp["gatew_img"][:, :], "gd")
    em.dma("gpsimd", b.d1_bf[:, :], b.inp["d1_img"][:, :], "gd")
    em.dma("gpsimd", b.d2_bf[:, :], b.inp["d2_img"][:, :], "gd")
    em.dma("gpsimd", b.agg_bf[:, :], b.inp["agg_img"][:, :], "gd")
    em.dma("gpsimd", b.bc_bf[:, :], b.inp["bc_img"][:, :], "gd")
    em.dma("gpsimd", b.d1b_bf[:, :], b.inp["d1b_row"][:, :], "gd")
    b.precast_v = em.val("gd")
    em.dma("sync", b.xres[:, :].rearrange("p (d t) -> p d t", t=TOK),
           b.inp["x_fm"][:, :].rearrange("(d p) t -> p d t", p=P), "sd")
    start_gate = [("sd", em.val("sd")), ("gd", em.val("gd"))]
    for e in ENGINES:
        em.op(e, lambda q: q.nop(), waits=start_gate)
    for part in parts:
        em.phase = "f" if part.startswith("f") else part
        if part == "a":
            emit_attn_a(em, b)
        elif part == "b":
            emit_attn_b(em, b)
        else:
            emit_ffn(em, b, part)
        em.phase = None
    ve_f = em.val("ve")
    em.dma("sync", b.out_fm[:, :].rearrange("(d p) t -> p d t", p=P),
           b.xres[:, :].rearrange("p (d t) -> p d t", t=TOK), "sd",
           waits=[("ve", ve_f)])
    if debug_attn_a:
        allw = [(n, v) for n, v in em.cnt.items() if v > 0]
        for (dst, src) in ((b.d_oa, b.oa_in), (b.d_qk, b.qk_scr),
                           (b.d_v, b.v_scr), (b.d_z, b.z_scr),
                           (b.d_rows, b.rows_scr), (b.d_cols, b.cols_scr)):
            em.dma("sync", dst[:, :], src[:, :], "sd", waits=allw)
    em.op("sync", lambda q: q.nop(), waits=[("sd", em.val("sd"))])
    em.replay()
    if ret_b:
        return nc, em, b
    return nc, em


def kernel(**inputs):
    maps = prep_inputs(inputs)
    nc, em = build_program(maps[0])
    res = run_bass_kernel_spmd(nc, maps, core_ids=list(range(NCORES)))
    em.close()
    shards = [np.asarray(r["out_fm"], dtype=np.float32).T for r in res.results]
    out = np.concatenate(shards, axis=0).reshape(2, S, D)
    return out


# version 51
# speedup vs baseline: 1.2232x; 1.0433x over previous
import numpy as np
import ml_dtypes
import concourse.bass as bass
import concourse.mybir as mybir
from concourse.bass_utils import run_bass_kernel_spmd
from concourse.aot_env import allow_ici_all_to_all

F32 = mybir.dt.float32
BF16 = mybir.dt.bfloat16
AF = mybir.ActivationFunctionType
ALU = mybir.AluOpType

P = 128
D = 2048
DT = 16
TOK = 1024
TALL = 8192
NCORES = 8
S = 4096
BFNP = ml_dtypes.bfloat16
RSQ128 = float(np.log(1.0 / np.sqrt(128.0)))

ENGINES = ("sync", "tensor", "vector", "scalar", "gpsimd")


PARANOID = False
PARANOID_PHASES = {"fth"}


class Em:
    def __init__(self, nc):
        self.nc = nc
        self.ops = {e: [] for e in ENGINES}
        self.sems = {}
        self.cnt = {}
        self._ctxs = []
        self._hw = {}
        self.pslast = {}

    def sem(self, name):
        if name not in self.sems:
            cm = self.nc.semaphore(name)
            self.sems[name] = cm.__enter__()
            self._ctxs.append(cm)
            self.cnt[name] = 0
        return self.sems[name]

    def val(self, name):
        return self.cnt.get(name, 0)

    def op(self, eng, fn, waits=(), inc=None, inc_amt=1):
        if PARANOID or getattr(self, "phase", None) in PARANOID_PHASES:
            waits = list(waits) + [(n, v) for n, v in self.cnt.items() if v > 0]
        ws = [(s, v) for (s, v) in waits if v > 0]
        for s, _ in ws:
            self.sem(s)
        self.ops[eng].append((fn, ws, inc, inc_amt))
        if inc is not None:
            self.sem(inc)
            self.cnt[inc] += inc_amt
            return self.cnt[inc]
        return None

    def dma(self, eng, out, in_, inc, waits=(), **kw):
        def fn(q):
            return q.dma_start(out=out, in_=in_, **kw)
        return self.op(eng, fn, waits, inc, 16)

    def psw(self, *names):
        out = []
        for n in names:
            if n in self.pslast:
                out.append(self.pslast[n])
        return out

    def psr(self, name, sem):
        self.pslast[name] = (sem, self.val(sem))

    def replay(self):
        nc = self.nc
        with nc.Block() as block:
            for eng in ENGINES:
                if not self.ops[eng]:
                    continue

                def mk(eng):
                    def run(q):
                        for (fn, ws, inc, amt) in self.ops[eng]:
                            for (s, v) in ws:
                                key = (eng, s)
                                if self._hw.get(key, 0) < v:
                                    q.wait_ge(self.sems[s], v)
                                    self._hw[key] = v
                            inst = fn(q)
                            if inc is not None:
                                inst.then_inc(self.sems[inc], amt)
                    return run
                getattr(block, eng)(mk(eng))

    def close(self):
        for cm in reversed(self._ctxs):
            cm.__exit__(None, None, None)


def f32c(x):
    return np.ascontiguousarray(np.asarray(x), dtype=np.float32)



def host_constants():
    c = {}
    half = 64
    thetas = 10000.0 ** (-np.arange(half, dtype=np.float64) / half)
    ang = np.arange(S, dtype=np.float64)[None, :] * thetas[:, None]
    C = np.empty((P, S), np.float64)
    S2 = np.empty((P, S), np.float64)
    C[0::2, :] = np.cos(ang)
    C[1::2, :] = np.cos(ang)
    S2[0::2, :] = -np.sin(ang)
    S2[1::2, :] = np.sin(ang)
    c["rope_c"] = f32c(C)
    c["rope_s"] = f32c(S2)
    sw = np.zeros((P, P), np.float32)
    for i in range(0, P, 2):
        sw[i, i + 1] = 1.0
        sw[i + 1, i] = 1.0
    c["swap_pm"] = sw
    c["ident"] = np.eye(P, dtype=np.float32)
    c["ones_col"] = np.ones((P, 1), np.float32)
    c["ones_row"] = np.ones((1, P), np.float32)
    ii = np.arange(P)
    mD = np.where(ii[None, :] >= ii[:, None], 0.0, -1e30).astype(np.float32)
    c["maskD"] = np.tile(mD, (1, 8))
    mD2 = np.where(ii[:, None] > ii[None, :], 0.0, -1e30).astype(np.float32)
    c["maskD2"] = np.tile(mD2, (1, 8))
    m01 = np.where(ii[None, :] > ii[:, None], 1.0, 0.0).astype(np.float32)
    c["mask01u"] = np.tile(m01, (1, 8))
    c["eyeR"] = np.tile(np.eye(P, dtype=np.float32), (1, 8))
    cm = np.empty((P, 4 * 512), np.float32)
    for r in range(4):
        cm[:, r * 512:(r + 1) * 512] = np.where(
            np.arange(512)[None, :] >= (r * P + ii[:, None]), 0.0, -1e30)
    c["cmask"] = cm
    bn = np.zeros((4, 512), np.float32)
    for r in range(4):
        bn[r, r * P:(r + 1) * P] = -1.0
    c["blockneg"] = bn
    return c


def lhsT_image(w):
    K, M = w.shape
    KT, MT = K // P, M // P
    t = w.reshape(KT, P, MT, P).transpose(1, 2, 0, 3)
    return f32c(t.reshape(P, MT * KT * P))


def rhs_image(w):
    K, N = w.shape
    KT = K // P
    t = w.reshape(KT, P, N).transpose(1, 0, 2)
    return f32c(t.reshape(P, KT * N))


def col_image(v):
    n = v.shape[0]
    return f32c(v.reshape(n // P, P).T)


def prep_inputs(I):
    cst = host_constants()
    maps = []
    x = f32c(I["L0_attn_a_x"]).reshape(TALL, D)
    for c in range(NCORES):
        m = {}
        m["x_fm"] = f32c(x[c * TOK:(c + 1) * TOK, :].T)
        m["sel"] = f32c(np.tile(np.eye(2, dtype=np.float32)[c // 4], (P, 1)))
        for k, v in cst.items():
            m["c_" + k] = v
        kh = [2 * c, 2 * c + 1]
        vh = [4 * c + i for i in range(4)]
        wqkv = f32c(I["L0_attn_a_w_qkv"])
        qc = np.concatenate([wqkv[:, 128 * h:128 * (h + 1)] for h in kh], 1)
        kc = np.concatenate([wqkv[:, 2048 + 128 * h:2048 + 128 * (h + 1)]
                             for h in kh], 1)
        vc = np.concatenate([wqkv[:, 4096 + 128 * h:4096 + 128 * (h + 1)]
                             for h in vh], 1)
        wz = f32c(I["L0_attn_a_w_z"])
        zc = np.concatenate([wz[:, 128 * h:128 * (h + 1)] for h in vh], 1)
        m["wsA_img"] = lhsT_image(np.concatenate([qc, kc, vc, zc], 1))
        cw = f32c(I["L0_attn_a_conv_w"])
        rows = ([cw[128 * h:128 * (h + 1), :] for h in kh]
                + [cw[2048 + 128 * h:2048 + 128 * (h + 1), :] for h in kh]
                + [cw[4096 + 128 * h:4096 + 128 * (h + 1), :] for h in vh])
        m["cw_img"] = f32c(np.concatenate(rows, 1))
        wb = f32c(I["L0_attn_a_w_b"])[:, vh]
        wa = f32c(I["L0_attn_a_w_a"])[:, vh]
        m["wb_img"] = f32c(wb.reshape(DT, P, 4).transpose(1, 0, 2).reshape(P, DT * 4))
        m["wa_img"] = f32c(wa.reshape(DT, P, 4).transpose(1, 0, 2).reshape(P, DT * 4))
        m["dtb_col"] = f32c(I["L0_attn_a_dt_bias"])[vh].reshape(4, 1)
        m["alog_col"] = f32c(I["L0_attn_a_a_log"])[vh].reshape(4, 1)
        m["normw_col"] = f32c(I["L0_attn_a_norm_w"]).reshape(P, 1)
        m["woutA_img"] = lhsT_image(f32c(I["L0_attn_a_w_out"]))
        bh = [2 * c, 2 * c + 1]
        wq = f32c(I["L0_attn_b_w_qkv"])
        qb = np.concatenate([wq[128 * h:128 * (h + 1), :].T for h in bh], 1)
        kb = np.concatenate([wq[2048 + 128 * h:2048 + 128 * (h + 1), :].T
                             for h in bh], 1)
        vb = np.concatenate([wq[4096 + 128 * h:4096 + 128 * (h + 1), :].T
                             for h in bh], 1)
        m["wsB_img"] = lhsT_image(np.concatenate([qb, kb], 1))
        m["wvB_img"] = rhs_image(f32c(vb))
        bq = f32c(I["L0_attn_b_b_qkv"])
        bqc = np.concatenate([bq[128 * h:128 * (h + 1)] for h in bh]
                             + [bq[2048 + 128 * h:2048 + 128 * (h + 1)]
                                for h in bh])
        m["bqk_col"] = f32c(bqc.reshape(4, P).T)
        m["bvB_row"] = f32c(np.concatenate(
            [bq[4096 + 128 * h:4096 + 128 * (h + 1)] for h in bh]).reshape(1, 256))
        m["woutB_img"] = lhsT_image(f32c(I["L0_attn_b_w_out"]).T)
        m["boutB_col"] = col_image(f32c(I["L0_attn_b_b_out"]))
        m["gatew_img"] = lhsT_image(f32c(I["L0_ffn_gate_w"]))
        m["gateb_col"] = col_image(f32c(I["L0_ffn_gate_b"]))
        hsl = slice(c * 1024, (c + 1) * 1024)
        m["d1_img"] = rhs_image(f32c(I["L0_ffn_d1_w"])[:, hsl])
        m["d1b_row"] = f32c(I["L0_ffn_d1_b"])[hsl].reshape(1, 1024)
        m["d2_img"] = lhsT_image(f32c(I["L0_ffn_d2_w"])[hsl, :])
        m["d2b_col"] = col_image(f32c(I["L0_ffn_d2_b"]))
        asl = slice(c * 256, (c + 1) * 256)
        m["agg_img"] = rhs_image(f32c(I["L0_ffn_agg_w"])[:, asl])
        m["aggb_row"] = f32c(np.tile(I["L0_ffn_agg_b"][asl].reshape(1, 256), (2, 1)))
        m["bc_img"] = rhs_image(f32c(I["L0_ffn_bc_w"])[:, asl])
        m["bcb_row"] = f32c(np.tile(I["L0_ffn_bc_b"][asl].reshape(1, 256), (2, 1)))
        for nm, key in (("ing", "L0_ffn_in_gamma"), ("inb", "L0_ffn_in_beta"),
                        ("png", "L0_ffn_pn_gamma"), ("pnb", "L0_ffn_pn_beta"),
                        ("outg", "L0_ffn_out_gamma"), ("outb", "L0_ffn_out_beta")):
            m[nm + "_col"] = col_image(f32c(I[key]))
        maps.append(m)
    return maps



class Bufs:
    pass


def make_b(nc, in_arrs):
    b = Bufs()
    b.nc = nc
    b.inp = {}
    for name, arr in in_arrs.items():
        b.inp[name] = nc.dram_tensor(name, list(arr.shape), F32,
                                     kind="ExternalInput").ap()
    b.out_fm = nc.dram_tensor("out_fm", [D, TOK], F32, kind="ExternalOutput").ap()
    dr = nc.dram_tensor
    b.gin = dr("gin", [D, TOK], BF16).ap()
    b.gxn = dr("gxn", [NCORES * D, TOK], BF16).ap()
    b.qk_scr = dr("qk_scr", [512, TALL], BF16).ap()
    b.v_scr = dr("v_scr", [512, TALL], BF16).ap()
    b.z_scr = dr("z_scr", [512, TALL], BF16).ap()
    b.ba_scr = dr("ba_scr", [8, TALL], F32).ap()
    b.krstd_scr = dr("krstd_scr", [2, TALL], F32).ap()
    b.rows_scr = dr("rows_scr", [32, 4 * 8 * P], BF16).ap()
    b.cols_scr = dr("cols_scr", [32, 5 * 8 * P], BF16).ap()
    b.oa_in = dr("oa_in", [NCORES * 512, TOK], BF16).ap()
    b.oa_out = dr("oa_out", [NCORES * 512, TOK], BF16).ap()
    b.qb_scr = dr("qb_scr", [12 * P, S], BF16).ap()
    b.ob_in = dr("ob_in", [NCORES * 256, TOK], BF16).ap()
    b.ob_out = dr("ob_out", [NCORES * 256, TOK], BF16).ap()
    b.th_in = dr("th_in", [1, D], BF16).ap()
    b.th_out = dr("th_out", [NCORES, D], BF16).ap()
    b.gatew_bf = dr("gatew_bf", [P, 32768], BF16).ap()
    b.d1_bf = dr("d1_bf", [P, 16384], BF16).ap()
    b.d2_bf = dr("d2_bf", [P, 16384], BF16).ap()
    b.agg_bf = dr("agg_bf", [P, 4096], BF16).ap()
    b.bc_bf = dr("bc_bf", [P, 4096], BF16).ap()
    b.d1b_bf = dr("d1b_bf", [1, 1024], BF16).ap()
    b.ar_in = dr("ar_in", [2, D], BF16).ap()
    b.ar_out = dr("ar_out", [2, D], BF16).ap()
    b.ag2_in = dr("ag2_in", [2, 256], BF16).ap()
    b.ag2_out = dr("ag2_out", [NCORES * 2, 256], BF16).ap()

    al = nc.alloc_sbuf_tensor
    b.xres = al("xres", [P, DT * TOK], F32)
    b.A = al("A", [P, 16384], BF16)
    b.Bb = al("Bb", [P, 16384], BF16)
    b.Cc = al("Cc", [P, 16384], BF16)
    b.Ee = al("Ee", [P, 8192], BF16)
    b.Dd = al("Dd", [P, 3712], F32)
    b.rowbf = al("rowbf", [1, 2048], BF16)
    b.ones_col = al("ones_col", [P, 1], BF16)
    b.ones_row = al("ones_row", [1, P], BF16)
    b.ident = al("ident", [P, P], BF16)
    b.swap_pm = al("swap_pm", [P, P], BF16)
    b.smallf = al("smallf", [P, 160], F32)
    b.cw_sb = al("cw_sb", [P, 32], F32)
    b.dtb_sb = al("dtb_sb", [4, 1], F32)
    b.alog_sb = al("alog_sb", [4, 1], F32)
    b.nealog = al("nealog", [4, 1], F32)
    b.aggb_sb = al("aggb_sb", [2, 256], BF16)
    b.bcb_sb = al("bcb_sb", [2, 256], BF16)
    b.bvB_sb = al("bvB_sb", [1, 256], BF16)
    b.wba_sb = al("wba_sb", [P, 128], BF16)
    b.th_fm = al("th_fm", [P, 32], F32)
    b.th_sc = al("th_sc", [P, 32], F32)
    b.th_bf = al("th_bf", [P, 32], BF16)
    b.sq_bf = al("sq_bf", [P, 32], BF16)
    b.r4lo = al("r4lo", [4, 512], BF16)
    b.h1_bf = al("h1_bf", [P, 16], BF16)
    b.h1_tm = al("h1_tm", [2, 1024], BF16)
    b.th_tm2 = al("th_tm2", [2, D], BF16)
    b.bcv_fm = al("bcv_fm", [P, 32], F32)
    b.bcsel = al("bcsel", [P, 16], BF16)

    ap_ = nc.alloc_psum_tensor
    for nm in "ABCDEFG":
        setattr(b, "ps" + nm, ap_("ps" + nm, [P, 512], F32))
    b.psT = ap_("psT", [P, 1024], BF16)
    return b


def load_smalls(em, b):
    i = b.inp
    em.dma("gpsimd", b.ones_col[:, :], i["c_ones_col"][:, :], "gd")
    em.dma("gpsimd", b.ones_row[:, :], i["c_ones_row"][:, :], "gd")
    em.dma("gpsimd", b.ident[:, :], i["c_ident"][:, :], "gd")
    em.dma("gpsimd", b.swap_pm[:, :], i["c_swap_pm"][:, :], "gd")
    em.dma("gpsimd", b.aggb_sb[:, :], i["aggb_row"][:, :], "gd")
    em.dma("gpsimd", b.bcb_sb[:, :], i["bcb_row"][:, :], "gd")
    em.dma("gpsimd", b.bvB_sb[:, :], i["bvB_row"][:, :], "gd")
    em.dma("gpsimd", b.wba_sb[:, 0:64], i["wb_img"][:, :], "gd")
    em.dma("gpsimd", b.wba_sb[:, 64:128], i["wa_img"][:, :], "gd")
    em.dma("sync", b.cw_sb[:, :], i["cw_img"][:, :], "sd")
    em.dma("sync", b.dtb_sb[:, :], i["dtb_col"][:, :], "sd")
    em.dma("sync", b.alog_sb[:, :], i["alog_col"][:, :], "sd")
    sm = b.smallf
    em.dma("sync", sm[:, 0:16], i["gateb_col"][:, :], "sd")
    em.dma("sync", sm[:, 16:32], i["outg_col"][:, :], "sd")
    em.dma("sync", sm[:, 32:48], i["outb_col"][:, :], "sd")
    em.dma("sync", sm[:, 48:64], i["boutB_col"][:, :], "sd")
    em.dma("sync", sm[:, 64:80], i["d2b_col"][:, :], "sd")
    em.dma("sync", sm[:, 80:96], i["ing_col"][:, :], "sd")
    em.dma("sync", sm[:, 96:112], i["inb_col"][:, :], "sd")
    em.dma("sync", sm[:, 112:128], i["png_col"][:, :], "sd")
    em.dma("sync", sm[:, 128:144], i["pnb_col"][:, :], "sd")
    em.dma("sync", sm[:, 144:145], i["normw_col"][:, :], "sd")
    em.dma("sync", sm[:, 145:149], i["bqk_col"][:, :], "sd")
    em.dma("sync", sm[:, 150:152], i["sel"][:, :], "sd")



def _ln_rows_and_bcast(em, b, eps):
    r_mu = b.Dd[0:1, 0:1024]
    r_var = b.Dd[0:1, 1024:2048]
    r_m2 = b.Dd[0:1, 2048:3072]
    st_v = em.val("te")
    em.op("vector", lambda q: q.tensor_scalar(r_mu[0:1, 0:512], b.psA[0:1, 0:512],
          1.0 / D, None, ALU.mult),
          waits=[("te", st_v), ("ae", em.val("ae"))], inc="ve")
    em.op("vector", lambda q: q.tensor_scalar(r_mu[0:1, 512:1024],
          b.psB[0:1, 0:512], 1.0 / D, None, ALU.mult), inc="ve")
    em.op("vector", lambda q: q.tensor_scalar(r_var[0:1, 0:512], b.psC[0:1, 0:512],
          1.0 / D, eps, ALU.mult, ALU.add), inc="ve")
    em.op("vector", lambda q: q.tensor_scalar(r_var[0:1, 512:1024],
          b.psD[0:1, 0:512], 1.0 / D, eps, ALU.mult, ALU.add), inc="ve")
    em.op("vector", lambda q: q.tensor_mul(r_m2[0:1, :], r_mu[0:1, :],
          r_mu[0:1, :]), inc="ve")
    em.op("vector", lambda q: q.tensor_sub(r_var[0:1, :], r_var[0:1, :],
          r_m2[0:1, :]), inc="ve")
    for nm in "ABCD":
        em.psr("ps" + nm, "ve")
    var_v = em.val("ve")
    em.op("scalar", lambda q: q.activation(r_var[0:1, :], r_var[0:1, :], AF.Ln),
          waits=[("ve", var_v)], inc="ae")
    ln_v = em.val("ae")
    em.op("scalar", lambda q: q.activation(r_var[0:1, :], r_var[0:1, :], AF.Exp,
          scale=-0.5), waits=[("ae", ln_v)], inc="ae")
    rstd_v = em.val("ae")
    em.op("vector", lambda q: q.tensor_copy(b.rowbf[0:1, 0:1024], r_mu[0:1, :]),
          waits=[("ae", rstd_v), ("te", em.val("te"))], inc="ve")
    em.op("vector", lambda q: q.tensor_copy(b.rowbf[0:1, 1024:2048],
          r_var[0:1, :]), inc="ve")
    rbv = em.val("ve")
    for h in range(2):
        em.op("tensor", lambda q, h=h: q.matmul(
            (b.psE if h == 0 else b.psF)[:, 0:512], b.ones_row[0:1, :],
            b.rowbf[0:1, h * 512:(h + 1) * 512], start=True, stop=True),
            waits=[("ve", rbv)] + em.psw("psE" if h == 0 else "psF"), inc="te")
        em.op("tensor", lambda q, h=h: q.matmul(
            (b.psG if h == 0 else b.psA)[:, 0:512], b.ones_row[0:1, :],
            b.rowbf[0:1, 1024 + h * 512:1024 + (h + 1) * 512], start=True,
            stop=True), waits=em.psw("psG" if h == 0 else "psA"), inc="te")
    bc_v = em.val("te")
    mubc = b.Ee[:, 4096:5120]
    rstdbc = b.Ee[:, 5120:6144]
    em.op("vector", lambda q: q.tensor_copy(mubc[:, 0:512], b.psE[:, 0:512]),
          waits=[("te", bc_v)], inc="ve")
    em.op("vector", lambda q: q.tensor_copy(mubc[:, 512:1024], b.psF[:, 0:512]),
          inc="ve")
    em.op("vector", lambda q: q.tensor_copy(rstdbc[:, 0:512], b.psG[:, 0:512]),
          inc="ve")
    em.op("vector", lambda q: q.tensor_copy(rstdbc[:, 512:1024],
          b.psA[:, 0:512]), inc="ve")
    for nm in "AEFG":
        em.psr("ps" + nm, "ve")
    return em.val("ve")


def emit_ln(em, b, xn_dst, gin=None, eps=1e-5):
    xbf = lambda i: b.Ee[:, (i % 2) * 1024:(i % 2) * 1024 + 1024]
    xsq = lambda i: b.Ee[:, 2048 + (i % 2) * 1024:2048 + (i % 2) * 1024 + 1024]
    mm_wait = em.psw("psA", "psB", "psC", "psD")
    ee_war = [("te", em.val("te")), ("gd", em.val("gd")),
              ("sd", em.val("sd")), ("ae", em.val("ae"))]
    for d in range(DT):
        sl = slice(d * TOK, (d + 1) * TOK)
        wv = [("te", em.val("te"))] if d >= 2 else list(ee_war)
        em.op("vector", lambda q, o=xbf(d), isl=sl: q.tensor_copy(o, b.xres[:, isl]),
              waits=wv, inc="ve")
        cast_v = em.val("ve")
        em.op("scalar", lambda q, o=xsq(d), i=xbf(d): q.activation(o, i, AF.Square),
              waits=[("ve", cast_v)] + ([("te", em.val("te"))] if d >= 2 else []),
              inc="ae")
        sq_v = em.val("ae")
        for h in range(2):
            hs = slice(h * 512, (h + 1) * 512)
            em.op("tensor", lambda q, d=d, h=h, i=xbf(d), hs=hs: q.matmul(
                (b.psA if h == 0 else b.psB)[0:1, 0:512], b.ones_col[:, 0:1],
                i[:, hs], start=(d == 0), stop=(d == DT - 1)),
                waits=[("ve", cast_v)] + (mm_wait if d == 0 else []), inc="te")
            em.op("tensor", lambda q, d=d, h=h, i=xsq(d), hs=hs: q.matmul(
                (b.psC if h == 0 else b.psD)[0:1, 0:512], b.ones_col[:, 0:1],
                i[:, hs], start=(d == 0), stop=(d == DT - 1)),
                waits=[("ae", sq_v)], inc="te")
    bcs = _ln_rows_and_bcast(em, b, eps)
    mubc = b.Ee[:, 4096:5120]
    rstdbc = b.Ee[:, 5120:6144]
    sd_last = em.val("sd")
    dst_war = [("te", em.val("te")), ("sd", em.val("sd")),
               ("gd", em.val("gd"))]
    for d in range(DT):
        sl = slice(d * TOK, (d + 1) * TOK)
        em.op("vector", lambda q, sl=sl: q.tensor_sub(xn_dst[:, sl],
              b.xres[:, sl], mubc[:, :]),
              waits=[("ve", bcs)] + (dst_war if d == 0 else []), inc="ve")
        em.op("vector", lambda q, sl=sl: q.tensor_mul(xn_dst[:, sl],
              xn_dst[:, sl], rstdbc[:, :]), inc="ve")
        if gin is not None:
            sd_last = em.dma("sync", gin[d * P:(d + 1) * P, :], xn_dst[:, sl],
                             "sd", waits=[("ve", em.val("ve"))])
    return em.val("ve"), sd_last


def emit_out_ln_add(em, b, src_bf, gci, bci, eps=1e-6, src_ready=()):
    sm = b.smallf
    sq = lambda i: b.Ee[:, 2048 + (i % 2) * 1024:2048 + (i % 2) * 1024 + 1024]
    mm_wait = em.psw("psA", "psB", "psC", "psD")
    for d in range(DT):
        sl = slice(d * TOK, (d + 1) * TOK)
        wv = [("te", em.val("te"))] if d >= 2 else []
        em.op("scalar", lambda q, o=sq(d), sl=sl: q.activation(o, src_bf[:, sl],
              AF.Square), waits=wv + list(src_ready), inc="ae")
        sq_v = em.val("ae")
        for h in range(2):
            hs = slice(h * 512, (h + 1) * 512)
            em.op("tensor", lambda q, d=d, h=h, sl=sl, hs=hs: q.matmul(
                (b.psA if h == 0 else b.psB)[0:1, 0:512], b.ones_col[:, 0:1],
                src_bf[:, sl][:, hs], start=(d == 0), stop=(d == DT - 1)),
                waits=(mm_wait if d == 0 else []) + (list(src_ready) if d == 0 else []),
                inc="te")
            em.op("tensor", lambda q, d=d, h=h, o=sq(d), hs=hs: q.matmul(
                (b.psC if h == 0 else b.psD)[0:1, 0:512], b.ones_col[:, 0:1],
                o[:, hs], start=(d == 0), stop=(d == DT - 1)),
                waits=[("ae", sq_v)], inc="te")
    bcs = _ln_rows_and_bcast(em, b, eps)
    mubc = b.Ee[:, 4096:5120]
    rstdbc = b.Ee[:, 5120:6144]
    tmp = lambda i: b.Ee[:, (i % 2) * 1024:(i % 2) * 1024 + 1024]
    for d in range(DT):
        sl = slice(d * TOK, (d + 1) * TOK)
        em.op("vector", lambda q, d=d, sl=sl: q.tensor_sub(tmp(d), src_bf[:, sl],
              mubc[:, :]), waits=[("ve", bcs)], inc="ve")
        em.op("vector", lambda q, d=d: q.tensor_mul(tmp(d), tmp(d), rstdbc[:, :]),
              inc="ve")
        em.op("vector", lambda q, d=d, sl=sl: q.scalar_tensor_tensor(
            b.xres[:, sl], tmp(d), sm[:, gci + d:gci + d + 1], b.xres[:, sl],
            ALU.mult, ALU.add), inc="ve")
        em.op("vector", lambda q, d=d, sl=sl: q.tensor_scalar(
            b.xres[:, sl], b.xres[:, sl], sm[:, bci + d:bci + d + 1], None,
            ALU.add), inc="ve")
    return em.val("ve")


def emit_collective(em, b, kind, cc_in, cc_out, wait_sems, op=None):
    op = op or (ALU.add if kind == "AllReduce" else ALU.bypass)

    def fn(q):
        if kind == "AllToAll":
            with allow_ici_all_to_all():
                return q.collective_compute(
                    kind, op, replica_groups=[list(range(NCORES))],
                    ins=[cc_in.tensor.ap().opt()],
                    outs=[cc_out.tensor.ap().opt()])
        return q.collective_compute(
            kind, op, replica_groups=[list(range(NCORES))],
            ins=[cc_in.tensor.ap().opt()], outs=[cc_out.tensor.ap().opt()])
    em.op("gpsimd", fn, waits=wait_sems, inc="cc")
    return em.val("cc")



def ap_mid_bcast(x, n):
    a = [list(p) for p in x.ap]
    return bass.AP(x.tensor, x.offset, [a[0], [0, n], a[1]])


def emit_small_ln_fm(em, b, src, dst, gci, bci, eps, first_te=None):
    em.op("vector", lambda q: q.tensor_copy(b.th_bf[:, :], src[:, :]),
          waits=[("te", em.val("te") if first_te is None else first_te)],
          inc="ve")
    cast_v = em.val("ve")
    em.op("vector", lambda q: q.tensor_mul(b.sq_bf[:, :], b.th_bf[:, :],
          b.th_bf[:, :]), inc="ve")
    sq_v = em.val("ve")
    em.op("tensor", lambda q: q.matmul(
        b.psG[0:1, 0:32], b.ones_col[:, 0:1], b.th_bf[:, :],
        start=True, stop=True),
        waits=[("ve", cast_v)] + em.psw("psG"), inc="te")
    em.op("tensor", lambda q: q.matmul(
        b.psE[0:1, 0:32], b.ones_col[:, 0:1], b.sq_bf[:, :],
        start=True, stop=True),
        waits=[("ve", sq_v)] + em.psw("psE"), inc="te")
    st_v = em.val("te")
    r = b.Dd[0:1, 2048:2056]
    em.op("vector", lambda q: q.tensor_reduce(
        r[0:1, 6:8], b.psG[0:1, 0:32].rearrange("p (d t) -> p t d", t=2),
        mybir.AxisListType.X, ALU.add),
        waits=[("te", st_v), ("ae", em.val("ae"))], inc="ve")
    em.op("vector", lambda q: q.tensor_reduce(
        r[0:1, 2:4], b.psE[0:1, 0:32].rearrange("p (d t) -> p t d", t=2),
        mybir.AxisListType.X, ALU.add), inc="ve")
    red_v = em.val("ve")
    em.op("vector", lambda q: q.tensor_scalar(r[0:1, 0:2], r[0:1, 6:8],
          1.0 / D, None, ALU.mult), waits=[("ve", red_v)], inc="ve")
    em.op("vector", lambda q: q.tensor_scalar(r[0:1, 2:4], r[0:1, 2:4],
          1.0 / D, eps, ALU.mult, ALU.add), inc="ve")
    em.op("vector", lambda q: q.tensor_mul(r[0:1, 4:6], r[0:1, 0:2],
          r[0:1, 0:2]), inc="ve")
    em.op("vector", lambda q: q.tensor_sub(r[0:1, 2:4], r[0:1, 2:4],
          r[0:1, 4:6]), inc="ve")
    em.psr("psG", "ve")
    em.psr("psE", "ve")
    rv = em.val("ve")
    em.op("scalar", lambda q: q.activation(r[0:1, 2:4], r[0:1, 2:4], AF.Ln),
          waits=[("ve", rv)], inc="ae")
    ln_v = em.val("ae")
    em.op("scalar", lambda q: q.activation(r[0:1, 2:4], r[0:1, 2:4], AF.Exp,
          scale=-0.5), waits=[("ae", ln_v)], inc="ae")
    rstd_v = em.val("ae")
    em.op("vector", lambda q: q.tensor_copy(b.rowbf[0:1, 0:2], r[0:1, 0:2]),
          waits=[("ae", rstd_v), ("te", em.val("te"))], inc="ve")
    em.op("vector", lambda q: q.tensor_copy(b.rowbf[0:1, 2:4], r[0:1, 2:4]),
          inc="ve")
    rbv = em.val("ve")
    em.op("tensor", lambda q: q.matmul(b.psG[:, 0:2], b.ones_row[0:1, :],
          b.rowbf[0:1, 0:2], start=True, stop=True),
          waits=[("ve", rbv)] + em.psw("psG"), inc="te")
    em.op("tensor", lambda q: q.matmul(b.psF[:, 0:2], b.ones_row[0:1, :],
          b.rowbf[0:1, 2:4], start=True, stop=True),
          waits=em.psw("psF"), inc="te")
    bc_v = em.val("te")
    dst3 = dst[:, :].rearrange("p (d t) -> p d t", t=2)
    src3 = src[:, :].rearrange("p (d t) -> p d t", t=2)
    mu2 = b.Dd[:, 2056:2058]
    rs2 = b.Dd[:, 2058:2060]
    em.op("vector", lambda q: q.tensor_copy(mu2, b.psG[:, 0:2]),
          waits=[("te", bc_v)], inc="ve")
    em.op("vector", lambda q: q.tensor_copy(rs2, b.psF[:, 0:2]), inc="ve")
    em.psr("psG", "ve")
    em.psr("psF", "ve")
    ev_v = em.val("ve")
    em.op("vector", lambda q: q.tensor_tensor(
        dst3, src3, ap_mid_bcast(mu2, 16), ALU.subtract),
        waits=[("ve", ev_v)], inc="ve")
    em.op("vector", lambda q: q.tensor_tensor(
        dst3, dst3, ap_mid_bcast(rs2, 16), ALU.mult), inc="ve")
    em.op("vector", lambda q: q.tensor_tensor(
        dst3, dst3, _colbc(b, gci), ALU.mult), inc="ve")
    em.op("vector", lambda q: q.tensor_tensor(dst3, dst3, _colbc(b, bci),
          ALU.add), inc="ve")
    return em.val("ve")


def _colbc(b, ci):
    ap = b.smallf[:, ci:ci + 16]
    return ap.broadcast_to([P, 16, 2])



def emit_ffn(em, b, tag):
    i = b.inp
    sm = b.smallf
    em.phase = "fln"
    ve_ln, _ = emit_ln(em, b, b.A, gin=None, eps=1e-5)
    pre_fg_te = em.val("te")
    em.phase = "fth"
    sd_agg = em.dma("gpsimd", b.Ee[:, 4096:8192], b.agg_bf[:, :], "gd",
                    waits=[("te", pre_fg_te), ("ve", em.val("ve"))])
    for d in range(DT):
        em.dma("sync", b.th_in[0, d * P:(d + 1) * P],
               b.A[:, d * TOK + 1023:d * TOK + 1024], "sd",
               waits=[("ve", ve_ln)])
    cc_th = emit_collective(em, b, "AllGather", b.th_in, b.th_out,
                            [("sd", em.val("sd"))])
    sd_d1b = em.dma("gpsimd", b.rowbf[0:1, 1024:2048], b.d1b_bf[0:1, :],
                    "gd", waits=[("te", pre_fg_te), ("ve", em.val("ve"))])
    sd_d1 = em.dma("gpsimd", b.Cc[:, :], b.d1_bf[:, :], "gd",
                   waits=[("te", pre_fg_te), ("ve", em.val("ve"))])
    em.dma("sync", b.th_tm2[0:1, :], b.th_out[3, :], "sd",
           waits=[("cc", cc_th)])
    em.dma("sync", b.th_tm2[1:2, :], b.th_out[7, :], "sd")
    sd_xl = em.val("sd")
    gwbuf = lambda m: b.Ee[:, (m % 2) * 2048:(m % 2) * 2048 + 2048]
    ee_free = em.val("te")
    cons = {}

    def fg_part(m0, m1):
        em.phase = "fg"
        for m in range(m0, m1):
            wv = [("te", cons.get(m - 2, ee_free))]
            sc_v = em.op("scalar", lambda q, m=m: q.dma_start(
                out=gwbuf(m), in_=b.gatew_bf[:, m * 2048:(m + 1) * 2048]),
                waits=wv + [("gd", b.precast_v)] +
                      ([("ve", em.val("ve"))] if m == 0 else []),
                inc="sc", inc_amt=16)
            for h in range(2):
                ps = b.psC if h == 0 else b.psD
                psn = "psC" if h == 0 else "psD"
                for k in range(DT):
                    em.op("tensor", lambda q, m=m, h=h, k=k, ps=ps: q.matmul(
                        ps[:, 0:512], gwbuf(m)[:, k * P:(k + 1) * P],
                        b.A[:, k * TOK + h * 512:k * TOK + (h + 1) * 512],
                        start=(k == 0), stop=(k == DT - 1)),
                        waits=[("sc", sc_v), ("ve", ve_ln)] +
                              (em.psw(psn) if k == 0 else []), inc="te")
                te_v = em.val("te")
                cons[m] = te_v
                em.op("scalar", lambda q, m=m, h=h, ps=ps: q.activation(
                    b.Bb[:, m * TOK + h * 512:m * TOK + (h + 1) * 512],
                    ps[:, 0:512],
                    AF.Sigmoid, bias=sm[:, m:m + 1], scale=1.0),
                    waits=[("te", te_v)] +
                          ([("ve", em.val("ve"))] if (m == 0 and h == 0)
                           else []),
                    inc="ae")
                em.psr(psn, "ae")
        em.phase = "fth"

    fg_part(0, 6)
    _tm_to_fm(em, b, b.th_tm2, b.th_sc, DT, [("sd", sd_xl)])
    emit_small_ln_fm(em, b, b.th_sc, b.th_fm, 80, 96, eps=1e-6)
    _thought_proj(em, b, "agg_img", b.aggb_sb, tag + "agg",
                  img_ready=[("gd", sd_agg)])
    fg_part(6, DT)
    gate_done = em.val("ae")
    em.phase = "fth"
    for s_ in range(4):
        _thought_step(em, b, sd_d1, tag, s_)
    _thought_proj(em, b, "bc_img", b.bcb_sb, tag + "bc", to_bcv=True)
    bcv_v = em.val("ve")
    em.op("vector", lambda q: q.tensor_scalar(
        b.bcsel[:, :], b.bcv_fm[:, 0:32:2], sm[:, 150:151], None, ALU.mult),
        waits=[("ve", bcv_v)], inc="ve")
    em.op("vector", lambda q: q.scalar_tensor_tensor(
        b.bcsel[:, :], b.bcv_fm[:, 1:32:2], sm[:, 151:152], b.bcsel[:, :],
        ALU.mult, ALU.add), inc="ve")
    bcsel_v = em.val("ve")
    em.phase = "fc"
    for d in range(DT):
        sl = slice(d * TOK, (d + 1) * TOK)
        em.op("vector", lambda q, d=d, sl=sl: q.scalar_tensor_tensor(
            b.Cc[:, sl], b.Bb[:, sl], b.bcsel[:, d:d + 1], b.A[:, sl],
            ALU.mult, ALU.add),
            waits=[("ve", bcsel_v), ("ae", gate_done)], inc="ve")
    emit_out_ln_add(em, b, b.Cc, 16, 32, eps=1e-6,
                    src_ready=[("ve", em.val("ve"))])


def _tm_to_fm(em, b, src_tm, dst_fm, ntiles, waits):
    for d in range(ntiles):
        em.op("tensor", lambda q, d=d: q.transpose(
            b.psT[:, d * 2:d * 2 + 2], src_tm[0:2, d * P:(d + 1) * P],
            b.ident[0:2, 0:2]),
            waits=(waits if d == 0 else []) + (em.psw("psT") if d == 0 else []),
            inc="te")
    tr_v = em.val("te")
    em.op("vector", lambda q: q.tensor_copy(dst_fm[:, 0:2 * ntiles],
          b.psT[:, 0:2 * ntiles]), waits=[("te", tr_v)], inc="ve")
    em.psr("psT", "ve")
    return em.val("ve")


def _thought_proj(em, b, img, bias_row, tag, to_bcv=False, img_ready=None):
    em.op("vector", lambda q: q.tensor_copy(b.th_bf[:, :], b.th_fm[:, :]),
          waits=[("te", em.val("te"))], inc="ve")
    tbv = em.val("ve")
    if img_ready is None:
        src = b.agg_bf if img == "agg_img" else b.bc_bf
        sd_v = em.dma(
            "gpsimd", b.Ee[:, 4096:8192], src[:, :],
            "gd", waits=[("te", em.val("te")), ("ve", em.val("ve"))])
        img_ready = [("gd", sd_v)]
    for k in range(DT):
        em.op("tensor", lambda q, k=k: q.matmul(
            b.psB[0:2, 0:256], b.th_bf[:, k * 2:(k + 1) * 2],
            b.Ee[:, 4096 + k * 256:4096 + (k + 1) * 256],
            start=(k == 0), stop=(k == DT - 1)),
            waits=[("ve", tbv)] + img_ready + (em.psw("psB") if k == 0 else []),
            inc="te")
    pj_v = em.val("te")
    em.op("vector", lambda q: q.tensor_copy(b.th_tm2[0:2, 0:256],
          b.psB[0:2, 0:256]), waits=[("te", pj_v)], inc="ve")

    em.psr("psB", "ve")
    em.op("vector", lambda q: q.tensor_tensor(
        b.th_tm2[0:2, 0:256], b.th_tm2[0:2, 0:256],
        bias_row[0:2, :], ALU.add), inc="ve")
    em.dma("sync", b.ag2_in[:, :], b.th_tm2[0:2, 0:256], "sd",
           waits=[("ve", em.val("ve"))])
    cc_v = emit_collective(em, b, "AllGather", b.ag2_in, b.ag2_out,
                           [("sd", em.val("sd"))])
    for r in range(NCORES):
        em.dma("sync", b.th_tm2[0:2, r * 256:(r + 1) * 256],
               b.ag2_out[2 * r:2 * r + 2, :], "sd", waits=[("cc", cc_v)])
    dst = b.bcv_fm if to_bcv else b.th_fm
    _tm_to_fm(em, b, b.th_tm2, dst, DT, [("sd", em.val("sd"))])


def _thought_step(em, b, sd_d1, tag, s_):
    i = b.inp
    sm = b.smallf
    emit_small_ln_fm(em, b, b.th_fm, b.th_sc, 112, 128, eps=1e-6)
    em.op("vector", lambda q: q.tensor_copy(b.th_bf[:, :], b.th_sc[:, :]),
          waits=[("te", em.val("te"))], inc="ve")
    tb = em.val("ve")
    for half in range(2):
        ps = b.psB if half == 0 else b.psG
        psn = "psB" if half == 0 else "psG"
        for k in range(DT):
            em.op("tensor", lambda q, k=k, half=half, ps=ps: q.matmul(
                ps[0:2, 0:512], b.th_bf[:, k * 2:(k + 1) * 2],
                b.Cc[:, k * 1024 + half * 512:k * 1024 + (half + 1) * 512],
                start=(k == 0), stop=False),
                waits=[("ve", tb), ("gd", sd_d1)] +
                      (em.psw(psn) if k == 0 else []))
        em.op("tensor", lambda q, half=half, ps=ps: q.matmul(
            ps[0:2, 0:512], b.ones_row[0:1, 0:2],
            b.rowbf[0:1, 1024 + half * 512:1024 + (half + 1) * 512],
            start=False, stop=True), inc="te")
    d1_done = em.val("te")
    em.op("scalar", lambda q: q.activation(b.h1_tm[0:2, 0:512],
          b.psB[0:2, 0:512], AF.Gelu), waits=[("te", d1_done)], inc="ae")
    em.op("scalar", lambda q: q.activation(b.h1_tm[0:2, 512:1024],
          b.psG[0:2, 0:512], AF.Gelu), inc="ae")
    em.psr("psB", "ae")
    em.psr("psG", "ae")
    g_v = em.val("ae")
    for dd in range(8):
        em.op("tensor", lambda q, dd=dd: q.transpose(
            b.psT[:, dd * 2:dd * 2 + 2], b.h1_tm[0:2, dd * P:(dd + 1) * P],
            b.ident[0:2, 0:2]),
            waits=[("ae", g_v)] + (em.psw("psT") if dd == 0 else []), inc="te")
    em.op("vector", lambda q: q.tensor_copy(b.h1_bf[:, :], b.psT[:, 0:16]),
          waits=[("te", em.val("te"))], inc="ve")
    em.psr("psT", "ve")
    h1f = em.val("ve")
    for grp in range(2):
        sd_v = em.dma("gpsimd", b.Ee[:, 0:8192],
                      b.d2_bf[:, grp * 8192:(grp + 1) * 8192], "gd",
                      waits=[("te", em.val("te")), ("ve", em.val("ve"))])
        for mm_ in range(8):
            m = grp * 8 + mm_
            ps = b.psE if (m % 2 == 0) else b.psF
            psn = "psE" if (m % 2 == 0) else "psF"
            for k in range(8):
                em.op("tensor", lambda q, mm_=mm_, k=k, ps=ps: q.matmul(
                    ps[0:2, 0:128], b.h1_bf[:, k * 2:(k + 1) * 2],
                    b.Ee[:, (mm_ * 8 + k) * P:(mm_ * 8 + k + 1) * P],
                    start=(k == 0), stop=(k == 7)),
                    waits=[("ve", h1f), ("gd", sd_v)] +
                          (em.psw(psn) if k == 0 else []), inc="te")
            te_v = em.val("te")
            em.op("vector", lambda q, m=m, ps=ps: q.tensor_copy(
                b.th_tm2[0:2, m * P:(m + 1) * P], ps[0:2, 0:128]),
                waits=[("te", te_v)], inc="ve")
            em.psr(psn, "ve")
    d2e = em.val("ve")
    em.dma("sync", b.ar_in[:, :], b.th_tm2[0:2, :], "sd", waits=[("ve", d2e)])
    cc_v = emit_collective(em, b, "AllReduce", b.ar_in, b.ar_out,
                           [("sd", em.val("sd"))])
    em.dma("sync", b.th_tm2[0:2, :], b.ar_out[0:2, :], "sd",
           waits=[("cc", cc_v)])
    sd_v = em.val("sd")
    for d in range(DT):
        em.op("tensor", lambda q, d=d: q.transpose(
            b.psT[:, d * 2:d * 2 + 2], b.th_tm2[0:2, d * P:(d + 1) * P],
            b.ident[0:2, 0:2]),
            waits=[("sd", sd_v)] + (em.psw("psT") if d == 0 else []), inc="te")
    tr_v = em.val("te")
    em.op("vector", lambda q: q.tensor_add(b.th_sc[:, :], b.psT[:, 0:32],
          b.th_fm[:, :]), waits=[("te", tr_v)], inc="ve")
    em.psr("psT", "ve")
    em.op("vector", lambda q: q.tensor_tensor(
        b.th_fm[:, :].rearrange("p (d t) -> p d t", t=2),
        b.th_sc[:, :].rearrange("p (d t) -> p d t", t=2),
        _colbc(b, 64), ALU.add), inc="ve")



def dup_ap(sl, nblk=4, dup=2, w=P):
    a = [list(p) for p in sl.ap]
    return bass.AP(sl.tensor, sl.offset, [a[0], [w, nblk], [0, dup], [1, w]])


def v3(sl, nu=8, w=P):
    return sl.rearrange("p (u w) -> p u w", w=w)


def emit_attn_a(em, b):
    i = b.inp
    sm = b.smallf
    em.phase = "a0"
    ve_ln, sd_ln = emit_ln(em, b, b.A, gin=b.gin, eps=1e-5)
    cc_g = emit_collective(em, b, "AllGather", b.gin, b.gxn, [("sd", sd_ln)])
    em.op("scalar", lambda q: q.activation(b.nealog[0:4, 0:1],
          b.alog_sb[0:4, 0:1], AF.Exp), inc="ae")
    em.op("vector", lambda q: q.tensor_scalar(b.nealog[0:4, 0:1],
          b.nealog[0:4, 0:1], -1.0, None, ALU.mult),
          waits=[("ae", em.val("ae"))], inc="ve")
    em.phase = "a1"
    CV = 4608
    SQ = 7680
    BC = 9728
    FO = 11776
    for pas in range(2):
        war = [("te", em.val("te")), ("ve", em.val("ve")), ("ae", em.val("ae")),
               ("gd", em.val("gd")), ("sd", em.val("sd"))]
        gd_w = em.dma("gpsimd", b.Bb[:, 0:12288],
                      i["wsA_img"][:, pas * 12288:(pas + 1) * 12288], "gd",
                      waits=war)
        pcons = {}
        for n in range(16):
            r, off = n // 2, (n % 2) * 512
            bi = n // 8
            xc = b.Ee[:, 0:8192] if n % 2 == 0 else b.A[:, 0:8192]
            xw = pcons.get(n - 2, [("cc", cc_g), ("te", em.val("te")),
                                   ("ae", em.val("ae")), ("ve", em.val("ve")),
                                   ("sd", em.val("sd"))])
            sd_x = em.dma(
                "sync", xc.rearrange("p (k t) -> p k t", t=512),
                b.gxn[2048 * r:2048 * (r + 1), off:off + 512].rearrange(
                    "(k p) t -> p k t", p=P), "sd", waits=xw)
            if pas == 0:
                for (img0, ps, psn) in ((0, b.psE, "psE"), (64, b.psF, "psF")):
                    for k in range(DT):
                        em.op("tensor", lambda q, img0=img0, k=k, ps=ps, xc=xc:
                              q.matmul(
                            ps[0:4, 0:512], b.wba_sb[:, img0 + k * 4:img0 + (k + 1) * 4],
                            xc[:, k * 512:(k + 1) * 512],
                            start=(k == 0), stop=(k == DT - 1)),
                            waits=[("sd", sd_x)] +
                                  (em.psw(psn) if k == 0 else []), inc="te")
                ba_v = em.val("te")
                stg = b.Dd[0:4, 3072:3584]
                em.op("vector", lambda q: q.tensor_copy(stg, b.psE[0:4, 0:512]),
                      waits=[("te", ba_v), ("sd", em.val("sd"))], inc="ve")
                em.psr("psE", "ve")
                sd_b = em.dma("sync", b.ba_scr[0:4, n * 512:(n + 1) * 512], stg,
                              "sd", waits=[("ve", em.val("ve"))])
                em.op("vector", lambda q: q.tensor_copy(stg, b.psF[0:4, 0:512]),
                      waits=[("te", ba_v), ("sd", sd_b)], inc="ve")
                em.psr("psF", "ve")
                em.dma("sync", b.ba_scr[4:8, n * 512:(n + 1) * 512], stg,
                       "sd", waits=[("ve", em.val("ve"))])
            qk_tiles = []
            for m in range(6):
                gm = pas * 6 + m
                kind = ("q", "q", "k", "k", "v", "v")[m] if pas == 0 else                        ("v", "v", "z", "z", "z", "z")[m]
                ps, psn = (b.psA, "psA") if m % 2 == 0 else (b.psB, "psB")
                for k in range(DT):
                    em.op("tensor", lambda q, m=m, k=k, ps=ps, xc=xc: q.matmul(
                        ps[:, 0:512], b.Bb[:, (m * DT + k) * P:(m * DT + k + 1) * P],
                        xc[:, k * 512:(k + 1) * 512],
                        start=(k == 0), stop=(k == DT - 1)),
                        waits=[("gd", gd_w), ("sd", sd_x)] +
                              (em.psw(psn) if k == 0 else []), inc="te")
                mm_v = em.val("te")
                if kind == "z":
                    zslot = b.Cc[:, FO + m * 512:FO + (m + 1) * 512]
                    em.op("scalar", lambda q, ps=ps, zslot=zslot: q.activation(
                        zslot, ps[:, 0:512], AF.Silu),
                        waits=[("te", mm_v), ("sd", em.val("sd"))], inc="ae")
                    em.psr(psn, "ae")
                    zr = (gm - 8) * P
                    em.dma("sync", b.z_scr[zr:zr + P, n * 512:(n + 1) * 512],
                           zslot, "sd", waits=[("ae", em.val("ae"))])
                    continue
                rawb = b.Cc[:, m * 768:m * 768 + 515]
                cvi = gm if gm < 8 else None
                if n == 0 or n == 8:
                    em.op("vector", lambda q, rawb=rawb: q.memset(
                        rawb[:, 0:3], 0.0), waits=[("ae", em.val("ae"))], inc="ve")
                else:
                    em.op("vector", lambda q, rawb=rawb: q.tensor_copy(
                        rawb[:, 0:3], rawb[:, 512:515]),
                        waits=[("ae", em.val("ae")), ("gd", em.val("gd"))],
                        inc="ve")
                cw_v = em.val("ve")
                em.op("scalar", lambda q, ps=ps, rawb=rawb: q.activation(
                    rawb[:, 3:515], ps[:, 0:512], AF.Identity),
                    waits=[("te", mm_v), ("ve", cw_v)], inc="ae")
                em.psr(psn, "ae")
                ev_v = em.val("ae")
                cvo = b.Cc[:, CV + m * 512:CV + (m + 1) * 512]
                em.op("vector", lambda q, rawb=rawb, cvo=cvo, cvi=cvi:
                      q.tensor_scalar(cvo, rawb[:, 0:512],
                                      b.cw_sb[:, cvi * 4:cvi * 4 + 1], None,
                                      ALU.mult),
                      waits=[("ae", ev_v)], inc="ve")
                for j in range(1, 4):
                    em.op("vector", lambda q, rawb=rawb, cvo=cvo, cvi=cvi, j=j:
                          q.scalar_tensor_tensor(
                              cvo, rawb[:, j:512 + j],
                              b.cw_sb[:, cvi * 4 + j:cvi * 4 + j + 1], cvo,
                              ALU.mult, ALU.add), inc="ve")
                cv_v = em.val("ve")
                fo = b.Cc[:, FO + m * 512:FO + (m + 1) * 512]
                em.op("scalar", lambda q, cvo=cvo, fo=fo: q.activation(
                    fo, cvo, AF.Silu),
                    waits=[("ve", cv_v), ("sd", em.val("sd"))], inc="ae")
                si_v = em.val("ae")
                if kind == "v":
                    vr = ((gm - 4) if pas == 0 else (gm - 4)) * P
                    em.dma("sync", b.v_scr[vr:vr + P, n * 512:(n + 1) * 512],
                           fo, "sd", waits=[("ae", si_v)])
                    continue
                qk_tiles.append((m, gm, kind, fo, si_v))
            stage = []
            for (m, gm, kind, fo, si_v) in qk_tiles:
                mq = m
                sq = b.Cc[:, SQ + mq * 512:SQ + (mq + 1) * 512]
                em.op("vector", lambda q, fo=fo, sq=sq: q.tensor_mul(sq, fo, fo),
                      waits=[("ae", si_v), ("te", em.val("te"))], inc="ve")
                sq_v = em.val("ve")
                ps1, ps1n = [(b.psC, "psC"), (b.psE, "psE"),
                             (b.psG, "psG"), (b.psD, "psD")][mq]
                em.op("tensor", lambda q, sq=sq, ps1=ps1: q.matmul(
                    ps1[0:1, 0:512], b.ones_col[:, 0:1], sq,
                    start=True, stop=True),
                    waits=[("ve", sq_v)] + em.psw(ps1n), inc="te")
                stage.append((m, gm, kind, fo, ps1, ps1n, em.val("te")))
            for (m, gm, kind, fo, ps1, ps1n, ss_v) in stage:
                mq = m
                rr = b.Dd[0:1, 1024 + mq * 512:1536 + mq * 512]
                em.op("vector", lambda q, rr=rr, ps1=ps1: q.tensor_scalar(
                    rr, ps1[0:1, 0:512], 1.0, 1e-6, ALU.mult, ALU.add),
                    waits=[("te", ss_v), ("ae", em.val("ae"))], inc="ve")
                em.psr(ps1n, "ve")
                rv = em.val("ve")
                em.op("scalar", lambda q, rr=rr: q.activation(rr, rr, AF.Ln),
                      waits=[("ve", rv)], inc="ae")
                ln_v = em.val("ae")
                em.op("scalar", lambda q, rr=rr: q.activation(
                    rr, rr, AF.Exp, scale=-0.5),
                    waits=[("ae", ln_v)], inc="ae")
                rst_v = em.val("ae")
                if kind == "q":
                    em.op("vector", lambda q, rr=rr: q.tensor_scalar(
                        rr, rr, 0.08838834764831845, None, ALU.mult),
                        waits=[("ae", rst_v)], inc="ve")
                em.op("vector", lambda q, rr=rr, mq=mq: q.tensor_copy(
                    b.rowbf[0:1, mq * 512:(mq + 1) * 512], rr),
                    waits=[("ae", rst_v), ("te", em.val("te"))], inc="ve")
                rb_v = em.val("ve")
                ps2, ps2n = ps1, ps1n
                em.op("tensor", lambda q, mq=mq, ps2=ps2: q.matmul(
                    ps2[:, 0:512], b.ones_row[0:1, :],
                    b.rowbf[0:1, mq * 512:(mq + 1) * 512],
                    start=True, stop=True),
                    waits=[("ve", rb_v)] + em.psw(ps2n), inc="te")
                bc_v = em.val("te")
                bce = b.Cc[:, BC + mq * 512:BC + (mq + 1) * 512]
                em.op("scalar", lambda q, bce=bce, ps2=ps2: q.activation(
                    bce, ps2[:, 0:512], AF.Identity),
                    waits=[("te", bc_v), ("ve", em.val("ve"))], inc="ae")
                em.psr(ps2n, "ae")
                em.op("vector", lambda q, fo=fo, bce=bce: q.tensor_mul(
                    fo, fo, bce), waits=[("ae", em.val("ae"))], inc="ve")
                nm_v = em.val("ve")
                qr = gm * P
                em.dma("sync", b.qk_scr[qr:qr + P, n * 512:(n + 1) * 512],
                       fo, "sd", waits=[("ve", nm_v)])
            pcons[n] = [("te", em.val("te")), ("ve", em.val("ve")),
                        ("ae", em.val("ae")), ("sd", em.val("sd")),
                        ("gd", em.val("gd"))]
    em.phase = "a2"
    ones4 = b.Dd[0:4, 3584:3712]
    em.op("vector", lambda q: q.memset(ones4, 1.0), inc="ve")
    sl_ = lambda k: b.Dd[0:4, k * 512:(k + 1) * 512]
    proj_sd = em.val("sd")
    for c in range(16):
        bi = c // 8
        i0 = (c % 8) * 4
        tok = slice(c * 512, (c + 1) * 512)
        war = [("ve", em.val("ve")), ("ae", em.val("ae")),
               ("gd", em.val("gd"))]
        sd_r = em.dma("sync", sl_(0), b.ba_scr[0:4, tok], "sd",
                      waits=[("sd", proj_sd)] + war)
        sd_r = em.dma("sync", sl_(2), b.ba_scr[4:8, tok], "sd")
        ld_v = em.val("sd")
        em.op("scalar", lambda q: q.activation(sl_(1), sl_(0), AF.Sigmoid),
              waits=[("sd", ld_v), ("ve", em.val("ve")),
                     ("gd", em.val("gd"))], inc="ae")
        b_v = em.val("ae")
        em.op("vector", lambda q: q.tensor_scalar(sl_(0), sl_(1), -1.0, None,
              ALU.mult), waits=[("ae", b_v)], inc="ve")
        em.op("vector", lambda q: q.tensor_scalar(sl_(2), sl_(2),
              b.dtb_sb[0:4, 0:1], None, ALU.add),
              waits=[("sd", ld_v), ("gd", em.val("gd")), ("ae", em.val("ae"))],
              inc="ve")
        a_v = em.val("ve")
        em.op("vector", lambda q: q.tensor_scalar_max(sl_(6), sl_(2), 0.0),
              inc="ve")
        ab_v = em.val("ve")
        em.op("vector", lambda q: q.scalar_tensor_tensor(
            sl_(3), sl_(2), -1.0, sl_(2), ALU.mult, ALU.max), inc="ve")
        abs_v = em.val("ve")
        em.op("scalar", lambda q: q.activation(sl_(4), sl_(3), AF.Exp,
              scale=-1.0), waits=[("ve", abs_v)], inc="ae")
        en_v = em.val("ae")
        em.op("scalar", lambda q: q.activation(sl_(3), sl_(4), AF.Ln,
              bias=1.0), waits=[("ae", en_v)], inc="ae")
        l1_v = em.val("ae")
        em.op("vector", lambda q: q.tensor_add(sl_(6), sl_(6), sl_(3)),
              waits=[("ae", l1_v)], inc="ve")
        em.op("vector", lambda q: q.tensor_scalar(sl_(6), sl_(6),
              b.nealog[0:4, 0:1], None, ALU.mult), inc="ve")
        g_v = em.val("ve")
        for cs in range(4):
            em.op("vector", lambda q, cs=cs: q.tensor_tensor_scan(
                sl_(2)[0:4, cs * P:(cs + 1) * P], ones4,
                sl_(6)[0:4, cs * P:(cs + 1) * P], 0.0, ALU.mult, ALU.add),
                waits=[("ve", g_v)], inc="ve")
        gc_v = em.val("ve")

        def rowout(qid, slot, dst, rl=4096):
            em.dma("gpsimd",
                   bass.AP(dst.tensor, dst.offset + i0 * rl + qid * 1024 + bi * 512,
                           [[P, 4], [rl, 4], [1, P]]),
                   slot.rearrange("v (s w) -> v s w", w=P), "gd",
                   waits=[("ve", em.val("ve")), ("ae", em.val("ae"))])
        rowout(0, sl_(2), b.rows_scr)
        em.op("vector", lambda q: q.tensor_copy(b.r4lo[0:4, :], sl_(2)),
              waits=[("ve", gc_v)], inc="ve")
        hb_v = em.val("ve")
        em.op("vector", lambda q: q.tensor_copy(sl_(3), b.r4lo[0:4, :]),
              waits=[("ve", hb_v)], inc="ve")
        em.op("vector", lambda q: q.tensor_sub(sl_(3), sl_(2), sl_(3)),
              inc="ve")
        rowout(3, sl_(3), b.rows_scr)
        em.op("scalar", lambda q: q.activation(sl_(4), sl_(2), AF.Exp),
              waits=[("ve", gc_v)], inc="ae")
        eg_v = em.val("ae")
        rowout(1, sl_(4), b.rows_scr)
        rowout(2, sl_(0), b.rows_scr)
        em.op("vector", lambda q: q.tensor_copy(
            sl_(3).rearrange("v (s w) -> v s w", w=P),
            sl_(2)[0:4, P - 1::P].broadcast_to([4, 4, P])),
            waits=[("ae", eg_v), ("gd", em.val("gd"))], inc="ve")
        em.op("vector", lambda q: q.tensor_sub(sl_(6), sl_(3), sl_(2)),
              inc="ve")
        sb_v = em.val("ve")
        em.op("scalar", lambda q: q.activation(sl_(3), sl_(6), AF.Exp),
              waits=[("ve", sb_v)], inc="ae")
        egl_v = em.val("ae")
        rowout(1, sl_(3), b.cols_scr, rl=5120)
        em.op("vector", lambda q: q.tensor_mul(sl_(6), sl_(3), sl_(4)),
              waits=[("ae", egl_v)], inc="ve")
        rowout(3, sl_(6), b.cols_scr, rl=5120)
        em.op("vector", lambda q: q.tensor_mul(sl_(5), sl_(1), sl_(4)),
              inc="ve")
        rowout(0, sl_(5), b.cols_scr, rl=5120)
        rowout(2, sl_(1), b.cols_scr, rl=5120)
        rowout(4, sl_(0), b.cols_scr, rl=5120)
    em.phase = "a3"
    war = [("te", em.val("te")), ("ve", em.val("ve")), ("ae", em.val("ae")),
           ("sd", em.val("sd"))]
    gd_m1 = em.dma("gpsimd", b.Ee[:, 0:1024], i["c_maskD"][:, :], "gd",
                   waits=war)
    gd_m2 = em.dma("gpsimd", b.Ee[:, 1024:2048], i["c_maskD2"][:, :], "gd")
    gd_m3 = em.dma("gpsimd", b.Ee[:, 2048:3072], i["c_mask01u"][:, :], "gd")
    gd_m4 = em.dma("gpsimd", b.Ee[:, 3072:4096], i["c_eyeR"][:, :], "gd")
    em.dma("gpsimd", b.r4lo[0:4, :], i["c_blockneg"][:, :], "gd",
           waits=[("ve", em.val("ve"))])
    msk_v = em.val("gd")
    mD = b.Ee[:, 0:1024]
    mD2 = b.Ee[:, 1024:2048]
    m01 = b.Ee[:, 2048:3072]
    eyeR = b.Ee[:, 3072:4096]
    cons_after = {}
    Aa = lambda k: b.A[:, k * 1024:(k + 1) * 1024]
    Dw, D2w, Xs, XTs, Ts = Aa(0), Aa(1), Aa(2), Aa(3), Aa(4)
    qeg, aiT, vcs, kcds, vnew = Aa(5), Aa(6), Aa(7), Aa(8), Aa(9)
    vtil, stbf, osb, ozq, ofin = Aa(10), Aa(11), Aa(12), Aa(13), Aa(14)
    vbw = b.Cc[:, 0:1024]
    kbegw = b.Cc[:, 1024:2048]
    state = b.Dd[:, 0:1024]
    tmpD = b.Dd[:, 1024:2048]
    tmp2 = b.Dd[:, 2048:3072]
    em.op("vector", lambda q: q.memset(state[:, :], 0.0),
          waits=[("te", em.val("te")), ("gd", em.val("gd")),
                 ("sd", em.val("sd")), ("ae", em.val("ae"))], inc="ve")
    em.op("vector", lambda q: q.memset(stbf[:, :], 0.0), inc="ve")
    st0_v = em.val("ve")
    L = {}

    def _scan_loads(it):
        pp = it % 2
        em.phase = "a3a"
        ldw = cons_after.get(it - 2,
                             [("te", em.val("te")), ("ve", em.val("ve")),
                              ("ae", em.val("ae")), ("sc", em.val("sc")),
                              ("gd", em.val("gd")), ("sd", em.val("sd"))]
                             if it < 2 else None)
        rbase = 8192 + pp * 4096
        rows_sd = em.dma("sync", b.Bb[0:1, rbase:rbase + 4096],
                         b.rows_scr[it, 0:4096], "sd", waits=ldw)
        crow_sd = em.dma("sync",
                         b.Cc[0:40, 4096 + pp * P:4096 + (pp + 1) * P],
                         b.cols_scr[it, :].rearrange("(u w) -> u w", w=P),
                         "sd", waits=ldw)
        em.op("tensor", lambda q, pp=pp: q.transpose(
            b.psT[:, pp * 40:pp * 40 + 32],
            b.Cc[0:32, 4096 + pp * P:4096 + (pp + 1) * P],
            b.ident[0:32, 0:32]),
            waits=[("sd", em.val("sd"))] + em.psw("psT"), inc="te")
        em.op("tensor", lambda q, pp=pp: q.transpose(
            b.psT[:, pp * 40 + 32:pp * 40 + 40],
            b.Cc[32:40, 4096 + pp * P:4096 + (pp + 1) * P],
            b.ident[32:40, 32:40]), inc="te")
        em.op("vector", lambda q, pp=pp: q.tensor_copy(
            b.Dd[:, 3072 + pp * 40:3112 + pp * 40],
            b.psT[:, pp * 40:pp * 40 + 40]),
            waits=[("te", em.val("te"))], inc="ve")
        em.psr("psT", "ve")
        r4base = 15360 + pp * 512
        ktm = b.Bb[:, 4096 + pp * 1024:4096 + (pp + 1) * 1024]
        vstep = b.Bb[:, 6144 + pp * 1024:6144 + (pp + 1) * 1024]
        zstep = b.Cc[:, 2048 + pp * 1024:2048 + (pp + 1) * 1024]
        qstep = b.Bb[:, 0 + pp * 1024:0 + (pp + 1) * 1024]
        kstep = b.Bb[:, 2048 + pp * 1024:2048 + (pp + 1) * 1024]
        for u in range(8):
            bi2, vh = u // 4, u % 4
            kh = vh // 2
            cb = bi2 * S + it * P
            em.dma("sync", qstep[:, u * P:(u + 1) * P],
                   b.qk_scr[kh * P:(kh + 1) * P, cb:cb + P], "sd")
            em.dma("sync", kstep[:, u * P:(u + 1) * P],
                   b.qk_scr[256 + kh * P:256 + (kh + 1) * P, cb:cb + P], "sd")
            em.op("sync", lambda q, u=u, kh=kh, cb=cb, ktm=ktm:
                  q.dma_start_transpose(
                      ktm[:, u * P:(u + 1) * P],
                      b.qk_scr[256 + kh * P:256 + (kh + 1) * P, cb:cb + P]),
                  waits=(ldw if u == 0 else ()), inc="sc", inc_amt=16)
            em.op("sync", lambda q, u=u, vh=vh, cb=cb, vstep=vstep:
                  q.dma_start_transpose(
                      vstep[:, u * P:(u + 1) * P],
                      b.v_scr[vh * P:(vh + 1) * P, cb:cb + P]),
                  waits=(), inc="sc", inc_amt=16)
        for bi2 in range(2):
            cb = bi2 * S + it * P
            em.dma("sync",
                   zstep[:, bi2 * 512:(bi2 + 1) * 512].rearrange(
                       "p (v w) -> p v w", w=P),
                   b.z_scr[0:512, cb:cb + P].rearrange("(v p) w -> p v w", p=P),
                   "sd")
        bar = cons_after.get(it - 1, ldw)
        for qi, qq in ((0, 0), (1, 3)):
            em.dma("sync",
                   b.A[0:4, r4base + qi * 256:r4base + (qi + 1) * 256].rearrange(
                       "r (h w) -> r h w", w=P),
                   b.rows_scr[it, qq * 1024:(qq + 1) * 1024].rearrange(
                       "(h r w) -> r h w", h=2, w=P),
                   "sd", waits=bar)
        L[it] = dict(r4_sd=em.val("sd"), ld_sd=em.val("sd"),
                     ld_sc=em.val("sc"), ld_ve=em.val("ve"))

    def _scan_step(it, pp):
        li = L[it]
        r4_sd, ld_sd = li["r4_sd"], li["ld_sd"]
        ld_sc, ld_ve = li["ld_sc"], li["ld_ve"]
        rbase = 8192 + pp * 4096
        colT = b.Dd[:, 3072 + pp * 40:3112 + pp * 40]
        ccol = lambda qid, u: colT[:, qid * 8 + u:qid * 8 + u + 1]
        r4base = 15360 + pp * 512
        rows4 = b.A[0:4, r4base:r4base + 512]
        qstep = b.Bb[:, 0 + pp * 1024:0 + (pp + 1) * 1024]
        kstep = b.Bb[:, 2048 + pp * 1024:2048 + (pp + 1) * 1024]
        ktm = b.Bb[:, 4096 + pp * 1024:4096 + (pp + 1) * 1024]
        vstep = b.Bb[:, 6144 + pp * 1024:6144 + (pp + 1) * 1024]
        zstep = b.Cc[:, 2048 + pp * 1024:2048 + (pp + 1) * 1024]
        em.phase = "a3d"
        rowsh = lambda qid, h: b.Bb[0:1, rbase + qid * 1024 + h * 512:
                                    rbase + qid * 1024 + h * 512 + 512]
        for h in range(2):
            ps = b.psC if h == 0 else b.psD
            psn = "psC" if h == 0 else "psD"
            em.op("tensor", lambda q, h=h, ps=ps: q.matmul(
                ps[:, 0:512], b.ones_row[0:1, :],
                rowsh(0, h), start=True, stop=False),
                waits=[("sd", r4_sd), ("gd", msk_v)] + em.psw(psn), inc="te")
            em.op("tensor", lambda q, h=h, ps=ps: q.matmul(
                ps[:, 0:512], b.ones_row[0:1, :],
                rowsh(3, h), start=False, stop=False), inc="te")
            em.op("tensor", lambda q, h=h, ps=ps, rows4=rows4: q.matmul(
                ps[:, 0:512], rows4[0:4, h * P:(h + 1) * P],
                b.r4lo[0:4, :], start=False, stop=False), inc="te")
            em.op("tensor", lambda q, h=h, ps=ps, rows4=rows4: q.matmul(
                ps[:, 0:512], rows4[0:4, 256 + h * P:256 + (h + 1) * P],
                b.r4lo[0:4, :], start=False, stop=True), inc="te")
        g_v = em.val("te")
        em.op("vector", lambda q: q.scalar_tensor_tensor(
            tmp2[:, 0:512], b.psC[:, 0:512], 1.0, mD[:, 0:512],
            ALU.mult, ALU.add),
            waits=[("te", g_v), ("ae", em.val("ae")), ("gd", msk_v)], inc="ve")
        em.op("vector", lambda q: q.scalar_tensor_tensor(
            tmp2[:, 512:1024], b.psD[:, 0:512], 1.0, mD[:, 512:1024],
            ALU.mult, ALU.add), inc="ve")
        dw_pre = em.val("ve")
        em.op("vector", lambda q: q.scalar_tensor_tensor(
            tmpD[:, 0:512], b.psC[:, 0:512], -1.0, mD2[:, 0:512],
            ALU.mult, ALU.add), inc="ve")
        em.op("vector", lambda q: q.scalar_tensor_tensor(
            tmpD[:, 512:1024], b.psD[:, 0:512], -1.0, mD2[:, 512:1024],
            ALU.mult, ALU.add), inc="ve")
        em.psr("psC", "ve")
        em.psr("psD", "ve")
        d2_pre = em.val("ve")
        em.op("scalar", lambda q: q.activation(Dw, tmp2[:, :], AF.Exp),
              waits=[("ve", dw_pre), ("te", em.val("te"))], inc="ae")
        d_v = em.val("ae")
        em.op("scalar", lambda q: q.activation(D2w, tmpD[:, :], AF.Exp),
              waits=[("ve", d2_pre)], inc="ae")
        d2_v = em.val("ae")
        em.phase = "a3a2"
        for u in range(8):
            ps = b.psA if u < 4 else b.psB
            psn = "psA" if u < 4 else "psB"
            em.op("tensor", lambda q, u=u, ps=ps: q.matmul(
                ps[:, (u % 4) * P:(u % 4 + 1) * P], kstep[:, u * P:(u + 1) * P],
                kstep[:, u * P:(u + 1) * P], start=True, stop=True),
                waits=[("sd", ld_sd)] + (em.psw(psn) if u % 4 == 0 else []),
                inc="te")
            ps2 = b.psE if u < 4 else b.psF
            ps2n = "psE" if u < 4 else "psF"
            em.op("tensor", lambda q, u=u, ps2=ps2: q.matmul(
                ps2[:, (u % 4) * P:(u % 4 + 1) * P], kstep[:, u * P:(u + 1) * P],
                qstep[:, u * P:(u + 1) * P], start=True, stop=True),
                waits=[("sd", ld_sd)] + (em.psw(ps2n) if u % 4 == 0 else []),
                inc="te")
        kk_v = em.val("te")
        em.op("vector", lambda q: q.tensor_tensor(aiT[:, 0:512], b.psE[:, 0:512],
              Dw[:, 0:512], ALU.mult), waits=[("te", kk_v), ("ae", d_v)],
              inc="ve")
        em.op("vector", lambda q: q.tensor_tensor(aiT[:, 512:1024],
              b.psF[:, 0:512], Dw[:, 512:1024], ALU.mult), inc="ve")
        em.psr("psE", "ve")
        em.psr("psF", "ve")
        em.op("vector", lambda q: q.tensor_tensor(Xs[:, 0:512], b.psA[:, 0:512],
              Dw[:, 0:512], ALU.mult), inc="ve")
        em.op("vector", lambda q: q.tensor_tensor(Xs[:, 512:1024],
              b.psB[:, 0:512], Dw[:, 512:1024], ALU.mult), inc="ve")
        for u in range(8):
            psx = b.psA if u < 4 else b.psB
            em.op("vector", lambda q, u=u, psx=psx: q.scalar_tensor_tensor(
                XTs[:, u * P:(u + 1) * P],
                psx[:, (u % 4) * P:(u % 4 + 1) * P], ccol(4, u),
                D2w[:, u * P:(u + 1) * P], ALU.mult, ALU.mult),
                waits=[("ae", d2_v), ("ve", ld_ve)], inc="ve")
        em.psr("psA", "ve")
        em.psr("psB", "ve")
        for h in range(2):
            ps = b.psC if h == 0 else b.psD
            psn = "psC" if h == 0 else "psD"
            em.op("tensor", lambda q, h=h, ps=ps: q.matmul(
                ps[:, 0:512], b.ones_row[0:1, :],
                rowsh(2, h), start=True, stop=True),
                waits=em.psw(psn), inc="te")
        bw_v = em.val("te")
        em.op("vector", lambda q: q.tensor_tensor(Xs[:, 0:512], Xs[:, 0:512],
              b.psC[:, 0:512], ALU.mult), waits=[("te", bw_v)], inc="ve")
        em.op("vector", lambda q: q.tensor_tensor(Xs[:, 512:1024],
              Xs[:, 512:1024], b.psD[:, 0:512], ALU.mult), inc="ve")
        em.psr("psC", "ve")
        em.psr("psD", "ve")
        em.op("vector", lambda q: q.tensor_mul(Xs, Xs, m01), inc="ve")
        for h in range(2):
            ps = b.psE if h == 0 else b.psF
            psn = "psE" if h == 0 else "psF"
            em.op("tensor", lambda q, h=h, ps=ps: q.matmul(
                ps[:, 0:512], b.ones_row[0:1, :],
                rowsh(1, h), start=True, stop=True),
                waits=em.psw(psn), inc="te")
        eb_v = em.val("te")
        em.op("vector", lambda q: q.tensor_tensor(qeg[:, 0:512], qstep[:, 0:512],
              b.psE[:, 0:512], ALU.mult), waits=[("te", eb_v)], inc="ve")
        em.op("vector", lambda q: q.tensor_tensor(qeg[:, 512:1024],
              qstep[:, 512:1024], b.psF[:, 0:512], ALU.mult), inc="ve")
        em.psr("psE", "ve")
        em.psr("psF", "ve")
        pbuilt_v = em.val("ve")
        em.phase = "a3b"
        em.op("vector", lambda q: q.tensor_add(Ts, Xs, eyeR),
              waits=[("ve", pbuilt_v)], inc="ve")
        ts0_v = em.val("ve")

        def _sq_mms(first_waits, with_x):
            for u in range(8):
                if with_x:
                    ps = b.psC if u < 4 else b.psD
                    psn = "psC" if u < 4 else "psD"
                    em.op("tensor", lambda q, u=u, ps=ps: q.matmul(
                        ps[:, (u % 4) * P:(u % 4 + 1) * P],
                        XTs[:, u * P:(u + 1) * P], Xs[:, u * P:(u + 1) * P],
                        start=True, stop=True),
                        waits=(first_waits if u == 0 else []) +
                              (em.psw(psn) if u % 4 == 0 else []), inc="te")
                ps2 = b.psE if u < 4 else b.psF
                ps2n = "psE" if u < 4 else "psF"
                em.op("tensor", lambda q, u=u, ps2=ps2: q.matmul(
                    ps2[:, (u % 4) * P:(u % 4 + 1) * P],
                    Xs[:, u * P:(u + 1) * P], XTs[:, u * P:(u + 1) * P],
                    start=True, stop=True),
                    waits=(first_waits if u == 0 and not with_x else []) +
                          (em.psw(ps2n) if u % 4 == 0 else []), inc="te")
            return em.val("te")

        def _sq_evac(sq_v, with_x):
            if with_x:
                em.op("vector", lambda q: q.tensor_copy(Xs[:, 0:512],
                      b.psC[:, 0:512]), waits=[("te", sq_v)], inc="ve")
                em.op("vector", lambda q: q.tensor_copy(Xs[:, 512:1024],
                      b.psD[:, 0:512]), inc="ve")
                em.psr("psC", "ve")
                em.psr("psD", "ve")
            em.op("scalar", lambda q: q.activation(XTs[:, 0:512],
                  b.psE[:, 0:512], AF.Identity), waits=[("te", sq_v)],
                  inc="ae")
            em.op("scalar", lambda q: q.activation(XTs[:, 512:1024],
                  b.psF[:, 0:512], AF.Identity), inc="ae")
            em.psr("psE", "ae")
            em.psr("psF", "ae")
            return em.val("ve"), em.val("ae")

        sq_v = _sq_mms([("ve", ts0_v)], True)
        xe_ve, xe_ae = _sq_evac(sq_v, True)
        tadd_prev = ts0_v
        for lv in range(1, 7):
            for u in range(8):
                ps = b.psA if u < 4 else b.psB
                psn = "psA" if u < 4 else "psB"
                em.op("tensor", lambda q, u=u, ps=ps: q.matmul(
                    ps[:, (u % 4) * P:(u % 4 + 1) * P],
                    XTs[:, u * P:(u + 1) * P], Ts[:, u * P:(u + 1) * P],
                    start=True, stop=True),
                    waits=([("ve", max(xe_ve, tadd_prev)), ("ae", xe_ae)]
                           if u == 0 else []) +
                          (em.psw(psn) if u % 4 == 0 else []), inc="te")
            t_v = em.val("te")
            if lv < 6:
                sq_v = _sq_mms([], lv < 5)
            em.op("vector", lambda q: q.tensor_tensor(Ts[:, 0:512],
                  Ts[:, 0:512], b.psA[:, 0:512], ALU.add),
                  waits=[("te", t_v)], inc="ve")
            em.op("vector", lambda q: q.tensor_tensor(Ts[:, 512:1024],
                  Ts[:, 512:1024], b.psB[:, 0:512], ALU.add), inc="ve")
            em.psr("psA", "ve")
            em.psr("psB", "ve")
            tadd_prev = em.val("ve")
            if lv < 6:
                xe_ve, xe_ae = _sq_evac(sq_v, lv < 5)
        inv_v = em.val("ve")
        em.phase = "a3c"
        for u in range(8):
            em.op("vector", lambda q, u=u: q.tensor_scalar(
                vbw[:, u * P:(u + 1) * P], vstep[:, u * P:(u + 1) * P],
                ccol(2, u), None, ALU.mult),
                waits=[("sd", ld_sd), ("sc", ld_sc), ("ve", ld_ve)], inc="ve")
            em.op("vector", lambda q, u=u: q.tensor_scalar(
                kbegw[:, u * P:(u + 1) * P], ktm[:, u * P:(u + 1) * P],
                ccol(0, u), None, ALU.mult), inc="ve")
        vk_v = em.val("ve")
        for u in range(8):
            ps = b.psC if u < 4 else b.psD
            psn = "psC" if u < 4 else "psD"
            em.op("tensor", lambda q, u=u, ps=ps: q.matmul(
                ps[:, (u % 4) * P:(u % 4 + 1) * P], Ts[:, u * P:(u + 1) * P],
                vbw[:, u * P:(u + 1) * P], start=True, stop=True),
                waits=[("ve", vk_v)] + (em.psw(psn) if u % 4 == 0 else []),
                inc="te")
            ps2 = b.psE if u < 4 else b.psF
            ps2n = "psE" if u < 4 else "psF"
            em.op("tensor", lambda q, u=u, ps2=ps2: q.matmul(
                ps2[:, (u % 4) * P:(u % 4 + 1) * P], kbegw[:, u * P:(u + 1) * P],
                Ts[:, u * P:(u + 1) * P], start=True, stop=True),
                waits=[("ve", vk_v)] + (em.psw(ps2n) if u % 4 == 0 else []),
                inc="te")
        vc_v = em.val("te")
        em.op("vector", lambda q: q.tensor_copy(vcs[:, 0:512], b.psC[:, 0:512]),
              waits=[("te", vc_v)], inc="ve")
        em.op("vector", lambda q: q.tensor_copy(vcs[:, 512:1024],
              b.psD[:, 0:512]), inc="ve")
        em.psr("psC", "ve")
        em.psr("psD", "ve")
        em.op("scalar", lambda q: q.activation(kcds[:, 0:512], b.psE[:, 0:512],
              AF.Identity), waits=[("te", vc_v)], inc="ae")
        em.op("scalar", lambda q: q.activation(kcds[:, 512:1024],
              b.psF[:, 0:512], AF.Identity), inc="ae")
        em.psr("psE", "ae")
        em.psr("psF", "ae")
        kcd_v = em.val("ve")
        kcd_ae = em.val("ae")
        for u in range(8):
            ps = b.psC if u < 4 else b.psD
            psn = "psC" if u < 4 else "psD"
            em.op("tensor", lambda q, u=u, ps=ps: q.matmul(
                ps[:, (u % 4) * P:(u % 4 + 1) * P], kcds[:, u * P:(u + 1) * P],
                stbf[:, u * P:(u + 1) * P], start=True, stop=True),
                waits=[("ve", max(kcd_v, st0_v)), ("ae", kcd_ae)] +
                      (em.psw(psn) if u % 4 == 0 else []), inc="te")
        t1_v = em.val("te")
        em.op("vector", lambda q: q.tensor_sub(vnew[:, 0:512], vcs[:, 0:512],
              b.psC[:, 0:512]), waits=[("te", t1_v)], inc="ve")
        em.op("vector", lambda q: q.tensor_sub(vnew[:, 512:1024],
              vcs[:, 512:1024], b.psD[:, 0:512]), inc="ve")
        em.psr("psC", "ve")
        em.psr("psD", "ve")
        vn_v = em.val("ve")
        for u in range(8):
            ps = b.psE if u < 4 else b.psF
            psn = "psE" if u < 4 else "psF"
            em.op("tensor", lambda q, u=u, ps=ps: q.matmul(
                ps[:, (u % 4) * P:(u % 4 + 1) * P], stbf[:, u * P:(u + 1) * P],
                qeg[:, u * P:(u + 1) * P], start=True, stop=False),
                waits=[("ve", vn_v)] + (em.psw(psn) if u % 4 == 0 else []),
                inc="te")
            em.op("tensor", lambda q, u=u, ps=ps: q.matmul(
                ps[:, (u % 4) * P:(u % 4 + 1) * P], vnew[:, u * P:(u + 1) * P],
                aiT[:, u * P:(u + 1) * P], start=False, stop=True), inc="te")
        o_v = em.val("te")
        em.op("scalar", lambda q: q.activation(osb[:, 0:512], b.psE[:, 0:512],
              AF.Identity), waits=[("te", o_v), ("ve", em.val("ve"))],
              inc="ae")
        em.op("scalar", lambda q: q.activation(osb[:, 512:1024], b.psF[:, 0:512],
              AF.Identity), inc="ae")
        em.psr("psE", "ae")
        em.psr("psF", "ae")
        oe_v = em.val("ae")
        for u in range(8):
            em.op("vector", lambda q, u=u: q.tensor_scalar(
                vtil[:, u * P:(u + 1) * P], vnew[:, u * P:(u + 1) * P],
                ccol(1, u), None, ALU.mult), waits=[("ve", vn_v)], inc="ve")
        vt_v = em.val("ve")
        for u in range(8):
            ps = b.psC if u < 4 else b.psD
            psn = "psC" if u < 4 else "psD"
            em.op("tensor", lambda q, u=u, ps=ps: q.matmul(
                ps[:, (u % 4) * P:(u % 4 + 1) * P], ktm[:, u * P:(u + 1) * P],
                vtil[:, u * P:(u + 1) * P], start=True, stop=True),
                waits=[("ve", vt_v)] + (em.psw(psn) if u % 4 == 0 else []),
                inc="te")
        t4_v = em.val("te")
        for u in range(8):
            em.op("vector", lambda q, u=u: q.tensor_scalar(
                state[:, u * P:(u + 1) * P], state[:, u * P:(u + 1) * P],
                ccol(3, u), None, ALU.mult),
                waits=[("ve", max(st0_v, ld_ve))], inc="ve")
        em.op("vector", lambda q: q.tensor_add(state[:, 0:512], state[:, 0:512],
              b.psC[:, 0:512]), waits=[("te", t4_v)], inc="ve")
        em.op("vector", lambda q: q.tensor_add(state[:, 512:1024],
              state[:, 512:1024], b.psD[:, 0:512]), inc="ve")
        em.psr("psC", "ve")
        em.psr("psD", "ve")
        em.op("vector", lambda q: q.tensor_copy(stbf, state),
              waits=[("te", t4_v)], inc="ve")
        em.op("vector", lambda q: q.tensor_mul(ozq, osb, osb),
              waits=[("ae", oe_v)], inc="ve")
        oq_v = em.val("ve")
        for h in range(2):
            ps = b.psA if h == 0 else b.psB
            psn = "psA" if h == 0 else "psB"
            em.op("tensor", lambda q, h=h, ps=ps: q.matmul(
                ps[0:1, 0:512], b.ones_col[:, 0:1],
                ozq[:, h * 512:(h + 1) * 512], start=True, stop=True),
                waits=[("ve", oq_v)] + em.psw(psn), inc="te")
        os_v = em.val("te")
        rrow = b.Dd[0:1, 1024:2048]
        em.op("vector", lambda q: q.tensor_scalar(rrow[0:1, 0:512],
              b.psA[0:1, 0:512], 1.0 / P, 1e-6, ALU.mult, ALU.add),
              waits=[("te", os_v), ("ae", em.val("ae"))], inc="ve")
        em.op("vector", lambda q: q.tensor_scalar(rrow[0:1, 512:1024],
              b.psB[0:1, 0:512], 1.0 / P, 1e-6, ALU.mult, ALU.add), inc="ve")
        em.psr("psA", "ve")
        em.psr("psB", "ve")
        rr_v = em.val("ve")
        em.op("scalar", lambda q: q.activation(rrow, rrow, AF.Ln),
              waits=[("ve", rr_v)], inc="ae")
        rl_v = em.val("ae")
        em.op("scalar", lambda q: q.activation(rrow, rrow, AF.Exp, scale=-0.5),
              waits=[("ae", rl_v)], inc="ae")
        re_v = em.val("ae")
        em.op("vector", lambda q: q.tensor_copy(b.rowbf[0:1, 0:1024], rrow),
              waits=[("ae", re_v), ("te", em.val("te"))], inc="ve")
        rb_v = em.val("ve")
        for h in range(2):
            ps = b.psA if h == 0 else b.psB
            psn = "psA" if h == 0 else "psB"
            em.op("tensor", lambda q, h=h, ps=ps: q.matmul(
                ps[:, 0:512], b.ones_row[0:1, :],
                b.rowbf[0:1, h * 512:(h + 1) * 512], start=True, stop=True),
                waits=[("ve", rb_v)] + em.psw(psn), inc="te")
        rbc_v = em.val("te")
        em.op("vector", lambda q: q.scalar_tensor_tensor(
            osb[:, 0:512], osb[:, 0:512], sm[:, 144:145], b.psA[:, 0:512],
            ALU.mult, ALU.mult), waits=[("te", rbc_v)], inc="ve")
        em.op("vector", lambda q: q.scalar_tensor_tensor(
            osb[:, 512:1024], osb[:, 512:1024], sm[:, 144:145],
            b.psB[:, 0:512], ALU.mult, ALU.mult), inc="ve")
        em.psr("psA", "ve")
        em.psr("psB", "ve")
        em.op("vector", lambda q, zstep=zstep: q.tensor_mul(ofin, osb, zstep),
              waits=[("sd", ld_sd), ("gd", em.val("gd"))], inc="ve")
        of_v = em.val("ve")
        for u in range(8):
            bi2, vh = u // 4, u % 4
            j = bi2 * 4 + it // 8
            col = (it % 8) * P
            em.dma("gpsimd", b.oa_in[j * 512 + vh * P:j * 512 + (vh + 1) * P,
                                     col:col + P],
                   ofin[:, u * P:(u + 1) * P], "gd", waits=[("ve", of_v)])
        cons_after[it] = [("te", em.val("te")), ("ve", em.val("ve")),
                          ("ae", em.val("ae")), ("sd", em.val("sd")),
                          ("sc", em.val("sc")), ("gd", em.val("gd"))]
    _scan_loads(0)
    for it in range(32):
        _scan_step(it, it % 2)
        if it + 1 < 32:
            _scan_loads(it + 1)

    em.phase = "a4"
    cc_o = emit_collective(em, b, "AllToAll", b.oa_in, b.oa_out,
                           [("sd", em.val("sd")), ("gd", em.val("gd"))])
    war = [("te", em.val("te")), ("ve", em.val("ve")), ("ae", em.val("ae")),
           ("gd", em.val("gd"))]
    wo = lambda m: b.Cc[:, 8192 + (m % 2) * 2048:8192 + (m % 2) * 2048 + 2048]
    cons = {}
    free_v = em.val("te")
    for half in range(2):
        sd_o = em.dma("sync", b.Bb[:, :].rearrange("p (k t) -> p k t", t=TOK),
                      b.oa_out[half * 2048:(half + 1) * 2048, :].rearrange(
                          "(k p) t -> p k t", p=P), "sd",
                      waits=[("cc", cc_o), ("te", em.val("te")),
                             ("ve", em.val("ve")), ("ae", em.val("ae"))])
        for m in range(DT):
            gd_v = em.dma("gpsimd", wo(m),
                          i["woutA_img"][:, (m * 32 + half * 16) * P:
                                         (m * 32 + half * 16 + 16) * P], "gd",
                          waits=[("te", cons.get((half, m - 2),
                                                 cons.get((half - 1, DT - 1),
                                                          free_v)))])
            for hh in range(2):
                ps, psn = (b.psA, "psA") if hh == 0 else (b.psB, "psB")
                for k in range(DT):
                    em.op("tensor", lambda q, m=m, hh=hh, k=k, ps=ps: q.matmul(
                        ps[:, 0:512], wo(m)[:, k * P:(k + 1) * P],
                        b.Bb[:, k * TOK + hh * 512:k * TOK + (hh + 1) * 512],
                        start=(k == 0), stop=(k == DT - 1)),
                        waits=[("gd", gd_v), ("sd", sd_o)] +
                              (em.psw(psn) if k == 0 else []), inc="te")
                te_v = em.val("te")
                cons[(half, m)] = te_v
                em.op("vector", lambda q, m=m, hh=hh, ps=ps: q.tensor_tensor(
                    b.xres[:, m * TOK + hh * 512:m * TOK + (hh + 1) * 512],
                    b.xres[:, m * TOK + hh * 512:m * TOK + (hh + 1) * 512],
                    ps[:, 0:512], ALU.add), waits=[("te", te_v)], inc="ve")
                em.psr(psn, "ve")


def emit_attn_b(em, b):
    i = b.inp
    sm = b.smallf
    em.phase = "b0"
    ve_ln, sd_ln = emit_ln(em, b, b.A, gin=b.gin, eps=1e-5)
    cc_g = emit_collective(em, b, "AllGather", b.gin, b.gxn,
                           [("sd", sd_ln)])
    war = [("te", em.val("te")), ("ve", em.val("ve")), ("ae", em.val("ae"))]
    gd_ws = em.dma("gpsimd", b.Bb[:, 0:8192], i["wsB_img"][:, :], "gd",
                   waits=war)
    gd_wv = em.dma("gpsimd", b.Cc[:, 0:4096], i["wvB_img"][:, :], "gd",
                   waits=war)
    gd_rc = em.dma("gpsimd", b.A[:, 0:4096], i["c_rope_c"][:, :], "gd",
                   waits=[("sd", sd_ln)] + war)
    gd_rs = em.dma("gpsimd", b.A[:, 4096:8192], i["c_rope_s"][:, :], "gd")
    gd_cm = em.dma("gpsimd", b.A[:, 12288:14336], i["c_cmask"][:, :], "gd")
    em.phase = "b1"
    st1 = b.Bb[:, 8192:8704]
    st2 = b.Bb[:, 8704:9216]
    st4 = b.Bb[:, 9216:9728]
    psq = [b.psA, b.psB, b.psC, b.psD]
    psqn = ["psA", "psB", "psC", "psD"]
    pcons = {}
    for n in range(16):
        r, off = n // 2, (n % 2) * 512
        bi = n // 8
        pos0 = (n % 8) * 512
        xc = b.Ee[:, 0:8192] if n % 2 == 0 else b.Cc[:, 4096:12288]
        xw = pcons.get(n - 2, [("cc", cc_g), ("te", em.val("te")),
                               ("ae", em.val("ae")), ("ve", em.val("ve")),
                               ("sd", em.val("sd"))])
        sd_x = em.dma(
            "sync", xc.rearrange("p (k t) -> p k t", t=512),
            b.gxn[2048 * r:2048 * (r + 1), off:off + 512].rearrange(
                "(k p) t -> p k t", p=P), "sd", waits=xw)
        for m in range(4):
            ps, psn = psq[m], psqn[m]
            for k in range(DT):
                em.op("tensor", lambda q, m=m, k=k, ps=ps, xc=xc: q.matmul(
                    ps[:, 0:512], b.Bb[:, (m * DT + k) * P:(m * DT + k + 1) * P],
                    xc[:, k * 512:(k + 1) * 512],
                    start=(k == 0), stop=(k == DT - 1)),
                    waits=[("gd", gd_ws), ("sd", sd_x)] +
                          (em.psw(psn) if k == 0 else []), inc="te")
            mm_v = em.val("te")
            em.op("scalar", lambda q, m=m, ps=ps: q.activation(
                st1, ps[:, 0:512], AF.Identity, bias=sm[:, 145 + m:146 + m]),
                waits=[("te", mm_v), ("ve", em.val("ve"))], inc="ae")
            ev_v = em.val("ae")
            em.psr(psn, "ae")
            em.op("tensor", lambda q: q.matmul(
                b.psE[:, 0:512], b.swap_pm[:, :], st1, start=True, stop=True),
                waits=[("ae", ev_v)] + em.psw("psE"), inc="te")
            sw_v = em.val("te")
            em.op("vector", lambda q, pos0=pos0: q.tensor_mul(
                st2, st1, b.A[:, pos0:pos0 + 512]),
                waits=[("ae", ev_v), ("gd", gd_rc)], inc="ve")
            em.op("vector", lambda q, pos0=pos0: q.tensor_mul(
                st4, b.psE[:, 0:512], b.A[:, 4096 + pos0:4096 + pos0 + 512]),
                waits=[("te", sw_v), ("sd", em.val("sd")), ("gd", gd_rs)],
                inc="ve")
            em.psr("psE", "ve")
            em.op("vector", lambda q: q.tensor_add(st4, st2, st4), inc="ve")
            rp_v = em.val("ve")
            h = m % 2
            qk = m // 2
            u = bi * 2 + h
            em.dma("sync", b.qb_scr[u * 384 + qk * P:u * 384 + qk * P + P,
                                    pos0:pos0 + 512], st4, "sd",
                   waits=[("ve", rp_v)])
        for tt in range(4):
            ps, psn = (b.psF, "psF") if tt % 2 == 0 else (b.psG, "psG")
            for k in range(DT):
                em.op("tensor", lambda q, tt=tt, k=k, ps=ps, xc=xc: q.matmul(
                    ps[:, 0:256], xc[:, k * 512 + tt * P:k * 512 + (tt + 1) * P],
                    b.Cc[:, k * 256:(k + 1) * 256],
                    start=(k == 0), stop=False),
                    waits=[("gd", gd_wv), ("sd", sd_x)] +
                          (em.psw(psn) if k == 0 else []), inc="te")
            em.op("tensor", lambda q, ps=ps: q.matmul(
                ps[:, 0:256], b.ones_row[0:1, :], b.bvB_sb[0:1, :],
                start=False, stop=True), inc="te")
            mm_v = em.val("te")
            em.op("scalar", lambda q, ps=ps: q.activation(
                st1[:, 0:256], ps[:, 0:256], AF.Identity),
                waits=[("te", mm_v), ("ve", em.val("ve")),
                       ("sd", em.val("sd"))], inc="ae")
            ev_v = em.val("ae")
            em.psr(psn, "ae")
            tg = (n % 8) * 4 + tt
            for h in range(2):
                u = bi * 2 + h
                em.dma("sync", b.qb_scr[u * 384 + 256:u * 384 + 384,
                                        tg * P:(tg + 1) * P],
                       st1[:, h * P:(h + 1) * P], "sd",
                       waits=[("ae", ev_v)])
        pcons[n] = [("te", em.val("te")), ("ve", em.val("ve")),
                    ("ae", em.val("ae")), ("sd", em.val("sd")),
                    ("gd", em.val("gd"))]
    em.phase = "b2"
    proj_sd = em.val("sd")
    RS = float(1.0 / np.sqrt(128.0))
    for u in range(4):
        bi, h = u // 2, u % 2
        war = [("te", em.val("te")), ("ve", em.val("ve")), ("ae", em.val("ae"))]
        sd_u = em.dma("sync",
                      b.Bb[:, 0:12288].rearrange("p (s t) -> p s t", t=S),
                      b.qb_scr[u * 384:(u + 1) * 384, :].rearrange(
                          "(s p) t -> p s t", p=P), "sd",
                      waits=[("sd", proj_sd)] + war)
        qblk = lambda c0, w: b.Bb[:, c0:c0 + w]
        for qp in range(4):
            kmax = (qp + 1) * 8
            for qb in range(2):
                pvp, pvn = (b.psC, "psC") if qb == 0 else (b.psD, "psD")
                smp, smn = (b.psE, "psE") if qb == 0 else (b.psF, "psF")
                kl = qp * 8 + qb * 4 + 4

                def scores(kt, qp=qp, qb=qb):
                    scp, scn = (b.psA, "psA") if kt % 2 == 0 else (b.psB, "psB")
                    em.op("tensor", lambda q, kt=kt, qp=qp, qb=qb, scp=scp:
                          q.matmul(scp[:, 0:512],
                                   b.Bb[:, 4096 + kt * P:4096 + (kt + 1) * P],
                                   b.Bb[:, qp * 1024 + qb * 512:
                                        qp * 1024 + (qb + 1) * 512],
                                   start=True, stop=True),
                          waits=[("sd", sd_u)] + em.psw(scn), inc="te")
                    sc_v = em.val("te")
                    r0 = kt - (qp * 8 + qb * 4)
                    mw = []
                    if 0 <= r0 < 4:
                        em.op("vector", lambda q, r0=r0, scp=scp:
                              q.tensor_tensor(
                            scp[:, 0:512], scp[:, 0:512],
                            b.A[:, 12288 + r0 * 512:12288 + (r0 + 1) * 512],
                            ALU.add), waits=[("te", sc_v), ("gd", gd_cm)],
                            inc="ve")
                        mw = [("ve", em.val("ve"))]
                    return scp, scn, sc_v, mw

                pend = scores(0)
                for kt in range(kl):
                    scp, scn, sc_v, mw = pend
                    if kt + 1 < kl:
                        pend = scores(kt + 1)
                    slot = b.Ee[:, qb * 1024 + (kt % 2) * 512:
                                qb * 1024 + (kt % 2) * 512 + 512]
                    em.op("scalar", lambda q, scp=scp, slot=slot: q.activation(
                        slot, scp[:, 0:512], AF.Exp, scale=RS),
                        waits=[("te", sc_v)] + mw, inc="ae")
                    ex_v = em.val("ae")
                    em.psr(scn, "ae")
                    em.op("tensor", lambda q, kt=kt, pvp=pvp, slot=slot:
                          q.matmul(pvp[:, 0:512],
                                   b.Bb[:, 8192 + kt * P:8192 + (kt + 1) * P],
                                   slot, start=(kt == 0), stop=(kt == kl - 1)),
                          waits=[("ae", ex_v)] +
                                (em.psw(pvn) if kt == 0 else []), inc="te")
                    em.op("tensor", lambda q, kt=kt, smp=smp, slot=slot:
                          q.matmul(smp[0:1, 0:512], b.ones_col[:, 0:1],
                                   slot, start=(kt == 0), stop=(kt == kl - 1)),
                          waits=(em.psw(smn) if kt == 0 else []), inc="te")
                pv_v = em.val("te")
                rrow = b.Dd[0:1, 3072:3584]
                em.op("vector", lambda q, smp=smp: q.tensor_copy(
                    rrow, smp[0:1, 0:512]),
                    waits=[("te", pv_v), ("ae", em.val("ae"))], inc="ve")
                em.psr(smn, "ve")
                cp_v = em.val("ve")
                em.op("scalar", lambda q: q.activation(
                    rrow[0:1, 0:512], rrow[0:1, 0:512], AF.Ln),
                    waits=[("ve", cp_v)], inc="ae")
                ln_v = em.val("ae")
                em.op("scalar", lambda q: q.activation(
                    rrow[0:1, 0:512], rrow[0:1, 0:512], AF.Exp, scale=-1.0),
                    waits=[("ae", ln_v)], inc="ae")
                rc_v = em.val("ae")
                em.op("vector", lambda q: q.tensor_copy(
                    b.rowbf[0:1, 0:512], rrow),
                    waits=[("ae", rc_v), ("te", em.val("te"))], inc="ve")
                rb_v = em.val("ve")
                scp2, scn2 = (b.psA, "psA")
                em.op("tensor", lambda q: q.matmul(
                    b.psA[:, 0:512], b.ones_row[0:1, :], b.rowbf[0:1, 0:512],
                    start=True, stop=True),
                    waits=[("ve", rb_v)] + em.psw("psA"), inc="te")
                bc_v = em.val("te")
                oslot = b.A[:, 8192 + qp * 1024 + qb * 512:
                            8192 + qp * 1024 + qb * 512 + 512]
                em.op("scalar", lambda q, pvp=pvp, oslot=oslot: q.activation(
                    oslot, pvp[:, 0:512], AF.Identity),
                    waits=[("te", bc_v), ("ve", em.val("ve")),
                           ("sd", em.val("sd"))], inc="ae")
                em.psr(pvn, "ae")
                em.op("vector", lambda q, oslot=oslot: q.tensor_mul(
                    oslot, oslot, b.psA[:, 0:512]),
                    waits=[("ae", em.val("ae")), ("te", bc_v)], inc="ve")
                em.psr("psA", "ve")
            o_v = em.val("ve")
            j = bi * 4 + qp
            em.dma("sync", b.ob_in[j * 256 + h * P:j * 256 + (h + 1) * P, :],
                   b.A[:, 8192 + qp * 1024:8192 + (qp + 1) * 1024], "sd",
                   waits=[("ve", o_v)])
    em.phase = "b3"
    cc_o = emit_collective(em, b, "AllToAll", b.ob_in, b.ob_out,
                           [("sd", em.val("sd"))])
    war = [("te", em.val("te")), ("ve", em.val("ve")), ("ae", em.val("ae"))]
    sd_o = em.dma("sync", b.Bb[:, :].rearrange("p (k t) -> p k t", t=TOK),
                  b.ob_out[:, :].rearrange("(k p) t -> p k t", p=P), "sd",
                  waits=[("cc", cc_o)] + war)
    wob = lambda m: b.Cc[:, 12288 + (m % 2) * 2048:12288 + (m % 2) * 2048 + 2048]
    cons = {}
    free_v = em.val("te")
    for m in range(DT):
        gd_v = em.dma("gpsimd", wob(m),
                      i["woutB_img"][:, m * 2048:(m + 1) * 2048], "gd",
                      waits=[("te", cons.get(m - 2, free_v))])
        for hh in range(2):
            ps, psn = (b.psA, "psA") if hh == 0 else (b.psB, "psB")
            for k in range(DT):
                em.op("tensor", lambda q, m=m, hh=hh, k=k, ps=ps: q.matmul(
                    ps[:, 0:512], wob(m)[:, k * P:(k + 1) * P],
                    b.Bb[:, k * TOK + hh * 512:k * TOK + (hh + 1) * 512],
                    start=(k == 0), stop=(k == DT - 1)),
                    waits=[("gd", gd_v), ("sd", sd_o)] +
                          (em.psw(psn) if k == 0 else []), inc="te")
            te_v = em.val("te")
            cons[m] = te_v
            em.op("vector", lambda q, m=m, hh=hh, ps=ps: q.scalar_tensor_tensor(
                b.xres[:, m * TOK + hh * 512:m * TOK + (hh + 1) * 512],
                ps[:, 0:512], sm[:, 48 + m:49 + m],
                b.xres[:, m * TOK + hh * 512:m * TOK + (hh + 1) * 512],
                ALU.add, ALU.add), waits=[("te", te_v)], inc="ve")
            em.psr(psn, "ve")



def build_program(in_arrs, parts=("a", "f1", "b", "f2"), ret_b=False,
                  debug_attn_a=False):
    nc = bass.Bass("TRN2", target_bir_lowering=False)
    b = make_b(nc, in_arrs)
    em = Em(nc)
    if debug_attn_a:
        b.d_oa = nc.dram_tensor("d_oa", [NCORES * 512, TOK], BF16,
                                kind="ExternalOutput").ap()
        b.d_qk = nc.dram_tensor("d_qk", [512, TALL], BF16,
                                kind="ExternalOutput").ap()
        b.d_v = nc.dram_tensor("d_v", [512, TALL], BF16,
                               kind="ExternalOutput").ap()
        b.d_z = nc.dram_tensor("d_z", [512, TALL], BF16,
                               kind="ExternalOutput").ap()
        b.d_rows = nc.dram_tensor("d_rows", [32, 4096], BF16,
                                  kind="ExternalOutput").ap()
        b.d_cols = nc.dram_tensor("d_cols", [32, 5120], BF16,
                                  kind="ExternalOutput").ap()
        b.d_misc = nc.dram_tensor("d_misc", [P, 4096], F32,
                                  kind="ExternalOutput").ap()
        b.d_misc2 = nc.dram_tensor("d_misc2", [P, 4096], F32,
                                   kind="ExternalOutput").ap()
    load_smalls(em, b)
    em.dma("gpsimd", b.gatew_bf[:, :], b.inp["gatew_img"][:, :], "gd")
    em.dma("gpsimd", b.d1_bf[:, :], b.inp["d1_img"][:, :], "gd")
    em.dma("gpsimd", b.d2_bf[:, :], b.inp["d2_img"][:, :], "gd")
    em.dma("gpsimd", b.agg_bf[:, :], b.inp["agg_img"][:, :], "gd")
    em.dma("gpsimd", b.bc_bf[:, :], b.inp["bc_img"][:, :], "gd")
    em.dma("gpsimd", b.d1b_bf[:, :], b.inp["d1b_row"][:, :], "gd")
    b.precast_v = em.val("gd")
    em.dma("sync", b.xres[:, :].rearrange("p (d t) -> p d t", t=TOK),
           b.inp["x_fm"][:, :].rearrange("(d p) t -> p d t", p=P), "sd")
    start_gate = [("sd", em.val("sd")), ("gd", em.val("gd"))]
    for e in ENGINES:
        em.op(e, lambda q: q.nop(), waits=start_gate)
    for part in parts:
        em.phase = "f" if part.startswith("f") else part
        if part == "a":
            emit_attn_a(em, b)
        elif part == "b":
            emit_attn_b(em, b)
        else:
            emit_ffn(em, b, part)
        em.phase = None
    ve_f = em.val("ve")
    em.dma("sync", b.out_fm[:, :].rearrange("(d p) t -> p d t", p=P),
           b.xres[:, :].rearrange("p (d t) -> p d t", t=TOK), "sd",
           waits=[("ve", ve_f)])
    if debug_attn_a:
        allw = [(n, v) for n, v in em.cnt.items() if v > 0]
        for (dst, src) in ((b.d_oa, b.oa_in), (b.d_qk, b.qk_scr),
                           (b.d_v, b.v_scr), (b.d_z, b.z_scr),
                           (b.d_rows, b.rows_scr), (b.d_cols, b.cols_scr)):
            em.dma("sync", dst[:, :], src[:, :], "sd", waits=allw)
    em.op("sync", lambda q: q.nop(), waits=[("sd", em.val("sd"))])
    em.replay()
    if ret_b:
        return nc, em, b
    return nc, em


def kernel(**inputs):
    maps = prep_inputs(inputs)
    nc, em = build_program(maps[0])
    res = run_bass_kernel_spmd(nc, maps, core_ids=list(range(NCORES)))
    em.close()
    shards = [np.asarray(r["out_fm"], dtype=np.float32).T for r in res.results]
    out = np.concatenate(shards, axis=0).reshape(2, S, D)
    return out
